# version 02
# baseline (speedup 1.0000x reference)
import numpy as np
import ml_dtypes
from contextlib import ExitStack

import concourse.bass as bass
import concourse.mybir as mybir
from concourse.bass_utils import run_bass_kernel_spmd

F32 = mybir.dt.float32
BF16 = mybir.dt.bfloat16
U8 = mybir.dt.uint8
AF = mybir.ActivationFunctionType
ALU = mybir.AluOpType
AX = mybir.AxisListType

NCORES = 8
D = 1024
T = 16384
TPC = 2048
NTC = 16
NSTRIPE = 16
NSTEP = 16
MAGIC = 12582912.0
SHIFT = -12.0
CLOSED = -30012.0
MASKNEG = -30000.0


class Ctr:
    def __init__(self, sem):
        self.sem = sem
        self.v = 0

    def inc(self, n=1):
        self.v += n
        return self.v


def wge(eng, ctr, v):
    if v > 0:
        eng.wait_ge(ctr.sem, v)


def emit_barrier(nc, block, S):
    bar = S["bar"]
    tgt = bar.v + 5
    for engname in ("tensor", "vector", "scalar", "gpsimd", "sync"):
        def _sec(eng, tgt=tgt):
            eng.drain().then_inc(bar.sem, 1)
            eng.wait_ge(bar.sem, tgt)
        getattr(block, engname)(_sec)
    bar.v = tgt


def _posblock_src(b):
    rank = b if b <= 7 else 15 - b
    sub = 0 if b <= 7 else 1
    return rank, sub


C_STAGE = 9
RACECHECK = False


def build_nc(n_phases=4, dbg=False):
    nc = bass.Bass("TRN2", target_bir_lowering=False,
                   detect_race_conditions=RACECHECK)

    x_in = nc.dram_tensor("x_in", [NTC, 128, D], F32, kind="ExternalInput").ap()
    btab_in = nc.dram_tensor("btab", [128, 2 * 8 * NSTEP], F32, kind="ExternalInput").ap()
    wq_in = nc.dram_tensor("w_q", [8, 128, D], F32, kind="ExternalInput").ap()
    wk_in = nc.dram_tensor("w_k", [8, 128, D], F32, kind="ExternalInput").ap()
    wv_in = nc.dram_tensor("w_v", [8, 128, D], F32, kind="ExternalInput").ap()
    k0a_in = nc.dram_tensor("k0a", [8, 128, 4096], U8, kind="ExternalInput").ap()
    k0b_in = nc.dram_tensor("k0b", [8, 128, 4096], U8, kind="ExternalInput").ap()
    k1_in = nc.dram_tensor("k1", [32, 128, 4096], U8, kind="ExternalInput").ap()
    k2_in = nc.dram_tensor("k2", [32, 128, 1024], U8, kind="ExternalInput").ap()
    s0_in = nc.dram_tensor("s0", [128, 1], F32, kind="ExternalInput").ap()
    s1_in = nc.dram_tensor("s1", [128, 1], F32, kind="ExternalInput").ap()
    s2_in = nc.dram_tensor("s2", [128, 1], F32, kind="ExternalInput").ap()
    wqkvT_in = nc.dram_tensor("wqkvT", [8, 128, 3072], F32, kind="ExternalInput").ap()
    wgT_in = nc.dram_tensor("wgT", [8, 128, 4096], F32, kind="ExternalInput").ap()
    wuT_in = nc.dram_tensor("wuT", [8, 128, 4096], F32, kind="ExternalInput").ap()
    wdT_in = nc.dram_tensor("wdT", [32, 128, 1024], F32, kind="ExternalInput").ap()
    y_out = nc.dram_tensor("y_out", [NTC, 128, D], F32, kind="ExternalOutput").ap()
    dbg_h = (nc.dram_tensor("dbg_h", [2, 128, 32, 512], F32, kind="ExternalOutput").ap()
             if dbg else None)
    dbg_w = (nc.dram_tensor("dbg_w", [2, 128, 8, 512], F32, kind="ExternalOutput").ap()
             if dbg else None)

    agin_k_h = nc.dram_tensor("agin_k", [8, 128, TPC], BF16)
    agin_v_h = nc.dram_tensor("agin_v", [NTC, 128, D], BF16)
    agout_k_h = nc.dram_tensor("agout_k", [NCORES * 8, 128, TPC], BF16)
    agout_v_h = nc.dram_tensor("agout_v", [NCORES * NTC, 128, D], BF16)
    arin_h = nc.dram_tensor("arin_kv", [64, 1040], F32)
    arout_h = nc.dram_tensor("arout_kv", [64, 1040], F32)
    w0c_h = nc.dram_tensor("w0c", [8, 128, 4096], BF16)
    w1c_h = nc.dram_tensor("w1c", [16, 32, 128, 256], BF16)
    w2c_h = nc.dram_tensor("w2c", [32, 128, 1024], BF16)
    wgc_h = nc.dram_tensor("wgc", [8, 128, 4096], BF16)
    wuc_h = nc.dram_tensor("wuc", [8, 128, 4096], BF16)
    wdc_h = nc.dram_tensor("wdc", [32, 128, 1024], BF16)
    dram = dict(
        agin_k=agin_k_h.ap(), agin_v=agin_v_h.ap(),
        agout_k=agout_k_h.ap(), agout_v=agout_v_h.ap(),
        arin=arin_h.ap(), arout=arout_h.ap(),
        w0c=w0c_h.ap(), w1c=w1c_h.ap(), w2c=w2c_h.ap(),
        wgc=wgc_h.ap(), wuc=wuc_h.ap(), wdc=wdc_h.ap(),
        agin_k_h=agin_k_h, agin_v_h=agin_v_h, agout_k_h=agout_k_h,
        agout_v_h=agout_v_h, arin_h=arin_h, arout_h=arout_h,
    )

    ident_d = nc.inline_tensor(np.eye(128, dtype=ml_dtypes.bfloat16), "identc")
    tri_np = np.where(np.arange(128)[None, :] <= np.arange(128)[:, None],
                      np.float32(0.0), np.float32(MASKNEG)).astype(np.float32)
    tri_d = nc.inline_tensor(tri_np, "tric")
    onebf_d = nc.inline_tensor(np.ones((128, 1), dtype=ml_dtypes.bfloat16), "onebfc")
    onesf_d = nc.inline_tensor(np.ones((128, 1), dtype=np.float32), "onesfc")
    onesrow_d = nc.inline_tensor(np.ones((1, 128), dtype=np.float32), "onesrowc")

    with ExitStack() as top:
        e = top.enter_context
        resid = e(nc.sbuf_tensor("resid", [128, NTC, D], F32))
        btab = e(nc.sbuf_tensor("btabs", [128, 2 * 8 * NSTEP], F32))
        ident = e(nc.sbuf_tensor("ident_s", [128, 128], BF16))
        tri = e(nc.sbuf_tensor("tri_s", [128, 128], F32))
        onebf = e(nc.sbuf_tensor("onebf_s", [128, 1], BF16))
        onesf = e(nc.sbuf_tensor("onesf_s", [128, 1], F32))
        onesrow = e(nc.sbuf_tensor("onesrow_s", [1, 128], F32))
        epsc = e(nc.sbuf_tensor("epsc", [128, 1], F32))
        nmagic = e(nc.sbuf_tensor("nmagic", [128, 1], F32))
        s0c = e(nc.sbuf_tensor("s0c", [128, 1], F32))
        s1c = e(nc.sbuf_tensor("s1c", [128, 1], F32))
        s2c = e(nc.sbuf_tensor("s2c", [128, 1], F32))

        S = {}
        for name in ("ld", "pv", "psc", "pd", "pd1", "pd2", "pd3", "pd4", "pst", "ptn",
                     "lnv", "lns", "lnt", "lnev",
                     "pjw", "pjt", "pjev", "pjst", "cc",
                     "kld0", "kld1", "vld0", "vld1",
                     "smm", "sexp", "stmm", "sptev", "spv", "svadd",
                     "vmask", "fin", "fv",
                     "fl00", "fl01", "fl10", "fl11", "fl20", "fl21", "ft", "fev", "fs",
                     "cld", "ct", "cev", "cv", "cs", "ccc", "cst",
                     "dg0", "dg1", "dw0", "dw1", "dt", "dev", "dv", "ds",
                     "st", "bar"):
            S[name] = Ctr(e(nc.semaphore("sem_" + name)))

        block = e(nc.Block())
        ld = S["ld"]

        @block.sync
        def _(sync):
            for tc in range(NTC):
                sync.dma_start(out=resid[:, tc, :], in_=x_in[tc]).then_inc(ld.sem, 16)
                ld.inc(16)
            for dst, src in ((btab, btab_in), (ident, ident_d.ap()),
                             (tri, tri_d.ap()), (onebf, onebf_d.ap()),
                             (onesf, onesf_d.ap()), (onesrow, onesrow_d.ap()),
                             (s0c, s0_in), (s1c, s1_in), (s2c, s2_in)):
                sync.dma_start(out=dst[:], in_=src).then_inc(ld.sem, 16)
                ld.inc(16)
        ld_all = ld.v

        @block.vector
        def _(vector):
            vector.memset(epsc[:], 1e-5)
            vector.memset(nmagic[:], -MAGIC)

        emit_weight_prep(nc, block, S, ld_all,
                         k0a_in, k0b_in, k1_in, k2_in, s0c, s1c, s2c,
                         wgT_in, wuT_in, wdT_in, dram, onesf, onesrow)

        emit_l0_attn(nc, block, S, ld_all, resid, btab, ident, tri, epsc,
                     wq_in, wk_in, wv_in, dram)

        if n_phases >= 2:
            emit_barrier(nc, block, S)
            emit_l0_ffn(nc, block, S, resid, ident, epsc, dram, dbg_h, dbg_w)
        if n_phases >= 3:
            emit_barrier(nc, block, S)
            emit_l1_attn(nc, block, S, resid, ident, epsc, onebf,
                         wqkvT_in, dram)
        if n_phases >= 4:
            emit_barrier(nc, block, S)
            emit_l1_ffn(nc, block, S, resid, ident, epsc, nmagic, dram, dbg_w, dbg_h)

        emit_barrier(nc, block, S)
        st = S["st"]

        @block.sync
        def _(sync):
            for tc in range(NTC):
                sync.dma_start(out=y_out[tc], in_=resid[:, tc, :]).then_inc(st.sem, 16)
                st.inc(16)
            sync.wait_ge(st.sem, st.v)

    return nc


def emit_ln(block, S, resid, tc, pp, scr, epsc, out_ap, first_wait=None,
            norm_waits=()):
    lnv, lns = S["lnv"], S["lns"]
    bnstat, mv, lnvv, rstd = scr

    @block.vector
    def _(vector):
        if first_wait is not None:
            sem, val = first_wait
            if val > 0:
                vector.wait_ge(sem, val)
        wge(vector, lns, lns.v)
        vector.bn_stats(out=bnstat[:, pp, 0:6], in_=resid[:, tc, 0:512])
        vector.bn_stats(out=bnstat[:, pp, 6:12], in_=resid[:, tc, 512:1024]).then_inc(lnv.sem, 1)
        lnv.inc()
        vector.wait_ge(lnv.sem, lnv.v)
        vector.bn_aggr(out=mv[:, pp, :], in_=bnstat[:, pp, :]).then_inc(lnv.sem, 1)
        lnv.inc()

    @block.scalar
    def _(scalar):
        scalar.wait_ge(lnv.sem, lnv.v)
        scalar.activation(out=lnvv[:, pp, :], in_=mv[:, pp, 1:2], func=AF.Ln,
                          bias=epsc[:, 0:1], scale=1.0)
        scalar.activation(out=rstd[:, pp, :], in_=lnvv[:, pp, :], func=AF.Exp,
                          scale=-0.5).then_inc(lns.sem, 1)
        lns.inc()

    @block.vector
    def _(vector):
        vector.wait_ge(lns.sem, lns.v)
        for sem, val in norm_waits:
            if val > 0:
                vector.wait_ge(sem, val)
        vector.tensor_scalar(out=out_ap, in0=resid[:, tc, :],
                             scalar1=mv[:, pp, 0:1], scalar2=rstd[:, pp, 0:1],
                             op0=ALU.subtract, op1=ALU.mult).then_inc(lnv.sem, 1)
        lnv.inc()


def _ln_scratch(nc, e, tag):
    return (e(nc.sbuf_tensor("bnstat" + tag, [128, 2, 12], F32)),
            e(nc.sbuf_tensor("mv" + tag, [128, 2, 2], F32)),
            e(nc.sbuf_tensor("lnvv" + tag, [128, 2, 1], F32)),
            e(nc.sbuf_tensor("rstd" + tag, [128, 2, 1], F32)))


def emit_transpose_1024(block, S, src_ap, ptT, pp, dst_ap, ident,
                        tensor_waits=(), vector_waits=(), ping=True):
    lnt, lnev = S["lnt"], S["lnev"]

    @block.tensor
    def _(tensor):
        for sem, val in tensor_waits:
            if val > 0:
                tensor.wait_ge(sem, val)
        wge(tensor, lnev, lnev.v - (1 if ping else 0))
        ins = None
        for ic in range(8):
            ins = tensor.transpose(ptT[:, pp, ic * 128:(ic + 1) * 128],
                                   src_ap[:, ic * 128:(ic + 1) * 128],
                                   ident[:])
        ins.then_inc(lnt.sem, 1)
        lnt.inc()

    @block.vector
    def _(vector):
        vector.wait_ge(lnt.sem, lnt.v)
        for sem, val in vector_waits:
            if val > 0:
                vector.wait_ge(sem, val)
        src2 = ptT[:, pp, :]
        if len(dst_ap.shape) == 3:
            src2 = src2.rearrange("p (a b) -> p a b", a=dst_ap.shape[1])
        vector.tensor_copy(out=dst_ap, in_=src2).then_inc(lnev.sem, 1)
        lnev.inc()


def emit_weight_prep(nc, block, S, ld_all,
                     k0a_in, k0b_in, k1_in, k2_in, s0c, s1c, s2c,
                     wgT_in, wuT_in, wdT_in, dram, onesf, onesrow):
    pv, psc, pd, pst, ptn = S["pv"], S["psc"], S["pd"], S["pst"], S["ptn"]
    pd1, pd2, pd3, pd4 = S["pd1"], S["pd2"], S["pd3"], S["pd4"]
    w0c, w1c, w2c = dram["w0c"], dram["w1c"], dram["w2c"]
    wgc, wuc, wdc = dram["wgc"], dram["wuc"], dram["wdc"]

    with ExitStack() as ctx:
        e = ctx.enter_context
        au8 = e(nc.sbuf_tensor("au8", [128, 2, 4096], U8))
        bu8 = e(nc.sbuf_tensor("bu8", [128, 2, 4096], U8))
        wtmp = e(nc.sbuf_tensor("wtmp", [128, 2, 4096], F32))
        wbf = e(nc.sbuf_tensor("wbf", [128, 2, 4096], BF16))
        wf32 = e(nc.sbuf_tensor("wf32", [128, 4096], F32))
        cols = e(nc.sbuf_tensor("pcols", [128, 8], F32))
        absacc = e(nc.sbuf_tensor("absacc", [128, 3], F32))
        redc = e(nc.sbuf_tensor("redc", [128, 1], F32))
        scl = e(nc.sbuf_tensor("scl", [128, 2, 3], F32))
        tot = e(nc.sbuf_tensor("tot", [1, 3], F32))
        pbc = e(nc.psum_tensor("pbc", [128, 8], F32))

        @block.vector
        def _(vector):
            vector.wait_ge(S["ld"].sem, ld_all)
            vector.tensor_scalar_mul(out=cols[:, 0:1], in0=s0c[:], scalar1=2.0)
            vector.tensor_scalar_mul(out=cols[:, 1:2], in0=s1c[:], scalar1=4.0)
            vector.tensor_scalar_mul(out=cols[:, 2:3], in0=s1c[:], scalar1=-2.0)
            vector.tensor_scalar_mul(out=cols[:, 3:4], in0=s2c[:], scalar1=4.0)
            vector.tensor_scalar_mul(out=cols[:, 4:5], in0=s2c[:], scalar1=-2.0).then_inc(pv.sem, 1)
            pv.inc()
        scales_ready = pv.v

        for it in range(4):
            @block.gpsimd
            def _(g):
                wge(g, pv, pv.v)
                g.dma_start(out=au8[:, 0, :], in_=k0a_in[2 * it]).then_inc(pd.sem, 16)
                g.dma_start(out=au8[:, 1, :], in_=k0a_in[2 * it + 1]).then_inc(pd.sem, 16)
                g.dma_start(out=bu8[:, 0, :], in_=k0b_in[2 * it]).then_inc(pd.sem, 16)
                g.dma_start(out=bu8[:, 1, :], in_=k0b_in[2 * it + 1]).then_inc(pd.sem, 16)
                pd.inc(64)

            @block.vector
            def _(vector):
                vector.wait_ge(pd.sem, pd.v)
                vector.wait_ge(pv.sem, scales_ready)
                wge(vector, pst, pst.v)
                vector.tensor_sub(out=wtmp[:, :, :], in0=au8[:, :, :],
                                  in1=bu8[:, :, :]).then_inc(pv.sem, 1)
                pv.inc()
                vector.wait_ge(pv.sem, pv.v)
                vector.tensor_scalar_mul(out=wbf[:, :, :], in0=wtmp[:, :, :],
                                         scalar1=cols[:, 0:1]).then_inc(pv.sem, 1)
                pv.inc()

            @block.sync
            def _(sync):
                sync.wait_ge(pv.sem, pv.v)
                sync.dma_start(out=w0c[2 * it], in_=wbf[:, 0, :]).then_inc(pst.sem, 16)
                sync.dma_start(out=w0c[2 * it + 1], in_=wbf[:, 1, :]).then_inc(pst.sem, 16)
                pst.inc(32)

        for it in range(16):
            @block.gpsimd
            def _(g):
                if it == 0:
                    wge(g, pv, pv.v)
                wge(g, psc, psc.v)
                g.dma_start(out=au8[:, 0, :], in_=k1_in[2 * it]).then_inc(pd1.sem, 16)
                g.dma_start(out=au8[:, 1, :], in_=k1_in[2 * it + 1]).then_inc(pd1.sem, 16)
                pd1.inc(32)

            @block.scalar
            def _(scalar):
                scalar.wait_ge(pd1.sem, pd1.v)
                scalar.wait_ge(pv.sem, scales_ready)
                wge(scalar, pst, pst.v)
                scalar.activation(out=wbf[:, :, :], in_=au8[:, :, :], func=AF.Identity,
                                  scale=cols[:, 1:2], bias=cols[:, 2:3]).then_inc(psc.sem, 1)
                psc.inc()

            @block.sync
            def _(sync):
                sync.wait_ge(psc.sem, psc.v)
                for g16 in range(16):
                    sync.dma_start(out=w1c[g16, 2 * it],
                                   in_=wbf[:, 0, g16 * 256:(g16 + 1) * 256]).then_inc(pst.sem, 16)
                    sync.dma_start(out=w1c[g16, 2 * it + 1],
                                   in_=wbf[:, 1, g16 * 256:(g16 + 1) * 256]).then_inc(pst.sem, 16)
                    pst.inc(32)

        for it in range(8):
            @block.gpsimd
            def _(g):
                wge(g, psc, psc.v)
                for j in range(4):
                    g.dma_start(out=au8[:, 0, j * 1024:(j + 1) * 1024],
                                in_=k2_in[4 * it + j]).then_inc(pd2.sem, 16)
                    pd2.inc(16)

            @block.scalar
            def _(scalar):
                scalar.wait_ge(pd2.sem, pd2.v)
                wge(scalar, pst, pst.v)
                scalar.activation(out=wbf[:, 0, :], in_=au8[:, 0, :], func=AF.Identity,
                                  scale=cols[:, 3:4], bias=cols[:, 4:5]).then_inc(psc.sem, 1)
                psc.inc()

            @block.sync
            def _(sync):
                sync.wait_ge(psc.sem, psc.v)
                for j in range(4):
                    sync.dma_start(out=w2c[4 * it + j],
                                   in_=wbf[:, 0, j * 1024:(j + 1) * 1024]).then_inc(pst.sem, 16)
                    pst.inc(16)

        specs = [(wgT_in, wgc, 8, 4096, 0), (wuT_in, wuc, 8, 4096, 1),
                 (wdT_in, wdc, 32, 1024, 2)]
        for src, dst, nic, F, mi in specs:
            for ic in range(nic):
                @block.gpsimd
                def _(g):
                    wge(g, pv, pv.v)
                    g.dma_start(out=wf32[:, 0:F], in_=src[ic]).then_inc(pd3.sem, 16)
                    pd3.inc(16)

                @block.vector
                def _(vector):
                    vector.wait_ge(pd3.sem, pd3.v)
                    vector.reduce_sum(out=redc[:], in_=wf32[:, 0:F], axis=AX.X,
                                      apply_absolute_value=True).then_inc(pv.sem, 1)
                    pv.inc()
                    vector.wait_ge(pv.sem, pv.v)
                    if ic == 0:
                        vector.tensor_copy(out=absacc[:, mi:mi + 1],
                                           in_=redc[:]).then_inc(pv.sem, 1)
                    else:
                        vector.tensor_add(out=absacc[:, mi:mi + 1],
                                          in0=absacc[:, mi:mi + 1],
                                          in1=redc[:]).then_inc(pv.sem, 1)
                    pv.inc()

        @block.tensor
        def _(tensor):
            tensor.wait_ge(pv.sem, pv.v)
            tensor.matmul(pbc[0:1, 4:7], onesf[:, 0:1], absacc[:, 0:3],
                          start=True, stop=True).then_inc(ptn.sem, 1)
            ptn.inc()

        @block.vector
        def _(vector):
            vector.wait_ge(ptn.sem, ptn.v)
            vector.tensor_copy(out=tot[0:1, 0:3], in_=pbc[0:1, 4:7]).then_inc(pv.sem, 1)
            pv.inc()

        @block.tensor
        def _(tensor):
            tensor.wait_ge(pv.sem, pv.v)
            tensor.matmul(pbc[:, 0:3], onesrow[0:1, :],
                          tot[0:1, 0:3], start=True, stop=True).then_inc(ptn.sem, 1)
            ptn.inc()

        @block.vector
        def _(vector):
            vector.wait_ge(ptn.sem, ptn.v)
            for mi, numel in ((0, 8 * 128 * 4096), (1, 8 * 128 * 4096),
                              (2, 32 * 128 * 1024)):
                vector.tensor_scalar(out=scl[:, 0, mi:mi + 1], in0=pbc[:, mi:mi + 1],
                                     scalar1=1.0 / numel, scalar2=1e-5,
                                     op0=ALU.mult, op1=ALU.max).then_inc(pv.sem, 1)
                pv.inc()
            vector.wait_ge(pv.sem, pv.v)
            vector.reciprocal(out=scl[:, 1, 0:1], in_=scl[:, 0, 0:1])
            vector.reciprocal(out=scl[:, 1, 1:2], in_=scl[:, 0, 1:2])
            vector.reciprocal(out=scl[:, 1, 2:3], in_=scl[:, 0, 2:3]).then_inc(pv.sem, 1)
            pv.inc()
        qready = pv.v

        for src, dst, nic, F, mi in specs:
            for ic in range(nic):
                @block.gpsimd
                def _(g):
                    wge(g, pv, pv.v)
                    g.dma_start(out=wf32[:, 0:F], in_=src[ic]).then_inc(pd4.sem, 16)
                    pd4.inc(16)

                @block.vector
                def _(vector):
                    vector.wait_ge(pd4.sem, pd4.v)
                    vector.wait_ge(pv.sem, qready)
                    wge(vector, pst, pst.v)
                    vector.tensor_scalar(out=wtmp[:, 0, 0:F], in0=wf32[:, 0:F],
                                         scalar1=scl[:, 1, mi:mi + 1], scalar2=1.4999999,
                                         op0=ALU.mult, op1=ALU.min).then_inc(pv.sem, 1)
                    pv.inc()
                    vector.wait_ge(pv.sem, pv.v)
                    vector.tensor_scalar(out=wtmp[:, 1, 0:F], in0=wtmp[:, 0, 0:F],
                                         scalar1=-1.4999999, scalar2=MAGIC,
                                         op0=ALU.max, op1=ALU.add).then_inc(pv.sem, 1)
                    pv.inc()
                    vector.wait_ge(pv.sem, pv.v)
                    vector.tensor_scalar(out=wbf[:, 0, 0:F], in0=wtmp[:, 1, 0:F],
                                         scalar1=-MAGIC, scalar2=scl[:, 0, mi:mi + 1],
                                         op0=ALU.add, op1=ALU.mult).then_inc(pv.sem, 1)
                    pv.inc()

                @block.sync
                def _(sync):
                    sync.wait_ge(pv.sem, pv.v)
                    sync.dma_start(out=dst[ic], in_=wbf[:, 0, 0:F]).then_inc(pst.sem, 16)
                    pst.inc(16)

        @block.sync
        def _(sync):
            sync.wait_ge(pst.sem, pst.v)

        @block.gpsimd
        def _(g):
            g.wait_ge(pd.sem, pd.v)
            g.wait_ge(pd1.sem, pd1.v)
            g.wait_ge(pd2.sem, pd2.v)
            g.wait_ge(pd3.sem, pd3.v)
            g.wait_ge(pd4.sem, pd4.v)


def emit_l0_attn(nc, block, S, ld_all, resid, btab, ident, tri, epsc,
                 wq_in, wk_in, wv_in, dram):
    lnv = S["lnv"]
    pjw, pjt, pjev, pjst, cc = S["pjw"], S["pjt"], S["pjev"], S["pjst"], S["cc"]
    agin_k, agin_v = dram["agin_k"], dram["agin_v"]
    agout_k, agout_v = dram["agout_k"], dram["agout_v"]

    with ExitStack() as actx:
        e = actx.enter_context
        qfm = e(nc.sbuf_tensor("qfm", [128, 8, TPC], BF16))
        osb = e(nc.sbuf_tensor("osb", [128, 8, D], F32))
        lbuf = e(nc.sbuf_tensor("lbuf", [128, 8, NSTEP], F32))
        lrec = e(nc.sbuf_tensor("lrec", [128, 8], F32))

        with ExitStack() as ctx:
            e2 = ctx.enter_context
            lnb = e2(nc.sbuf_tensor("lnbA", [128, 2, D], BF16))
            lnT = e2(nc.sbuf_tensor("lnTA", [128, 8, TPC], BF16))
            wsb = e2(nc.sbuf_tensor("wsbA", [128, 8, D], BF16))
            scr = _ln_scratch(nc, e2, "A")
            stg = e2(nc.sbuf_tensor("stgA", [128, 2, 512], BF16))
            ptT = e2(nc.psum_tensor("ptTA", [128, 2, 1024], BF16))
            pj = e2(nc.psum_tensor("pjA", [128, 2, 512], F32))

            for tc in range(NTC):
                pp = tc % 2
                emit_ln(block, S, resid, tc, pp, scr, epsc, lnb[:, pp, :],
                        first_wait=(S["ld"].sem, ld_all if tc == 0 else 0),
                        norm_waits=((S["lnt"].sem, S["lnt"].v - 1),))
                emit_transpose_1024(
                    block, S, lnb[:, pp, :], ptT, pp,
                    lnT[:, :, tc * 128:(tc + 1) * 128],
                    ident, tensor_waits=((lnv.sem, lnv.v),))
            lnT_done = S["lnev"].v

            @block.gpsimd
            def _(g):
                for ic in range(8):
                    g.dma_start(out=wsb[:, ic, :], in_=wq_in[ic]).then_inc(pjw.sem, 16)
                    pjw.inc(16)

            nmm = 0
            for oc in range(8):
                for ns in range(4):
                    pp = nmm % 2

                    @block.tensor
                    def _(tensor):
                        if nmm == 0:
                            tensor.wait_ge(pjw.sem, pjw.v)
                            tensor.wait_ge(S["lnev"].sem, lnT_done)
                        wge(tensor, pjev, nmm - 1)
                        ins = None
                        for ic in range(8):
                            ins = tensor.matmul(pj[:, pp, :],
                                                wsb[:, ic, oc * 128:(oc + 1) * 128],
                                                lnT[:, ic, ns * 512:(ns + 1) * 512],
                                                start=(ic == 0), stop=(ic == 7))
                        ins.then_inc(pjt.sem, 1)
                        pjt.inc()

                    @block.vector
                    def _(vector):
                        vector.wait_ge(pjt.sem, pjt.v)
                        vector.tensor_scalar_mul(out=qfm[:, oc, ns * 512:(ns + 1) * 512],
                                                 in0=pj[:, pp, :],
                                                 scalar1=0.03125).then_inc(pjev.sem, 1)
                        pjev.inc()
                    nmm += 1
            q_done = pjt.v

            @block.gpsimd
            def _(g):
                g.wait_ge(pjt.sem, q_done)
                for ic in range(8):
                    g.dma_start(out=wsb[:, ic, :], in_=wk_in[ic]).then_inc(pjw.sem, 16)
                    pjw.inc(16)
            kw_ready = pjw.v

            nst = 0
            for oc in range(8):
                for ns in range(4):
                    pp = nmm % 2
                    sp = nst % 2

                    @block.tensor
                    def _(tensor):
                        if nmm == 32:
                            tensor.wait_ge(pjw.sem, kw_ready)
                        wge(tensor, pjev, nmm - 1)
                        ins = None
                        for ic in range(8):
                            ins = tensor.matmul(pj[:, pp, :],
                                                wsb[:, ic, oc * 128:(oc + 1) * 128],
                                                lnT[:, ic, ns * 512:(ns + 1) * 512],
                                                start=(ic == 0), stop=(ic == 7))
                        ins.then_inc(pjt.sem, 1)
                        pjt.inc()

                    @block.vector
                    def _(vector):
                        vector.wait_ge(pjt.sem, pjt.v)
                        wge(vector, pjst, (nst - 1) * 16)
                        vector.tensor_copy(out=stg[:, sp, :],
                                           in_=pj[:, pp, :]).then_inc(pjev.sem, 1)
                        pjev.inc()

                    @block.sync
                    def _(sync):
                        sync.wait_ge(pjev.sem, pjev.v)
                        sync.dma_start(out=agin_k[oc][:, ns * 512:(ns + 1) * 512],
                                       in_=stg[:, sp, :]).then_inc(pjst.sem, 16)
                        pjst.inc(16)
                    nmm += 1
                    nst += 1
            k_done = pjt.v

            @block.gpsimd
            def _(g):
                g.wait_ge(pjt.sem, k_done)
                for ic in range(8):
                    g.dma_start(out=wsb[:, ic, :], in_=wv_in[ic]).then_inc(pjw.sem, 16)
                    pjw.inc(16)
            vw_ready = pjw.v

            for tc in range(NTC):
                for nh in range(2):
                    pp = nmm % 2
                    sp = nst % 2

                    @block.tensor
                    def _(tensor):
                        if nmm == 64:
                            tensor.wait_ge(pjw.sem, vw_ready)
                        wge(tensor, pjev, nmm - 1)
                        ins = None
                        for ic in range(8):
                            ins = tensor.matmul(pj[:, pp, :],
                                                lnT[:, ic, tc * 128:(tc + 1) * 128],
                                                wsb[:, ic, nh * 512:(nh + 1) * 512],
                                                start=(ic == 0), stop=(ic == 7))
                        ins.then_inc(pjt.sem, 1)
                        pjt.inc()

                    @block.vector
                    def _(vector):
                        vector.wait_ge(pjt.sem, pjt.v)
                        wge(vector, pjst, (nst - 1) * 16)
                        vector.tensor_copy(out=stg[:, sp, :],
                                           in_=pj[:, pp, :]).then_inc(pjev.sem, 1)
                        pjev.inc()

                    @block.sync
                    def _(sync):
                        sync.wait_ge(pjev.sem, pjev.v)
                        sync.dma_start(out=agin_v[tc][:, nh * 512:(nh + 1) * 512],
                                       in_=stg[:, sp, :]).then_inc(pjst.sem, 16)
                        pjst.inc(16)
                    nmm += 1
                    nst += 1
            agin_done = pjst.v

            @block.gpsimd
            def _(g):
                g.wait_ge(pjst.sem, agin_done)
                g.collective_compute("AllGather", ALU.bypass,
                                     replica_groups=[list(range(NCORES))],
                                     ins=[dram["agin_k_h"].ap().opt()],
                                     outs=[dram["agout_k_h"].ap().opt()]).then_inc(cc.sem, 1)
                cc.inc()
                g.collective_compute("AllGather", ALU.bypass,
                                     replica_groups=[list(range(NCORES))],
                                     ins=[dram["agin_v_h"].ap().opt()],
                                     outs=[dram["agout_v_h"].ap().opt()]).then_inc(cc.sem, 1)
                cc.inc()
            ag_done = cc.v

        with ExitStack() as ctx:
            e2 = ctx.enter_context
            kt = e2(nc.sbuf_tensor("ktile", [128, 2, 8, 1024], BF16))
            vt = e2(nc.sbuf_tensor("vtile", [128, 2, 8, 1024], BF16))
            pbuf = e2(nc.sbuf_tensor("pbuf", [128, 2, 1024], BF16))
            ptsb = e2(nc.sbuf_tensor("ptsb", [128, 2, 1024], BF16))
            spsum = e2(nc.psum_tensor("spsum", [128, 2, 1024], F32))
            ptp = e2(nc.psum_tensor("ptp", [128, 2, 1024], BF16))
            pvp = e2(nc.psum_tensor("pvp", [128, 1024], F32))

            kldp = [S["kld0"], S["kld1"]]
            vldp = [S["vld0"], S["vld1"]]
            smm, sexp = S["smm"], S["sexp"]
            stmm, sptev, spv = S["stmm"], S["sptev"], S["spv"]
            svadd, vmask, fin = S["svadd"], S["vmask"], S["fin"]

            def emit_transp(c):
                qs, sp = c["qs"], c["sp"]

                @block.tensor
                def _(tensor):
                    tensor.wait_ge(sexp.sem, qs + 1)
                    wge(tensor, sptev, qs - 1)
                    ins = None
                    for kc in range(8):
                        ins = tensor.transpose(ptp[:, sp, kc * 128:(kc + 1) * 128],
                                               pbuf[:, sp, kc * 128:(kc + 1) * 128],
                                               ident[:])
                    ins.then_inc(stmm.sem, 1)
                    stmm.inc()

                @block.vector
                def _(vector):
                    vector.wait_ge(stmm.sem, qs + 1)
                    wge(vector, spv, qs - 1)
                    vector.tensor_copy(out=ptsb[:, sp, :],
                                       in_=ptp[:, sp, :]).then_inc(sptev.sem, 1)
                    sptev.inc()

            def emit_exps(specs):
                @block.scalar
                def _(scalar):
                    for qs, sp, r, j, s_idx, vm in specs:
                        scalar.wait_ge(smm.sem, qs + 1)
                        if vm is not None:
                            scalar.wait_ge(vmask.sem, vm)
                        if s_idx == 1 and j == 0 and r == 0:
                            scalar.wait_ge(fin.sem, 1)
                        bcol = (s_idx * 8 + r) * NSTEP + j
                        scalar.activation(out=pbuf[:, sp, :], in_=spsum[:, sp, :],
                                          func=AF.Exp, bias=btab[:, bcol:bcol + 1],
                                          scale=1.0,
                                          accum_out=lbuf[:, r, j:j + 1]).then_inc(sexp.sem, 1)

            def emit_pv(c):
                qs, sp, ring, r, j = c["qs"], c["sp"], c["ring"], c["r"], c["j"]

                @block.tensor
                def _(tensor):
                    tensor.wait_ge(sptev.sem, qs + 1)
                    wge(tensor, svadd, qs)
                    ins = None
                    for kc in range(8):
                        for h in range(2):
                            ins = tensor.matmul(
                                pvp[:, h * 512:(h + 1) * 512],
                                ptsb[:, sp, kc * 128:(kc + 1) * 128],
                                vt[:, ring, kc, h * 512:(h + 1) * 512],
                                start=(kc == 0), stop=(kc == 7))
                    ins.then_inc(spv.sem, 1)
                    spv.inc()

                @block.vector
                def _(vector):
                    vector.wait_ge(spv.sem, qs + 1)
                    if j == 0:
                        vector.tensor_copy(out=osb[:, r, :], in_=pvp[:, :]).then_inc(svadd.sem, 1)
                    else:
                        vector.tensor_add(out=osb[:, r, :], in0=osb[:, r, :],
                                          in1=pvp[:, :]).then_inc(svadd.sem, 1)
                    svadd.inc()

            for s_idx in range(2):
                pend = []
                for j in range(NSTEP):
                    exp_specs = []
                    g = s_idx * NSTEP + j
                    ring = g % 2

                    @block.sync
                    def _(sync):
                        if g == 0:
                            sync.wait_ge(pjst.sem, agin_done)
                        if g == 1:
                            sync.wait_ge(cc.sem, ag_done)
                        wge(sync, smm, 8 * (g - 1))
                        if j == 0:
                            for ic in range(8):
                                sync.dma_start(
                                    out=kt[:, ring, ic, :],
                                    in_=agin_k[ic][:, s_idx * 1024:(s_idx + 1) * 1024],
                                ).then_inc(kldp[ring].sem, 16)
                                kldp[ring].inc(16)
                        else:
                            rank, sub = _posblock_src(j - 1)
                            for ic in range(8):
                                sync.dma_start(
                                    out=kt[:, ring, ic, :],
                                    in_=agout_k[8 * rank + ic][:, sub * 1024:(sub + 1) * 1024],
                                ).then_inc(kldp[ring].sem, 16)
                                kldp[ring].inc(16)

                    @block.gpsimd
                    def _(gq):
                        if g == 0:
                            gq.wait_ge(pjst.sem, agin_done)
                        if g == 1:
                            gq.wait_ge(cc.sem, ag_done)
                        wge(gq, spv, 8 * (g - 1))
                        if j == 0:
                            for kc in range(8):
                                gq.dma_start(out=vt[:, ring, kc, :],
                                             in_=agin_v[s_idx * 8 + kc]).then_inc(vldp[ring].sem, 16)
                                vldp[ring].inc(16)
                        else:
                            rank, sub = _posblock_src(j - 1)
                            for kc in range(8):
                                gq.dma_start(
                                    out=vt[:, ring, kc, :],
                                    in_=agout_v[NTC * rank + sub * 8 + kc]).then_inc(vldp[ring].sem, 16)
                                vldp[ring].inc(16)
                    k_tgt, v_tgt = kldp[ring].v, vldp[ring].v

                    for r in range(8):
                        qs = g * 8 + r
                        sp = qs % 2
                        qcol0 = s_idx * 1024

                        @block.tensor
                        def _(tensor):
                            if r == 0:
                                tensor.wait_ge(kldp[ring].sem, k_tgt)
                                tensor.wait_ge(vldp[ring].sem, v_tgt)
                            wge(tensor, sexp, qs - 1)
                            ins = None
                            for ic in range(8):
                                for h in range(2):
                                    ins = tensor.matmul(
                                        spsum[:, sp, h * 512:(h + 1) * 512],
                                        qfm[:, ic, qcol0 + r * 128:qcol0 + (r + 1) * 128],
                                        kt[:, ring, ic, h * 512:(h + 1) * 512],
                                        start=(ic == 0), stop=(ic == 7))
                            ins.then_inc(smm.sem, 1)
                            smm.inc()

                        if j == 0:
                            @block.vector
                            def _(vector):
                                vector.wait_ge(smm.sem, qs + 1)
                                ins = vector.tensor_add(
                                    out=spsum[:, sp, r * 128:(r + 1) * 128],
                                    in0=spsum[:, sp, r * 128:(r + 1) * 128],
                                    in1=tri[:])
                                if r < 7:
                                    ins = vector.tensor_scalar_add(
                                        out=spsum[:, sp, (r + 1) * 128:1024],
                                        in0=spsum[:, sp, (r + 1) * 128:1024],
                                        scalar1=MASKNEG)
                                ins.then_inc(vmask.sem, 1)
                                vmask.inc()

                        exp_specs.append((qs, sp, r, j, s_idx,
                                          vmask.v if j == 0 else None))
                        sexp.inc()

                        ctxq = dict(qs=qs, sp=sp, ring=ring, r=r, j=j)
                        pend.append(ctxq)
                        i_local = len(pend) - 1
                        if i_local >= 1:
                            emit_transp(pend[i_local - 1])
                        if i_local >= 2:
                            emit_pv(pend[i_local - 2])
                    emit_exps(exp_specs)

                emit_transp(pend[-1])
                emit_pv(pend[-2])
                emit_pv(pend[-1])
                pend.clear()

                @block.vector
                def _(vector):
                    vector.wait_ge(svadd.sem, svadd.v)
                    vector.reduce_sum(out=lrec[:, 0:8].rearrange("p a -> p a ()"),
                                      in_=lbuf[:, :, :], axis=AX.X).then_inc(S["fv"].sem, 1)
                    S["fv"].inc()
                    vector.wait_ge(S["fv"].sem, S["fv"].v)
                    vector.reciprocal(out=lrec[:, 0:8], in_=lrec[:, 0:8]).then_inc(S["fv"].sem, 1)
                    S["fv"].inc()
                    vector.wait_ge(S["fv"].sem, S["fv"].v)
                    ins = None
                    for r in range(8):
                        tc = s_idx * 8 + r
                        ins = vector.scalar_tensor_tensor(
                            out=resid[:, tc, :], in0=osb[:, r, :],
                            scalar=lrec[:, r:r + 1], in1=resid[:, tc, :],
                            op0=ALU.mult, op1=ALU.add)
                    ins.then_inc(fin.sem, 1)
                    fin.inc()


def emit_l0_ffn(nc, block, S, resid, ident, epsc, dram, dbg_h=None, dbg_w=None):
    fl0 = [S["fl00"], S["fl01"]]
    fl1 = [S["fl10"], S["fl11"]]
    fl2 = [S["fl20"], S["fl21"]]
    ft, fev, fv = S["ft"], S["fev"], S["fv"]
    lnv = S["lnv"]
    w0c, w1c, w2c = dram["w0c"], dram["w1c"], dram["w2c"]

    with ExitStack() as ctx:
        e = ctx.enter_context
        lnb = e(nc.sbuf_tensor("lnbB", [128, 2, D], BF16))
        lnS = e(nc.sbuf_tensor("lnSB", [128, 8, 512], BF16))
        h1 = e(nc.sbuf_tensor("h1B", [128, 32, 512], BF16))
        h2 = e(nc.sbuf_tensor("h2B", [128, 32, 512], BF16))
        w0b = e(nc.sbuf_tensor("w0bB", [128, 2, 8, 512], BF16))
        w1b = e(nc.sbuf_tensor("w1bB", [128, 2, 32, 256], BF16))
        w2b = e(nc.sbuf_tensor("w2bB", [128, 2, 1024], BF16))
        scr = _ln_scratch(nc, e, "B")
        ptT = e(nc.psum_tensor("ptTB", [128, 2, 1024], BF16))
        hp = e(nc.psum_tensor("hpB", [128, 2, 512], F32))
        h3p = e(nc.psum_tensor("h3pB", [128, 2, 1024], F32))

        for t in range(4):
            for i in range(4):
                tc = t * 4 + i
                pp = i % 2
                emit_ln(block, S, resid, tc, pp, scr, epsc, lnb[:, pp, :],
                        norm_waits=((S["lnt"].sem, S["lnt"].v - 1),))
                emit_transpose_1024(
                    block, S, lnb[:, pp, :], ptT, pp,
                    lnS[:, :, i * 128:(i + 1) * 128],
                    ident, tensor_waits=((lnv.sem, lnv.v),),
                    vector_waits=((ft.sem, ft.v),))
            lnS_done = S["lnev"].v

            for f1g in range(8):
                gp = f1g % 2

                @block.gpsimd
                def _(g):
                    wge(g, ft, ft.v - 4 if (t > 0 or f1g >= 2) else 0)
                    g.dma_start(out=w0b[:, gp, :, :],
                                in_=w0c[:, :, f1g * 512:(f1g + 1) * 512]
                                .rearrange("a p b -> p a b")).then_inc(fl0[gp].sem, 16)
                    fl0[gp].inc(16)

                for lf in range(4):
                    f1c = f1g * 4 + lf
                    pp = f1c % 2

                    @block.tensor
                    def _(tensor):
                        if lf == 0:
                            tensor.wait_ge(fl0[gp].sem, fl0[gp].v)
                        if f1c == 0:
                            tensor.wait_ge(S["lnev"].sem, lnS_done)
                        wge(tensor, fev, fev.v - 1)
                        ins = None
                        for ic in range(8):
                            ins = tensor.matmul(hp[:, pp, :],
                                                w0b[:, gp, ic, lf * 128:(lf + 1) * 128],
                                                lnS[:, ic, :],
                                                start=(ic == 0), stop=(ic == 7))
                        ins.then_inc(ft.sem, 1)
                        ft.inc()

                    @block.vector
                    def _(vector):
                        vector.wait_ge(ft.sem, ft.v)
                        vector.tensor_relu(out=h1[:, f1c, :],
                                           in_=hp[:, pp, :]).then_inc(fev.sem, 1)
                        fev.inc()

            for f2g in range(16):
                gp = f2g % 2

                @block.gpsimd
                def _(g):
                    wge(g, ft, ft.v - 2 if (t > 0 or f2g >= 2) else 0)
                    g.dma_start(out=w1b[:, gp, :, :],
                                in_=w1c[f2g].rearrange("a p b -> p a b")).then_inc(fl1[gp].sem, 16)
                    fl1[gp].inc(16)

                for lf in range(2):
                    f2c = f2g * 2 + lf
                    pp = f2c % 2

                    @block.tensor
                    def _(tensor):
                        if lf == 0:
                            tensor.wait_ge(fl1[gp].sem, fl1[gp].v)
                        wge(tensor, fev, fev.v - 1)
                        ins = None
                        for f1c in range(32):
                            ins = tensor.matmul(hp[:, pp, :],
                                                w1b[:, gp, f1c, lf * 128:(lf + 1) * 128],
                                                h1[:, f1c, :],
                                                start=(f1c == 0), stop=(f1c == 31))
                        ins.then_inc(ft.sem, 1)
                        ft.inc()

                    @block.vector
                    def _(vector):
                        vector.wait_ge(ft.sem, ft.v)
                        vector.tensor_relu(out=h2[:, f2c, :],
                                           in_=hp[:, pp, :]).then_inc(fev.sem, 1)
                        fev.inc()
            h2_done = fev.v

            if dbg_h is not None and t == 0:
                @block.gpsimd
                def _(g):
                    g.wait_ge(fev.sem, h2_done)
                    g.dma_start(out=dbg_h[0], in_=h1[:, :, :]).then_inc(S["st"].sem, 16)
                    g.dma_start(out=dbg_h[1], in_=h2[:, :, :]).then_inc(S["st"].sem, 16)
                    g.dma_start(out=dbg_w[0], in_=lnS[:, :, :]).then_inc(S["st"].sem, 16)
                    g.dma_start(out=dbg_w[1].rearrange("p a b -> a p b"),
                                in_=dram["w0c"][:, :, 6 * 512:7 * 512]).then_inc(S["st"].sem, 16)
                    S["st"].inc(64)

                @block.tensor
                def _(tensor):
                    tensor.wait_ge(S["st"].sem, S["st"].v)

            for i in range(4):
                tc = t * 4 + i
                hp3 = i % 2
                for f2c in range(32):
                    wp = (i * 32 + f2c) % 2

                    @block.gpsimd
                    def _(g):
                        wge(g, ft, ft.v - 1 if (t > 0 or i > 0 or f2c >= 2) else 0)
                        g.dma_start(out=w2b[:, wp, :], in_=w2c[f2c]).then_inc(fl2[wp].sem, 16)
                        fl2[wp].inc(16)

                    @block.tensor
                    def _(tensor):
                        tensor.wait_ge(fl2[wp].sem, fl2[wp].v)
                        if i == 0 and f2c == 0:
                            tensor.wait_ge(fev.sem, h2_done)
                        if f2c == 0:
                            wge(tensor, fv, fv.v - 1)
                        ins = None
                        for h in range(2):
                            ins = tensor.matmul(h3p[:, hp3, h * 512:(h + 1) * 512],
                                                h2[:, f2c, i * 128:(i + 1) * 128],
                                                w2b[:, wp, h * 512:(h + 1) * 512],
                                                start=(f2c == 0), stop=(f2c == 31))
                        ins.then_inc(ft.sem, 1)
                        ft.inc()

                @block.vector
                def _(vector):
                    vector.wait_ge(ft.sem, ft.v)
                    vector.tensor_add(out=resid[:, tc, :], in0=resid[:, tc, :],
                                      in1=h3p[:, hp3, :]).then_inc(fv.sem, 1)
                    fv.inc()


def emit_l1_attn(nc, block, S, resid, ident, epsc, onebf, wqkvT_in, dram):
    cld, ct, cev, cv, cs = S["cld"], S["ct"], S["cev"], S["cv"], S["cs"]
    ccc, cst = S["ccc"], S["cst"]
    lnv = S["lnv"]
    arin, arout = dram["arin"], dram["arout"]

    with ExitStack() as ctx:
        e = ctx.enter_context
        lnb = e(nc.sbuf_tensor("lnbC", [128, 2, D], BF16))
        qhfm = e(nc.sbuf_tensor("qhfmC", [128, 8, TPC], BF16))
        hs = e(nc.sbuf_tensor("hsC", [128, 2, 16], F32))
        kvsb = e(nc.sbuf_tensor("kvsbC", [128, 1040], F32))
        scr = _ln_scratch(nc, e, "C")

        with ExitStack() as pctx:
            e2 = pctx.enter_context
            lnT = e2(nc.sbuf_tensor("lnTC", [128, 8, TPC], BF16))
            khf = e2(nc.sbuf_tensor("khfC", [128, NTC, D], BF16))
            wqb = e2(nc.sbuf_tensor("wqbC", [128, 8, 1024], BF16))
            pt32 = e2(nc.sbuf_tensor("pt32C", [128, 2, D], BF16))
            sqt = e2(nc.sbuf_tensor("sqtC", [128, D], F32))
            qhat = e2(nc.sbuf_tensor("qhatC", [128, 2, D], BF16))
            vta = e2(nc.sbuf_tensor("vtaC", [128, 2, 1024], BF16))
            ptT = e2(nc.psum_tensor("ptTC", [128, 1, 1024], BF16))
            pjc = e2(nc.psum_tensor("pjcC", [128, 2, 1024], F32))
            kvp = e2(nc.psum_tensor("kvpC", [64, 1024], F32))
            kvq = e2(nc.psum_tensor("kvqC", [64, 16], F32))

            for tc in range(NTC):
                pp = tc % 2
                emit_ln(block, S, resid, tc, pp, scr, epsc, lnb[:, pp, :],
                        norm_waits=((S["lnt"].sem, S["lnt"].v - 1),))
                emit_transpose_1024(
                    block, S, lnb[:, pp, :], ptT, 0,
                    lnT[:, :, tc * 128:(tc + 1) * 128],
                    ident, tensor_waits=((lnv.sem, lnv.v),), ping=False)
            lnT_done = S["lnev"].v

            @block.gpsimd
            def _(g):
                for ic in range(8):
                    g.dma_start(out=wqb[:, ic, :],
                                in_=wqkvT_in[ic][:, 0:1024]).then_inc(cld.sem, 16)
                    cld.inc(16)
            qw_ready = cld.v

            for tc in range(NTC):
                pp = tc % 2

                @block.tensor
                def _(tensor):
                    if tc == 0:
                        tensor.wait_ge(cld.sem, qw_ready)
                        tensor.wait_ge(S["lnev"].sem, lnT_done)
                    wge(tensor, cev, cev.v - 1)
                    ins = None
                    for ic in range(8):
                        for h in range(2):
                            ins = tensor.matmul(pjc[:, pp, h * 512:(h + 1) * 512],
                                                lnT[:, ic, tc * 128:(tc + 1) * 128],
                                                wqb[:, ic, h * 512:(h + 1) * 512],
                                                start=(ic == 0), stop=(ic == 7))
                    ins.then_inc(ct.sem, 1)
                    ct.inc()

                @block.vector
                def _(vector):
                    vector.wait_ge(ct.sem, ct.v)
                    wge(vector, cs, cs.v)
                    vector.tensor_copy(out=pt32[:, pp, :], in_=pjc[:, pp, :]).then_inc(cev.sem, 1)
                    cev.inc()

                _emit_spl2(nc, block, S, pt32, sqt, hs, qhat[:, pp, :], pp, cs, cv, cev)

                emit_transpose_1024(
                    block, S, qhat[:, pp, :], ptT, 0,
                    qhfm[:, :, tc * 128:(tc + 1) * 128],
                    ident, tensor_waits=((cs.sem, cs.v),), ping=False)

            if C_STAGE < 2:
                return
            @block.gpsimd
            def _(g):
                g.wait_ge(ct.sem, ct.v)
                for ic in range(8):
                    g.dma_start(out=wqb[:, ic, :],
                                in_=wqkvT_in[ic][:, 1024:2048]).then_inc(cld.sem, 16)
                    cld.inc(16)
            kw_ready = cld.v

            for tc in range(NTC):
                pp = tc % 2

                @block.tensor
                def _(tensor):
                    if tc == 0:
                        tensor.wait_ge(cld.sem, kw_ready)
                    wge(tensor, cev, cev.v - 1)
                    ins = None
                    for ic in range(8):
                        for h in range(2):
                            ins = tensor.matmul(pjc[:, pp, h * 512:(h + 1) * 512],
                                                lnT[:, ic, tc * 128:(tc + 1) * 128],
                                                wqb[:, ic, h * 512:(h + 1) * 512],
                                                start=(ic == 0), stop=(ic == 7))
                    ins.then_inc(ct.sem, 1)
                    ct.inc()

                @block.vector
                def _(vector):
                    vector.wait_ge(ct.sem, ct.v)
                    wge(vector, cs, cs.v)
                    vector.tensor_copy(out=pt32[:, pp, :], in_=pjc[:, pp, :]).then_inc(cev.sem, 1)
                    cev.inc()

                _emit_spl2(nc, block, S, pt32, sqt, hs, khf[:, tc, :], pp, cs, cv, cev)

            khf_done = cs.v

            if C_STAGE < 3:
                return
            @block.gpsimd
            def _(g):
                g.wait_ge(ct.sem, ct.v)
                for ic in range(8):
                    g.dma_start(out=wqb[:, ic, :],
                                in_=wqkvT_in[ic][:, 2048:3072]).then_inc(cld.sem, 16)
                    cld.inc(16)
            vw_ready = cld.v

            for tc in range(NTC):
                pp = tc % 2

                @block.tensor
                def _(tensor):
                    if tc == 0:
                        tensor.wait_ge(cld.sem, vw_ready)
                        tensor.wait_ge(cs.sem, khf_done)
                    wge(tensor, cev, cev.v - 1)
                    ins = None
                    for ic in range(8):
                        for h in range(2):
                            ins = tensor.matmul(pjc[:, pp, h * 512:(h + 1) * 512],
                                                lnT[:, ic, tc * 128:(tc + 1) * 128],
                                                wqb[:, ic, h * 512:(h + 1) * 512],
                                                start=(ic == 0), stop=(ic == 7))
                    ins.then_inc(ct.sem, 1)
                    ct.inc()

                @block.vector
                def _(vector):
                    vector.wait_ge(ct.sem, ct.v)
                    if tc >= 2:
                        wge(vector, ct, ct.v)
                    vector.tensor_copy(out=vta[:, pp, :],
                                       in_=pjc[:, pp, :]).then_inc(cev.sem, 1)
                    cev.inc()

                @block.tensor
                def _(tensor):
                    tensor.wait_ge(cev.sem, cev.v)
                    ins = None
                    for h in range(16):
                        tensor.matmul(kvp[0:64, h * 64:(h + 1) * 64],
                                      khf[:, tc, h * 64:(h + 1) * 64],
                                      vta[:, pp, h * 64:(h + 1) * 64],
                                      start=(tc == 0 and h % 8 == 0),
                                      stop=(tc == NTC - 1 and h % 8 == 7))
                        ins = tensor.matmul(kvq[0:64, h:h + 1],
                                            khf[:, tc, h * 64:(h + 1) * 64],
                                            onebf[:, 0:1],
                                            start=(tc == 0 and h == 0),
                                            stop=(tc == NTC - 1 and h == 15))
                    ins.then_inc(ct.sem, 1)
                    ct.inc()
            kv_done = ct.v

            @block.vector
            def _(vector):
                vector.wait_ge(ct.sem, kv_done)
                vector.tensor_scalar_mul(out=kvsb[0:64, 0:1024], in0=kvp[0:64, :],
                                         scalar1=0.125)
                vector.tensor_copy(out=kvsb[0:64, 1024:1040],
                                   in_=kvq[0:64, :]).then_inc(cv.sem, 1)
                cv.inc()

            @block.sync
            def _(sync):
                sync.wait_ge(cv.sem, cv.v)
                sync.dma_start(out=arin, in_=kvsb[0:64, :]).then_inc(cst.sem, 16)
                cst.inc(16)

            if C_STAGE < 4:
                return

            @block.gpsimd
            def _(g):
                g.wait_ge(cst.sem, cst.v)
                g.collective_compute("AllReduce", ALU.add,
                                     replica_groups=[list(range(NCORES))],
                                     ins=[dram["arin_h"].ap().opt()],
                                     outs=[dram["arout_h"].ap().opt()]).then_inc(ccc.sem, 1)
                ccc.inc()

            @block.sync
            def _(sync):
                sync.wait_ge(ccc.sem, ccc.v)
                sync.dma_start(out=kvsb[0:64, :], in_=arout).then_inc(cst.sem, 16)
                sync.dma_start(out=kvsb[64:128, :], in_=arout).then_inc(cst.sem, 16)
                cst.inc(32)

            @block.vector
            def _(vector):
                vector.wait_ge(cst.sem, cst.v)
                vector.tensor_scalar_add(out=kvsb[:, 1024:1040],
                                         in0=kvsb[:, 1024:1040],
                                         scalar1=1e-6).then_inc(cv.sem, 1)
                cv.inc()
        if C_STAGE < 5:
            return
        with ExitStack() as pctx:
            e2 = pctx.enter_context
            normc = e2(nc.sbuf_tensor("normcC", [128, 2, 2, 8], F32))
            otmp = e2(nc.sbuf_tensor("otmpC", [128, 2, D], BF16))
            kvsbb = e2(nc.sbuf_tensor("kvsbbC", [128, 1040], BF16))
            op = e2(nc.psum_tensor("opC", [128, 2, 2, 512], F32))
            oq = e2(nc.psum_tensor("oqC", [128, 2, 2, 512], F32))

            @block.vector
            def _(vector):
                vector.wait_ge(cv.sem, cv.v)
                vector.tensor_copy(out=kvsbb[:, :], in_=kvsb[:, :]).then_inc(cv.sem, 1)
                cv.inc()
            kv_ready = cv.v

            for tc in range(NTC):
                pp = tc % 2

                @block.tensor
                def _(tensor):
                    if tc == 0:
                        tensor.wait_ge(cv.sem, kv_ready)
                    wge(tensor, cv, kv_ready + 3 * (tc - 1))
                    ins = None
                    for h in range(16):
                        po = (h % 2) * 64
                        par = h % 2
                        hh = h // 2
                        tensor.matmul(op[:, pp, par, hh * 64:(hh + 1) * 64],
                                      qhfm[po:po + 64, hh, tc * 128:(tc + 1) * 128],
                                      kvsbb[po:po + 64, h * 64:(h + 1) * 64],
                                      start=(hh == 0), stop=(hh == 7))
                        ins = tensor.matmul(oq[:, pp, par, hh:hh + 1],
                                            qhfm[po:po + 64, hh, tc * 128:(tc + 1) * 128],
                                            kvsbb[po:po + 64, 1024 + h:1025 + h],
                                            start=(hh == 0), stop=(hh == 7))
                    ins.then_inc(ct.sem, 1)
                    ct.inc()

                @block.vector
                def _(vector):
                    vector.wait_ge(ct.sem, ct.v)
                    vector.reciprocal(out=normc[:, pp, :, :],
                                      in_=oq[:, pp, :, 0:8]).then_inc(cv.sem, 1)
                    cv.inc()
                    vector.wait_ge(cv.sem, cv.v)
                    vector.tensor_tensor(
                        out=otmp[:, pp, :].rearrange("p (a q b) -> p a q b", a=8, q=2),
                        in0=op[:, pp, :, :].rearrange("p q (a b) -> p a q b", a=8),
                        in1=normc[:, pp, :, :].rearrange("p q a -> p a q ()").broadcast_to([128, 8, 2, 64]),
                        op=ALU.mult).then_inc(cv.sem, 1)
                    cv.inc()
                    vector.wait_ge(cv.sem, cv.v)
                    vector.tensor_add(out=resid[:, tc, :], in0=resid[:, tc, :],
                                      in1=otmp[:, pp, :]).then_inc(cv.sem, 1)
                    cv.inc()


def _emit_spl2(nc, block, S, src, sqt, hs, dst_ap, pp, cs, cv, cev):
    @block.vector
    def _(vector):
        vector.wait_ge(cev.sem, cev.v)
        vector.tensor_mul(out=sqt[:, :], in0=src[:, pp, :],
                          in1=src[:, pp, :]).then_inc(cv.sem, 1)
        cv.inc()
        vector.wait_ge(cv.sem, cv.v)
        vector.reduce_sum(out=hs[:, pp, :].rearrange("p a -> p a ()"),
                          in_=sqt[:, :].rearrange("p (a b) -> p a b", a=16),
                          axis=AX.X).then_inc(cv.sem, 1)
        cv.inc()
        vector.wait_ge(cv.sem, cv.v)
        vector.tensor_scalar_max(out=hs[:, pp, :], in0=hs[:, pp, :],
                                 scalar1=1e-24).then_inc(cv.sem, 1)
        cv.inc()
    hs_ready = cv.v

    @block.scalar
    def _(scalar):
        scalar.wait_ge(cv.sem, hs_ready)
        scalar.activation(out=hs[:, pp, :], in_=hs[:, pp, :], func=AF.Ln)
        scalar.activation(out=hs[:, pp, :], in_=hs[:, pp, :], func=AF.Exp,
                          scale=-0.5).then_inc(cs.sem, 1)
        cs.inc()

    @block.vector
    def _(vector):
        vector.wait_ge(cs.sem, cs.v)
        vector.tensor_tensor(
            out=sqt[:, :].rearrange("p (a b) -> p a b", a=16),
            in0=src[:, pp, :].rearrange("p (a b) -> p a b", a=16),
            in1=hs[:, pp, :].rearrange("p a -> p a ()").broadcast_to([128, 16, 64]),
            op=ALU.mult).then_inc(cv.sem, 1)
        cv.inc()
    mult_ready = cv.v

    @block.scalar
    def _(scalar):
        scalar.wait_ge(cv.sem, mult_ready)
        scalar.activation(out=sqt[:, :], in_=sqt[:, :],
                          func=AF.Exp).then_inc(cs.sem, 1)
        cs.inc()

    @block.vector
    def _(vector):
        vector.wait_ge(cs.sem, cs.v)
        vector.tensor_scalar_add(out=sqt[:, :], in0=sqt[:, :],
                                 scalar1=1.0).then_inc(cv.sem, 1)
        cv.inc()
    add_ready = cv.v

    @block.scalar
    def _(scalar):
        scalar.wait_ge(cv.sem, add_ready)
        scalar.activation(out=dst_ap, in_=sqt[:, :],
                          func=AF.Ln).then_inc(cs.sem, 1)
        cs.inc()


def emit_l1_ffn(nc, block, S, resid, ident, epsc, nmagic, dram, dbg_w=None, dbg_h=None):
    dg = [S["dg0"], S["dg1"]]
    dw = [S["dw0"], S["dw1"]]
    dt, dev, dv, ds = S["dt"], S["dev"], S["dv"], S["ds"]
    lnv, lnt, lnev = S["lnv"], S["lnt"], S["lnev"]
    wgc, wuc, wdc = dram["wgc"], dram["wuc"], dram["wdc"]

    with ExitStack() as ctx:
        e = ctx.enter_context
        lnf = e(nc.sbuf_tensor("lnfD", [128, 2, D], F32))
        xqb = e(nc.sbuf_tensor("xqbD", [128, 2, D], BF16))
        xqfm = e(nc.sbuf_tensor("xqfmD", [128, 2, 8, 128], BF16))
        wgb = e(nc.sbuf_tensor("wgbD", [128, 2, 8, 512], BF16))
        wub = e(nc.sbuf_tensor("wubD", [128, 2, 8, 512], BF16))
        wdb = e(nc.sbuf_tensor("wdbD", [128, 2, 1024], BF16))
        gs = e(nc.sbuf_tensor("gsD", [128, 4096], F32))
        gr = e(nc.sbuf_tensor("grD", [128, 4096], F32))
        us = e(nc.sbuf_tensor("usD", [128, 4096], F32))
        hqb = e(nc.sbuf_tensor("hqbD", [128, 4096], BF16))
        hqfm = e(nc.sbuf_tensor("hqfmD", [128, 32, 128], BF16))
        colq = e(nc.sbuf_tensor("colqD", [128, 8], F32))
        scr = _ln_scratch(nc, e, "D")
        ptT = e(nc.psum_tensor("ptTD", [128, 2, 1024], BF16))
        gup = e(nc.psum_tensor("gupD", [128, 2, 2, 512], F32))
        dnp = e(nc.psum_tensor("dnpD", [128, 1024], F32))

        if dbg_w is not None:
            @block.gpsimd
            def _(g):
                g.dma_start(out=dbg_w[0][:, 0:8, 0:512],
                            in_=dram["wgc"][0][:, 0:4096].rearrange("p (a b) -> p a b", a=8)).then_inc(S["st"].sem, 16)
                g.dma_start(out=dbg_w[1][:, 0:2, 0:512],
                            in_=dram["wdc"][0][:, 0:1024].rearrange("p (a b) -> p a b", a=2)).then_inc(S["st"].sem, 16)
                S["st"].inc(32)

        devac_tgt = []
        for tc in range(NTC):
            pp = tc % 2
            emit_ln(block, S, resid, tc, pp, scr, epsc, lnf[:, pp, :],
                    norm_waits=((S["ds"].sem, S["ds"].v),))

            @block.vector
            def _(vector):
                vector.wait_ge(lnv.sem, lnv.v)
                wge(vector, dt, dt.v)
                wge(vector, ds, ds.v)
                vector.reduce_max(out=colq[:, 0:1], in_=lnf[:, pp, :], axis=AX.X,
                                  apply_absolute_value=True).then_inc(dv.sem, 1)
                dv.inc()
                vector.wait_ge(dv.sem, dv.v)
                vector.tensor_scalar_max(out=colq[:, 0:1], in0=colq[:, 0:1],
                                         scalar1=1e-5).then_inc(dv.sem, 1)
                dv.inc()
                vector.wait_ge(dv.sem, dv.v)
                vector.reciprocal(out=colq[:, 6:7], in_=colq[:, 0:1])
                vector.tensor_scalar_mul(out=colq[:, 2:3], in0=colq[:, 0:1],
                                         scalar1=1.0 / 127.0).then_inc(dv.sem, 1)
                dv.inc()
                vector.wait_ge(dv.sem, dv.v)
                vector.tensor_scalar_mul(out=colq[:, 1:2], in0=colq[:, 6:7],
                                         scalar1=127.0).then_inc(dv.sem, 1)
                dv.inc()
                vector.wait_ge(dv.sem, dv.v)
                vector.tensor_scalar(out=lnf[:, pp, :], in0=lnf[:, pp, :],
                                     scalar1=colq[:, 1:2], scalar2=MAGIC,
                                     op0=ALU.mult, op1=ALU.add).then_inc(dv.sem, 1)
                dv.inc()

            @block.vector
            def _(vector):
                vector.wait_ge(dv.sem, dv.v)
                vector.tensor_scalar_add(out=xqb[:, pp, :], in0=lnf[:, pp, :],
                                         scalar1=nmagic[:, 0:1]).then_inc(dv.sem, 1)
                dv.inc()

            emit_transpose_1024(
                block, S, xqb[:, pp, :], ptT, pp,
                xqfm[:, pp, :, :],
                ident, tensor_waits=((dv.sem, dv.v),),
                vector_waits=((dt.sem, dt.v),))
            xq_done = lnev.v

            for og in range(8):
                wp = og % 2
                pp2 = og % 2

                @block.gpsimd
                def _(g):
                    wge(g, dt, dt.v - 1 if (tc > 0 or og >= 2) else 0)
                    g.dma_start(out=wgb[:, wp, :, :],
                                in_=wgc[:, :, og * 512:(og + 1) * 512]
                                .rearrange("a p b -> p a b")).then_inc(dg[wp].sem, 16)
                    g.dma_start(out=wub[:, wp, :, :],
                                in_=wuc[:, :, og * 512:(og + 1) * 512]
                                .rearrange("a p b -> p a b")).then_inc(dg[wp].sem, 16)
                    dg[wp].inc(32)

                @block.tensor
                def _(tensor):
                    tensor.wait_ge(dg[wp].sem, dg[wp].v)
                    if og == 0:
                        tensor.wait_ge(lnev.sem, xq_done)
                    wge(tensor, dev, dev.v - 1)
                    for ic in range(8):
                        tensor.matmul(gup[:, pp2, 0, :], xqfm[:, pp, ic, :],
                                      wgb[:, wp, ic, :],
                                      start=(ic == 0), stop=(ic == 7))
                    ins = None
                    for ic in range(8):
                        ins = tensor.matmul(gup[:, pp2, 1, :], xqfm[:, pp, ic, :],
                                            wub[:, wp, ic, :],
                                            start=(ic == 0), stop=(ic == 7))
                    ins.then_inc(dt.sem, 1)
                    dt.inc()

                @block.vector
                def _(vector):
                    vector.wait_ge(dt.sem, dt.v)
                    vector.tensor_copy(out=gr[:, og * 512:(og + 1) * 512],
                                       in_=gup[:, pp2, 0, :])
                    vector.tensor_scalar_mul(out=us[:, og * 512:(og + 1) * 512],
                                             in0=gup[:, pp2, 1, :],
                                             scalar1=colq[:, 2:3]).then_inc(dev.sem, 1)
                    dev.inc()
            gu_v = dev.v

            @block.scalar
            def _(scalar):
                scalar.wait_ge(dev.sem, gu_v)
                scalar.activation(out=gs[:, :], in_=gr[:, :], func=AF.Silu,
                                  scale=colq[:, 2:3]).then_inc(ds.sem, 1)
                ds.inc()
            gu_s = ds.v

            if dbg_w is not None and tc == 0:
                @block.gpsimd
                def _(g):
                    g.wait_ge(ds.sem, gu_s)
                    g.wait_ge(dev.sem, gu_v)
                    g.dma_start(out=dbg_h[0][:, 0:8, :], in_=gs[:, :].rearrange("p (a b) -> p a b", a=8)).then_inc(S["st"].sem, 16)
                    g.dma_start(out=dbg_h[1][:, 0:8, :], in_=us[:, :].rearrange("p (a b) -> p a b", a=8)).then_inc(S["st"].sem, 16)
                    S["st"].inc(32)
                dbg_st = S["st"].v

            @block.vector
            def _(vector):
                vector.wait_ge(ds.sem, gu_s)
                vector.wait_ge(dev.sem, gu_v)
                if dbg_w is not None and tc == 0:
                    vector.wait_ge(S["st"].sem, dbg_st)
                vector.tensor_mul(out=gs[:, :], in0=gs[:, :], in1=us[:, :]).then_inc(dv.sem, 1)
                dv.inc()
                vector.wait_ge(dv.sem, dv.v)
                vector.reduce_max(out=colq[:, 3:4], in_=gs[:, :], axis=AX.X,
                                  apply_absolute_value=True).then_inc(dv.sem, 1)
                dv.inc()
                vector.wait_ge(dv.sem, dv.v)
                vector.tensor_scalar_max(out=colq[:, 3:4], in0=colq[:, 3:4],
                                         scalar1=1e-5).then_inc(dv.sem, 1)
                dv.inc()
                vector.wait_ge(dv.sem, dv.v)
                vector.reciprocal(out=colq[:, 6:7], in_=colq[:, 3:4])
                vector.tensor_scalar_mul(out=colq[:, 5:6], in0=colq[:, 3:4],
                                         scalar1=1.0 / 127.0).then_inc(dv.sem, 1)
                dv.inc()
                vector.wait_ge(dv.sem, dv.v)
                vector.tensor_scalar_mul(out=colq[:, 4:5], in0=colq[:, 6:7],
                                         scalar1=127.0).then_inc(dv.sem, 1)
                dv.inc()
                vector.wait_ge(dv.sem, dv.v)
                vector.tensor_scalar(out=gs[:, :], in0=gs[:, :],
                                     scalar1=colq[:, 4:5], scalar2=MAGIC,
                                     op0=ALU.mult, op1=ALU.add).then_inc(dv.sem, 1)
                dv.inc()

            @block.vector
            def _(vector):
                vector.wait_ge(dv.sem, dv.v)
                vector.tensor_scalar_add(out=hqb[:, :], in0=gs[:, :],
                                         scalar1=nmagic[:, 0:1]).then_inc(dv.sem, 1)
                dv.inc()
            hq_done = dv.v

            if dbg_w is not None and tc == 0:
                @block.gpsimd
                def _(g):
                    g.wait_ge(ds.sem, hq_done)
                    g.dma_start(out=dbg_w[0][:, 0:8, 0:128],
                                in_=xqb[:, pp, :].rearrange("p (a b) -> p a b", a=8)).then_inc(S["st"].sem, 16)
                    g.dma_start(out=dbg_w[1][:, 0:8, 0:512],
                                in_=hqb[:, :].rearrange("p (a b) -> p a b", a=8)).then_inc(S["st"].sem, 16)
                    g.dma_start(out=dbg_w[0][:, 0:8, 128:256],
                                in_=gs[:, 0:1024].rearrange("p (a b) -> p a b", a=8)).then_inc(S["st"].sem, 16)
                    g.dma_start(out=dbg_w[0][:, 0, 256:264],
                                in_=colq[:, :]).then_inc(S["st"].sem, 16)
                    S["st"].inc(64)

                @block.tensor
                def _(tensor):
                    tensor.wait_ge(S["st"].sem, S["st"].v)

            for qd in range(4):
                ppq = qd % 2

                @block.tensor
                def _(tensor):
                    if qd == 0:
                        tensor.wait_ge(dv.sem, hq_done)
                    wge(tensor, lnev, lnev.v - 1)
                    ins = None
                    for ic in range(8):
                        ins = tensor.transpose(
                            ptT[:, ppq, ic * 128:(ic + 1) * 128],
                            hqb[:, (qd * 8 + ic) * 128:(qd * 8 + ic + 1) * 128],
                            ident[:])
                    ins.then_inc(lnt.sem, 1)
                    lnt.inc()

                @block.vector
                def _(vector):
                    vector.wait_ge(lnt.sem, lnt.v)
                    if qd == 0:
                        wge(vector, dt, dt.v)
                    vector.tensor_copy(
                        out=hqfm[:, qd * 8:(qd + 1) * 8, :],
                        in_=ptT[:, ppq, :].rearrange("p (a b) -> p a b", a=8)).then_inc(lnev.sem, 1)
                    lnev.inc()
            hqfm_done = lnev.v

            for f2c in range(32):
                wp = (tc * 32 + f2c) % 2

                @block.gpsimd
                def _(g):
                    wge(g, dt, dt.v - 1 if (tc > 0 or f2c >= 2) else 0)
                    g.dma_start(out=wdb[:, wp, :], in_=wdc[f2c]).then_inc(dw[wp].sem, 16)
                    dw[wp].inc(16)

                @block.tensor
                def _(tensor):
                    tensor.wait_ge(dw[wp].sem, dw[wp].v)
                    if f2c == 0:
                        tensor.wait_ge(lnev.sem, hqfm_done)
                        if tc >= 1:
                            tensor.wait_ge(dv.sem, devac_tgt[tc - 1])
                    ins = None
                    for h in range(2):
                        ins = tensor.matmul(dnp[:, h * 512:(h + 1) * 512],
                                            hqfm[:, f2c, :],
                                            wdb[:, wp, h * 512:(h + 1) * 512],
                                            start=(f2c == 0), stop=(f2c == 31))
                    ins.then_inc(dt.sem, 1)
                    dt.inc()

            @block.vector
            def _(vector):
                vector.wait_ge(dt.sem, dt.v)
                vector.scalar_tensor_tensor(out=resid[:, tc, :], in0=dnp[:, :],
                                            scalar=colq[:, 5:6], in1=resid[:, tc, :],
                                            op0=ALU.mult, op1=ALU.add).then_inc(dv.sem, 1)
                dv.inc()
            devac_tgt.append(dv.v)


_NC_CACHE = {}


DEBUG = False


def _get_nc(n_phases=4):
    key = (n_phases, DEBUG)
    if key not in _NC_CACHE:
        _NC_CACHE[key] = build_nc(n_phases, dbg=DEBUG)
    return _NC_CACHE[key]


def kernel(L0_attn_x, L0_attn_w_q, L0_attn_w_k, L0_attn_w_v,
           L0_ffn_k0, L0_ffn_k1, L0_ffn_k2, L0_ffn_s0, L0_ffn_s1, L0_ffn_s2,
           L1_attn_W_qkv, L1_ffn_w_gate, L1_ffn_w_up, L1_ffn_w_down,
           _phases=4):
    x = np.ascontiguousarray(np.asarray(L0_attn_x, dtype=np.float32).reshape(T, D))
    wq = np.ascontiguousarray(np.asarray(L0_attn_w_q, np.float32).reshape(8, 128, D))
    wk = np.ascontiguousarray(np.asarray(L0_attn_w_k, np.float32).reshape(8, 128, D))
    wv = np.ascontiguousarray(np.asarray(L0_attn_w_v, np.float32).reshape(8, 128, D))
    k0u = np.asarray(L0_ffn_k0).view(np.uint8)
    k0a = np.ascontiguousarray(k0u[:1024].reshape(8, 128, 4096))
    k0b = np.ascontiguousarray(k0u[1024:].reshape(8, 128, 4096))
    k1u = np.ascontiguousarray(np.asarray(L0_ffn_k1).view(np.uint8).reshape(32, 128, 4096))
    k2u = np.ascontiguousarray(np.asarray(L0_ffn_k2).view(np.uint8).reshape(32, 128, 1024))
    s0 = np.ascontiguousarray(np.broadcast_to(
        np.asarray(L0_ffn_s0, np.float32).reshape(1, 1), (128, 1)))
    s1 = np.ascontiguousarray(np.broadcast_to(
        np.asarray(L0_ffn_s1, np.float32).reshape(1, 1), (128, 1)))
    s2 = np.ascontiguousarray(np.broadcast_to(
        np.asarray(L0_ffn_s2, np.float32).reshape(1, 1), (128, 1)))
    wqkvT = np.ascontiguousarray(
        np.asarray(L1_attn_W_qkv, np.float32).T.reshape(8, 128, 3072))
    wgT = np.ascontiguousarray(
        np.asarray(L1_ffn_w_gate, np.float32).T.reshape(8, 128, 4096))
    wuT = np.ascontiguousarray(
        np.asarray(L1_ffn_w_up, np.float32).T.reshape(8, 128, 4096))
    wdT = np.ascontiguousarray(
        np.asarray(L1_ffn_w_down, np.float32).T.reshape(32, 128, 1024))

    in_maps = []
    for c in range(NCORES):
        sA, sB = c, NSTRIPE - 1 - c
        xs = np.ascontiguousarray(
            np.concatenate([x[sA * 1024:(sA + 1) * 1024],
                            x[sB * 1024:(sB + 1) * 1024]], axis=0)).reshape(NTC, 128, D)
        btab_row = np.empty((2, 8, NSTEP), np.float32)
        for si, stripe in enumerate((sA, sB)):
            for j in range(NSTEP):
                btab_row[si, :, j] = SHIFT if (j == 0 or (j - 1 < stripe)) else CLOSED
        btab = np.ascontiguousarray(
            np.broadcast_to(btab_row.reshape(1, -1), (128, 2 * 8 * NSTEP)))
        in_maps.append({
            "x_in": xs, "btab": btab,
            "w_q": wq, "w_k": wk, "w_v": wv,
            "k0a": k0a, "k0b": k0b, "k1": k1u, "k2": k2u,
            "s0": s0, "s1": s1, "s2": s2,
            "wqkvT": wqkvT, "wgT": wgT, "wuT": wuT, "wdT": wdT,
        })

    nc = _get_nc(_phases)
    res = run_bass_kernel_spmd(nc, in_maps, core_ids=list(range(NCORES)))

    y = np.empty((T, D), np.float32)
    for c in range(NCORES):
        out = np.asarray(res.results[c]["y_out"]).reshape(TPC, D)
        sA, sB = c, NSTRIPE - 1 - c
        y[sA * 1024:(sA + 1) * 1024] = out[:1024]
        y[sB * 1024:(sB + 1) * 1024] = out[1024:]
    return y.reshape(1, T, D)


# version 03
# speedup vs baseline: 1.0157x; 1.0157x over previous
import numpy as np
import ml_dtypes
from contextlib import ExitStack

import concourse.bass as bass
import concourse.mybir as mybir
from concourse.bass_utils import run_bass_kernel_spmd

F32 = mybir.dt.float32
BF16 = mybir.dt.bfloat16
U8 = mybir.dt.uint8
AF = mybir.ActivationFunctionType
ALU = mybir.AluOpType
AX = mybir.AxisListType

NCORES = 8
D = 1024
T = 16384
TPC = 2048
NTC = 16
NSTRIPE = 16
NSTEP = 16
MAGIC = 12582912.0
SHIFT = -12.0
CLOSED = -30012.0
MASKNEG = -30000.0


class Ctr:
    def __init__(self, sem):
        self.sem = sem
        self.v = 0

    def inc(self, n=1):
        self.v += n
        return self.v


def wge(eng, ctr, v):
    if v > 0:
        eng.wait_ge(ctr.sem, v)


def emit_barrier(nc, block, S):
    bar = S["bar"]
    tgt = bar.v + 5
    for engname in ("tensor", "vector", "scalar", "gpsimd", "sync"):
        def _sec(eng, tgt=tgt):
            eng.drain().then_inc(bar.sem, 1)
            eng.wait_ge(bar.sem, tgt)
        getattr(block, engname)(_sec)
    bar.v = tgt


def _posblock_src(b):
    rank = b if b <= 7 else 15 - b
    sub = 0 if b <= 7 else 1
    return rank, sub


C_STAGE = 9
RACECHECK = False


def build_nc(n_phases=4, dbg=False):
    nc = bass.Bass("TRN2", target_bir_lowering=False,
                   detect_race_conditions=RACECHECK)

    x_in = nc.dram_tensor("x_in", [NTC, 128, D], F32, kind="ExternalInput").ap()
    btab_in = nc.dram_tensor("btab", [128, 2 * 8 * NSTEP], F32, kind="ExternalInput").ap()
    wq_in = nc.dram_tensor("w_q", [8, 128, D], F32, kind="ExternalInput").ap()
    wk_in = nc.dram_tensor("w_k", [8, 128, D], F32, kind="ExternalInput").ap()
    wv_in = nc.dram_tensor("w_v", [8, 128, D], F32, kind="ExternalInput").ap()
    k0a_in = nc.dram_tensor("k0a", [8, 128, 4096], U8, kind="ExternalInput").ap()
    k0b_in = nc.dram_tensor("k0b", [8, 128, 4096], U8, kind="ExternalInput").ap()
    k1_in = nc.dram_tensor("k1", [32, 128, 4096], U8, kind="ExternalInput").ap()
    k2_in = nc.dram_tensor("k2", [32, 128, 1024], U8, kind="ExternalInput").ap()
    s0_in = nc.dram_tensor("s0", [128, 1], F32, kind="ExternalInput").ap()
    s1_in = nc.dram_tensor("s1", [128, 1], F32, kind="ExternalInput").ap()
    s2_in = nc.dram_tensor("s2", [128, 1], F32, kind="ExternalInput").ap()
    wqkvT_in = nc.dram_tensor("wqkvT", [8, 128, 3072], F32, kind="ExternalInput").ap()
    wgT_in = nc.dram_tensor("wgT", [8, 128, 4096], F32, kind="ExternalInput").ap()
    wuT_in = nc.dram_tensor("wuT", [8, 128, 4096], F32, kind="ExternalInput").ap()
    wdT_in = nc.dram_tensor("wdT", [32, 128, 1024], F32, kind="ExternalInput").ap()
    y_out = nc.dram_tensor("y_out", [NTC, 128, D], F32, kind="ExternalOutput").ap()
    dbg_h = (nc.dram_tensor("dbg_h", [2, 128, 32, 512], F32, kind="ExternalOutput").ap()
             if dbg else None)
    dbg_w = (nc.dram_tensor("dbg_w", [2, 128, 8, 512], F32, kind="ExternalOutput").ap()
             if dbg else None)

    agin_k_h = nc.dram_tensor("agin_k", [8, 128, TPC], BF16)
    agin_v_h = nc.dram_tensor("agin_v", [NTC, 128, D], BF16)
    agout_k_h = nc.dram_tensor("agout_k", [NCORES * 8, 128, TPC], BF16)
    agout_v_h = nc.dram_tensor("agout_v", [NCORES * NTC, 128, D], BF16)
    arin_h = nc.dram_tensor("arin_kv", [64, 1040], F32)
    arout_h = nc.dram_tensor("arout_kv", [64, 1040], F32)
    w0c_h = nc.dram_tensor("w0c", [8, 128, 4096], BF16)
    w1c_h = nc.dram_tensor("w1c", [16, 32, 128, 256], BF16)
    w2c_h = nc.dram_tensor("w2c", [32, 128, 1024], BF16)
    wgc_h = nc.dram_tensor("wgc", [8, 128, 4096], BF16)
    wuc_h = nc.dram_tensor("wuc", [8, 128, 4096], BF16)
    wdc_h = nc.dram_tensor("wdc", [32, 128, 1024], BF16)
    dram = dict(
        agin_k=agin_k_h.ap(), agin_v=agin_v_h.ap(),
        agout_k=agout_k_h.ap(), agout_v=agout_v_h.ap(),
        arin=arin_h.ap(), arout=arout_h.ap(),
        w0c=w0c_h.ap(), w1c=w1c_h.ap(), w2c=w2c_h.ap(),
        wgc=wgc_h.ap(), wuc=wuc_h.ap(), wdc=wdc_h.ap(),
        agin_k_h=agin_k_h, agin_v_h=agin_v_h, agout_k_h=agout_k_h,
        agout_v_h=agout_v_h, arin_h=arin_h, arout_h=arout_h,
    )

    ident_d = nc.inline_tensor(np.eye(128, dtype=ml_dtypes.bfloat16), "identc")
    tri_np = np.where(np.arange(128)[None, :] <= np.arange(128)[:, None],
                      np.float32(0.0), np.float32(MASKNEG)).astype(np.float32)
    tri_d = nc.inline_tensor(tri_np, "tric")
    onebf_d = nc.inline_tensor(np.ones((128, 1), dtype=ml_dtypes.bfloat16), "onebfc")
    onesf_d = nc.inline_tensor(np.ones((128, 1), dtype=np.float32), "onesfc")
    onesrow_d = nc.inline_tensor(np.ones((1, 128), dtype=np.float32), "onesrowc")

    with ExitStack() as top:
        e = top.enter_context
        resid = e(nc.sbuf_tensor("resid", [128, NTC, D], F32))
        btab = e(nc.sbuf_tensor("btabs", [128, 2 * 8 * NSTEP], F32))
        ident = e(nc.sbuf_tensor("ident_s", [128, 128], BF16))
        tri = e(nc.sbuf_tensor("tri_s", [128, 128], F32))
        onebf = e(nc.sbuf_tensor("onebf_s", [128, 1], BF16))
        onesf = e(nc.sbuf_tensor("onesf_s", [128, 1], F32))
        onesrow = e(nc.sbuf_tensor("onesrow_s", [1, 128], F32))
        epsc = e(nc.sbuf_tensor("epsc", [128, 1], F32))
        nmagic = e(nc.sbuf_tensor("nmagic", [128, 1], F32))
        s0c = e(nc.sbuf_tensor("s0c", [128, 1], F32))
        s1c = e(nc.sbuf_tensor("s1c", [128, 1], F32))
        s2c = e(nc.sbuf_tensor("s2c", [128, 1], F32))

        S = {}
        for name in ("ld", "pv", "psc", "pd", "pd1", "pd2", "pd3", "pd4", "pst", "ptn",
                     "lnv", "lns", "lnt", "lnev",
                     "pjw", "pjt", "pjev", "pjst", "cc",
                     "kld0", "kld1", "vld0", "vld1",
                     "smm", "sexp", "stmm", "sptev", "spv", "svadd",
                     "vmask", "fin", "fv",
                     "fl00", "fl01", "fl10", "fl11", "fl20", "fl21", "ft", "fev", "fs",
                     "cld", "ct", "cev", "cv", "cs", "ccc", "cst",
                     "dg0", "dg1", "dw0", "dw1", "dt", "dev", "dv", "ds", "ppd",
                     "st", "bar"):
            S[name] = Ctr(e(nc.semaphore("sem_" + name)))

        block = e(nc.Block())
        ld = S["ld"]

        @block.sync
        def _(sync):
            for tc in range(NTC):
                sync.dma_start(out=resid[:, tc, :], in_=x_in[tc]).then_inc(ld.sem, 16)
                ld.inc(16)
            for dst, src in ((btab, btab_in), (ident, ident_d.ap()),
                             (tri, tri_d.ap()), (onebf, onebf_d.ap()),
                             (onesf, onesf_d.ap()), (onesrow, onesrow_d.ap()),
                             (s0c, s0_in), (s1c, s1_in), (s2c, s2_in)):
                sync.dma_start(out=dst[:], in_=src).then_inc(ld.sem, 16)
                ld.inc(16)
        ld_all = ld.v

        @block.vector
        def _(vector):
            vector.memset(epsc[:], 1e-5)
            vector.memset(nmagic[:], -MAGIC)

        prep_alloc, prep_steps = emit_weight_prep(nc, block, S, ld_all,
                                                  k0a_in, k0b_in, k1_in, k2_in,
                                                  s0c, s1c, s2c,
                                                  wgT_in, wuT_in, wdT_in, dram,
                                                  onesf, onesrow)

        def drain_prep(k):
            while prep_steps and k > 0:
                prep_steps.pop(0)()
                k -= 1
        drain_prep.alloc = prep_alloc

        emit_l0_attn(nc, block, S, ld_all, resid, btab, ident, tri, epsc,
                     wq_in, wk_in, wv_in, dram, drain_prep)

        if n_phases >= 2:
            emit_barrier(nc, block, S)
            emit_l0_ffn(nc, block, S, resid, ident, epsc, dram, dbg_h, dbg_w)
        if n_phases >= 3:
            emit_barrier(nc, block, S)
            emit_l1_attn(nc, block, S, resid, ident, epsc, onebf,
                         wqkvT_in, dram)
        if n_phases >= 4:
            emit_barrier(nc, block, S)
            emit_l1_ffn(nc, block, S, resid, ident, epsc, nmagic, dram, dbg_w, dbg_h)

        emit_barrier(nc, block, S)
        st = S["st"]

        @block.sync
        def _(sync):
            for tc in range(NTC):
                sync.dma_start(out=y_out[tc], in_=resid[:, tc, :]).then_inc(st.sem, 16)
                st.inc(16)
            sync.wait_ge(st.sem, st.v)

    return nc


def emit_ln(block, S, resid, tc, pp, scr, epsc, out_ap, first_wait=None,
            norm_waits=()):
    lnv, lns = S["lnv"], S["lns"]
    bnstat, mv, lnvv, rstd = scr

    @block.vector
    def _(vector):
        if first_wait is not None:
            sem, val = first_wait
            if val > 0:
                vector.wait_ge(sem, val)
        wge(vector, lns, lns.v)
        vector.bn_stats(out=bnstat[:, pp, 0:6], in_=resid[:, tc, 0:512])
        vector.bn_stats(out=bnstat[:, pp, 6:12], in_=resid[:, tc, 512:1024]).then_inc(lnv.sem, 1)
        lnv.inc()
        vector.wait_ge(lnv.sem, lnv.v)
        vector.bn_aggr(out=mv[:, pp, :], in_=bnstat[:, pp, :]).then_inc(lnv.sem, 1)
        lnv.inc()

    @block.scalar
    def _(scalar):
        scalar.wait_ge(lnv.sem, lnv.v)
        scalar.activation(out=lnvv[:, pp, :], in_=mv[:, pp, 1:2], func=AF.Ln,
                          bias=epsc[:, 0:1], scale=1.0)
        scalar.activation(out=rstd[:, pp, :], in_=lnvv[:, pp, :], func=AF.Exp,
                          scale=-0.5).then_inc(lns.sem, 1)
        lns.inc()

    @block.vector
    def _(vector):
        vector.wait_ge(lns.sem, lns.v)
        for sem, val in norm_waits:
            if val > 0:
                vector.wait_ge(sem, val)
        vector.tensor_scalar(out=out_ap, in0=resid[:, tc, :],
                             scalar1=mv[:, pp, 0:1], scalar2=rstd[:, pp, 0:1],
                             op0=ALU.subtract, op1=ALU.mult).then_inc(lnv.sem, 1)
        lnv.inc()


def _ln_scratch(nc, e, tag):
    return (e(nc.sbuf_tensor("bnstat" + tag, [128, 2, 12], F32)),
            e(nc.sbuf_tensor("mv" + tag, [128, 2, 2], F32)),
            e(nc.sbuf_tensor("lnvv" + tag, [128, 2, 1], F32)),
            e(nc.sbuf_tensor("rstd" + tag, [128, 2, 1], F32)))


def emit_transpose_1024(block, S, src_ap, ptT, pp, dst_ap, ident,
                        tensor_waits=(), vector_waits=(), ping=True):
    lnt, lnev = S["lnt"], S["lnev"]

    @block.tensor
    def _(tensor):
        for sem, val in tensor_waits:
            if val > 0:
                tensor.wait_ge(sem, val)
        wge(tensor, lnev, lnev.v - (1 if ping else 0))
        ins = None
        for ic in range(8):
            ins = tensor.transpose(ptT[:, pp, ic * 128:(ic + 1) * 128],
                                   src_ap[:, ic * 128:(ic + 1) * 128],
                                   ident[:])
        ins.then_inc(lnt.sem, 1)
        lnt.inc()

    @block.vector
    def _(vector):
        vector.wait_ge(lnt.sem, lnt.v)
        for sem, val in vector_waits:
            if val > 0:
                vector.wait_ge(sem, val)
        src2 = ptT[:, pp, :]
        if len(dst_ap.shape) == 3:
            src2 = src2.rearrange("p (a b) -> p a b", a=dst_ap.shape[1])
        vector.tensor_copy(out=dst_ap, in_=src2).then_inc(lnev.sem, 1)
        lnev.inc()


def emit_weight_prep(nc, block, S, ld_all,
                     k0a_in, k0b_in, k1_in, k2_in, s0c, s1c, s2c,
                     wgT_in, wuT_in, wdT_in, dram, onesf, onesrow):
    pv, psc, pst, ptn = S["pv"], S["psc"], S["pst"], S["ptn"]
    pd, pd1, pd2, pd3, pd4 = S["pd"], S["pd1"], S["pd2"], S["pd3"], S["pd4"]
    w0c, w1c, w2c = dram["w0c"], dram["w1c"], dram["w2c"]
    wgc, wuc, wdc = dram["wgc"], dram["wuc"], dram["wdc"]

    B = {}

    def alloc():
        ctx = ExitStack()
        S["prep_ctx"] = ctx
        e = ctx.enter_context
        B["au8"] = e(nc.sbuf_tensor("au8", [128, 4096], U8))
        B["bu8"] = e(nc.sbuf_tensor("bu8", [128, 4096], U8))
        B["wtmp"] = e(nc.sbuf_tensor("wtmp", [128, 4096], F32))
        B["wbf"] = e(nc.sbuf_tensor("wbf", [128, 4096], BF16))
        B["cols"] = e(nc.sbuf_tensor("pcols", [128, 8], F32))
        B["absacc"] = e(nc.sbuf_tensor("absacc", [128, 3], F32))
        B["redc"] = e(nc.sbuf_tensor("redc", [128, 1], F32))
        B["scl"] = e(nc.sbuf_tensor("scl", [128, 2, 3], F32))
        B["tot"] = e(nc.sbuf_tensor("tot", [1, 3], F32))
        B["pbc"] = e(nc.psum_tensor("pbc", [128, 8], F32))

    class _Lazy:
        def __init__(self, key):
            self._key = key

        def __getitem__(self, item):
            return B[self._key][item]

    au8 = _Lazy("au8"); bu8 = _Lazy("bu8"); wtmp = _Lazy("wtmp")
    wbf = _Lazy("wbf"); cols = _Lazy("cols"); absacc = _Lazy("absacc")
    redc = _Lazy("redc"); scl = _Lazy("scl"); tot = _Lazy("tot")
    pbc = _Lazy("pbc")

    steps = []

    def s_cols():
        @block.vector
        def _(vector):
            vector.wait_ge(S["ld"].sem, ld_all)
            vector.tensor_scalar_mul(out=cols[:, 0:1], in0=s0c[:], scalar1=2.0)
            vector.tensor_scalar_mul(out=cols[:, 1:2], in0=s1c[:], scalar1=4.0)
            vector.tensor_scalar_mul(out=cols[:, 2:3], in0=s1c[:], scalar1=-2.0)
            vector.tensor_scalar_mul(out=cols[:, 3:4], in0=s2c[:], scalar1=4.0)
            vector.tensor_scalar_mul(out=cols[:, 4:5], in0=s2c[:], scalar1=-2.0).then_inc(pv.sem, 1)
            pv.inc()
    steps.append(s_cols)
    scales_tgt = [None]

    for ic in range(8):
        def s_w0(ic=ic):
            @block.gpsimd
            def _(g):
                wge(g, pv, pv.v)
                g.dma_start(out=au8[:], in_=k0a_in[ic]).then_inc(pd.sem, 16)
                g.dma_start(out=bu8[:], in_=k0b_in[ic]).then_inc(pd.sem, 16)
                pd.inc(32)

            @block.vector
            def _(vector):
                vector.wait_ge(pd.sem, pd.v)
                if ic == 0:
                    vector.wait_ge(pv.sem, 1)
                wge(vector, pst, pst.v)
                vector.tensor_sub(out=wtmp[:], in0=au8[:], in1=bu8[:]).then_inc(pv.sem, 1)
                pv.inc()
                vector.wait_ge(pv.sem, pv.v)
                vector.tensor_scalar_mul(out=wbf[:], in0=wtmp[:],
                                         scalar1=cols[:, 0:1]).then_inc(pv.sem, 1)
                pv.inc()

            @block.sync
            def _(sync):
                sync.wait_ge(pv.sem, pv.v)
                sync.dma_start(out=w0c[ic], in_=wbf[:]).then_inc(pst.sem, 16)
                pst.inc(16)
        steps.append(s_w0)

    for ic in range(32):
        def s_w1(ic=ic):
            @block.gpsimd
            def _(g):
                wge(g, pv, pv.v)
                wge(g, psc, psc.v)
                g.dma_start(out=au8[:], in_=k1_in[ic]).then_inc(pd1.sem, 16)
                pd1.inc(16)

            @block.scalar
            def _(scalar):
                scalar.wait_ge(pd1.sem, pd1.v)
                scalar.wait_ge(pv.sem, 1)
                wge(scalar, pst, pst.v)
                scalar.activation(out=wbf[:], in_=au8[:], func=AF.Identity,
                                  scale=cols[:, 1:2], bias=cols[:, 2:3]).then_inc(psc.sem, 1)
                psc.inc()

            @block.sync
            def _(sync):
                sync.wait_ge(psc.sem, psc.v)
                for g16 in range(16):
                    sync.dma_start(out=w1c[g16, ic],
                                   in_=wbf[:, g16 * 256:(g16 + 1) * 256]).then_inc(pst.sem, 16)
                    pst.inc(16)
        steps.append(s_w1)

    for it in range(8):
        def s_w2(it=it):
            @block.gpsimd
            def _(g):
                wge(g, psc, psc.v)
                for j in range(4):
                    g.dma_start(out=au8[:, j * 1024:(j + 1) * 1024],
                                in_=k2_in[4 * it + j]).then_inc(pd2.sem, 16)
                    pd2.inc(16)

            @block.scalar
            def _(scalar):
                scalar.wait_ge(pd2.sem, pd2.v)
                wge(scalar, pst, pst.v)
                scalar.activation(out=wbf[:], in_=au8[:], func=AF.Identity,
                                  scale=cols[:, 3:4], bias=cols[:, 4:5]).then_inc(psc.sem, 1)
                psc.inc()

            @block.sync
            def _(sync):
                sync.wait_ge(psc.sem, psc.v)
                for j in range(4):
                    sync.dma_start(out=w2c[4 * it + j],
                                   in_=wbf[:, j * 1024:(j + 1) * 1024]).then_inc(pst.sem, 16)
                    pst.inc(16)
        steps.append(s_w2)

    specs = [(wgT_in, wgc, 8, 4096, 0), (wuT_in, wuc, 8, 4096, 1),
             (wdT_in, wdc, 32, 1024, 2)]

    for src_, dst, nic, F, mi in specs:
        for ic in range(nic):
            def s_p1(src_=src_, ic=ic, F=F, mi=mi):
                @block.gpsimd
                def _(g):
                    wge(g, pv, pv.v)
                    g.dma_start(out=wtmp[:, 0:F], in_=src_[ic]).then_inc(pd3.sem, 16)
                    pd3.inc(16)

                @block.vector
                def _(vector):
                    vector.wait_ge(pd3.sem, pd3.v)
                    vector.reduce_sum(out=redc[:], in_=wtmp[:, 0:F], axis=AX.X,
                                      apply_absolute_value=True).then_inc(pv.sem, 1)
                    pv.inc()
                    vector.wait_ge(pv.sem, pv.v)
                    if ic == 0:
                        vector.tensor_copy(out=absacc[:, mi:mi + 1],
                                           in_=redc[:]).then_inc(pv.sem, 1)
                    else:
                        vector.tensor_add(out=absacc[:, mi:mi + 1],
                                          in0=absacc[:, mi:mi + 1],
                                          in1=redc[:]).then_inc(pv.sem, 1)
                    pv.inc()
            steps.append(s_p1)

    def s_scale():
        @block.tensor
        def _(tensor):
            tensor.wait_ge(pv.sem, pv.v)
            tensor.matmul(pbc[0:1, 4:7], onesf[:, 0:1], absacc[:, 0:3],
                          start=True, stop=True).then_inc(ptn.sem, 1)
            ptn.inc()

        @block.vector
        def _(vector):
            vector.wait_ge(ptn.sem, ptn.v)
            vector.tensor_copy(out=tot[0:1, 0:3], in_=pbc[0:1, 4:7]).then_inc(pv.sem, 1)
            pv.inc()

        @block.tensor
        def _(tensor):
            tensor.wait_ge(pv.sem, pv.v)
            tensor.matmul(pbc[:, 0:3], onesrow[0:1, :],
                          tot[0:1, 0:3], start=True, stop=True).then_inc(ptn.sem, 1)
            ptn.inc()

        @block.vector
        def _(vector):
            vector.wait_ge(ptn.sem, ptn.v)
            for mi, numel in ((0, 8 * 128 * 4096), (1, 8 * 128 * 4096),
                              (2, 32 * 128 * 1024)):
                vector.tensor_scalar(out=scl[:, 0, mi:mi + 1], in0=pbc[:, mi:mi + 1],
                                     scalar1=1.0 / numel, scalar2=1e-5,
                                     op0=ALU.mult, op1=ALU.max).then_inc(pv.sem, 1)
                pv.inc()
            vector.wait_ge(pv.sem, pv.v)
            vector.reciprocal(out=scl[:, 1, 0:1], in_=scl[:, 0, 0:1])
            vector.reciprocal(out=scl[:, 1, 1:2], in_=scl[:, 0, 1:2])
            vector.reciprocal(out=scl[:, 1, 2:3], in_=scl[:, 0, 2:3]).then_inc(pv.sem, 1)
            pv.inc()
    steps.append(s_scale)

    for src_, dst, nic, F, mi in specs:
        for ic in range(nic):
            def s_p2(src_=src_, dst=dst, ic=ic, F=F, mi=mi):
                @block.gpsimd
                def _(g):
                    wge(g, pv, pv.v)
                    g.dma_start(out=wtmp[:, 0:F], in_=src_[ic]).then_inc(pd4.sem, 16)
                    pd4.inc(16)

                @block.vector
                def _(vector):
                    vector.wait_ge(pd4.sem, pd4.v)
                    wge(vector, pst, pst.v)
                    vector.tensor_scalar(out=wtmp[:, 0:F], in0=wtmp[:, 0:F],
                                         scalar1=scl[:, 1, mi:mi + 1], scalar2=1.4999999,
                                         op0=ALU.mult, op1=ALU.min).then_inc(pv.sem, 1)
                    pv.inc()
                    vector.wait_ge(pv.sem, pv.v)
                    vector.tensor_scalar(out=wtmp[:, 0:F], in0=wtmp[:, 0:F],
                                         scalar1=-1.4999999, scalar2=MAGIC,
                                         op0=ALU.max, op1=ALU.add).then_inc(pv.sem, 1)
                    pv.inc()
                    vector.wait_ge(pv.sem, pv.v)
                    vector.tensor_scalar(out=wbf[:, 0:F], in0=wtmp[:, 0:F],
                                         scalar1=-MAGIC, scalar2=scl[:, 0, mi:mi + 1],
                                         op0=ALU.add, op1=ALU.mult).then_inc(pv.sem, 1)
                    pv.inc()

                @block.sync
                def _(sync):
                    sync.wait_ge(pv.sem, pv.v)
                    sync.dma_start(out=dst[ic], in_=wbf[:, 0:F]).then_inc(pst.sem, 16)
                    pst.inc(16)
            steps.append(s_p2)

    def s_fin():
        @block.sync
        def _(sync):
            sync.wait_ge(pst.sem, pst.v)
            sync.nop().then_inc(S["ppd"].sem, 1)
            S["ppd"].inc()

        @block.gpsimd
        def _(g):
            g.wait_ge(pd.sem, pd.v)
            g.wait_ge(pd1.sem, pd1.v)
            g.wait_ge(pd2.sem, pd2.v)
            g.wait_ge(pd3.sem, pd3.v)
            g.wait_ge(pd4.sem, pd4.v)
            g.nop().then_inc(S["ppd"].sem, 1)
            S["ppd"].inc()
        S["prep_ctx"].close()
    steps.append(s_fin)
    return alloc, steps


def emit_l0_attn(nc, block, S, ld_all, resid, btab, ident, tri, epsc,
                 wq_in, wk_in, wv_in, dram, drain_prep=lambda k: None):
    lnv = S["lnv"]
    pjw, pjt, pjev, pjst, cc = S["pjw"], S["pjt"], S["pjev"], S["pjst"], S["cc"]
    agin_k, agin_v = dram["agin_k"], dram["agin_v"]
    agout_k, agout_v = dram["agout_k"], dram["agout_v"]

    with ExitStack() as actx:
        e = actx.enter_context
        qfm = e(nc.sbuf_tensor("qfm", [128, 8, TPC], BF16))

        with ExitStack() as ctx:
            e2 = ctx.enter_context
            lnb = e2(nc.sbuf_tensor("lnbA", [128, 2, D], BF16))
            lnT = e2(nc.sbuf_tensor("lnTA", [128, 8, TPC], BF16))
            wsb = e2(nc.sbuf_tensor("wsbA", [128, 8, D], BF16))
            scr = _ln_scratch(nc, e2, "A")
            stg = e2(nc.sbuf_tensor("stgA", [128, 2, 512], BF16))
            ptT = e2(nc.psum_tensor("ptTA", [128, 2, 1024], BF16))
            pj = e2(nc.psum_tensor("pjA", [128, 2, 512], F32))
            if hasattr(drain_prep, "alloc"):
                drain_prep.alloc()

            for tc in range(NTC):
                pp = tc % 2
                emit_ln(block, S, resid, tc, pp, scr, epsc, lnb[:, pp, :],
                        first_wait=(S["ld"].sem, ld_all if tc == 0 else 0),
                        norm_waits=((S["lnt"].sem, S["lnt"].v - 1),))
                emit_transpose_1024(
                    block, S, lnb[:, pp, :], ptT, pp,
                    lnT[:, :, tc * 128:(tc + 1) * 128],
                    ident, tensor_waits=((lnv.sem, lnv.v),))
                drain_prep(2)
            lnT_done = S["lnev"].v

            @block.gpsimd
            def _(g):
                for ic in range(8):
                    g.dma_start(out=wsb[:, ic, :], in_=wq_in[ic]).then_inc(pjw.sem, 16)
                    pjw.inc(16)

            nmm = 0
            for oc in range(8):
                for ns in range(4):
                    pp = nmm % 2

                    @block.tensor
                    def _(tensor):
                        if nmm == 0:
                            tensor.wait_ge(pjw.sem, pjw.v)
                            tensor.wait_ge(S["lnev"].sem, lnT_done)
                        wge(tensor, pjev, nmm - 1)
                        ins = None
                        for ic in range(8):
                            ins = tensor.matmul(pj[:, pp, :],
                                                wsb[:, ic, oc * 128:(oc + 1) * 128],
                                                lnT[:, ic, ns * 512:(ns + 1) * 512],
                                                start=(ic == 0), stop=(ic == 7))
                        ins.then_inc(pjt.sem, 1)
                        pjt.inc()

                    @block.vector
                    def _(vector):
                        vector.wait_ge(pjt.sem, pjt.v)
                        vector.tensor_scalar_mul(out=qfm[:, oc, ns * 512:(ns + 1) * 512],
                                                 in0=pj[:, pp, :],
                                                 scalar1=0.03125).then_inc(pjev.sem, 1)
                        pjev.inc()
                    nmm += 1
                    drain_prep(2)
            q_done = pjt.v

            @block.gpsimd
            def _(g):
                g.wait_ge(pjt.sem, q_done)
                for ic in range(8):
                    g.dma_start(out=wsb[:, ic, :], in_=wk_in[ic]).then_inc(pjw.sem, 16)
                    pjw.inc(16)
            kw_ready = pjw.v

            nst = 0
            for oc in range(8):
                for ns in range(4):
                    pp = nmm % 2
                    sp = nst % 2

                    @block.tensor
                    def _(tensor):
                        if nmm == 32:
                            tensor.wait_ge(pjw.sem, kw_ready)
                        wge(tensor, pjev, nmm - 1)
                        ins = None
                        for ic in range(8):
                            ins = tensor.matmul(pj[:, pp, :],
                                                wsb[:, ic, oc * 128:(oc + 1) * 128],
                                                lnT[:, ic, ns * 512:(ns + 1) * 512],
                                                start=(ic == 0), stop=(ic == 7))
                        ins.then_inc(pjt.sem, 1)
                        pjt.inc()

                    @block.vector
                    def _(vector):
                        vector.wait_ge(pjt.sem, pjt.v)
                        wge(vector, pjst, (nst - 1) * 16)
                        vector.tensor_copy(out=stg[:, sp, :],
                                           in_=pj[:, pp, :]).then_inc(pjev.sem, 1)
                        pjev.inc()

                    @block.sync
                    def _(sync):
                        sync.wait_ge(pjev.sem, pjev.v)
                        sync.dma_start(out=agin_k[oc][:, ns * 512:(ns + 1) * 512],
                                       in_=stg[:, sp, :]).then_inc(pjst.sem, 16)
                        pjst.inc(16)
                    nmm += 1
                    nst += 1
                    drain_prep(1)
            k_done = pjt.v

            @block.gpsimd
            def _(g):
                g.wait_ge(pjt.sem, k_done)
                for ic in range(8):
                    g.dma_start(out=wsb[:, ic, :], in_=wv_in[ic]).then_inc(pjw.sem, 16)
                    pjw.inc(16)
            vw_ready = pjw.v

            for tc in range(NTC):
                for nh in range(2):
                    pp = nmm % 2
                    sp = nst % 2

                    @block.tensor
                    def _(tensor):
                        if nmm == 64:
                            tensor.wait_ge(pjw.sem, vw_ready)
                        wge(tensor, pjev, nmm - 1)
                        ins = None
                        for ic in range(8):
                            ins = tensor.matmul(pj[:, pp, :],
                                                lnT[:, ic, tc * 128:(tc + 1) * 128],
                                                wsb[:, ic, nh * 512:(nh + 1) * 512],
                                                start=(ic == 0), stop=(ic == 7))
                        ins.then_inc(pjt.sem, 1)
                        pjt.inc()

                    @block.vector
                    def _(vector):
                        vector.wait_ge(pjt.sem, pjt.v)
                        wge(vector, pjst, (nst - 1) * 16)
                        vector.tensor_copy(out=stg[:, sp, :],
                                           in_=pj[:, pp, :]).then_inc(pjev.sem, 1)
                        pjev.inc()

                    @block.sync
                    def _(sync):
                        sync.wait_ge(pjev.sem, pjev.v)
                        sync.dma_start(out=agin_v[tc][:, nh * 512:(nh + 1) * 512],
                                       in_=stg[:, sp, :]).then_inc(pjst.sem, 16)
                        pjst.inc(16)
                    nmm += 1
                    nst += 1
                    drain_prep(1)
            agin_done = pjst.v

            @block.gpsimd
            def _(g):
                g.wait_ge(pjst.sem, agin_done)
                g.collective_compute("AllGather", ALU.bypass,
                                     replica_groups=[list(range(NCORES))],
                                     ins=[dram["agin_k_h"].ap().opt()],
                                     outs=[dram["agout_k_h"].ap().opt()]).then_inc(cc.sem, 1)
                cc.inc()
                g.collective_compute("AllGather", ALU.bypass,
                                     replica_groups=[list(range(NCORES))],
                                     ins=[dram["agin_v_h"].ap().opt()],
                                     outs=[dram["agout_v_h"].ap().opt()]).then_inc(cc.sem, 1)
                cc.inc()
            ag_done = cc.v

            drain_prep(1 << 30)

        with ExitStack() as ctx:
            e2 = ctx.enter_context
            osb = e2(nc.sbuf_tensor("osb", [128, 8, D], F32))
            lbuf = e2(nc.sbuf_tensor("lbuf", [128, 8, NSTEP], F32))
            lrec = e2(nc.sbuf_tensor("lrec", [128, 8], F32))
            kt = e2(nc.sbuf_tensor("ktile", [128, 2, 8, 1024], BF16))
            vt = e2(nc.sbuf_tensor("vtile", [128, 2, 8, 1024], BF16))
            pbuf = e2(nc.sbuf_tensor("pbuf", [128, 2, 1024], BF16))
            ptsb = e2(nc.sbuf_tensor("ptsb", [128, 2, 1024], BF16))
            spsum = e2(nc.psum_tensor("spsum", [128, 2, 1024], F32))
            ptp = e2(nc.psum_tensor("ptp", [128, 2, 1024], BF16))
            pvp = e2(nc.psum_tensor("pvp", [128, 1024], F32))

            kldp = [S["kld0"], S["kld1"]]
            vldp = [S["vld0"], S["vld1"]]
            smm, sexp = S["smm"], S["sexp"]
            stmm, sptev, spv = S["stmm"], S["sptev"], S["spv"]
            svadd, vmask, fin = S["svadd"], S["vmask"], S["fin"]

            def emit_transp(c):
                qs, sp = c["qs"], c["sp"]

                @block.tensor
                def _(tensor):
                    tensor.wait_ge(sexp.sem, qs + 1)
                    wge(tensor, sptev, qs - 1)
                    ins = None
                    for kc in range(8):
                        ins = tensor.transpose(ptp[:, sp, kc * 128:(kc + 1) * 128],
                                               pbuf[:, sp, kc * 128:(kc + 1) * 128],
                                               ident[:])
                    ins.then_inc(stmm.sem, 1)
                    stmm.inc()

                @block.vector
                def _(vector):
                    vector.wait_ge(stmm.sem, qs + 1)
                    wge(vector, spv, qs - 1)
                    vector.tensor_copy(out=ptsb[:, sp, :],
                                       in_=ptp[:, sp, :]).then_inc(sptev.sem, 1)
                    sptev.inc()

            def emit_exps(specs):
                @block.scalar
                def _(scalar):
                    for qs, sp, r, j, s_idx, vm in specs:
                        scalar.wait_ge(smm.sem, qs + 1)
                        if vm is not None:
                            scalar.wait_ge(vmask.sem, vm)
                        if s_idx == 1 and j == 0 and r == 0:
                            scalar.wait_ge(fin.sem, 1)
                        bcol = (s_idx * 8 + r) * NSTEP + j
                        scalar.activation(out=pbuf[:, sp, :], in_=spsum[:, sp, :],
                                          func=AF.Exp, bias=btab[:, bcol:bcol + 1],
                                          scale=1.0,
                                          accum_out=lbuf[:, r, j:j + 1]).then_inc(sexp.sem, 1)

            def emit_pv(c):
                qs, sp, ring, r, j = c["qs"], c["sp"], c["ring"], c["r"], c["j"]

                @block.tensor
                def _(tensor):
                    tensor.wait_ge(sptev.sem, qs + 1)
                    wge(tensor, svadd, qs)
                    ins = None
                    for kc in range(8):
                        for h in range(2):
                            ins = tensor.matmul(
                                pvp[:, h * 512:(h + 1) * 512],
                                ptsb[:, sp, kc * 128:(kc + 1) * 128],
                                vt[:, ring, kc, h * 512:(h + 1) * 512],
                                start=(kc == 0), stop=(kc == 7))
                    ins.then_inc(spv.sem, 1)
                    spv.inc()

                @block.vector
                def _(vector):
                    vector.wait_ge(spv.sem, qs + 1)
                    if j == 0:
                        vector.tensor_copy(out=osb[:, r, :], in_=pvp[:, :]).then_inc(svadd.sem, 1)
                    else:
                        vector.tensor_add(out=osb[:, r, :], in0=osb[:, r, :],
                                          in1=pvp[:, :]).then_inc(svadd.sem, 1)
                    svadd.inc()

            for s_idx in range(2):
                pend = []
                for j in range(NSTEP):
                    exp_specs = []
                    g = s_idx * NSTEP + j
                    ring = g % 2

                    @block.sync
                    def _(sync):
                        if g == 0:
                            sync.wait_ge(pjst.sem, agin_done)
                            sync.wait_ge(S["ppd"].sem, 2)
                        if g == 1:
                            sync.wait_ge(cc.sem, ag_done)
                        wge(sync, smm, 8 * (g - 1))
                        if j == 0:
                            for ic in range(8):
                                sync.dma_start(
                                    out=kt[:, ring, ic, :],
                                    in_=agin_k[ic][:, s_idx * 1024:(s_idx + 1) * 1024],
                                ).then_inc(kldp[ring].sem, 16)
                                kldp[ring].inc(16)
                        else:
                            rank, sub = _posblock_src(j - 1)
                            for ic in range(8):
                                sync.dma_start(
                                    out=kt[:, ring, ic, :],
                                    in_=agout_k[8 * rank + ic][:, sub * 1024:(sub + 1) * 1024],
                                ).then_inc(kldp[ring].sem, 16)
                                kldp[ring].inc(16)

                    @block.gpsimd
                    def _(gq):
                        if g == 0:
                            gq.wait_ge(pjst.sem, agin_done)
                            gq.wait_ge(S["ppd"].sem, 2)
                        if g == 1:
                            gq.wait_ge(cc.sem, ag_done)
                        wge(gq, spv, 8 * (g - 1))
                        if j == 0:
                            for kc in range(8):
                                gq.dma_start(out=vt[:, ring, kc, :],
                                             in_=agin_v[s_idx * 8 + kc]).then_inc(vldp[ring].sem, 16)
                                vldp[ring].inc(16)
                        else:
                            rank, sub = _posblock_src(j - 1)
                            for kc in range(8):
                                gq.dma_start(
                                    out=vt[:, ring, kc, :],
                                    in_=agout_v[NTC * rank + sub * 8 + kc]).then_inc(vldp[ring].sem, 16)
                                vldp[ring].inc(16)
                    k_tgt, v_tgt = kldp[ring].v, vldp[ring].v

                    for r in range(8):
                        qs = g * 8 + r
                        sp = qs % 2
                        qcol0 = s_idx * 1024

                        @block.tensor
                        def _(tensor):
                            if r == 0:
                                tensor.wait_ge(kldp[ring].sem, k_tgt)
                                tensor.wait_ge(vldp[ring].sem, v_tgt)
                            wge(tensor, sexp, qs - 1)
                            ins = None
                            for ic in range(8):
                                for h in range(2):
                                    ins = tensor.matmul(
                                        spsum[:, sp, h * 512:(h + 1) * 512],
                                        qfm[:, ic, qcol0 + r * 128:qcol0 + (r + 1) * 128],
                                        kt[:, ring, ic, h * 512:(h + 1) * 512],
                                        start=(ic == 0), stop=(ic == 7))
                            ins.then_inc(smm.sem, 1)
                            smm.inc()

                        if j == 0:
                            @block.vector
                            def _(vector):
                                vector.wait_ge(smm.sem, qs + 1)
                                ins = vector.tensor_add(
                                    out=spsum[:, sp, r * 128:(r + 1) * 128],
                                    in0=spsum[:, sp, r * 128:(r + 1) * 128],
                                    in1=tri[:])
                                if r < 7:
                                    ins = vector.tensor_scalar_add(
                                        out=spsum[:, sp, (r + 1) * 128:1024],
                                        in0=spsum[:, sp, (r + 1) * 128:1024],
                                        scalar1=MASKNEG)
                                ins.then_inc(vmask.sem, 1)
                                vmask.inc()

                        exp_specs.append((qs, sp, r, j, s_idx,
                                          vmask.v if j == 0 else None))
                        sexp.inc()

                        ctxq = dict(qs=qs, sp=sp, ring=ring, r=r, j=j)
                        pend.append(ctxq)
                        i_local = len(pend) - 1
                        if i_local >= 1:
                            emit_transp(pend[i_local - 1])
                        if i_local >= 2:
                            emit_pv(pend[i_local - 2])
                    emit_exps(exp_specs)

                emit_transp(pend[-1])
                emit_pv(pend[-2])
                emit_pv(pend[-1])
                pend.clear()

                @block.vector
                def _(vector):
                    vector.wait_ge(svadd.sem, svadd.v)
                    vector.reduce_sum(out=lrec[:, 0:8].rearrange("p a -> p a ()"),
                                      in_=lbuf[:, :, :], axis=AX.X).then_inc(S["fv"].sem, 1)
                    S["fv"].inc()
                    vector.wait_ge(S["fv"].sem, S["fv"].v)
                    vector.reciprocal(out=lrec[:, 0:8], in_=lrec[:, 0:8]).then_inc(S["fv"].sem, 1)
                    S["fv"].inc()
                    vector.wait_ge(S["fv"].sem, S["fv"].v)
                    ins = None
                    for r in range(8):
                        tc = s_idx * 8 + r
                        ins = vector.scalar_tensor_tensor(
                            out=resid[:, tc, :], in0=osb[:, r, :],
                            scalar=lrec[:, r:r + 1], in1=resid[:, tc, :],
                            op0=ALU.mult, op1=ALU.add)
                    ins.then_inc(fin.sem, 1)
                    fin.inc()


def emit_l0_ffn(nc, block, S, resid, ident, epsc, dram, dbg_h=None, dbg_w=None):
    fl0 = [S["fl00"], S["fl01"]]
    fl1 = [S["fl10"], S["fl11"]]
    fl2 = [S["fl20"], S["fl21"]]
    ft, fev, fv = S["ft"], S["fev"], S["fv"]
    lnv = S["lnv"]
    w0c, w1c, w2c = dram["w0c"], dram["w1c"], dram["w2c"]

    with ExitStack() as ctx:
        e = ctx.enter_context
        lnb = e(nc.sbuf_tensor("lnbB", [128, 2, D], BF16))
        lnS = e(nc.sbuf_tensor("lnSB", [128, 8, 512], BF16))
        h1 = e(nc.sbuf_tensor("h1B", [128, 32, 512], BF16))
        h2 = e(nc.sbuf_tensor("h2B", [128, 32, 512], BF16))
        w0b = e(nc.sbuf_tensor("w0bB", [128, 2, 8, 512], BF16))
        w1b = e(nc.sbuf_tensor("w1bB", [128, 2, 32, 256], BF16))
        w2b = e(nc.sbuf_tensor("w2bB", [128, 2, 1024], BF16))
        scr = _ln_scratch(nc, e, "B")
        ptT = e(nc.psum_tensor("ptTB", [128, 2, 1024], BF16))
        hp = e(nc.psum_tensor("hpB", [128, 2, 512], F32))
        h3p = e(nc.psum_tensor("h3pB", [128, 2, 1024], F32))

        for t in range(4):
            for i in range(4):
                tc = t * 4 + i
                pp = i % 2
                emit_ln(block, S, resid, tc, pp, scr, epsc, lnb[:, pp, :],
                        norm_waits=((S["lnt"].sem, S["lnt"].v - 1),))
                emit_transpose_1024(
                    block, S, lnb[:, pp, :], ptT, pp,
                    lnS[:, :, i * 128:(i + 1) * 128],
                    ident, tensor_waits=((lnv.sem, lnv.v),),
                    vector_waits=((ft.sem, ft.v),))
            lnS_done = S["lnev"].v

            for f1g in range(8):
                gp = f1g % 2

                @block.gpsimd
                def _(g):
                    wge(g, ft, ft.v - 4 if (t > 0 or f1g >= 2) else 0)
                    g.dma_start(out=w0b[:, gp, :, :],
                                in_=w0c[:, :, f1g * 512:(f1g + 1) * 512]
                                .rearrange("a p b -> p a b")).then_inc(fl0[gp].sem, 16)
                    fl0[gp].inc(16)

                for lf in range(4):
                    f1c = f1g * 4 + lf
                    pp = f1c % 2

                    @block.tensor
                    def _(tensor):
                        if lf == 0:
                            tensor.wait_ge(fl0[gp].sem, fl0[gp].v)
                        if f1c == 0:
                            tensor.wait_ge(S["lnev"].sem, lnS_done)
                        wge(tensor, fev, fev.v - 1)
                        ins = None
                        for ic in range(8):
                            ins = tensor.matmul(hp[:, pp, :],
                                                w0b[:, gp, ic, lf * 128:(lf + 1) * 128],
                                                lnS[:, ic, :],
                                                start=(ic == 0), stop=(ic == 7))
                        ins.then_inc(ft.sem, 1)
                        ft.inc()

                    @block.vector
                    def _(vector):
                        vector.wait_ge(ft.sem, ft.v)
                        vector.tensor_relu(out=h1[:, f1c, :],
                                           in_=hp[:, pp, :]).then_inc(fev.sem, 1)
                        fev.inc()

            for f2g in range(16):
                gp = f2g % 2

                @block.gpsimd
                def _(g):
                    wge(g, ft, ft.v - 2 if (t > 0 or f2g >= 2) else 0)
                    g.dma_start(out=w1b[:, gp, :, :],
                                in_=w1c[f2g].rearrange("a p b -> p a b")).then_inc(fl1[gp].sem, 16)
                    fl1[gp].inc(16)

                for lf in range(2):
                    f2c = f2g * 2 + lf
                    pp = f2c % 2

                    @block.tensor
                    def _(tensor):
                        if lf == 0:
                            tensor.wait_ge(fl1[gp].sem, fl1[gp].v)
                        wge(tensor, fev, fev.v - 1)
                        ins = None
                        for f1c in range(32):
                            ins = tensor.matmul(hp[:, pp, :],
                                                w1b[:, gp, f1c, lf * 128:(lf + 1) * 128],
                                                h1[:, f1c, :],
                                                start=(f1c == 0), stop=(f1c == 31))
                        ins.then_inc(ft.sem, 1)
                        ft.inc()

                    @block.vector
                    def _(vector):
                        vector.wait_ge(ft.sem, ft.v)
                        vector.tensor_relu(out=h2[:, f2c, :],
                                           in_=hp[:, pp, :]).then_inc(fev.sem, 1)
                        fev.inc()
            h2_done = fev.v

            if dbg_h is not None and t == 0:
                @block.gpsimd
                def _(g):
                    g.wait_ge(fev.sem, h2_done)
                    g.dma_start(out=dbg_h[0], in_=h1[:, :, :]).then_inc(S["st"].sem, 16)
                    g.dma_start(out=dbg_h[1], in_=h2[:, :, :]).then_inc(S["st"].sem, 16)
                    g.dma_start(out=dbg_w[0], in_=lnS[:, :, :]).then_inc(S["st"].sem, 16)
                    g.dma_start(out=dbg_w[1].rearrange("p a b -> a p b"),
                                in_=dram["w0c"][:, :, 6 * 512:7 * 512]).then_inc(S["st"].sem, 16)
                    S["st"].inc(64)

                @block.tensor
                def _(tensor):
                    tensor.wait_ge(S["st"].sem, S["st"].v)

            for i in range(4):
                tc = t * 4 + i
                hp3 = i % 2
                for f2c in range(32):
                    wp = (i * 32 + f2c) % 2

                    @block.gpsimd
                    def _(g):
                        wge(g, ft, ft.v - 1 if (t > 0 or i > 0 or f2c >= 2) else 0)
                        g.dma_start(out=w2b[:, wp, :], in_=w2c[f2c]).then_inc(fl2[wp].sem, 16)
                        fl2[wp].inc(16)

                    @block.tensor
                    def _(tensor):
                        tensor.wait_ge(fl2[wp].sem, fl2[wp].v)
                        if i == 0 and f2c == 0:
                            tensor.wait_ge(fev.sem, h2_done)
                        if f2c == 0:
                            wge(tensor, fv, fv.v - 1)
                        ins = None
                        for h in range(2):
                            ins = tensor.matmul(h3p[:, hp3, h * 512:(h + 1) * 512],
                                                h2[:, f2c, i * 128:(i + 1) * 128],
                                                w2b[:, wp, h * 512:(h + 1) * 512],
                                                start=(f2c == 0), stop=(f2c == 31))
                        ins.then_inc(ft.sem, 1)
                        ft.inc()

                @block.vector
                def _(vector):
                    vector.wait_ge(ft.sem, ft.v)
                    vector.tensor_add(out=resid[:, tc, :], in0=resid[:, tc, :],
                                      in1=h3p[:, hp3, :]).then_inc(fv.sem, 1)
                    fv.inc()


def emit_l1_attn(nc, block, S, resid, ident, epsc, onebf, wqkvT_in, dram):
    cld, ct, cev, cv, cs = S["cld"], S["ct"], S["cev"], S["cv"], S["cs"]
    ccc, cst = S["ccc"], S["cst"]
    lnv = S["lnv"]
    arin, arout = dram["arin"], dram["arout"]

    with ExitStack() as ctx:
        e = ctx.enter_context
        lnb = e(nc.sbuf_tensor("lnbC", [128, 2, D], BF16))
        qhfm = e(nc.sbuf_tensor("qhfmC", [128, 8, TPC], BF16))
        hs = e(nc.sbuf_tensor("hsC", [128, 2, 16], F32))
        kvsb = e(nc.sbuf_tensor("kvsbC", [128, 1040], F32))
        scr = _ln_scratch(nc, e, "C")

        with ExitStack() as pctx:
            e2 = pctx.enter_context
            lnT = e2(nc.sbuf_tensor("lnTC", [128, 8, TPC], BF16))
            khf = e2(nc.sbuf_tensor("khfC", [128, NTC, D], BF16))
            wqb = e2(nc.sbuf_tensor("wqbC", [128, 8, 1024], BF16))
            pt32 = e2(nc.sbuf_tensor("pt32C", [128, 2, D], BF16))
            sqt = e2(nc.sbuf_tensor("sqtC", [128, D], F32))
            qhat = e2(nc.sbuf_tensor("qhatC", [128, 2, D], BF16))
            vta = e2(nc.sbuf_tensor("vtaC", [128, 2, 1024], BF16))
            ptT = e2(nc.psum_tensor("ptTC", [128, 1, 1024], BF16))
            pjc = e2(nc.psum_tensor("pjcC", [128, 2, 1024], F32))
            kvp = e2(nc.psum_tensor("kvpC", [64, 1024], F32))
            kvq = e2(nc.psum_tensor("kvqC", [64, 16], F32))

            for tc in range(NTC):
                pp = tc % 2
                emit_ln(block, S, resid, tc, pp, scr, epsc, lnb[:, pp, :],
                        norm_waits=((S["lnt"].sem, S["lnt"].v - 1),))
                emit_transpose_1024(
                    block, S, lnb[:, pp, :], ptT, 0,
                    lnT[:, :, tc * 128:(tc + 1) * 128],
                    ident, tensor_waits=((lnv.sem, lnv.v),), ping=False)
            lnT_done = S["lnev"].v

            @block.gpsimd
            def _(g):
                for ic in range(8):
                    g.dma_start(out=wqb[:, ic, :],
                                in_=wqkvT_in[ic][:, 0:1024]).then_inc(cld.sem, 16)
                    cld.inc(16)
            qw_ready = cld.v

            for tc in range(NTC):
                pp = tc % 2

                @block.tensor
                def _(tensor):
                    if tc == 0:
                        tensor.wait_ge(cld.sem, qw_ready)
                        tensor.wait_ge(S["lnev"].sem, lnT_done)
                    wge(tensor, cev, cev.v - 1)
                    ins = None
                    for ic in range(8):
                        for h in range(2):
                            ins = tensor.matmul(pjc[:, pp, h * 512:(h + 1) * 512],
                                                lnT[:, ic, tc * 128:(tc + 1) * 128],
                                                wqb[:, ic, h * 512:(h + 1) * 512],
                                                start=(ic == 0), stop=(ic == 7))
                    ins.then_inc(ct.sem, 1)
                    ct.inc()

                @block.vector
                def _(vector):
                    vector.wait_ge(ct.sem, ct.v)
                    wge(vector, cs, cs.v)
                    vector.tensor_copy(out=pt32[:, pp, :], in_=pjc[:, pp, :]).then_inc(cev.sem, 1)
                    cev.inc()

                _emit_spl2(nc, block, S, pt32, sqt, hs, qhat[:, pp, :], pp, cs, cv, cev)

                emit_transpose_1024(
                    block, S, qhat[:, pp, :], ptT, 0,
                    qhfm[:, :, tc * 128:(tc + 1) * 128],
                    ident, tensor_waits=((cs.sem, cs.v),), ping=False)

            if C_STAGE < 2:
                return
            @block.gpsimd
            def _(g):
                g.wait_ge(ct.sem, ct.v)
                for ic in range(8):
                    g.dma_start(out=wqb[:, ic, :],
                                in_=wqkvT_in[ic][:, 1024:2048]).then_inc(cld.sem, 16)
                    cld.inc(16)
            kw_ready = cld.v

            for tc in range(NTC):
                pp = tc % 2

                @block.tensor
                def _(tensor):
                    if tc == 0:
                        tensor.wait_ge(cld.sem, kw_ready)
                    wge(tensor, cev, cev.v - 1)
                    ins = None
                    for ic in range(8):
                        for h in range(2):
                            ins = tensor.matmul(pjc[:, pp, h * 512:(h + 1) * 512],
                                                lnT[:, ic, tc * 128:(tc + 1) * 128],
                                                wqb[:, ic, h * 512:(h + 1) * 512],
                                                start=(ic == 0), stop=(ic == 7))
                    ins.then_inc(ct.sem, 1)
                    ct.inc()

                @block.vector
                def _(vector):
                    vector.wait_ge(ct.sem, ct.v)
                    wge(vector, cs, cs.v)
                    vector.tensor_copy(out=pt32[:, pp, :], in_=pjc[:, pp, :]).then_inc(cev.sem, 1)
                    cev.inc()

                _emit_spl2(nc, block, S, pt32, sqt, hs, khf[:, tc, :], pp, cs, cv, cev)

            khf_done = cs.v

            if C_STAGE < 3:
                return
            @block.gpsimd
            def _(g):
                g.wait_ge(ct.sem, ct.v)
                for ic in range(8):
                    g.dma_start(out=wqb[:, ic, :],
                                in_=wqkvT_in[ic][:, 2048:3072]).then_inc(cld.sem, 16)
                    cld.inc(16)
            vw_ready = cld.v

            for tc in range(NTC):
                pp = tc % 2

                @block.tensor
                def _(tensor):
                    if tc == 0:
                        tensor.wait_ge(cld.sem, vw_ready)
                        tensor.wait_ge(cs.sem, khf_done)
                    wge(tensor, cev, cev.v - 1)
                    ins = None
                    for ic in range(8):
                        for h in range(2):
                            ins = tensor.matmul(pjc[:, pp, h * 512:(h + 1) * 512],
                                                lnT[:, ic, tc * 128:(tc + 1) * 128],
                                                wqb[:, ic, h * 512:(h + 1) * 512],
                                                start=(ic == 0), stop=(ic == 7))
                    ins.then_inc(ct.sem, 1)
                    ct.inc()

                @block.vector
                def _(vector):
                    vector.wait_ge(ct.sem, ct.v)
                    if tc >= 2:
                        wge(vector, ct, ct.v)
                    vector.tensor_copy(out=vta[:, pp, :],
                                       in_=pjc[:, pp, :]).then_inc(cev.sem, 1)
                    cev.inc()

                @block.tensor
                def _(tensor):
                    tensor.wait_ge(cev.sem, cev.v)
                    ins = None
                    for h in range(16):
                        tensor.matmul(kvp[0:64, h * 64:(h + 1) * 64],
                                      khf[:, tc, h * 64:(h + 1) * 64],
                                      vta[:, pp, h * 64:(h + 1) * 64],
                                      start=(tc == 0 and h % 8 == 0),
                                      stop=(tc == NTC - 1 and h % 8 == 7))
                        ins = tensor.matmul(kvq[0:64, h:h + 1],
                                            khf[:, tc, h * 64:(h + 1) * 64],
                                            onebf[:, 0:1],
                                            start=(tc == 0 and h == 0),
                                            stop=(tc == NTC - 1 and h == 15))
                    ins.then_inc(ct.sem, 1)
                    ct.inc()
            kv_done = ct.v

            @block.vector
            def _(vector):
                vector.wait_ge(ct.sem, kv_done)
                vector.tensor_scalar_mul(out=kvsb[0:64, 0:1024], in0=kvp[0:64, :],
                                         scalar1=0.125)
                vector.tensor_copy(out=kvsb[0:64, 1024:1040],
                                   in_=kvq[0:64, :]).then_inc(cv.sem, 1)
                cv.inc()

            @block.sync
            def _(sync):
                sync.wait_ge(cv.sem, cv.v)
                sync.dma_start(out=arin, in_=kvsb[0:64, :]).then_inc(cst.sem, 16)
                cst.inc(16)

            if C_STAGE < 4:
                return

            @block.gpsimd
            def _(g):
                g.wait_ge(cst.sem, cst.v)
                g.collective_compute("AllReduce", ALU.add,
                                     replica_groups=[list(range(NCORES))],
                                     ins=[dram["arin_h"].ap().opt()],
                                     outs=[dram["arout_h"].ap().opt()]).then_inc(ccc.sem, 1)
                ccc.inc()

            @block.sync
            def _(sync):
                sync.wait_ge(ccc.sem, ccc.v)
                sync.dma_start(out=kvsb[0:64, :], in_=arout).then_inc(cst.sem, 16)
                sync.dma_start(out=kvsb[64:128, :], in_=arout).then_inc(cst.sem, 16)
                cst.inc(32)

            @block.vector
            def _(vector):
                vector.wait_ge(cst.sem, cst.v)
                vector.tensor_scalar_add(out=kvsb[:, 1024:1040],
                                         in0=kvsb[:, 1024:1040],
                                         scalar1=1e-6).then_inc(cv.sem, 1)
                cv.inc()
        if C_STAGE < 5:
            return
        with ExitStack() as pctx:
            e2 = pctx.enter_context
            normc = e2(nc.sbuf_tensor("normcC", [128, 2, 2, 8], F32))
            otmp = e2(nc.sbuf_tensor("otmpC", [128, 2, D], BF16))
            kvsbb = e2(nc.sbuf_tensor("kvsbbC", [128, 1040], BF16))
            op = e2(nc.psum_tensor("opC", [128, 2, 2, 512], F32))
            oq = e2(nc.psum_tensor("oqC", [128, 2, 2, 512], F32))

            @block.vector
            def _(vector):
                vector.wait_ge(cv.sem, cv.v)
                vector.tensor_copy(out=kvsbb[:, :], in_=kvsb[:, :]).then_inc(cv.sem, 1)
                cv.inc()
            kv_ready = cv.v

            for tc in range(NTC):
                pp = tc % 2

                @block.tensor
                def _(tensor):
                    if tc == 0:
                        tensor.wait_ge(cv.sem, kv_ready)
                    wge(tensor, cv, kv_ready + 3 * (tc - 1))
                    ins = None
                    for h in range(16):
                        po = (h % 2) * 64
                        par = h % 2
                        hh = h // 2
                        tensor.matmul(op[:, pp, par, hh * 64:(hh + 1) * 64],
                                      qhfm[po:po + 64, hh, tc * 128:(tc + 1) * 128],
                                      kvsbb[po:po + 64, h * 64:(h + 1) * 64],
                                      start=(hh == 0), stop=(hh == 7))
                        ins = tensor.matmul(oq[:, pp, par, hh:hh + 1],
                                            qhfm[po:po + 64, hh, tc * 128:(tc + 1) * 128],
                                            kvsbb[po:po + 64, 1024 + h:1025 + h],
                                            start=(hh == 0), stop=(hh == 7))
                    ins.then_inc(ct.sem, 1)
                    ct.inc()

                @block.vector
                def _(vector):
                    vector.wait_ge(ct.sem, ct.v)
                    vector.reciprocal(out=normc[:, pp, :, :],
                                      in_=oq[:, pp, :, 0:8]).then_inc(cv.sem, 1)
                    cv.inc()
                    vector.wait_ge(cv.sem, cv.v)
                    vector.tensor_tensor(
                        out=otmp[:, pp, :].rearrange("p (a q b) -> p a q b", a=8, q=2),
                        in0=op[:, pp, :, :].rearrange("p q (a b) -> p a q b", a=8),
                        in1=normc[:, pp, :, :].rearrange("p q a -> p a q ()").broadcast_to([128, 8, 2, 64]),
                        op=ALU.mult).then_inc(cv.sem, 1)
                    cv.inc()
                    vector.wait_ge(cv.sem, cv.v)
                    vector.tensor_add(out=resid[:, tc, :], in0=resid[:, tc, :],
                                      in1=otmp[:, pp, :]).then_inc(cv.sem, 1)
                    cv.inc()


def _emit_spl2(nc, block, S, src, sqt, hs, dst_ap, pp, cs, cv, cev):
    @block.vector
    def _(vector):
        vector.wait_ge(cev.sem, cev.v)
        vector.tensor_mul(out=sqt[:, :], in0=src[:, pp, :],
                          in1=src[:, pp, :]).then_inc(cv.sem, 1)
        cv.inc()
        vector.wait_ge(cv.sem, cv.v)
        vector.reduce_sum(out=hs[:, pp, :].rearrange("p a -> p a ()"),
                          in_=sqt[:, :].rearrange("p (a b) -> p a b", a=16),
                          axis=AX.X).then_inc(cv.sem, 1)
        cv.inc()
        vector.wait_ge(cv.sem, cv.v)
        vector.tensor_scalar_max(out=hs[:, pp, :], in0=hs[:, pp, :],
                                 scalar1=1e-24).then_inc(cv.sem, 1)
        cv.inc()
    hs_ready = cv.v

    @block.scalar
    def _(scalar):
        scalar.wait_ge(cv.sem, hs_ready)
        scalar.activation(out=hs[:, pp, :], in_=hs[:, pp, :], func=AF.Ln)
        scalar.activation(out=hs[:, pp, :], in_=hs[:, pp, :], func=AF.Exp,
                          scale=-0.5).then_inc(cs.sem, 1)
        cs.inc()

    @block.vector
    def _(vector):
        vector.wait_ge(cs.sem, cs.v)
        vector.tensor_tensor(
            out=sqt[:, :].rearrange("p (a b) -> p a b", a=16),
            in0=src[:, pp, :].rearrange("p (a b) -> p a b", a=16),
            in1=hs[:, pp, :].rearrange("p a -> p a ()").broadcast_to([128, 16, 64]),
            op=ALU.mult).then_inc(cv.sem, 1)
        cv.inc()
    mult_ready = cv.v

    @block.scalar
    def _(scalar):
        scalar.wait_ge(cv.sem, mult_ready)
        scalar.activation(out=sqt[:, :], in_=sqt[:, :],
                          func=AF.Exp).then_inc(cs.sem, 1)
        cs.inc()

    @block.vector
    def _(vector):
        vector.wait_ge(cs.sem, cs.v)
        vector.tensor_scalar_add(out=sqt[:, :], in0=sqt[:, :],
                                 scalar1=1.0).then_inc(cv.sem, 1)
        cv.inc()
    add_ready = cv.v

    @block.scalar
    def _(scalar):
        scalar.wait_ge(cv.sem, add_ready)
        scalar.activation(out=dst_ap, in_=sqt[:, :],
                          func=AF.Ln).then_inc(cs.sem, 1)
        cs.inc()


def emit_l1_ffn(nc, block, S, resid, ident, epsc, nmagic, dram, dbg_w=None, dbg_h=None):
    dg = [S["dg0"], S["dg1"]]
    dw = [S["dw0"], S["dw1"]]
    dt, dev, dv, ds = S["dt"], S["dev"], S["dv"], S["ds"]
    lnv, lnt, lnev = S["lnv"], S["lnt"], S["lnev"]
    wgc, wuc, wdc = dram["wgc"], dram["wuc"], dram["wdc"]

    with ExitStack() as ctx:
        e = ctx.enter_context
        lnf = e(nc.sbuf_tensor("lnfD", [128, 2, D], F32))
        xqb = e(nc.sbuf_tensor("xqbD", [128, 2, D], BF16))
        xqfm = e(nc.sbuf_tensor("xqfmD", [128, 2, 8, 128], BF16))
        wgb = e(nc.sbuf_tensor("wgbD", [128, 2, 8, 512], BF16))
        wub = e(nc.sbuf_tensor("wubD", [128, 2, 8, 512], BF16))
        wdb = e(nc.sbuf_tensor("wdbD", [128, 2, 1024], BF16))
        gs = e(nc.sbuf_tensor("gsD", [128, 4096], F32))
        gr = e(nc.sbuf_tensor("grD", [128, 4096], F32))
        us = e(nc.sbuf_tensor("usD", [128, 4096], F32))
        hqb = e(nc.sbuf_tensor("hqbD", [128, 4096], BF16))
        hqfm = e(nc.sbuf_tensor("hqfmD", [128, 32, 128], BF16))
        colq = e(nc.sbuf_tensor("colqD", [128, 8], F32))
        scr = _ln_scratch(nc, e, "D")
        ptT = e(nc.psum_tensor("ptTD", [128, 2, 1024], BF16))
        gup = e(nc.psum_tensor("gupD", [128, 2, 2, 512], F32))
        dnp = e(nc.psum_tensor("dnpD", [128, 1024], F32))

        if dbg_w is not None:
            @block.gpsimd
            def _(g):
                g.dma_start(out=dbg_w[0][:, 0:8, 0:512],
                            in_=dram["wgc"][0][:, 0:4096].rearrange("p (a b) -> p a b", a=8)).then_inc(S["st"].sem, 16)
                g.dma_start(out=dbg_w[1][:, 0:2, 0:512],
                            in_=dram["wdc"][0][:, 0:1024].rearrange("p (a b) -> p a b", a=2)).then_inc(S["st"].sem, 16)
                S["st"].inc(32)

        devac_tgt = []
        for tc in range(NTC):
            pp = tc % 2
            emit_ln(block, S, resid, tc, pp, scr, epsc, lnf[:, pp, :],
                    norm_waits=((S["ds"].sem, S["ds"].v),))

            @block.vector
            def _(vector):
                vector.wait_ge(lnv.sem, lnv.v)
                wge(vector, dt, dt.v)
                wge(vector, ds, ds.v)
                vector.reduce_max(out=colq[:, 0:1], in_=lnf[:, pp, :], axis=AX.X,
                                  apply_absolute_value=True).then_inc(dv.sem, 1)
                dv.inc()
                vector.wait_ge(dv.sem, dv.v)
                vector.tensor_scalar_max(out=colq[:, 0:1], in0=colq[:, 0:1],
                                         scalar1=1e-5).then_inc(dv.sem, 1)
                dv.inc()
                vector.wait_ge(dv.sem, dv.v)
                vector.reciprocal(out=colq[:, 6:7], in_=colq[:, 0:1])
                vector.tensor_scalar_mul(out=colq[:, 2:3], in0=colq[:, 0:1],
                                         scalar1=1.0 / 127.0).then_inc(dv.sem, 1)
                dv.inc()
                vector.wait_ge(dv.sem, dv.v)
                vector.tensor_scalar_mul(out=colq[:, 1:2], in0=colq[:, 6:7],
                                         scalar1=127.0).then_inc(dv.sem, 1)
                dv.inc()
                vector.wait_ge(dv.sem, dv.v)
                vector.tensor_scalar(out=lnf[:, pp, :], in0=lnf[:, pp, :],
                                     scalar1=colq[:, 1:2], scalar2=MAGIC,
                                     op0=ALU.mult, op1=ALU.add).then_inc(dv.sem, 1)
                dv.inc()

            @block.vector
            def _(vector):
                vector.wait_ge(dv.sem, dv.v)
                vector.tensor_scalar_add(out=xqb[:, pp, :], in0=lnf[:, pp, :],
                                         scalar1=nmagic[:, 0:1]).then_inc(dv.sem, 1)
                dv.inc()

            emit_transpose_1024(
                block, S, xqb[:, pp, :], ptT, pp,
                xqfm[:, pp, :, :],
                ident, tensor_waits=((dv.sem, dv.v),),
                vector_waits=((dt.sem, dt.v),))
            xq_done = lnev.v

            for og in range(8):
                wp = og % 2
                pp2 = og % 2

                @block.gpsimd
                def _(g):
                    wge(g, dt, dt.v - 1 if (tc > 0 or og >= 2) else 0)
                    g.dma_start(out=wgb[:, wp, :, :],
                                in_=wgc[:, :, og * 512:(og + 1) * 512]
                                .rearrange("a p b -> p a b")).then_inc(dg[wp].sem, 16)
                    g.dma_start(out=wub[:, wp, :, :],
                                in_=wuc[:, :, og * 512:(og + 1) * 512]
                                .rearrange("a p b -> p a b")).then_inc(dg[wp].sem, 16)
                    dg[wp].inc(32)

                @block.tensor
                def _(tensor):
                    tensor.wait_ge(dg[wp].sem, dg[wp].v)
                    if og == 0:
                        tensor.wait_ge(lnev.sem, xq_done)
                    wge(tensor, dev, dev.v - 1)
                    for ic in range(8):
                        tensor.matmul(gup[:, pp2, 0, :], xqfm[:, pp, ic, :],
                                      wgb[:, wp, ic, :],
                                      start=(ic == 0), stop=(ic == 7))
                    ins = None
                    for ic in range(8):
                        ins = tensor.matmul(gup[:, pp2, 1, :], xqfm[:, pp, ic, :],
                                            wub[:, wp, ic, :],
                                            start=(ic == 0), stop=(ic == 7))
                    ins.then_inc(dt.sem, 1)
                    dt.inc()

                @block.vector
                def _(vector):
                    vector.wait_ge(dt.sem, dt.v)
                    vector.tensor_copy(out=gr[:, og * 512:(og + 1) * 512],
                                       in_=gup[:, pp2, 0, :])
                    vector.tensor_scalar_mul(out=us[:, og * 512:(og + 1) * 512],
                                             in0=gup[:, pp2, 1, :],
                                             scalar1=colq[:, 2:3]).then_inc(dev.sem, 1)
                    dev.inc()
            gu_v = dev.v

            @block.scalar
            def _(scalar):
                scalar.wait_ge(dev.sem, gu_v)
                scalar.activation(out=gs[:, :], in_=gr[:, :], func=AF.Silu,
                                  scale=colq[:, 2:3]).then_inc(ds.sem, 1)
                ds.inc()
            gu_s = ds.v

            if dbg_w is not None and tc == 0:
                @block.gpsimd
                def _(g):
                    g.wait_ge(ds.sem, gu_s)
                    g.wait_ge(dev.sem, gu_v)
                    g.dma_start(out=dbg_h[0][:, 0:8, :], in_=gs[:, :].rearrange("p (a b) -> p a b", a=8)).then_inc(S["st"].sem, 16)
                    g.dma_start(out=dbg_h[1][:, 0:8, :], in_=us[:, :].rearrange("p (a b) -> p a b", a=8)).then_inc(S["st"].sem, 16)
                    S["st"].inc(32)
                dbg_st = S["st"].v

            @block.vector
            def _(vector):
                vector.wait_ge(ds.sem, gu_s)
                vector.wait_ge(dev.sem, gu_v)
                if dbg_w is not None and tc == 0:
                    vector.wait_ge(S["st"].sem, dbg_st)
                vector.tensor_mul(out=gs[:, :], in0=gs[:, :], in1=us[:, :]).then_inc(dv.sem, 1)
                dv.inc()
                vector.wait_ge(dv.sem, dv.v)
                vector.reduce_max(out=colq[:, 3:4], in_=gs[:, :], axis=AX.X,
                                  apply_absolute_value=True).then_inc(dv.sem, 1)
                dv.inc()
                vector.wait_ge(dv.sem, dv.v)
                vector.tensor_scalar_max(out=colq[:, 3:4], in0=colq[:, 3:4],
                                         scalar1=1e-5).then_inc(dv.sem, 1)
                dv.inc()
                vector.wait_ge(dv.sem, dv.v)
                vector.reciprocal(out=colq[:, 6:7], in_=colq[:, 3:4])
                vector.tensor_scalar_mul(out=colq[:, 5:6], in0=colq[:, 3:4],
                                         scalar1=1.0 / 127.0).then_inc(dv.sem, 1)
                dv.inc()
                vector.wait_ge(dv.sem, dv.v)
                vector.tensor_scalar_mul(out=colq[:, 4:5], in0=colq[:, 6:7],
                                         scalar1=127.0).then_inc(dv.sem, 1)
                dv.inc()
                vector.wait_ge(dv.sem, dv.v)
                vector.tensor_scalar(out=gs[:, :], in0=gs[:, :],
                                     scalar1=colq[:, 4:5], scalar2=MAGIC,
                                     op0=ALU.mult, op1=ALU.add).then_inc(dv.sem, 1)
                dv.inc()

            @block.vector
            def _(vector):
                vector.wait_ge(dv.sem, dv.v)
                vector.tensor_scalar_add(out=hqb[:, :], in0=gs[:, :],
                                         scalar1=nmagic[:, 0:1]).then_inc(dv.sem, 1)
                dv.inc()
            hq_done = dv.v

            if dbg_w is not None and tc == 0:
                @block.gpsimd
                def _(g):
                    g.wait_ge(ds.sem, hq_done)
                    g.dma_start(out=dbg_w[0][:, 0:8, 0:128],
                                in_=xqb[:, pp, :].rearrange("p (a b) -> p a b", a=8)).then_inc(S["st"].sem, 16)
                    g.dma_start(out=dbg_w[1][:, 0:8, 0:512],
                                in_=hqb[:, :].rearrange("p (a b) -> p a b", a=8)).then_inc(S["st"].sem, 16)
                    g.dma_start(out=dbg_w[0][:, 0:8, 128:256],
                                in_=gs[:, 0:1024].rearrange("p (a b) -> p a b", a=8)).then_inc(S["st"].sem, 16)
                    g.dma_start(out=dbg_w[0][:, 0, 256:264],
                                in_=colq[:, :]).then_inc(S["st"].sem, 16)
                    S["st"].inc(64)

                @block.tensor
                def _(tensor):
                    tensor.wait_ge(S["st"].sem, S["st"].v)

            for qd in range(4):
                ppq = qd % 2

                @block.tensor
                def _(tensor):
                    if qd == 0:
                        tensor.wait_ge(dv.sem, hq_done)
                    wge(tensor, lnev, lnev.v - 1)
                    ins = None
                    for ic in range(8):
                        ins = tensor.transpose(
                            ptT[:, ppq, ic * 128:(ic + 1) * 128],
                            hqb[:, (qd * 8 + ic) * 128:(qd * 8 + ic + 1) * 128],
                            ident[:])
                    ins.then_inc(lnt.sem, 1)
                    lnt.inc()

                @block.vector
                def _(vector):
                    vector.wait_ge(lnt.sem, lnt.v)
                    if qd == 0:
                        wge(vector, dt, dt.v)
                    vector.tensor_copy(
                        out=hqfm[:, qd * 8:(qd + 1) * 8, :],
                        in_=ptT[:, ppq, :].rearrange("p (a b) -> p a b", a=8)).then_inc(lnev.sem, 1)
                    lnev.inc()
            hqfm_done = lnev.v

            for f2c in range(32):
                wp = (tc * 32 + f2c) % 2

                @block.gpsimd
                def _(g):
                    wge(g, dt, dt.v - 1 if (tc > 0 or f2c >= 2) else 0)
                    g.dma_start(out=wdb[:, wp, :], in_=wdc[f2c]).then_inc(dw[wp].sem, 16)
                    dw[wp].inc(16)

                @block.tensor
                def _(tensor):
                    tensor.wait_ge(dw[wp].sem, dw[wp].v)
                    if f2c == 0:
                        tensor.wait_ge(lnev.sem, hqfm_done)
                        if tc >= 1:
                            tensor.wait_ge(dv.sem, devac_tgt[tc - 1])
                    ins = None
                    for h in range(2):
                        ins = tensor.matmul(dnp[:, h * 512:(h + 1) * 512],
                                            hqfm[:, f2c, :],
                                            wdb[:, wp, h * 512:(h + 1) * 512],
                                            start=(f2c == 0), stop=(f2c == 31))
                    ins.then_inc(dt.sem, 1)
                    dt.inc()

            @block.vector
            def _(vector):
                vector.wait_ge(dt.sem, dt.v)
                vector.scalar_tensor_tensor(out=resid[:, tc, :], in0=dnp[:, :],
                                            scalar=colq[:, 5:6], in1=resid[:, tc, :],
                                            op0=ALU.mult, op1=ALU.add).then_inc(dv.sem, 1)
                dv.inc()
            devac_tgt.append(dv.v)


_NC_CACHE = {}


DEBUG = False


def _get_nc(n_phases=4):
    key = (n_phases, DEBUG)
    if key not in _NC_CACHE:
        _NC_CACHE[key] = build_nc(n_phases, dbg=DEBUG)
    return _NC_CACHE[key]


def kernel(L0_attn_x, L0_attn_w_q, L0_attn_w_k, L0_attn_w_v,
           L0_ffn_k0, L0_ffn_k1, L0_ffn_k2, L0_ffn_s0, L0_ffn_s1, L0_ffn_s2,
           L1_attn_W_qkv, L1_ffn_w_gate, L1_ffn_w_up, L1_ffn_w_down,
           _phases=4):
    x = np.ascontiguousarray(np.asarray(L0_attn_x, dtype=np.float32).reshape(T, D))
    wq = np.ascontiguousarray(np.asarray(L0_attn_w_q, np.float32).reshape(8, 128, D))
    wk = np.ascontiguousarray(np.asarray(L0_attn_w_k, np.float32).reshape(8, 128, D))
    wv = np.ascontiguousarray(np.asarray(L0_attn_w_v, np.float32).reshape(8, 128, D))
    k0u = np.asarray(L0_ffn_k0).view(np.uint8)
    k0a = np.ascontiguousarray(k0u[:1024].reshape(8, 128, 4096))
    k0b = np.ascontiguousarray(k0u[1024:].reshape(8, 128, 4096))
    k1u = np.ascontiguousarray(np.asarray(L0_ffn_k1).view(np.uint8).reshape(32, 128, 4096))
    k2u = np.ascontiguousarray(np.asarray(L0_ffn_k2).view(np.uint8).reshape(32, 128, 1024))
    s0 = np.ascontiguousarray(np.broadcast_to(
        np.asarray(L0_ffn_s0, np.float32).reshape(1, 1), (128, 1)))
    s1 = np.ascontiguousarray(np.broadcast_to(
        np.asarray(L0_ffn_s1, np.float32).reshape(1, 1), (128, 1)))
    s2 = np.ascontiguousarray(np.broadcast_to(
        np.asarray(L0_ffn_s2, np.float32).reshape(1, 1), (128, 1)))
    wqkvT = np.ascontiguousarray(
        np.asarray(L1_attn_W_qkv, np.float32).T.reshape(8, 128, 3072))
    wgT = np.ascontiguousarray(
        np.asarray(L1_ffn_w_gate, np.float32).T.reshape(8, 128, 4096))
    wuT = np.ascontiguousarray(
        np.asarray(L1_ffn_w_up, np.float32).T.reshape(8, 128, 4096))
    wdT = np.ascontiguousarray(
        np.asarray(L1_ffn_w_down, np.float32).T.reshape(32, 128, 1024))

    in_maps = []
    for c in range(NCORES):
        sA, sB = c, NSTRIPE - 1 - c
        xs = np.ascontiguousarray(
            np.concatenate([x[sA * 1024:(sA + 1) * 1024],
                            x[sB * 1024:(sB + 1) * 1024]], axis=0)).reshape(NTC, 128, D)
        btab_row = np.empty((2, 8, NSTEP), np.float32)
        for si, stripe in enumerate((sA, sB)):
            for j in range(NSTEP):
                btab_row[si, :, j] = SHIFT if (j == 0 or (j - 1 < stripe)) else CLOSED
        btab = np.ascontiguousarray(
            np.broadcast_to(btab_row.reshape(1, -1), (128, 2 * 8 * NSTEP)))
        in_maps.append({
            "x_in": xs, "btab": btab,
            "w_q": wq, "w_k": wk, "w_v": wv,
            "k0a": k0a, "k0b": k0b, "k1": k1u, "k2": k2u,
            "s0": s0, "s1": s1, "s2": s2,
            "wqkvT": wqkvT, "wgT": wgT, "wuT": wuT, "wdT": wdT,
        })

    nc = _get_nc(_phases)
    res = run_bass_kernel_spmd(nc, in_maps, core_ids=list(range(NCORES)))

    y = np.empty((T, D), np.float32)
    for c in range(NCORES):
        out = np.asarray(res.results[c]["y_out"]).reshape(TPC, D)
        sA, sB = c, NSTRIPE - 1 - c
        y[sA * 1024:(sA + 1) * 1024] = out[:1024]
        y[sB * 1024:(sB + 1) * 1024] = out[1024:]
    return y.reshape(1, T, D)


# version 04
# speedup vs baseline: 1.0510x; 1.0348x over previous
import numpy as np
import ml_dtypes
from contextlib import ExitStack

import concourse.bass as bass
import concourse.mybir as mybir
from concourse.bass_utils import run_bass_kernel_spmd

F32 = mybir.dt.float32
BF16 = mybir.dt.bfloat16
U8 = mybir.dt.uint8
AF = mybir.ActivationFunctionType
ALU = mybir.AluOpType
AX = mybir.AxisListType

NCORES = 8
D = 1024
T = 16384
TPC = 2048
NTC = 16
NSTRIPE = 16
NSTEP = 16
MAGIC = 12582912.0
SHIFT = -12.0
CLOSED = -30012.0
MASKNEG = -30000.0


class Ctr:
    def __init__(self, sem):
        self.sem = sem
        self.v = 0

    def inc(self, n=1):
        self.v += n
        return self.v


def wge(eng, ctr, v):
    if v > 0:
        eng.wait_ge(ctr.sem, v)


def emit_barrier(nc, block, S):
    bar = S["bar"]
    tgt = bar.v + 5
    for engname in ("tensor", "vector", "scalar", "gpsimd", "sync"):
        def _sec(eng, tgt=tgt):
            eng.drain().then_inc(bar.sem, 1)
            eng.wait_ge(bar.sem, tgt)
        getattr(block, engname)(_sec)
    bar.v = tgt


def _posblock_src(b):
    rank = b if b <= 7 else 15 - b
    sub = 0 if b <= 7 else 1
    return rank, sub


C_STAGE = 9
RACECHECK = False


def build_nc(n_phases=4, dbg=False):
    nc = bass.Bass("TRN2", target_bir_lowering=False,
                   detect_race_conditions=RACECHECK)

    x_in = nc.dram_tensor("x_in", [NTC, 128, D], F32, kind="ExternalInput").ap()
    btab_in = nc.dram_tensor("btab", [128, 2 * 8 * NSTEP], F32, kind="ExternalInput").ap()
    wq_in = nc.dram_tensor("w_q", [8, 128, D], F32, kind="ExternalInput").ap()
    wk_in = nc.dram_tensor("w_k", [8, 128, D], F32, kind="ExternalInput").ap()
    wv_in = nc.dram_tensor("w_v", [8, 128, D], F32, kind="ExternalInput").ap()
    k0a_in = nc.dram_tensor("k0a", [8, 128, 4096], U8, kind="ExternalInput").ap()
    k0b_in = nc.dram_tensor("k0b", [8, 128, 4096], U8, kind="ExternalInput").ap()
    k1_in = nc.dram_tensor("k1", [32, 128, 4096], U8, kind="ExternalInput").ap()
    k2_in = nc.dram_tensor("k2", [32, 128, 1024], U8, kind="ExternalInput").ap()
    s0_in = nc.dram_tensor("s0", [128, 1], F32, kind="ExternalInput").ap()
    s1_in = nc.dram_tensor("s1", [128, 1], F32, kind="ExternalInput").ap()
    s2_in = nc.dram_tensor("s2", [128, 1], F32, kind="ExternalInput").ap()
    wqkvT_in = nc.dram_tensor("wqkvT", [8, 128, 3072], F32, kind="ExternalInput").ap()
    wgT_in = nc.dram_tensor("wgT", [8, 128, 4096], F32, kind="ExternalInput").ap()
    wuT_in = nc.dram_tensor("wuT", [8, 128, 4096], F32, kind="ExternalInput").ap()
    wdT_in = nc.dram_tensor("wdT", [32, 128, 1024], F32, kind="ExternalInput").ap()
    y_out = nc.dram_tensor("y_out", [NTC, 128, D], F32, kind="ExternalOutput").ap()
    dbg_h = (nc.dram_tensor("dbg_h", [2, 128, 32, 512], F32, kind="ExternalOutput").ap()
             if dbg else None)
    dbg_w = (nc.dram_tensor("dbg_w", [2, 128, 8, 512], F32, kind="ExternalOutput").ap()
             if dbg else None)

    agin_k_h = nc.dram_tensor("agin_k", [8, 128, TPC], BF16)
    agin_v_h = nc.dram_tensor("agin_v", [NTC, 128, D], BF16)
    agout_k_h = nc.dram_tensor("agout_k", [NCORES * 8, 128, TPC], BF16)
    agout_v_h = nc.dram_tensor("agout_v", [NCORES * NTC, 128, D], BF16)
    arin_h = nc.dram_tensor("arin_kv", [64, 1040], F32)
    arout_h = nc.dram_tensor("arout_kv", [64, 1040], F32)
    w0c_h = nc.dram_tensor("w0c", [8, 128, 4096], BF16)
    w1c_h = nc.dram_tensor("w1c", [16, 32, 128, 256], BF16)
    w2c_h = nc.dram_tensor("w2c", [32, 128, 1024], BF16)
    wgc_h = nc.dram_tensor("wgc", [8, 128, 4096], BF16)
    wuc_h = nc.dram_tensor("wuc", [8, 128, 4096], BF16)
    wdc_h = nc.dram_tensor("wdc", [32, 128, 1024], BF16)
    dram = dict(
        agin_k=agin_k_h.ap(), agin_v=agin_v_h.ap(),
        agout_k=agout_k_h.ap(), agout_v=agout_v_h.ap(),
        arin=arin_h.ap(), arout=arout_h.ap(),
        w0c=w0c_h.ap(), w1c=w1c_h.ap(), w2c=w2c_h.ap(),
        wgc=wgc_h.ap(), wuc=wuc_h.ap(), wdc=wdc_h.ap(),
        agin_k_h=agin_k_h, agin_v_h=agin_v_h, agout_k_h=agout_k_h,
        agout_v_h=agout_v_h, arin_h=arin_h, arout_h=arout_h,
    )

    ident_d = nc.inline_tensor(np.eye(128, dtype=ml_dtypes.bfloat16), "identc")
    tri_np = np.where(np.arange(128)[None, :] <= np.arange(128)[:, None],
                      np.float32(0.0), np.float32(MASKNEG)).astype(np.float32)
    tri_d = nc.inline_tensor(tri_np, "tric")
    onebf_d = nc.inline_tensor(np.ones((128, 1), dtype=ml_dtypes.bfloat16), "onebfc")
    onesf_d = nc.inline_tensor(np.ones((128, 1), dtype=np.float32), "onesfc")
    onesrow_d = nc.inline_tensor(np.ones((1, 128), dtype=np.float32), "onesrowc")

    with ExitStack() as top:
        e = top.enter_context
        resid = e(nc.sbuf_tensor("resid", [128, NTC, D], F32))
        btab = e(nc.sbuf_tensor("btabs", [128, 2 * 8 * NSTEP], F32))
        ident = e(nc.sbuf_tensor("ident_s", [128, 128], BF16))
        tri = e(nc.sbuf_tensor("tri_s", [128, 128], F32))
        onebf = e(nc.sbuf_tensor("onebf_s", [128, 1], BF16))
        onesf = e(nc.sbuf_tensor("onesf_s", [128, 1], F32))
        onesrow = e(nc.sbuf_tensor("onesrow_s", [1, 128], F32))
        epsc = e(nc.sbuf_tensor("epsc", [128, 1], F32))
        nmagic = e(nc.sbuf_tensor("nmagic", [128, 1], F32))
        s0c = e(nc.sbuf_tensor("s0c", [128, 1], F32))
        s1c = e(nc.sbuf_tensor("s1c", [128, 1], F32))
        s2c = e(nc.sbuf_tensor("s2c", [128, 1], F32))

        S = {}
        for name in ("ld", "pv", "psc", "pd", "pd1", "pd2", "pd3", "pd4", "pst", "ptn",
                     "lnv", "lns", "lnt", "lnev",
                     "pjw", "pjt", "pjev", "pjst", "cc",
                     "kld0", "kld1", "vld0", "vld1",
                     "smm", "sexp", "stmm", "sptev", "spv", "svadd",
                     "vmask", "fin", "fv",
                     "fl00", "fl01", "fl10", "fl11", "fl20", "fl21", "ft", "fev", "fs",
                     "cld", "ct", "cev", "cv", "cs", "ccc", "cst",
                     "dg0", "dg1", "dw0", "dw1", "dt", "dev", "dv", "ds", "ppd",
                     "st", "bar"):
            S[name] = Ctr(e(nc.semaphore("sem_" + name)))

        block = e(nc.Block())
        ld = S["ld"]

        @block.sync
        def _(sync):
            for tc in range(NTC):
                sync.dma_start(out=resid[:, tc, :], in_=x_in[tc]).then_inc(ld.sem, 16)
                ld.inc(16)
            for dst, src in ((btab, btab_in), (ident, ident_d.ap()),
                             (tri, tri_d.ap()), (onebf, onebf_d.ap()),
                             (onesf, onesf_d.ap()), (onesrow, onesrow_d.ap()),
                             (s0c, s0_in), (s1c, s1_in), (s2c, s2_in)):
                sync.dma_start(out=dst[:], in_=src).then_inc(ld.sem, 16)
                ld.inc(16)
        ld_all = ld.v

        @block.vector
        def _(vector):
            vector.memset(epsc[:], 1e-5)
            vector.memset(nmagic[:], -MAGIC)

        prep_alloc, prep_steps = emit_weight_prep(nc, block, S, ld_all,
                                                  k0a_in, k0b_in, k1_in, k2_in,
                                                  s0c, s1c, s2c,
                                                  wgT_in, wuT_in, wdT_in, dram,
                                                  onesf, onesrow)

        def drain_prep(k):
            while prep_steps and k > 0:
                prep_steps.pop(0)()
                k -= 1
        drain_prep.alloc = prep_alloc

        emit_l0_attn(nc, block, S, ld_all, resid, btab, ident, tri, epsc,
                     wq_in, wk_in, wv_in, dram, drain_prep)

        if n_phases >= 2:
            emit_barrier(nc, block, S)
            emit_l0_ffn(nc, block, S, resid, ident, epsc, dram, dbg_h, dbg_w)
        if n_phases >= 3:
            emit_barrier(nc, block, S)
            emit_l1_attn(nc, block, S, resid, ident, epsc, onebf,
                         wqkvT_in, dram)
        if n_phases >= 4:
            emit_barrier(nc, block, S)
            emit_l1_ffn(nc, block, S, resid, ident, epsc, nmagic, dram, dbg_w, dbg_h)

        emit_barrier(nc, block, S)
        st = S["st"]

        @block.sync
        def _(sync):
            for tc in range(NTC):
                sync.dma_start(out=y_out[tc], in_=resid[:, tc, :]).then_inc(st.sem, 16)
                st.inc(16)
            sync.wait_ge(st.sem, st.v)

    return nc


def emit_ln(block, S, resid, tc, pp, scr, epsc, out_ap, first_wait=None,
            norm_waits=()):
    lnv, lns = S["lnv"], S["lns"]
    bnstat, mv, lnvv, rstd = scr

    @block.vector
    def _(vector):
        if first_wait is not None:
            sem, val = first_wait
            if val > 0:
                vector.wait_ge(sem, val)
        wge(vector, lns, lns.v)
        vector.bn_stats(out=bnstat[:, pp, 0:6], in_=resid[:, tc, 0:512])
        vector.bn_stats(out=bnstat[:, pp, 6:12], in_=resid[:, tc, 512:1024]).then_inc(lnv.sem, 1)
        lnv.inc()
        vector.wait_ge(lnv.sem, lnv.v)
        vector.bn_aggr(out=mv[:, pp, :], in_=bnstat[:, pp, :]).then_inc(lnv.sem, 1)
        lnv.inc()

    @block.scalar
    def _(scalar):
        scalar.wait_ge(lnv.sem, lnv.v)
        scalar.activation(out=lnvv[:, pp, :], in_=mv[:, pp, 1:2], func=AF.Ln,
                          bias=epsc[:, 0:1], scale=1.0)
        scalar.activation(out=rstd[:, pp, :], in_=lnvv[:, pp, :], func=AF.Exp,
                          scale=-0.5).then_inc(lns.sem, 1)
        lns.inc()

    @block.vector
    def _(vector):
        vector.wait_ge(lns.sem, lns.v)
        for sem, val in norm_waits:
            if val > 0:
                vector.wait_ge(sem, val)
        vector.tensor_scalar(out=out_ap, in0=resid[:, tc, :],
                             scalar1=mv[:, pp, 0:1], scalar2=rstd[:, pp, 0:1],
                             op0=ALU.subtract, op1=ALU.mult).then_inc(lnv.sem, 1)
        lnv.inc()


def _ln_scratch(nc, e, tag):
    return (e(nc.sbuf_tensor("bnstat" + tag, [128, 2, 12], F32)),
            e(nc.sbuf_tensor("mv" + tag, [128, 2, 2], F32)),
            e(nc.sbuf_tensor("lnvv" + tag, [128, 2, 1], F32)),
            e(nc.sbuf_tensor("rstd" + tag, [128, 2, 1], F32)))


def emit_transpose_1024(block, S, src_ap, ptT, pp, dst_ap, ident,
                        tensor_waits=(), vector_waits=(), ping=True):
    lnt, lnev = S["lnt"], S["lnev"]

    @block.tensor
    def _(tensor):
        for sem, val in tensor_waits:
            if val > 0:
                tensor.wait_ge(sem, val)
        wge(tensor, lnev, lnev.v - (1 if ping else 0))
        ins = None
        for ic in range(8):
            ins = tensor.transpose(ptT[:, pp, ic * 128:(ic + 1) * 128],
                                   src_ap[:, ic * 128:(ic + 1) * 128],
                                   ident[:])
        ins.then_inc(lnt.sem, 1)
        lnt.inc()

    @block.vector
    def _(vector):
        vector.wait_ge(lnt.sem, lnt.v)
        for sem, val in vector_waits:
            if val > 0:
                vector.wait_ge(sem, val)
        src2 = ptT[:, pp, :]
        if len(dst_ap.shape) == 3:
            src2 = src2.rearrange("p (a b) -> p a b", a=dst_ap.shape[1])
        vector.tensor_copy(out=dst_ap, in_=src2).then_inc(lnev.sem, 1)
        lnev.inc()


def emit_weight_prep(nc, block, S, ld_all,
                     k0a_in, k0b_in, k1_in, k2_in, s0c, s1c, s2c,
                     wgT_in, wuT_in, wdT_in, dram, onesf, onesrow):
    pv, psc, pst, ptn = S["pv"], S["psc"], S["pst"], S["ptn"]
    pd, pd1, pd2, pd3, pd4 = S["pd"], S["pd1"], S["pd2"], S["pd3"], S["pd4"]
    w0c, w1c, w2c = dram["w0c"], dram["w1c"], dram["w2c"]
    wgc, wuc, wdc = dram["wgc"], dram["wuc"], dram["wdc"]

    B = {}

    def alloc():
        ctx = ExitStack()
        S["prep_ctx"] = ctx
        e = ctx.enter_context
        B["au8"] = e(nc.sbuf_tensor("au8", [128, 4096], U8))
        B["bu8"] = e(nc.sbuf_tensor("bu8", [128, 4096], U8))
        B["wtmp"] = e(nc.sbuf_tensor("wtmp", [128, 4096], F32))
        B["wbf"] = e(nc.sbuf_tensor("wbf", [128, 4096], BF16))
        B["cols"] = e(nc.sbuf_tensor("pcols", [128, 8], F32))
        B["absacc"] = e(nc.sbuf_tensor("absacc", [128, 3], F32))
        B["redc"] = e(nc.sbuf_tensor("redc", [128, 1], F32))
        B["scl"] = e(nc.sbuf_tensor("scl", [128, 2, 3], F32))
        B["tot"] = e(nc.sbuf_tensor("tot", [1, 3], F32))
        B["pbc"] = e(nc.psum_tensor("pbc", [128, 8], F32))

    class _Lazy:
        def __init__(self, key):
            self._key = key

        def __getitem__(self, item):
            return B[self._key][item]

    au8 = _Lazy("au8"); bu8 = _Lazy("bu8"); wtmp = _Lazy("wtmp")
    wbf = _Lazy("wbf"); cols = _Lazy("cols"); absacc = _Lazy("absacc")
    redc = _Lazy("redc"); scl = _Lazy("scl"); tot = _Lazy("tot")
    pbc = _Lazy("pbc")

    steps = []

    def s_cols():
        @block.vector
        def _(vector):
            vector.wait_ge(S["ld"].sem, ld_all)
            vector.tensor_scalar_mul(out=cols[:, 0:1], in0=s0c[:], scalar1=2.0)
            vector.tensor_scalar_mul(out=cols[:, 1:2], in0=s1c[:], scalar1=4.0)
            vector.tensor_scalar_mul(out=cols[:, 2:3], in0=s1c[:], scalar1=-2.0)
            vector.tensor_scalar_mul(out=cols[:, 3:4], in0=s2c[:], scalar1=4.0)
            vector.tensor_scalar_mul(out=cols[:, 4:5], in0=s2c[:], scalar1=-2.0).then_inc(pv.sem, 1)
            pv.inc()
    steps.append(s_cols)
    scales_tgt = [None]

    for ic in range(8):
        def s_w0(ic=ic):
            @block.gpsimd
            def _(g):
                wge(g, pv, pv.v)
                g.dma_start(out=au8[:], in_=k0a_in[ic]).then_inc(pd.sem, 16)
                g.dma_start(out=bu8[:], in_=k0b_in[ic]).then_inc(pd.sem, 16)
                pd.inc(32)

            @block.vector
            def _(vector):
                vector.wait_ge(pd.sem, pd.v)
                if ic == 0:
                    vector.wait_ge(pv.sem, 1)
                wge(vector, pst, pst.v)
                vector.tensor_sub(out=wtmp[:], in0=au8[:], in1=bu8[:]).then_inc(pv.sem, 1)
                pv.inc()
                vector.wait_ge(pv.sem, pv.v)
                vector.tensor_scalar_mul(out=wbf[:], in0=wtmp[:],
                                         scalar1=cols[:, 0:1]).then_inc(pv.sem, 1)
                pv.inc()

            @block.sync
            def _(sync):
                sync.wait_ge(pv.sem, pv.v)
                sync.dma_start(out=w0c[ic], in_=wbf[:]).then_inc(pst.sem, 16)
                pst.inc(16)
        steps.append(s_w0)

    for ic in range(32):
        def s_w1(ic=ic):
            @block.gpsimd
            def _(g):
                wge(g, pv, pv.v)
                wge(g, psc, psc.v)
                g.dma_start(out=au8[:], in_=k1_in[ic]).then_inc(pd1.sem, 16)
                pd1.inc(16)

            @block.scalar
            def _(scalar):
                scalar.wait_ge(pd1.sem, pd1.v)
                scalar.wait_ge(pv.sem, 1)
                wge(scalar, pst, pst.v)
                scalar.activation(out=wbf[:], in_=au8[:], func=AF.Identity,
                                  scale=cols[:, 1:2], bias=cols[:, 2:3]).then_inc(psc.sem, 1)
                psc.inc()

            @block.sync
            def _(sync):
                sync.wait_ge(psc.sem, psc.v)
                for g16 in range(16):
                    sync.dma_start(out=w1c[g16, ic],
                                   in_=wbf[:, g16 * 256:(g16 + 1) * 256]).then_inc(pst.sem, 16)
                    pst.inc(16)
        steps.append(s_w1)

    for it in range(8):
        def s_w2(it=it):
            @block.gpsimd
            def _(g):
                wge(g, psc, psc.v)
                for j in range(4):
                    g.dma_start(out=au8[:, j * 1024:(j + 1) * 1024],
                                in_=k2_in[4 * it + j]).then_inc(pd2.sem, 16)
                    pd2.inc(16)

            @block.scalar
            def _(scalar):
                scalar.wait_ge(pd2.sem, pd2.v)
                wge(scalar, pst, pst.v)
                scalar.activation(out=wbf[:], in_=au8[:], func=AF.Identity,
                                  scale=cols[:, 3:4], bias=cols[:, 4:5]).then_inc(psc.sem, 1)
                psc.inc()

            @block.sync
            def _(sync):
                sync.wait_ge(psc.sem, psc.v)
                for j in range(4):
                    sync.dma_start(out=w2c[4 * it + j],
                                   in_=wbf[:, j * 1024:(j + 1) * 1024]).then_inc(pst.sem, 16)
                    pst.inc(16)
        steps.append(s_w2)

    specs = [(wgT_in, wgc, 8, 4096, 0), (wuT_in, wuc, 8, 4096, 1),
             (wdT_in, wdc, 32, 1024, 2)]

    for src_, dst, nic, F, mi in specs:
        for ic in range(nic):
            def s_p1(src_=src_, ic=ic, F=F, mi=mi):
                @block.gpsimd
                def _(g):
                    wge(g, pv, pv.v)
                    g.dma_start(out=wtmp[:, 0:F], in_=src_[ic]).then_inc(pd3.sem, 16)
                    pd3.inc(16)

                @block.vector
                def _(vector):
                    vector.wait_ge(pd3.sem, pd3.v)
                    vector.reduce_sum(out=redc[:], in_=wtmp[:, 0:F], axis=AX.X,
                                      apply_absolute_value=True).then_inc(pv.sem, 1)
                    pv.inc()
                    vector.wait_ge(pv.sem, pv.v)
                    if ic == 0:
                        vector.tensor_copy(out=absacc[:, mi:mi + 1],
                                           in_=redc[:]).then_inc(pv.sem, 1)
                    else:
                        vector.tensor_add(out=absacc[:, mi:mi + 1],
                                          in0=absacc[:, mi:mi + 1],
                                          in1=redc[:]).then_inc(pv.sem, 1)
                    pv.inc()
            steps.append(s_p1)

    def s_scale():
        @block.tensor
        def _(tensor):
            tensor.wait_ge(pv.sem, pv.v)
            tensor.matmul(pbc[0:1, 4:7], onesf[:, 0:1], absacc[:, 0:3],
                          start=True, stop=True).then_inc(ptn.sem, 1)
            ptn.inc()

        @block.vector
        def _(vector):
            vector.wait_ge(ptn.sem, ptn.v)
            vector.tensor_copy(out=tot[0:1, 0:3], in_=pbc[0:1, 4:7]).then_inc(pv.sem, 1)
            pv.inc()

        @block.tensor
        def _(tensor):
            tensor.wait_ge(pv.sem, pv.v)
            tensor.matmul(pbc[:, 0:3], onesrow[0:1, :],
                          tot[0:1, 0:3], start=True, stop=True).then_inc(ptn.sem, 1)
            ptn.inc()

        @block.vector
        def _(vector):
            vector.wait_ge(ptn.sem, ptn.v)
            for mi, numel in ((0, 8 * 128 * 4096), (1, 8 * 128 * 4096),
                              (2, 32 * 128 * 1024)):
                vector.tensor_scalar(out=scl[:, 0, mi:mi + 1], in0=pbc[:, mi:mi + 1],
                                     scalar1=1.0 / numel, scalar2=1e-5,
                                     op0=ALU.mult, op1=ALU.max).then_inc(pv.sem, 1)
                pv.inc()
            vector.wait_ge(pv.sem, pv.v)
            vector.reciprocal(out=scl[:, 1, 0:1], in_=scl[:, 0, 0:1])
            vector.reciprocal(out=scl[:, 1, 1:2], in_=scl[:, 0, 1:2])
            vector.reciprocal(out=scl[:, 1, 2:3], in_=scl[:, 0, 2:3]).then_inc(pv.sem, 1)
            pv.inc()
    steps.append(s_scale)

    for src_, dst, nic, F, mi in specs:
        for ic in range(nic):
            def s_p2(src_=src_, dst=dst, ic=ic, F=F, mi=mi):
                @block.gpsimd
                def _(g):
                    wge(g, pv, pv.v)
                    g.dma_start(out=wtmp[:, 0:F], in_=src_[ic]).then_inc(pd4.sem, 16)
                    pd4.inc(16)

                @block.vector
                def _(vector):
                    vector.wait_ge(pd4.sem, pd4.v)
                    wge(vector, pst, pst.v)
                    vector.tensor_scalar(out=wtmp[:, 0:F], in0=wtmp[:, 0:F],
                                         scalar1=scl[:, 1, mi:mi + 1], scalar2=1.4999999,
                                         op0=ALU.mult, op1=ALU.min).then_inc(pv.sem, 1)
                    pv.inc()
                    vector.wait_ge(pv.sem, pv.v)
                    vector.tensor_scalar(out=wtmp[:, 0:F], in0=wtmp[:, 0:F],
                                         scalar1=-1.4999999, scalar2=MAGIC,
                                         op0=ALU.max, op1=ALU.add).then_inc(pv.sem, 1)
                    pv.inc()
                    vector.wait_ge(pv.sem, pv.v)
                    vector.tensor_scalar(out=wbf[:, 0:F], in0=wtmp[:, 0:F],
                                         scalar1=-MAGIC, scalar2=scl[:, 0, mi:mi + 1],
                                         op0=ALU.add, op1=ALU.mult).then_inc(pv.sem, 1)
                    pv.inc()

                @block.sync
                def _(sync):
                    sync.wait_ge(pv.sem, pv.v)
                    sync.dma_start(out=dst[ic], in_=wbf[:, 0:F]).then_inc(pst.sem, 16)
                    pst.inc(16)
            steps.append(s_p2)

    def s_fin():
        @block.sync
        def _(sync):
            sync.wait_ge(pst.sem, pst.v)
            sync.nop().then_inc(S["ppd"].sem, 1)
            S["ppd"].inc()

        @block.gpsimd
        def _(g):
            g.wait_ge(pd.sem, pd.v)
            g.wait_ge(pd1.sem, pd1.v)
            g.wait_ge(pd2.sem, pd2.v)
            g.wait_ge(pd3.sem, pd3.v)
            g.wait_ge(pd4.sem, pd4.v)
            g.nop().then_inc(S["ppd"].sem, 1)
            S["ppd"].inc()
        S["prep_ctx"].close()
    steps.append(s_fin)
    return alloc, steps


def emit_l0_attn(nc, block, S, ld_all, resid, btab, ident, tri, epsc,
                 wq_in, wk_in, wv_in, dram, drain_prep=lambda k: None):
    lnv = S["lnv"]
    pjw, pjt, pjev, pjst, cc = S["pjw"], S["pjt"], S["pjev"], S["pjst"], S["cc"]
    agin_k, agin_v = dram["agin_k"], dram["agin_v"]
    agout_k, agout_v = dram["agout_k"], dram["agout_v"]

    with ExitStack() as actx:
        e = actx.enter_context
        qfm = e(nc.sbuf_tensor("qfm", [128, 8, TPC], BF16))

        with ExitStack() as ctx:
            e2 = ctx.enter_context
            lnb = e2(nc.sbuf_tensor("lnbA", [128, 2, D], BF16))
            lnT = e2(nc.sbuf_tensor("lnTA", [128, 8, TPC], BF16))
            wsb = e2(nc.sbuf_tensor("wsbA", [128, 8, D], BF16))
            scr = _ln_scratch(nc, e2, "A")
            stg = e2(nc.sbuf_tensor("stgA", [128, 2, 512], BF16))
            ptT = e2(nc.psum_tensor("ptTA", [128, 2, 1024], BF16))
            pj = e2(nc.psum_tensor("pjA", [128, 2, 512], F32))
            if hasattr(drain_prep, "alloc"):
                drain_prep.alloc()

            for tc in range(NTC):
                pp = tc % 2
                emit_ln(block, S, resid, tc, pp, scr, epsc, lnb[:, pp, :],
                        first_wait=(S["ld"].sem, ld_all if tc == 0 else 0),
                        norm_waits=((S["lnt"].sem, S["lnt"].v - 1),))
                emit_transpose_1024(
                    block, S, lnb[:, pp, :], ptT, pp,
                    lnT[:, :, tc * 128:(tc + 1) * 128],
                    ident, tensor_waits=((lnv.sem, lnv.v),))
                drain_prep(2)
            lnT_done = S["lnev"].v

            @block.gpsimd
            def _(g):
                for ic in range(8):
                    g.dma_start(out=wsb[:, ic, :], in_=wq_in[ic]).then_inc(pjw.sem, 16)
                    pjw.inc(16)

            nmm = 0
            for oc in range(8):
                for ns in range(4):
                    pp = nmm % 2

                    @block.tensor
                    def _(tensor):
                        if nmm == 0:
                            tensor.wait_ge(pjw.sem, pjw.v)
                            tensor.wait_ge(S["lnev"].sem, lnT_done)
                        wge(tensor, pjev, nmm - 1)
                        ins = None
                        for ic in range(8):
                            ins = tensor.matmul(pj[:, pp, :],
                                                wsb[:, ic, oc * 128:(oc + 1) * 128],
                                                lnT[:, ic, ns * 512:(ns + 1) * 512],
                                                start=(ic == 0), stop=(ic == 7))
                        ins.then_inc(pjt.sem, 1)
                        pjt.inc()

                    @block.vector
                    def _(vector):
                        vector.wait_ge(pjt.sem, pjt.v)
                        vector.tensor_scalar_mul(out=qfm[:, oc, ns * 512:(ns + 1) * 512],
                                                 in0=pj[:, pp, :],
                                                 scalar1=0.03125).then_inc(pjev.sem, 1)
                        pjev.inc()
                    nmm += 1
                    drain_prep(2)
            q_done = pjt.v

            @block.gpsimd
            def _(g):
                g.wait_ge(pjt.sem, q_done)
                for ic in range(8):
                    g.dma_start(out=wsb[:, ic, :], in_=wk_in[ic]).then_inc(pjw.sem, 16)
                    pjw.inc(16)
            kw_ready = pjw.v

            nst = 0
            for oc in range(8):
                for ns in range(4):
                    pp = nmm % 2
                    sp = nst % 2

                    @block.tensor
                    def _(tensor):
                        if nmm == 32:
                            tensor.wait_ge(pjw.sem, kw_ready)
                        wge(tensor, pjev, nmm - 1)
                        ins = None
                        for ic in range(8):
                            ins = tensor.matmul(pj[:, pp, :],
                                                wsb[:, ic, oc * 128:(oc + 1) * 128],
                                                lnT[:, ic, ns * 512:(ns + 1) * 512],
                                                start=(ic == 0), stop=(ic == 7))
                        ins.then_inc(pjt.sem, 1)
                        pjt.inc()

                    @block.vector
                    def _(vector):
                        vector.wait_ge(pjt.sem, pjt.v)
                        wge(vector, pjst, (nst - 1) * 16)
                        vector.tensor_copy(out=stg[:, sp, :],
                                           in_=pj[:, pp, :]).then_inc(pjev.sem, 1)
                        pjev.inc()

                    @block.sync
                    def _(sync):
                        sync.wait_ge(pjev.sem, pjev.v)
                        sync.dma_start(out=agin_k[oc][:, ns * 512:(ns + 1) * 512],
                                       in_=stg[:, sp, :]).then_inc(pjst.sem, 16)
                        pjst.inc(16)
                    nmm += 1
                    nst += 1
                    drain_prep(1)
            k_done = pjt.v

            @block.gpsimd
            def _(g):
                g.wait_ge(pjt.sem, k_done)
                for ic in range(8):
                    g.dma_start(out=wsb[:, ic, :], in_=wv_in[ic]).then_inc(pjw.sem, 16)
                    pjw.inc(16)
            vw_ready = pjw.v

            for tc in range(NTC):
                for nh in range(2):
                    pp = nmm % 2
                    sp = nst % 2

                    @block.tensor
                    def _(tensor):
                        if nmm == 64:
                            tensor.wait_ge(pjw.sem, vw_ready)
                        wge(tensor, pjev, nmm - 1)
                        ins = None
                        for ic in range(8):
                            ins = tensor.matmul(pj[:, pp, :],
                                                lnT[:, ic, tc * 128:(tc + 1) * 128],
                                                wsb[:, ic, nh * 512:(nh + 1) * 512],
                                                start=(ic == 0), stop=(ic == 7))
                        ins.then_inc(pjt.sem, 1)
                        pjt.inc()

                    @block.vector
                    def _(vector):
                        vector.wait_ge(pjt.sem, pjt.v)
                        wge(vector, pjst, (nst - 1) * 16)
                        vector.tensor_copy(out=stg[:, sp, :],
                                           in_=pj[:, pp, :]).then_inc(pjev.sem, 1)
                        pjev.inc()

                    @block.sync
                    def _(sync):
                        sync.wait_ge(pjev.sem, pjev.v)
                        sync.dma_start(out=agin_v[tc][:, nh * 512:(nh + 1) * 512],
                                       in_=stg[:, sp, :]).then_inc(pjst.sem, 16)
                        pjst.inc(16)
                    nmm += 1
                    nst += 1
                    drain_prep(1)
            agin_done = pjst.v

            @block.gpsimd
            def _(g):
                g.wait_ge(pjst.sem, agin_done)
                g.collective_compute("AllGather", ALU.bypass,
                                     replica_groups=[list(range(NCORES))],
                                     ins=[dram["agin_k_h"].ap().opt()],
                                     outs=[dram["agout_k_h"].ap().opt()]).then_inc(cc.sem, 1)
                cc.inc()
                g.collective_compute("AllGather", ALU.bypass,
                                     replica_groups=[list(range(NCORES))],
                                     ins=[dram["agin_v_h"].ap().opt()],
                                     outs=[dram["agout_v_h"].ap().opt()]).then_inc(cc.sem, 1)
                cc.inc()
            ag_done = cc.v

            drain_prep(1 << 30)

        with ExitStack() as ctx:
            e2 = ctx.enter_context
            osb = e2(nc.sbuf_tensor("osb", [128, 8, D], F32))
            lbuf = e2(nc.sbuf_tensor("lbuf", [128, 8, NSTEP], F32))
            lrec = e2(nc.sbuf_tensor("lrec", [128, 8], F32))
            kt = e2(nc.sbuf_tensor("ktile", [128, 2, 8, 1024], BF16))
            vt = e2(nc.sbuf_tensor("vtile", [128, 2, 8, 1024], BF16))
            pbuf = e2(nc.sbuf_tensor("pbuf", [128, 2, 1024], BF16))
            ptsb = e2(nc.sbuf_tensor("ptsb", [128, 2, 1024], BF16))
            spsum = e2(nc.psum_tensor("spsum", [128, 2, 1024], F32))
            ptp = e2(nc.psum_tensor("ptp", [128, 2, 1024], BF16))
            pvp = e2(nc.psum_tensor("pvp", [128, 1024], F32))

            kldp = [S["kld0"], S["kld1"]]
            vldp = [S["vld0"], S["vld1"]]
            smm, sexp = S["smm"], S["sexp"]
            stmm, sptev, spv = S["stmm"], S["sptev"], S["spv"]
            svadd, vmask, fin = S["svadd"], S["vmask"], S["fin"]

            def emit_transp(c):
                qs, sp = c["qs"], c["sp"]

                @block.tensor
                def _(tensor):
                    tensor.wait_ge(sexp.sem, qs + 1)
                    wge(tensor, sptev, qs - 1)
                    ins = None
                    for kc in range(8):
                        ins = tensor.transpose(ptp[:, sp, kc * 128:(kc + 1) * 128],
                                               pbuf[:, sp, kc * 128:(kc + 1) * 128],
                                               ident[:])
                    ins.then_inc(stmm.sem, 1)
                    stmm.inc()

                @block.vector
                def _(vector):
                    vector.wait_ge(stmm.sem, qs + 1)
                    wge(vector, spv, qs - 1)
                    vector.tensor_copy(out=ptsb[:, sp, :],
                                       in_=ptp[:, sp, :]).then_inc(sptev.sem, 1)
                    sptev.inc()

            def emit_exps(specs):
                @block.scalar
                def _(scalar):
                    for qs, sp, r, j, s_idx, vm in specs:
                        scalar.wait_ge(smm.sem, qs + 1)
                        if vm is not None:
                            scalar.wait_ge(vmask.sem, vm)
                        if s_idx == 1 and j == 0 and r == 0:
                            scalar.wait_ge(fin.sem, 1)
                        bcol = (s_idx * 8 + r) * NSTEP + j
                        scalar.activation(out=pbuf[:, sp, :], in_=spsum[:, sp, :],
                                          func=AF.Exp, bias=btab[:, bcol:bcol + 1],
                                          scale=1.0,
                                          accum_out=lbuf[:, r, j:j + 1]).then_inc(sexp.sem, 1)

            def emit_pv(c):
                qs, sp, ring, r, j = c["qs"], c["sp"], c["ring"], c["r"], c["j"]

                @block.tensor
                def _(tensor):
                    tensor.wait_ge(sptev.sem, qs + 1)
                    wge(tensor, svadd, qs)
                    ins = None
                    for kc in range(8):
                        for h in range(2):
                            ins = tensor.matmul(
                                pvp[:, h * 512:(h + 1) * 512],
                                ptsb[:, sp, kc * 128:(kc + 1) * 128],
                                vt[:, ring, kc, h * 512:(h + 1) * 512],
                                start=(kc == 0), stop=(kc == 7))
                    ins.then_inc(spv.sem, 1)
                    spv.inc()

                @block.vector
                def _(vector):
                    vector.wait_ge(spv.sem, qs + 1)
                    if j == 0:
                        vector.tensor_copy(out=osb[:, r, :], in_=pvp[:, :]).then_inc(svadd.sem, 1)
                    else:
                        vector.tensor_add(out=osb[:, r, :], in0=osb[:, r, :],
                                          in1=pvp[:, :]).then_inc(svadd.sem, 1)
                    svadd.inc()

            for s_idx in range(2):
                pend = []
                for j in range(NSTEP):
                    exp_specs = []
                    g = s_idx * NSTEP + j
                    ring = g % 2

                    @block.sync
                    def _(sync):
                        if g == 0:
                            sync.wait_ge(pjst.sem, agin_done)
                            sync.wait_ge(S["ppd"].sem, 2)
                        if g == 1:
                            sync.wait_ge(cc.sem, ag_done)
                        wge(sync, smm, 8 * (g - 1))
                        if j == 0:
                            for ic in range(8):
                                sync.dma_start(
                                    out=kt[:, ring, ic, :],
                                    in_=agin_k[ic][:, s_idx * 1024:(s_idx + 1) * 1024],
                                ).then_inc(kldp[ring].sem, 16)
                                kldp[ring].inc(16)
                        else:
                            rank, sub = _posblock_src(j - 1)
                            for ic in range(8):
                                sync.dma_start(
                                    out=kt[:, ring, ic, :],
                                    in_=agout_k[8 * rank + ic][:, sub * 1024:(sub + 1) * 1024],
                                ).then_inc(kldp[ring].sem, 16)
                                kldp[ring].inc(16)

                    @block.gpsimd
                    def _(gq):
                        if g == 0:
                            gq.wait_ge(pjst.sem, agin_done)
                            gq.wait_ge(S["ppd"].sem, 2)
                        if g == 1:
                            gq.wait_ge(cc.sem, ag_done)
                        wge(gq, spv, 8 * (g - 1))
                        if j == 0:
                            for kc in range(8):
                                gq.dma_start(out=vt[:, ring, kc, :],
                                             in_=agin_v[s_idx * 8 + kc]).then_inc(vldp[ring].sem, 16)
                                vldp[ring].inc(16)
                        else:
                            rank, sub = _posblock_src(j - 1)
                            for kc in range(8):
                                gq.dma_start(
                                    out=vt[:, ring, kc, :],
                                    in_=agout_v[NTC * rank + sub * 8 + kc]).then_inc(vldp[ring].sem, 16)
                                vldp[ring].inc(16)
                    k_tgt, v_tgt = kldp[ring].v, vldp[ring].v

                    for r in range(8):
                        qs = g * 8 + r
                        sp = qs % 2
                        qcol0 = s_idx * 1024

                        @block.tensor
                        def _(tensor):
                            if r == 0:
                                tensor.wait_ge(kldp[ring].sem, k_tgt)
                                tensor.wait_ge(vldp[ring].sem, v_tgt)
                            wge(tensor, sexp, qs - 1)
                            ins = None
                            for ic in range(8):
                                for h in range(2):
                                    ins = tensor.matmul(
                                        spsum[:, sp, h * 512:(h + 1) * 512],
                                        qfm[:, ic, qcol0 + r * 128:qcol0 + (r + 1) * 128],
                                        kt[:, ring, ic, h * 512:(h + 1) * 512],
                                        start=(ic == 0), stop=(ic == 7))
                            ins.then_inc(smm.sem, 1)
                            smm.inc()

                        if j == 0:
                            @block.vector
                            def _(vector):
                                vector.wait_ge(smm.sem, qs + 1)
                                ins = vector.tensor_add(
                                    out=spsum[:, sp, r * 128:(r + 1) * 128],
                                    in0=spsum[:, sp, r * 128:(r + 1) * 128],
                                    in1=tri[:])
                                if r < 7:
                                    ins = vector.tensor_scalar_add(
                                        out=spsum[:, sp, (r + 1) * 128:1024],
                                        in0=spsum[:, sp, (r + 1) * 128:1024],
                                        scalar1=MASKNEG)
                                ins.then_inc(vmask.sem, 1)
                                vmask.inc()

                        exp_specs.append((qs, sp, r, j, s_idx,
                                          vmask.v if j == 0 else None))
                        sexp.inc()

                        ctxq = dict(qs=qs, sp=sp, ring=ring, r=r, j=j)
                        pend.append(ctxq)
                        i_local = len(pend) - 1
                        if i_local >= 1:
                            emit_transp(pend[i_local - 1])
                        if i_local >= 2:
                            emit_pv(pend[i_local - 2])
                    emit_exps(exp_specs)

                emit_transp(pend[-1])
                emit_pv(pend[-2])
                emit_pv(pend[-1])
                pend.clear()

                @block.vector
                def _(vector):
                    vector.wait_ge(svadd.sem, svadd.v)
                    vector.reduce_sum(out=lrec[:, 0:8].rearrange("p a -> p a ()"),
                                      in_=lbuf[:, :, :], axis=AX.X).then_inc(S["fv"].sem, 1)
                    S["fv"].inc()
                    vector.wait_ge(S["fv"].sem, S["fv"].v)
                    vector.reciprocal(out=lrec[:, 0:8], in_=lrec[:, 0:8]).then_inc(S["fv"].sem, 1)
                    S["fv"].inc()
                    vector.wait_ge(S["fv"].sem, S["fv"].v)
                    ins = None
                    for r in range(8):
                        tc = s_idx * 8 + r
                        ins = vector.scalar_tensor_tensor(
                            out=resid[:, tc, :], in0=osb[:, r, :],
                            scalar=lrec[:, r:r + 1], in1=resid[:, tc, :],
                            op0=ALU.mult, op1=ALU.add)
                    ins.then_inc(fin.sem, 1)
                    fin.inc()


def emit_l0_ffn(nc, block, S, resid, ident, epsc, dram, dbg_h=None, dbg_w=None):
    fl0 = [S["fl00"], S["fl01"]]
    fl1 = [S["fl10"], S["fl11"]]
    fl2 = [S["fl20"], S["fl21"]]
    ft, fev, fv = S["ft"], S["fev"], S["fv"]
    lnv = S["lnv"]
    w0c, w1c, w2c = dram["w0c"], dram["w1c"], dram["w2c"]

    with ExitStack() as ctx:
        e = ctx.enter_context
        lnb = e(nc.sbuf_tensor("lnbB", [128, 2, D], BF16))
        lnS = e(nc.sbuf_tensor("lnSB", [128, 8, 512], BF16))
        h1 = e(nc.sbuf_tensor("h1B", [128, 32, 512], BF16))
        h2 = e(nc.sbuf_tensor("h2B", [128, 32, 512], BF16))
        w0b = e(nc.sbuf_tensor("w0bB", [128, 2, 8, 512], BF16))
        w1b = e(nc.sbuf_tensor("w1bB", [128, 2, 32, 256], BF16))
        w2b = e(nc.sbuf_tensor("w2bB", [128, 2, 1024], BF16))
        scr = _ln_scratch(nc, e, "B")
        ptT = e(nc.psum_tensor("ptTB", [128, 2, 1024], BF16))
        hp = e(nc.psum_tensor("hpB", [128, 2, 512], F32))
        h3p = e(nc.psum_tensor("h3pB", [128, 2, 1024], F32))

        for t in range(4):
            for i in range(4):
                tc = t * 4 + i
                pp = i % 2
                emit_ln(block, S, resid, tc, pp, scr, epsc, lnb[:, pp, :],
                        norm_waits=((S["lnt"].sem, S["lnt"].v - 1),))
                emit_transpose_1024(
                    block, S, lnb[:, pp, :], ptT, pp,
                    lnS[:, :, i * 128:(i + 1) * 128],
                    ident, tensor_waits=((lnv.sem, lnv.v),),
                    vector_waits=((ft.sem, ft.v),))
            lnS_done = S["lnev"].v

            for f1g in range(8):
                gp = f1g % 2

                @block.sync
                def _(g):
                    wge(g, ft, ft.v - 4 if (t > 0 or f1g >= 2) else 0)
                    g.dma_start(out=w0b[:, gp, :, :],
                                in_=w0c[:, :, f1g * 512:(f1g + 1) * 512]
                                .rearrange("a p b -> p a b")).then_inc(fl0[gp].sem, 16)
                    fl0[gp].inc(16)

                for lf in range(4):
                    f1c = f1g * 4 + lf
                    pp = f1c % 2

                    @block.tensor
                    def _(tensor):
                        if lf == 0:
                            tensor.wait_ge(fl0[gp].sem, fl0[gp].v)
                        if f1c == 0:
                            tensor.wait_ge(S["lnev"].sem, lnS_done)
                        wge(tensor, fev, fev.v - 1)
                        ins = None
                        for ic in range(8):
                            ins = tensor.matmul(hp[:, pp, :],
                                                w0b[:, gp, ic, lf * 128:(lf + 1) * 128],
                                                lnS[:, ic, :],
                                                start=(ic == 0), stop=(ic == 7))
                        ins.then_inc(ft.sem, 1)
                        ft.inc()

                    @block.vector
                    def _(vector):
                        vector.wait_ge(ft.sem, ft.v)
                        vector.tensor_relu(out=h1[:, f1c, :],
                                           in_=hp[:, pp, :]).then_inc(fev.sem, 1)
                        fev.inc()

            for f2g in range(16):
                gp = f2g % 2

                @block.sync
                def _(g):
                    wge(g, ft, ft.v - 2 if (t > 0 or f2g >= 2) else 0)
                    g.dma_start(out=w1b[:, gp, :, :],
                                in_=w1c[f2g].rearrange("a p b -> p a b")).then_inc(fl1[gp].sem, 16)
                    fl1[gp].inc(16)

                for lf in range(2):
                    f2c = f2g * 2 + lf
                    pp = f2c % 2

                    @block.tensor
                    def _(tensor):
                        if lf == 0:
                            tensor.wait_ge(fl1[gp].sem, fl1[gp].v)
                        wge(tensor, fev, fev.v - 1)
                        ins = None
                        for f1c in range(32):
                            ins = tensor.matmul(hp[:, pp, :],
                                                w1b[:, gp, f1c, lf * 128:(lf + 1) * 128],
                                                h1[:, f1c, :],
                                                start=(f1c == 0), stop=(f1c == 31))
                        ins.then_inc(ft.sem, 1)
                        ft.inc()

                    @block.vector
                    def _(vector):
                        vector.wait_ge(ft.sem, ft.v)
                        vector.tensor_relu(out=h2[:, f2c, :],
                                           in_=hp[:, pp, :]).then_inc(fev.sem, 1)
                        fev.inc()
            h2_done = fev.v

            if dbg_h is not None and t == 0:
                @block.gpsimd
                def _(g):
                    g.wait_ge(fev.sem, h2_done)
                    g.dma_start(out=dbg_h[0], in_=h1[:, :, :]).then_inc(S["st"].sem, 16)
                    g.dma_start(out=dbg_h[1], in_=h2[:, :, :]).then_inc(S["st"].sem, 16)
                    g.dma_start(out=dbg_w[0], in_=lnS[:, :, :]).then_inc(S["st"].sem, 16)
                    g.dma_start(out=dbg_w[1].rearrange("p a b -> a p b"),
                                in_=dram["w0c"][:, :, 6 * 512:7 * 512]).then_inc(S["st"].sem, 16)
                    S["st"].inc(64)

                @block.tensor
                def _(tensor):
                    tensor.wait_ge(S["st"].sem, S["st"].v)

            for i in range(4):
                tc = t * 4 + i
                hp3 = i % 2
                for f2c in range(32):
                    wp = (i * 32 + f2c) % 2

                    @block.sync
                    def _(g):
                        wge(g, ft, ft.v - 1 if (t > 0 or i > 0 or f2c >= 2) else 0)
                        g.dma_start(out=w2b[:, wp, :], in_=w2c[f2c]).then_inc(fl2[wp].sem, 16)
                        fl2[wp].inc(16)

                    @block.tensor
                    def _(tensor):
                        tensor.wait_ge(fl2[wp].sem, fl2[wp].v)
                        if i == 0 and f2c == 0:
                            tensor.wait_ge(fev.sem, h2_done)
                        if f2c == 0:
                            wge(tensor, fv, fv.v - 1)
                        ins = None
                        for h in range(2):
                            ins = tensor.matmul(h3p[:, hp3, h * 512:(h + 1) * 512],
                                                h2[:, f2c, i * 128:(i + 1) * 128],
                                                w2b[:, wp, h * 512:(h + 1) * 512],
                                                start=(f2c == 0), stop=(f2c == 31))
                        ins.then_inc(ft.sem, 1)
                        ft.inc()

                @block.vector
                def _(vector):
                    vector.wait_ge(ft.sem, ft.v)
                    vector.tensor_add(out=resid[:, tc, :], in0=resid[:, tc, :],
                                      in1=h3p[:, hp3, :]).then_inc(fv.sem, 1)
                    fv.inc()


def emit_l1_attn(nc, block, S, resid, ident, epsc, onebf, wqkvT_in, dram):
    cld, ct, cev, cv, cs = S["cld"], S["ct"], S["cev"], S["cv"], S["cs"]
    ccc, cst = S["ccc"], S["cst"]
    lnv = S["lnv"]
    arin, arout = dram["arin"], dram["arout"]

    with ExitStack() as ctx:
        e = ctx.enter_context
        lnb = e(nc.sbuf_tensor("lnbC", [128, 2, D], BF16))
        qhfm = e(nc.sbuf_tensor("qhfmC", [128, 8, TPC], BF16))
        hs = e(nc.sbuf_tensor("hsC", [128, 2, 16], F32))
        kvsb = e(nc.sbuf_tensor("kvsbC", [128, 1040], F32))
        scr = _ln_scratch(nc, e, "C")

        with ExitStack() as pctx:
            e2 = pctx.enter_context
            lnT = e2(nc.sbuf_tensor("lnTC", [128, 8, TPC], BF16))
            khf = e2(nc.sbuf_tensor("khfC", [128, NTC, D], BF16))
            wqb = e2(nc.sbuf_tensor("wqbC", [128, 8, 1024], BF16))
            pt32 = e2(nc.sbuf_tensor("pt32C", [128, 2, D], BF16))
            sqt = e2(nc.sbuf_tensor("sqtC", [128, D], F32))
            qhat = e2(nc.sbuf_tensor("qhatC", [128, 2, D], BF16))
            vta = e2(nc.sbuf_tensor("vtaC", [128, 2, 1024], BF16))
            ptT = e2(nc.psum_tensor("ptTC", [128, 1, 1024], BF16))
            pjc = e2(nc.psum_tensor("pjcC", [128, 2, 1024], F32))
            kvp = e2(nc.psum_tensor("kvpC", [64, 1024], F32))
            kvq = e2(nc.psum_tensor("kvqC", [64, 16], F32))

            for tc in range(NTC):
                pp = tc % 2
                emit_ln(block, S, resid, tc, pp, scr, epsc, lnb[:, pp, :],
                        norm_waits=((S["lnt"].sem, S["lnt"].v - 1),))
                emit_transpose_1024(
                    block, S, lnb[:, pp, :], ptT, 0,
                    lnT[:, :, tc * 128:(tc + 1) * 128],
                    ident, tensor_waits=((lnv.sem, lnv.v),), ping=False)
            lnT_done = S["lnev"].v

            @block.gpsimd
            def _(g):
                for ic in range(8):
                    g.dma_start(out=wqb[:, ic, :],
                                in_=wqkvT_in[ic][:, 0:1024]).then_inc(cld.sem, 16)
                    cld.inc(16)
            qw_ready = cld.v

            for tc in range(NTC):
                pp = tc % 2

                @block.tensor
                def _(tensor):
                    if tc == 0:
                        tensor.wait_ge(cld.sem, qw_ready)
                        tensor.wait_ge(S["lnev"].sem, lnT_done)
                    wge(tensor, cev, cev.v - 1)
                    ins = None
                    for ic in range(8):
                        for h in range(2):
                            ins = tensor.matmul(pjc[:, pp, h * 512:(h + 1) * 512],
                                                lnT[:, ic, tc * 128:(tc + 1) * 128],
                                                wqb[:, ic, h * 512:(h + 1) * 512],
                                                start=(ic == 0), stop=(ic == 7))
                    ins.then_inc(ct.sem, 1)
                    ct.inc()

                @block.vector
                def _(vector):
                    vector.wait_ge(ct.sem, ct.v)
                    wge(vector, cs, cs.v)
                    vector.tensor_copy(out=pt32[:, pp, :], in_=pjc[:, pp, :]).then_inc(cev.sem, 1)
                    cev.inc()

                _emit_spl2(nc, block, S, pt32, sqt, hs, qhat[:, pp, :], pp, cs, cv, cev)

                emit_transpose_1024(
                    block, S, qhat[:, pp, :], ptT, 0,
                    qhfm[:, :, tc * 128:(tc + 1) * 128],
                    ident, tensor_waits=((cs.sem, cs.v),), ping=False)

            if C_STAGE < 2:
                return
            @block.gpsimd
            def _(g):
                g.wait_ge(ct.sem, ct.v)
                for ic in range(8):
                    g.dma_start(out=wqb[:, ic, :],
                                in_=wqkvT_in[ic][:, 1024:2048]).then_inc(cld.sem, 16)
                    cld.inc(16)
            kw_ready = cld.v

            for tc in range(NTC):
                pp = tc % 2

                @block.tensor
                def _(tensor):
                    if tc == 0:
                        tensor.wait_ge(cld.sem, kw_ready)
                    wge(tensor, cev, cev.v - 1)
                    ins = None
                    for ic in range(8):
                        for h in range(2):
                            ins = tensor.matmul(pjc[:, pp, h * 512:(h + 1) * 512],
                                                lnT[:, ic, tc * 128:(tc + 1) * 128],
                                                wqb[:, ic, h * 512:(h + 1) * 512],
                                                start=(ic == 0), stop=(ic == 7))
                    ins.then_inc(ct.sem, 1)
                    ct.inc()

                @block.vector
                def _(vector):
                    vector.wait_ge(ct.sem, ct.v)
                    wge(vector, cs, cs.v)
                    vector.tensor_copy(out=pt32[:, pp, :], in_=pjc[:, pp, :]).then_inc(cev.sem, 1)
                    cev.inc()

                _emit_spl2(nc, block, S, pt32, sqt, hs, khf[:, tc, :], pp, cs, cv, cev)

            khf_done = cs.v

            if C_STAGE < 3:
                return
            @block.gpsimd
            def _(g):
                g.wait_ge(ct.sem, ct.v)
                for ic in range(8):
                    g.dma_start(out=wqb[:, ic, :],
                                in_=wqkvT_in[ic][:, 2048:3072]).then_inc(cld.sem, 16)
                    cld.inc(16)
            vw_ready = cld.v

            for tc in range(NTC):
                pp = tc % 2

                @block.tensor
                def _(tensor):
                    if tc == 0:
                        tensor.wait_ge(cld.sem, vw_ready)
                        tensor.wait_ge(cs.sem, khf_done)
                    wge(tensor, cev, cev.v - 1)
                    ins = None
                    for ic in range(8):
                        for h in range(2):
                            ins = tensor.matmul(pjc[:, pp, h * 512:(h + 1) * 512],
                                                lnT[:, ic, tc * 128:(tc + 1) * 128],
                                                wqb[:, ic, h * 512:(h + 1) * 512],
                                                start=(ic == 0), stop=(ic == 7))
                    ins.then_inc(ct.sem, 1)
                    ct.inc()

                @block.vector
                def _(vector):
                    vector.wait_ge(ct.sem, ct.v)
                    if tc >= 2:
                        wge(vector, ct, ct.v)
                    vector.tensor_copy(out=vta[:, pp, :],
                                       in_=pjc[:, pp, :]).then_inc(cev.sem, 1)
                    cev.inc()

                @block.tensor
                def _(tensor):
                    tensor.wait_ge(cev.sem, cev.v)
                    ins = None
                    for h in range(16):
                        tensor.matmul(kvp[0:64, h * 64:(h + 1) * 64],
                                      khf[:, tc, h * 64:(h + 1) * 64],
                                      vta[:, pp, h * 64:(h + 1) * 64],
                                      start=(tc == 0 and h % 8 == 0),
                                      stop=(tc == NTC - 1 and h % 8 == 7))
                        ins = tensor.matmul(kvq[0:64, h:h + 1],
                                            khf[:, tc, h * 64:(h + 1) * 64],
                                            onebf[:, 0:1],
                                            start=(tc == 0 and h == 0),
                                            stop=(tc == NTC - 1 and h == 15))
                    ins.then_inc(ct.sem, 1)
                    ct.inc()
            kv_done = ct.v

            @block.vector
            def _(vector):
                vector.wait_ge(ct.sem, kv_done)
                vector.tensor_scalar_mul(out=kvsb[0:64, 0:1024], in0=kvp[0:64, :],
                                         scalar1=0.125)
                vector.tensor_copy(out=kvsb[0:64, 1024:1040],
                                   in_=kvq[0:64, :]).then_inc(cv.sem, 1)
                cv.inc()

            @block.sync
            def _(sync):
                sync.wait_ge(cv.sem, cv.v)
                sync.dma_start(out=arin, in_=kvsb[0:64, :]).then_inc(cst.sem, 16)
                cst.inc(16)

            if C_STAGE < 4:
                return

            @block.gpsimd
            def _(g):
                g.wait_ge(cst.sem, cst.v)
                g.collective_compute("AllReduce", ALU.add,
                                     replica_groups=[list(range(NCORES))],
                                     ins=[dram["arin_h"].ap().opt()],
                                     outs=[dram["arout_h"].ap().opt()]).then_inc(ccc.sem, 1)
                ccc.inc()

            @block.sync
            def _(sync):
                sync.wait_ge(ccc.sem, ccc.v)
                sync.dma_start(out=kvsb[0:64, :], in_=arout).then_inc(cst.sem, 16)
                sync.dma_start(out=kvsb[64:128, :], in_=arout).then_inc(cst.sem, 16)
                cst.inc(32)

            @block.vector
            def _(vector):
                vector.wait_ge(cst.sem, cst.v)
                vector.tensor_scalar_add(out=kvsb[:, 1024:1040],
                                         in0=kvsb[:, 1024:1040],
                                         scalar1=1e-6).then_inc(cv.sem, 1)
                cv.inc()
        if C_STAGE < 5:
            return
        with ExitStack() as pctx:
            e2 = pctx.enter_context
            normc = e2(nc.sbuf_tensor("normcC", [128, 2, 2, 8], F32))
            otmp = e2(nc.sbuf_tensor("otmpC", [128, 2, D], BF16))
            kvsbb = e2(nc.sbuf_tensor("kvsbbC", [128, 1040], BF16))
            op = e2(nc.psum_tensor("opC", [128, 2, 2, 512], F32))
            oq = e2(nc.psum_tensor("oqC", [128, 2, 2, 512], F32))

            @block.vector
            def _(vector):
                vector.wait_ge(cv.sem, cv.v)
                vector.tensor_copy(out=kvsbb[:, :], in_=kvsb[:, :]).then_inc(cv.sem, 1)
                cv.inc()
            kv_ready = cv.v

            for tc in range(NTC):
                pp = tc % 2

                @block.tensor
                def _(tensor):
                    if tc == 0:
                        tensor.wait_ge(cv.sem, kv_ready)
                    wge(tensor, cv, kv_ready + 3 * (tc - 1))
                    ins = None
                    for h in range(16):
                        po = (h % 2) * 64
                        par = h % 2
                        hh = h // 2
                        tensor.matmul(op[:, pp, par, hh * 64:(hh + 1) * 64],
                                      qhfm[po:po + 64, hh, tc * 128:(tc + 1) * 128],
                                      kvsbb[po:po + 64, h * 64:(h + 1) * 64],
                                      start=(hh == 0), stop=(hh == 7))
                        ins = tensor.matmul(oq[:, pp, par, hh:hh + 1],
                                            qhfm[po:po + 64, hh, tc * 128:(tc + 1) * 128],
                                            kvsbb[po:po + 64, 1024 + h:1025 + h],
                                            start=(hh == 0), stop=(hh == 7))
                    ins.then_inc(ct.sem, 1)
                    ct.inc()

                @block.vector
                def _(vector):
                    vector.wait_ge(ct.sem, ct.v)
                    vector.reciprocal(out=normc[:, pp, :, :],
                                      in_=oq[:, pp, :, 0:8]).then_inc(cv.sem, 1)
                    cv.inc()
                    vector.wait_ge(cv.sem, cv.v)
                    vector.tensor_tensor(
                        out=otmp[:, pp, :].rearrange("p (a q b) -> p a q b", a=8, q=2),
                        in0=op[:, pp, :, :].rearrange("p q (a b) -> p a q b", a=8),
                        in1=normc[:, pp, :, :].rearrange("p q a -> p a q ()").broadcast_to([128, 8, 2, 64]),
                        op=ALU.mult).then_inc(cv.sem, 1)
                    cv.inc()
                    vector.wait_ge(cv.sem, cv.v)
                    vector.tensor_add(out=resid[:, tc, :], in0=resid[:, tc, :],
                                      in1=otmp[:, pp, :]).then_inc(cv.sem, 1)
                    cv.inc()


def _emit_spl2(nc, block, S, src, sqt, hs, dst_ap, pp, cs, cv, cev):
    @block.vector
    def _(vector):
        vector.wait_ge(cev.sem, cev.v)
        vector.tensor_mul(out=sqt[:, :], in0=src[:, pp, :],
                          in1=src[:, pp, :]).then_inc(cv.sem, 1)
        cv.inc()
        vector.wait_ge(cv.sem, cv.v)
        vector.reduce_sum(out=hs[:, pp, :].rearrange("p a -> p a ()"),
                          in_=sqt[:, :].rearrange("p (a b) -> p a b", a=16),
                          axis=AX.X).then_inc(cv.sem, 1)
        cv.inc()
        vector.wait_ge(cv.sem, cv.v)
        vector.tensor_scalar_max(out=hs[:, pp, :], in0=hs[:, pp, :],
                                 scalar1=1e-24).then_inc(cv.sem, 1)
        cv.inc()
    hs_ready = cv.v

    @block.scalar
    def _(scalar):
        scalar.wait_ge(cv.sem, hs_ready)
        scalar.activation(out=hs[:, pp, :], in_=hs[:, pp, :], func=AF.Ln)
        scalar.activation(out=hs[:, pp, :], in_=hs[:, pp, :], func=AF.Exp,
                          scale=-0.5).then_inc(cs.sem, 1)
        cs.inc()

    @block.vector
    def _(vector):
        vector.wait_ge(cs.sem, cs.v)
        vector.tensor_tensor(
            out=sqt[:, :].rearrange("p (a b) -> p a b", a=16),
            in0=src[:, pp, :].rearrange("p (a b) -> p a b", a=16),
            in1=hs[:, pp, :].rearrange("p a -> p a ()").broadcast_to([128, 16, 64]),
            op=ALU.mult).then_inc(cv.sem, 1)
        cv.inc()
    mult_ready = cv.v

    @block.scalar
    def _(scalar):
        scalar.wait_ge(cv.sem, mult_ready)
        scalar.activation(out=sqt[:, :], in_=sqt[:, :],
                          func=AF.Exp).then_inc(cs.sem, 1)
        cs.inc()

    @block.vector
    def _(vector):
        vector.wait_ge(cs.sem, cs.v)
        vector.tensor_scalar_add(out=sqt[:, :], in0=sqt[:, :],
                                 scalar1=1.0).then_inc(cv.sem, 1)
        cv.inc()
    add_ready = cv.v

    @block.scalar
    def _(scalar):
        scalar.wait_ge(cv.sem, add_ready)
        scalar.activation(out=dst_ap, in_=sqt[:, :],
                          func=AF.Ln).then_inc(cs.sem, 1)
        cs.inc()


def emit_l1_ffn(nc, block, S, resid, ident, epsc, nmagic, dram, dbg_w=None, dbg_h=None):
    dg = [S["dg0"], S["dg1"]]
    dw = [S["dw0"], S["dw1"]]
    dt, dev, dv, ds = S["dt"], S["dev"], S["dv"], S["ds"]
    lnv, lnt, lnev = S["lnv"], S["lnt"], S["lnev"]
    wgc, wuc, wdc = dram["wgc"], dram["wuc"], dram["wdc"]

    with ExitStack() as ctx:
        e = ctx.enter_context
        xqfm = e(nc.sbuf_tensor("xqfmD", [128, NTC, 8, 128], BF16))
        colqx = e(nc.sbuf_tensor("colqxD", [128, 2, NTC], F32))
        xq_tgt = [None] * NTC

        with ExitStack() as p1:
            e1 = p1.enter_context
            lnf = e1(nc.sbuf_tensor("lnfD", [128, 2, D], F32))
            xqb = e1(nc.sbuf_tensor("xqbD", [128, 2, D], BF16))
            tmpc = e1(nc.sbuf_tensor("tmpcD", [128, 1], F32))
            scr = _ln_scratch(nc, e1, "D")
            ptT = e1(nc.psum_tensor("ptTD1", [128, 2, 1024], BF16))

            for tc in range(NTC):
                pp = tc % 2
                emit_ln(block, S, resid, tc, pp, scr, epsc, lnf[:, pp, :],
                        norm_waits=((S["dv"].sem, S["dv"].v),))

                @block.vector
                def _(vector):
                    vector.wait_ge(lnv.sem, lnv.v)
                    vector.reduce_max(out=tmpc[:], in_=lnf[:, pp, :], axis=AX.X,
                                      apply_absolute_value=True).then_inc(dv.sem, 1)
                    dv.inc()
                    vector.wait_ge(dv.sem, dv.v)
                    vector.tensor_scalar_max(out=tmpc[:], in0=tmpc[:],
                                             scalar1=1e-5).then_inc(dv.sem, 1)
                    dv.inc()
                    vector.wait_ge(dv.sem, dv.v)
                    vector.tensor_scalar_mul(out=colqx[:, 1, tc:tc + 1], in0=tmpc[:],
                                             scalar1=1.0 / 127.0)
                    vector.reciprocal(out=tmpc[:], in_=tmpc[:]).then_inc(dv.sem, 1)
                    dv.inc()
                    vector.wait_ge(dv.sem, dv.v)
                    vector.tensor_scalar_mul(out=colqx[:, 0, tc:tc + 1], in0=tmpc[:],
                                             scalar1=127.0).then_inc(dv.sem, 1)
                    dv.inc()
                    vector.wait_ge(dv.sem, dv.v)
                    vector.tensor_scalar(out=lnf[:, pp, :], in0=lnf[:, pp, :],
                                         scalar1=colqx[:, 0, tc:tc + 1], scalar2=MAGIC,
                                         op0=ALU.mult, op1=ALU.add).then_inc(dv.sem, 1)
                    dv.inc()
                    vector.wait_ge(dv.sem, dv.v)
                    vector.tensor_scalar_add(out=xqb[:, pp, :], in0=lnf[:, pp, :],
                                             scalar1=nmagic[:, 0:1]).then_inc(dv.sem, 1)
                    dv.inc()

                emit_transpose_1024(
                    block, S, xqb[:, pp, :], ptT, pp,
                    xqfm[:, tc, :, :],
                    ident, tensor_waits=((dv.sem, dv.v),))
                xq_tgt[tc] = S["lnev"].v

        pass1_dv = S["dv"].v
        pass1_ev = S["lnev"].v

        with ExitStack() as p2:
            e2 = p2.enter_context
            wgb = e2(nc.sbuf_tensor("wgbD", [128, 2, 8, 512], BF16))
            wub = e2(nc.sbuf_tensor("wubD", [128, 2, 8, 512], BF16))
            wdb = e2(nc.sbuf_tensor("wdbD", [128, 2, 1024], BF16))
            gr = e2(nc.sbuf_tensor("grD", [128, 4096], BF16))
            us = e2(nc.sbuf_tensor("usD", [128, 4096], BF16))
            hbuf = e2(nc.sbuf_tensor("hbufD", [128, 4096], F32))
            hqb = e2(nc.sbuf_tensor("hqbD", [128, 2, 4096], BF16))
            hqfm = e2(nc.sbuf_tensor("hqfmD", [128, 2, 32, 128], BF16))
            colqh = e2(nc.sbuf_tensor("colqhD", [128, 2, 8], F32))
            gup = e2(nc.psum_tensor("gupD", [128, 2, 2, 512], F32))
            dnp = e2(nc.psum_tensor("dnpD", [128, 1024], F32))
            ptT2 = e2(nc.psum_tensor("ptTD2", [128, 2, 1024], BF16))

            hq_tgt = [None] * NTC
            hqfm_tgt = [None] * NTC
            devac_tgt = [None] * NTC

            def emit_gu(tc):
                tp = tc % 2
                silu_prev = ds.v

                for og in range(8):
                    wp = og % 2
                    pp2 = og % 2

                    @block.sync
                    def _(g):
                        if tc == 0 and og == 0:
                            g.wait_ge(S["dv"].sem, pass1_dv)
                            g.wait_ge(S["lnev"].sem, pass1_ev)
                        wge(g, dt, dt.v - 1 if (tc > 0 or og >= 2) else 0)
                        g.dma_start(out=wgb[:, wp, :, :],
                                    in_=wgc[:, :, og * 512:(og + 1) * 512]
                                    .rearrange("a p b -> p a b")).then_inc(dg[wp].sem, 16)
                        g.dma_start(out=wub[:, wp, :, :],
                                    in_=wuc[:, :, og * 512:(og + 1) * 512]
                                    .rearrange("a p b -> p a b")).then_inc(dg[wp].sem, 16)
                        dg[wp].inc(32)

                    @block.tensor
                    def _(tensor):
                        tensor.wait_ge(dg[wp].sem, dg[wp].v)
                        if og == 0:
                            tensor.wait_ge(S["lnev"].sem, xq_tgt[tc])
                        wge(tensor, dev, dev.v - 1)
                        for ic in range(8):
                            tensor.matmul(gup[:, pp2, 0, :], xqfm[:, tc, ic, :],
                                          wgb[:, wp, ic, :],
                                          start=(ic == 0), stop=(ic == 7))
                        ins = None
                        for ic in range(8):
                            ins = tensor.matmul(gup[:, pp2, 1, :], xqfm[:, tc, ic, :],
                                                wub[:, wp, ic, :],
                                                start=(ic == 0), stop=(ic == 7))
                        ins.then_inc(dt.sem, 1)
                        dt.inc()

                    @block.vector
                    def _(vector):
                        vector.wait_ge(dt.sem, dt.v)
                        if og == 0:
                            wge(vector, ds, silu_prev)
                        vector.tensor_copy(out=gr[:, og * 512:(og + 1) * 512],
                                           in_=gup[:, pp2, 0, :])
                        vector.tensor_scalar_mul(out=us[:, og * 512:(og + 1) * 512],
                                                 in0=gup[:, pp2, 1, :],
                                                 scalar1=colqx[:, 1, tc:tc + 1]).then_inc(dev.sem, 1)
                        dev.inc()
                gu_v = dev.v

                @block.scalar
                def _(scalar):
                    scalar.wait_ge(dev.sem, gu_v)
                    scalar.activation(out=gr[:, :], in_=gr[:, :], func=AF.Silu,
                                      scale=colqx[:, 1, tc:tc + 1]).then_inc(ds.sem, 1)
                    ds.inc()
                silu_done = ds.v

                @block.vector
                def _(vector):
                    vector.wait_ge(ds.sem, silu_done)
                    vector.tensor_mul(out=hbuf[:, :], in0=gr[:, :],
                                      in1=us[:, :]).then_inc(dv.sem, 1)
                    dv.inc()
                    vector.wait_ge(dv.sem, dv.v)
                    vector.reduce_max(out=colqh[:, tp, 0:1], in_=hbuf[:, :], axis=AX.X,
                                      apply_absolute_value=True).then_inc(dv.sem, 1)
                    dv.inc()
                    vector.wait_ge(dv.sem, dv.v)
                    vector.tensor_scalar_max(out=colqh[:, tp, 0:1], in0=colqh[:, tp, 0:1],
                                             scalar1=1e-5).then_inc(dv.sem, 1)
                    dv.inc()
                    vector.wait_ge(dv.sem, dv.v)
                    vector.tensor_scalar_mul(out=colqh[:, tp, 2:3], in0=colqh[:, tp, 0:1],
                                             scalar1=1.0 / 127.0)
                    vector.reciprocal(out=colqh[:, tp, 1:2],
                                      in_=colqh[:, tp, 0:1]).then_inc(dv.sem, 1)
                    dv.inc()
                    vector.wait_ge(dv.sem, dv.v)
                    vector.tensor_scalar_mul(out=colqh[:, tp, 3:4], in0=colqh[:, tp, 1:2],
                                             scalar1=127.0).then_inc(dv.sem, 1)
                    dv.inc()
                    vector.wait_ge(dv.sem, dv.v)
                    vector.tensor_scalar(out=hbuf[:, :], in0=hbuf[:, :],
                                         scalar1=colqh[:, tp, 3:4], scalar2=MAGIC,
                                         op0=ALU.mult, op1=ALU.add).then_inc(dv.sem, 1)
                    dv.inc()
                    vector.wait_ge(dv.sem, dv.v)
                    vector.tensor_scalar_add(out=hqb[:, tp, :], in0=hbuf[:, :],
                                             scalar1=nmagic[:, 0:1]).then_inc(dv.sem, 1)
                    dv.inc()
                hq_tgt[tc] = dv.v

            def emit_hqt(tc):
                tp = tc % 2
                for qd in range(4):
                    ppq = qd % 2

                    @block.tensor
                    def _(tensor):
                        if qd == 0:
                            tensor.wait_ge(dv.sem, hq_tgt[tc])
                        wge(tensor, lnev, lnev.v - 1)
                        ins = None
                        for ic in range(8):
                            ins = tensor.transpose(
                                ptT2[:, ppq, ic * 128:(ic + 1) * 128],
                                hqb[:, tp, (qd * 8 + ic) * 128:(qd * 8 + ic + 1) * 128],
                                ident[:])
                        ins.then_inc(lnt.sem, 1)
                        lnt.inc()

                    @block.vector
                    def _(vector):
                        vector.wait_ge(lnt.sem, lnt.v)
                        if tc >= 1 and qd == 0:
                            wge(vector, dv, devac_tgt[tc - 1])
                        vector.tensor_copy(
                            out=hqfm[:, tp, qd * 8:(qd + 1) * 8, :],
                            in_=ptT2[:, ppq, :].rearrange("p (a b) -> p a b", a=8)).then_inc(lnev.sem, 1)
                        lnev.inc()
                hqfm_tgt[tc] = lnev.v

            def emit_down(tc):
                tp = tc % 2
                for f2c in range(32):
                    wp = (tc * 32 + f2c) % 2

                    @block.sync
                    def _(g):
                        wge(g, dt, dt.v - 1 if (tc > 0 or f2c >= 2) else 0)
                        g.dma_start(out=wdb[:, wp, :], in_=wdc[f2c]).then_inc(dw[wp].sem, 16)
                        dw[wp].inc(16)

                    @block.tensor
                    def _(tensor):
                        tensor.wait_ge(dw[wp].sem, dw[wp].v)
                        if f2c == 0:
                            tensor.wait_ge(lnev.sem, hqfm_tgt[tc])
                            if tc >= 1:
                                tensor.wait_ge(dv.sem, devac_tgt[tc - 1])
                        ins = None
                        for h in range(2):
                            ins = tensor.matmul(dnp[:, h * 512:(h + 1) * 512],
                                                hqfm[:, tp, f2c, :],
                                                wdb[:, wp, h * 512:(h + 1) * 512],
                                                start=(f2c == 0), stop=(f2c == 31))
                        ins.then_inc(dt.sem, 1)
                        dt.inc()

                @block.vector
                def _(vector):
                    vector.wait_ge(dt.sem, dt.v)
                    vector.scalar_tensor_tensor(out=resid[:, tc, :], in0=dnp[:, :],
                                                scalar=colqh[:, tp, 2:3],
                                                in1=resid[:, tc, :],
                                                op0=ALU.mult, op1=ALU.add).then_inc(dv.sem, 1)
                    dv.inc()
                devac_tgt[tc] = dv.v

            for tc in range(NTC):
                emit_gu(tc)
                if tc >= 1:
                    emit_hqt(tc - 1)
                    emit_down(tc - 1)
            emit_hqt(NTC - 1)
            emit_down(NTC - 1)


_NC_CACHE = {}


DEBUG = False


def _get_nc(n_phases=4):
    key = (n_phases, DEBUG)
    if key not in _NC_CACHE:
        _NC_CACHE[key] = build_nc(n_phases, dbg=DEBUG)
    return _NC_CACHE[key]


def kernel(L0_attn_x, L0_attn_w_q, L0_attn_w_k, L0_attn_w_v,
           L0_ffn_k0, L0_ffn_k1, L0_ffn_k2, L0_ffn_s0, L0_ffn_s1, L0_ffn_s2,
           L1_attn_W_qkv, L1_ffn_w_gate, L1_ffn_w_up, L1_ffn_w_down,
           _phases=4):
    x = np.ascontiguousarray(np.asarray(L0_attn_x, dtype=np.float32).reshape(T, D))
    wq = np.ascontiguousarray(np.asarray(L0_attn_w_q, np.float32).reshape(8, 128, D))
    wk = np.ascontiguousarray(np.asarray(L0_attn_w_k, np.float32).reshape(8, 128, D))
    wv = np.ascontiguousarray(np.asarray(L0_attn_w_v, np.float32).reshape(8, 128, D))
    k0u = np.asarray(L0_ffn_k0).view(np.uint8)
    k0a = np.ascontiguousarray(k0u[:1024].reshape(8, 128, 4096))
    k0b = np.ascontiguousarray(k0u[1024:].reshape(8, 128, 4096))
    k1u = np.ascontiguousarray(np.asarray(L0_ffn_k1).view(np.uint8).reshape(32, 128, 4096))
    k2u = np.ascontiguousarray(np.asarray(L0_ffn_k2).view(np.uint8).reshape(32, 128, 1024))
    s0 = np.ascontiguousarray(np.broadcast_to(
        np.asarray(L0_ffn_s0, np.float32).reshape(1, 1), (128, 1)))
    s1 = np.ascontiguousarray(np.broadcast_to(
        np.asarray(L0_ffn_s1, np.float32).reshape(1, 1), (128, 1)))
    s2 = np.ascontiguousarray(np.broadcast_to(
        np.asarray(L0_ffn_s2, np.float32).reshape(1, 1), (128, 1)))
    wqkvT = np.ascontiguousarray(
        np.asarray(L1_attn_W_qkv, np.float32).T.reshape(8, 128, 3072))
    wgT = np.ascontiguousarray(
        np.asarray(L1_ffn_w_gate, np.float32).T.reshape(8, 128, 4096))
    wuT = np.ascontiguousarray(
        np.asarray(L1_ffn_w_up, np.float32).T.reshape(8, 128, 4096))
    wdT = np.ascontiguousarray(
        np.asarray(L1_ffn_w_down, np.float32).T.reshape(32, 128, 1024))

    in_maps = []
    for c in range(NCORES):
        sA, sB = c, NSTRIPE - 1 - c
        xs = np.ascontiguousarray(
            np.concatenate([x[sA * 1024:(sA + 1) * 1024],
                            x[sB * 1024:(sB + 1) * 1024]], axis=0)).reshape(NTC, 128, D)
        btab_row = np.empty((2, 8, NSTEP), np.float32)
        for si, stripe in enumerate((sA, sB)):
            for j in range(NSTEP):
                btab_row[si, :, j] = SHIFT if (j == 0 or (j - 1 < stripe)) else CLOSED
        btab = np.ascontiguousarray(
            np.broadcast_to(btab_row.reshape(1, -1), (128, 2 * 8 * NSTEP)))
        in_maps.append({
            "x_in": xs, "btab": btab,
            "w_q": wq, "w_k": wk, "w_v": wv,
            "k0a": k0a, "k0b": k0b, "k1": k1u, "k2": k2u,
            "s0": s0, "s1": s1, "s2": s2,
            "wqkvT": wqkvT, "wgT": wgT, "wuT": wuT, "wdT": wdT,
        })

    nc = _get_nc(_phases)
    res = run_bass_kernel_spmd(nc, in_maps, core_ids=list(range(NCORES)))

    y = np.empty((T, D), np.float32)
    for c in range(NCORES):
        out = np.asarray(res.results[c]["y_out"]).reshape(TPC, D)
        sA, sB = c, NSTRIPE - 1 - c
        y[sA * 1024:(sA + 1) * 1024] = out[:1024]
        y[sB * 1024:(sB + 1) * 1024] = out[1024:]
    return y.reshape(1, T, D)


# version 05
# speedup vs baseline: 1.1430x; 1.0875x over previous
import numpy as np
import ml_dtypes
from contextlib import ExitStack

import concourse.bass as bass
import concourse.mybir as mybir
from concourse.bass_utils import run_bass_kernel_spmd

F32 = mybir.dt.float32
BF16 = mybir.dt.bfloat16
U8 = mybir.dt.uint8
AF = mybir.ActivationFunctionType
ALU = mybir.AluOpType
AX = mybir.AxisListType

NCORES = 8
D = 1024
T = 16384
TPC = 2048
NTC = 16
NSTRIPE = 16
NSTEP = 16
MAGIC = 12582912.0
SHIFT = -12.0
CLOSED = -30012.0
MASKNEG = -30000.0


class Ctr:
    def __init__(self, sem):
        self.sem = sem
        self.v = 0

    def inc(self, n=1):
        self.v += n
        return self.v


def wge(eng, ctr, v):
    if v > 0:
        eng.wait_ge(ctr.sem, v)


def emit_barrier(nc, block, S):
    bar = S["bar"]
    tgt = bar.v + 5
    for engname in ("tensor", "vector", "scalar", "gpsimd", "sync"):
        def _sec(eng, tgt=tgt):
            eng.drain().then_inc(bar.sem, 1)
            eng.wait_ge(bar.sem, tgt)
        getattr(block, engname)(_sec)
    bar.v = tgt


def _posblock_src(b):
    rank = b if b <= 7 else 15 - b
    sub = 0 if b <= 7 else 1
    return rank, sub


C_STAGE = 9
RACECHECK = False


def build_nc(n_phases=4, dbg=False):
    nc = bass.Bass("TRN2", target_bir_lowering=False,
                   detect_race_conditions=RACECHECK)

    x_in = nc.dram_tensor("x_in", [NTC, 128, D], F32, kind="ExternalInput").ap()
    btab_in = nc.dram_tensor("btab", [128, 2 * 8 * NSTEP], F32, kind="ExternalInput").ap()
    wq_in = nc.dram_tensor("w_q", [8, 128, D], F32, kind="ExternalInput").ap()
    wk_in = nc.dram_tensor("w_k", [8, 128, D], F32, kind="ExternalInput").ap()
    wv_in = nc.dram_tensor("w_v", [8, 128, D], F32, kind="ExternalInput").ap()
    k0a_in = nc.dram_tensor("k0a", [8, 128, 4096], U8, kind="ExternalInput").ap()
    k0b_in = nc.dram_tensor("k0b", [8, 128, 4096], U8, kind="ExternalInput").ap()
    k1_in = nc.dram_tensor("k1", [32, 128, 4096], U8, kind="ExternalInput").ap()
    k2_in = nc.dram_tensor("k2", [32, 128, 1024], U8, kind="ExternalInput").ap()
    s0_in = nc.dram_tensor("s0", [128, 1], F32, kind="ExternalInput").ap()
    s1_in = nc.dram_tensor("s1", [128, 1], F32, kind="ExternalInput").ap()
    s2_in = nc.dram_tensor("s2", [128, 1], F32, kind="ExternalInput").ap()
    wqkvT_in = nc.dram_tensor("wqkvT", [8, 128, 3072], F32, kind="ExternalInput").ap()
    wgT_in = nc.dram_tensor("wgT", [8, 128, 4096], F32, kind="ExternalInput").ap()
    wuT_in = nc.dram_tensor("wuT", [8, 128, 4096], F32, kind="ExternalInput").ap()
    wdT_in = nc.dram_tensor("wdT", [32, 128, 1024], F32, kind="ExternalInput").ap()
    y_out = nc.dram_tensor("y_out", [NTC, 128, D], F32, kind="ExternalOutput").ap()
    dbg_h = (nc.dram_tensor("dbg_h", [2, 128, 32, 512], F32, kind="ExternalOutput").ap()
             if dbg else None)
    dbg_w = (nc.dram_tensor("dbg_w", [2, 128, 8, 512], F32, kind="ExternalOutput").ap()
             if dbg else None)

    agin_k_h = nc.dram_tensor("agin_k", [8, 128, TPC], BF16)
    agin_v_h = nc.dram_tensor("agin_v", [NTC, 128, D], BF16)
    agout_k_h = nc.dram_tensor("agout_k", [NCORES * 8, 128, TPC], BF16)
    agout_v_h = nc.dram_tensor("agout_v", [NCORES * NTC, 128, D], BF16)
    arin_h = nc.dram_tensor("arin_kv", [64, 1040], F32)
    arout_h = nc.dram_tensor("arout_kv", [64, 1040], F32)
    w0c_h = nc.dram_tensor("w0c", [8, 128, 4096], BF16)
    w1c_h = nc.dram_tensor("w1c", [16, 32, 128, 256], BF16)
    w2c_h = nc.dram_tensor("w2c", [32, 128, 1024], BF16)
    wgc_h = nc.dram_tensor("wgc", [8, 128, 4096], BF16)
    wuc_h = nc.dram_tensor("wuc", [8, 128, 4096], BF16)
    wdc_h = nc.dram_tensor("wdc", [32, 128, 1024], BF16)
    dram = dict(
        agin_k=agin_k_h.ap(), agin_v=agin_v_h.ap(),
        agout_k=agout_k_h.ap(), agout_v=agout_v_h.ap(),
        arin=arin_h.ap(), arout=arout_h.ap(),
        w0c=w0c_h.ap(), w1c=w1c_h.ap(), w2c=w2c_h.ap(),
        wgc=wgc_h.ap(), wuc=wuc_h.ap(), wdc=wdc_h.ap(),
        agin_k_h=agin_k_h, agin_v_h=agin_v_h, agout_k_h=agout_k_h,
        agout_v_h=agout_v_h, arin_h=arin_h, arout_h=arout_h,
    )

    ident_d = nc.inline_tensor(np.eye(128, dtype=ml_dtypes.bfloat16), "identc")
    tri_np = np.where(np.arange(128)[None, :] <= np.arange(128)[:, None],
                      np.float32(0.0), np.float32(MASKNEG)).astype(np.float32)
    tri_d = nc.inline_tensor(tri_np, "tric")
    onebf_d = nc.inline_tensor(np.ones((128, 1), dtype=ml_dtypes.bfloat16), "onebfc")
    onesf_d = nc.inline_tensor(np.ones((128, 1), dtype=np.float32), "onesfc")
    onesrow_d = nc.inline_tensor(np.ones((1, 128), dtype=np.float32), "onesrowc")

    with ExitStack() as top:
        e = top.enter_context
        resid = e(nc.sbuf_tensor("resid", [128, NTC, D], F32))
        btab = e(nc.sbuf_tensor("btabs", [128, 2 * 8 * NSTEP], F32))
        ident = e(nc.sbuf_tensor("ident_s", [128, 128], BF16))
        tri = e(nc.sbuf_tensor("tri_s", [128, 128], F32))
        onebf = e(nc.sbuf_tensor("onebf_s", [128, 1], BF16))
        onesf = e(nc.sbuf_tensor("onesf_s", [128, 1], F32))
        onesrow = e(nc.sbuf_tensor("onesrow_s", [1, 128], F32))
        epsc = e(nc.sbuf_tensor("epsc", [128, 1], F32))
        nmagic = e(nc.sbuf_tensor("nmagic", [128, 1], F32))
        s0c = e(nc.sbuf_tensor("s0c", [128, 1], F32))
        s1c = e(nc.sbuf_tensor("s1c", [128, 1], F32))
        s2c = e(nc.sbuf_tensor("s2c", [128, 1], F32))

        S = {}
        for name in ("ld", "pv", "psc", "pd", "pd1", "pd2", "pd3", "pd4", "pst", "ptn",
                     "lnv", "lns", "lnt", "lnev",
                     "pjw", "pjt", "pjev", "pjst", "cc",
                     "kld0", "kld1", "vld0", "vld1",
                     "smm", "sexp", "stmm", "sptev", "spv", "svadd",
                     "vmask", "fin", "fv",
                     "fl00", "fl01", "fl10", "fl11", "fl20", "fl21", "ft", "fev", "fs",
                     "cld", "ct", "cev", "cv", "cs", "ccc", "cst",
                     "dg0", "dg1", "dw0", "dw1", "dt", "dev", "dv", "ds", "ppd",
                     "st", "bar"):
            S[name] = Ctr(e(nc.semaphore("sem_" + name)))

        block = e(nc.Block())
        ld = S["ld"]

        @block.sync
        def _(sync):
            for tc in range(NTC):
                sync.dma_start(out=resid[:, tc, :], in_=x_in[tc]).then_inc(ld.sem, 16)
                ld.inc(16)
            for dst, src in ((btab, btab_in), (ident, ident_d.ap()),
                             (tri, tri_d.ap()), (onebf, onebf_d.ap()),
                             (onesf, onesf_d.ap()), (onesrow, onesrow_d.ap()),
                             (s0c, s0_in), (s1c, s1_in), (s2c, s2_in)):
                sync.dma_start(out=dst[:], in_=src).then_inc(ld.sem, 16)
                ld.inc(16)
        ld_all = ld.v

        @block.vector
        def _(vector):
            vector.memset(epsc[:], 1e-5)
            vector.memset(nmagic[:], -MAGIC)

        prep_alloc, prep_steps = emit_weight_prep(nc, block, S, ld_all,
                                                  k0a_in, k0b_in, k1_in, k2_in,
                                                  s0c, s1c, s2c,
                                                  wgT_in, wuT_in, wdT_in, dram,
                                                  onesf, onesrow)

        def drain_prep(k):
            while prep_steps and k > 0:
                prep_steps.pop(0)()
                k -= 1
        drain_prep.alloc = prep_alloc

        emit_l0_attn(nc, block, S, ld_all, resid, btab, ident, tri, epsc,
                     wq_in, wk_in, wv_in, dram, drain_prep)

        if n_phases >= 2:
            emit_barrier(nc, block, S)
            emit_l0_ffn(nc, block, S, resid, ident, epsc, dram, dbg_h, dbg_w)
        if n_phases >= 3:
            emit_barrier(nc, block, S)
            emit_l1_attn(nc, block, S, resid, ident, epsc, onebf,
                         wqkvT_in, dram)
        if n_phases >= 4:
            emit_barrier(nc, block, S)
            emit_l1_ffn(nc, block, S, resid, ident, epsc, nmagic, dram, dbg_w, dbg_h)

        emit_barrier(nc, block, S)
        st = S["st"]

        @block.sync
        def _(sync):
            for tc in range(NTC):
                sync.dma_start(out=y_out[tc], in_=resid[:, tc, :]).then_inc(st.sem, 16)
                st.inc(16)
            sync.wait_ge(st.sem, st.v)

    return nc


def emit_ln(block, S, resid, tc, pp, scr, epsc, out_ap, first_wait=None,
            norm_waits=()):
    lnv, lns = S["lnv"], S["lns"]
    bnstat, mv, lnvv, rstd = scr

    @block.vector
    def _(vector):
        if first_wait is not None:
            sem, val = first_wait
            if val > 0:
                vector.wait_ge(sem, val)
        wge(vector, lns, lns.v)
        vector.bn_stats(out=bnstat[:, pp, 0:6], in_=resid[:, tc, 0:512])
        vector.bn_stats(out=bnstat[:, pp, 6:12], in_=resid[:, tc, 512:1024]).then_inc(lnv.sem, 1)
        lnv.inc()
        vector.wait_ge(lnv.sem, lnv.v)
        vector.bn_aggr(out=mv[:, pp, :], in_=bnstat[:, pp, :]).then_inc(lnv.sem, 1)
        lnv.inc()

    @block.scalar
    def _(scalar):
        scalar.wait_ge(lnv.sem, lnv.v)
        scalar.activation(out=lnvv[:, pp, :], in_=mv[:, pp, 1:2], func=AF.Ln,
                          bias=epsc[:, 0:1], scale=1.0)
        scalar.activation(out=rstd[:, pp, :], in_=lnvv[:, pp, :], func=AF.Exp,
                          scale=-0.5).then_inc(lns.sem, 1)
        lns.inc()

    @block.vector
    def _(vector):
        vector.wait_ge(lns.sem, lns.v)
        for sem, val in norm_waits:
            if val > 0:
                vector.wait_ge(sem, val)
        vector.tensor_scalar(out=out_ap, in0=resid[:, tc, :],
                             scalar1=mv[:, pp, 0:1], scalar2=rstd[:, pp, 0:1],
                             op0=ALU.subtract, op1=ALU.mult).then_inc(lnv.sem, 1)
        lnv.inc()


def _ln_scratch(nc, e, tag):
    return (e(nc.sbuf_tensor("bnstat" + tag, [128, 2, 12], F32)),
            e(nc.sbuf_tensor("mv" + tag, [128, 2, 2], F32)),
            e(nc.sbuf_tensor("lnvv" + tag, [128, 2, 1], F32)),
            e(nc.sbuf_tensor("rstd" + tag, [128, 2, 1], F32)))


def emit_transpose_1024(block, S, src_ap, ptT, pp, dst_ap, ident,
                        tensor_waits=(), vector_waits=(), ping=True):
    lnt, lnev = S["lnt"], S["lnev"]

    @block.tensor
    def _(tensor):
        for sem, val in tensor_waits:
            if val > 0:
                tensor.wait_ge(sem, val)
        wge(tensor, lnev, lnev.v - (1 if ping else 0))
        ins = None
        for ic in range(8):
            ins = tensor.transpose(ptT[:, pp, ic * 128:(ic + 1) * 128],
                                   src_ap[:, ic * 128:(ic + 1) * 128],
                                   ident[:])
        ins.then_inc(lnt.sem, 1)
        lnt.inc()

    @block.vector
    def _(vector):
        vector.wait_ge(lnt.sem, lnt.v)
        for sem, val in vector_waits:
            if val > 0:
                vector.wait_ge(sem, val)
        src2 = ptT[:, pp, :]
        if len(dst_ap.shape) == 3:
            src2 = src2.rearrange("p (a b) -> p a b", a=dst_ap.shape[1])
        vector.tensor_copy(out=dst_ap, in_=src2).then_inc(lnev.sem, 1)
        lnev.inc()


def emit_weight_prep(nc, block, S, ld_all,
                     k0a_in, k0b_in, k1_in, k2_in, s0c, s1c, s2c,
                     wgT_in, wuT_in, wdT_in, dram, onesf, onesrow):
    pv, psc, pst, ptn = S["pv"], S["psc"], S["pst"], S["ptn"]
    pd, pd1, pd2, pd3, pd4 = S["pd"], S["pd1"], S["pd2"], S["pd3"], S["pd4"]
    w0c, w1c, w2c = dram["w0c"], dram["w1c"], dram["w2c"]
    wgc, wuc, wdc = dram["wgc"], dram["wuc"], dram["wdc"]

    B = {}

    def alloc():
        ctx = ExitStack()
        S["prep_ctx"] = ctx
        e = ctx.enter_context
        B["au8"] = e(nc.sbuf_tensor("au8", [128, 4096], U8))
        B["bu8"] = e(nc.sbuf_tensor("bu8", [128, 4096], U8))
        B["wtmp"] = e(nc.sbuf_tensor("wtmp", [128, 4096], F32))
        B["wbf"] = e(nc.sbuf_tensor("wbf", [128, 4096], BF16))
        B["cols"] = e(nc.sbuf_tensor("pcols", [128, 8], F32))
        B["absacc"] = e(nc.sbuf_tensor("absacc", [128, 3], F32))
        B["redc"] = e(nc.sbuf_tensor("redc", [128, 1], F32))
        B["scl"] = e(nc.sbuf_tensor("scl", [128, 2, 3], F32))
        B["tot"] = e(nc.sbuf_tensor("tot", [1, 3], F32))
        B["pbc"] = e(nc.psum_tensor("pbc", [128, 8], F32))

    class _Lazy:
        def __init__(self, key):
            self._key = key

        def __getitem__(self, item):
            return B[self._key][item]

    au8 = _Lazy("au8"); bu8 = _Lazy("bu8"); wtmp = _Lazy("wtmp")
    wbf = _Lazy("wbf"); cols = _Lazy("cols"); absacc = _Lazy("absacc")
    redc = _Lazy("redc"); scl = _Lazy("scl"); tot = _Lazy("tot")
    pbc = _Lazy("pbc")

    steps = []

    def s_cols():
        @block.vector
        def _(vector):
            vector.wait_ge(S["ld"].sem, ld_all)
            vector.tensor_scalar_mul(out=cols[:, 0:1], in0=s0c[:], scalar1=2.0)
            vector.tensor_scalar_mul(out=cols[:, 1:2], in0=s1c[:], scalar1=4.0)
            vector.tensor_scalar_mul(out=cols[:, 2:3], in0=s1c[:], scalar1=-2.0)
            vector.tensor_scalar_mul(out=cols[:, 3:4], in0=s2c[:], scalar1=4.0)
            vector.tensor_scalar_mul(out=cols[:, 4:5], in0=s2c[:], scalar1=-2.0).then_inc(pv.sem, 1)
            pv.inc()
    steps.append(s_cols)
    scales_tgt = [None]

    for ic in range(8):
        def s_w0(ic=ic):
            @block.gpsimd
            def _(g):
                wge(g, pv, pv.v)
                g.dma_start(out=au8[:], in_=k0a_in[ic]).then_inc(pd.sem, 16)
                g.dma_start(out=bu8[:], in_=k0b_in[ic]).then_inc(pd.sem, 16)
                pd.inc(32)

            @block.vector
            def _(vector):
                vector.wait_ge(pd.sem, pd.v)
                if ic == 0:
                    vector.wait_ge(pv.sem, 1)
                wge(vector, pst, pst.v)
                vector.tensor_sub(out=wtmp[:], in0=au8[:], in1=bu8[:]).then_inc(pv.sem, 1)
                pv.inc()
                vector.wait_ge(pv.sem, pv.v)
                vector.tensor_scalar_mul(out=wbf[:], in0=wtmp[:],
                                         scalar1=cols[:, 0:1]).then_inc(pv.sem, 1)
                pv.inc()

            @block.sync
            def _(sync):
                sync.wait_ge(pv.sem, pv.v)
                sync.dma_start(out=w0c[ic], in_=wbf[:]).then_inc(pst.sem, 16)
                pst.inc(16)
        steps.append(s_w0)

    for ic in range(32):
        def s_w1(ic=ic):
            @block.gpsimd
            def _(g):
                wge(g, pv, pv.v)
                wge(g, psc, psc.v)
                g.dma_start(out=au8[:], in_=k1_in[ic]).then_inc(pd1.sem, 16)
                pd1.inc(16)

            @block.scalar
            def _(scalar):
                scalar.wait_ge(pd1.sem, pd1.v)
                scalar.wait_ge(pv.sem, 1)
                wge(scalar, pst, pst.v)
                scalar.activation(out=wbf[:], in_=au8[:], func=AF.Identity,
                                  scale=cols[:, 1:2], bias=cols[:, 2:3]).then_inc(psc.sem, 1)
                psc.inc()

            @block.sync
            def _(sync):
                sync.wait_ge(psc.sem, psc.v)
                for g16 in range(16):
                    sync.dma_start(out=w1c[g16, ic],
                                   in_=wbf[:, g16 * 256:(g16 + 1) * 256]).then_inc(pst.sem, 16)
                    pst.inc(16)
        steps.append(s_w1)

    for it in range(8):
        def s_w2(it=it):
            @block.gpsimd
            def _(g):
                wge(g, psc, psc.v)
                for j in range(4):
                    g.dma_start(out=au8[:, j * 1024:(j + 1) * 1024],
                                in_=k2_in[4 * it + j]).then_inc(pd2.sem, 16)
                    pd2.inc(16)

            @block.scalar
            def _(scalar):
                scalar.wait_ge(pd2.sem, pd2.v)
                wge(scalar, pst, pst.v)
                scalar.activation(out=wbf[:], in_=au8[:], func=AF.Identity,
                                  scale=cols[:, 3:4], bias=cols[:, 4:5]).then_inc(psc.sem, 1)
                psc.inc()

            @block.sync
            def _(sync):
                sync.wait_ge(psc.sem, psc.v)
                for j in range(4):
                    sync.dma_start(out=w2c[4 * it + j],
                                   in_=wbf[:, j * 1024:(j + 1) * 1024]).then_inc(pst.sem, 16)
                    pst.inc(16)
        steps.append(s_w2)

    specs = [(wgT_in, wgc, 8, 4096, 0), (wuT_in, wuc, 8, 4096, 1),
             (wdT_in, wdc, 32, 1024, 2)]

    for src_, dst, nic, F, mi in specs:
        for ic in range(nic):
            def s_p1(src_=src_, ic=ic, F=F, mi=mi):
                @block.gpsimd
                def _(g):
                    wge(g, pv, pv.v)
                    g.dma_start(out=wtmp[:, 0:F], in_=src_[ic]).then_inc(pd3.sem, 16)
                    pd3.inc(16)

                @block.vector
                def _(vector):
                    vector.wait_ge(pd3.sem, pd3.v)
                    vector.reduce_sum(out=redc[:], in_=wtmp[:, 0:F], axis=AX.X,
                                      apply_absolute_value=True).then_inc(pv.sem, 1)
                    pv.inc()
                    vector.wait_ge(pv.sem, pv.v)
                    if ic == 0:
                        vector.tensor_copy(out=absacc[:, mi:mi + 1],
                                           in_=redc[:]).then_inc(pv.sem, 1)
                    else:
                        vector.tensor_add(out=absacc[:, mi:mi + 1],
                                          in0=absacc[:, mi:mi + 1],
                                          in1=redc[:]).then_inc(pv.sem, 1)
                    pv.inc()
            steps.append(s_p1)

    def s_scale():
        @block.tensor
        def _(tensor):
            tensor.wait_ge(pv.sem, pv.v)
            tensor.matmul(pbc[0:1, 4:7], onesf[:, 0:1], absacc[:, 0:3],
                          start=True, stop=True).then_inc(ptn.sem, 1)
            ptn.inc()

        @block.vector
        def _(vector):
            vector.wait_ge(ptn.sem, ptn.v)
            vector.tensor_copy(out=tot[0:1, 0:3], in_=pbc[0:1, 4:7]).then_inc(pv.sem, 1)
            pv.inc()

        @block.tensor
        def _(tensor):
            tensor.wait_ge(pv.sem, pv.v)
            tensor.matmul(pbc[:, 0:3], onesrow[0:1, :],
                          tot[0:1, 0:3], start=True, stop=True).then_inc(ptn.sem, 1)
            ptn.inc()

        @block.vector
        def _(vector):
            vector.wait_ge(ptn.sem, ptn.v)
            for mi, numel in ((0, 8 * 128 * 4096), (1, 8 * 128 * 4096),
                              (2, 32 * 128 * 1024)):
                vector.tensor_scalar(out=scl[:, 0, mi:mi + 1], in0=pbc[:, mi:mi + 1],
                                     scalar1=1.0 / numel, scalar2=1e-5,
                                     op0=ALU.mult, op1=ALU.max).then_inc(pv.sem, 1)
                pv.inc()
            vector.wait_ge(pv.sem, pv.v)
            vector.reciprocal(out=scl[:, 1, 0:1], in_=scl[:, 0, 0:1])
            vector.reciprocal(out=scl[:, 1, 1:2], in_=scl[:, 0, 1:2])
            vector.reciprocal(out=scl[:, 1, 2:3], in_=scl[:, 0, 2:3]).then_inc(pv.sem, 1)
            pv.inc()
    steps.append(s_scale)

    for src_, dst, nic, F, mi in specs:
        for ic in range(nic):
            def s_p2(src_=src_, dst=dst, ic=ic, F=F, mi=mi):
                @block.gpsimd
                def _(g):
                    wge(g, pv, pv.v)
                    g.dma_start(out=wtmp[:, 0:F], in_=src_[ic]).then_inc(pd4.sem, 16)
                    pd4.inc(16)

                @block.vector
                def _(vector):
                    vector.wait_ge(pd4.sem, pd4.v)
                    wge(vector, pst, pst.v)
                    H = F // 2
                    for hh_ in range(2):
                        sl = slice(hh_ * H, (hh_ + 1) * H)
                        vector.tensor_scalar(out=wtmp[:, sl], in0=wtmp[:, sl],
                                             scalar1=scl[:, 1, mi:mi + 1], scalar2=1.4999999,
                                             op0=ALU.mult, op1=ALU.min).then_inc(pv.sem, 1)
                        pv.inc()
                        vector.wait_ge(pv.sem, pv.v)
                        vector.tensor_scalar(out=wtmp[:, sl], in0=wtmp[:, sl],
                                             scalar1=-1.4999999, scalar2=MAGIC,
                                             op0=ALU.max, op1=ALU.add).then_inc(pv.sem, 1)
                        pv.inc()
                        vector.wait_ge(pv.sem, pv.v)
                        vector.tensor_scalar(out=wbf[:, sl], in0=wtmp[:, sl],
                                             scalar1=-MAGIC, scalar2=scl[:, 0, mi:mi + 1],
                                             op0=ALU.add, op1=ALU.mult).then_inc(pv.sem, 1)
                        pv.inc()

                @block.sync
                def _(sync):
                    sync.wait_ge(pv.sem, pv.v)
                    sync.dma_start(out=dst[ic], in_=wbf[:, 0:F]).then_inc(pst.sem, 16)
                    pst.inc(16)
            steps.append(s_p2)

    def s_fin():
        @block.sync
        def _(sync):
            sync.wait_ge(pst.sem, pst.v)
            sync.nop().then_inc(S["ppd"].sem, 1)
            S["ppd"].inc()

        @block.gpsimd
        def _(g):
            g.wait_ge(pd.sem, pd.v)
            g.wait_ge(pd1.sem, pd1.v)
            g.wait_ge(pd2.sem, pd2.v)
            g.wait_ge(pd3.sem, pd3.v)
            g.wait_ge(pd4.sem, pd4.v)
            g.nop().then_inc(S["ppd"].sem, 1)
            S["ppd"].inc()
        S["prep_ctx"].close()
    steps.append(s_fin)
    return alloc, steps


def emit_l0_attn(nc, block, S, ld_all, resid, btab, ident, tri, epsc,
                 wq_in, wk_in, wv_in, dram, drain_prep=lambda k: None):
    lnv = S["lnv"]
    pjw, pjt, pjev, pjst, cc = S["pjw"], S["pjt"], S["pjev"], S["pjst"], S["cc"]
    agin_k, agin_v = dram["agin_k"], dram["agin_v"]
    agout_k, agout_v = dram["agout_k"], dram["agout_v"]

    with ExitStack() as actx:
        e = actx.enter_context
        qfm = e(nc.sbuf_tensor("qfm", [128, 8, TPC], BF16))

        with ExitStack() as ctx:
            e2 = ctx.enter_context
            lnb = e2(nc.sbuf_tensor("lnbA", [128, 2, D], BF16))
            lnT = e2(nc.sbuf_tensor("lnTA", [128, 8, TPC], BF16))
            wsb = e2(nc.sbuf_tensor("wsbA", [128, 8, D], BF16))
            scr = _ln_scratch(nc, e2, "A")
            stg = e2(nc.sbuf_tensor("stgA", [128, 2, 512], BF16))
            ptT = e2(nc.psum_tensor("ptTA", [128, 2, 1024], BF16))
            pj = e2(nc.psum_tensor("pjA", [128, 4, 512], F32))
            if hasattr(drain_prep, "alloc"):
                drain_prep.alloc()

            for tc in range(NTC):
                pp = tc % 2
                emit_ln(block, S, resid, tc, pp, scr, epsc, lnb[:, pp, :],
                        first_wait=(S["ld"].sem, ld_all if tc == 0 else 0),
                        norm_waits=((S["lnt"].sem, S["lnt"].v - 1),))
                emit_transpose_1024(
                    block, S, lnb[:, pp, :], ptT, pp,
                    lnT[:, :, tc * 128:(tc + 1) * 128],
                    ident, tensor_waits=((lnv.sem, lnv.v),))
                drain_prep(2)
            lnT_done = S["lnev"].v

            @block.gpsimd
            def _(g):
                for ic in range(8):
                    g.dma_start(out=wsb[:, ic, :], in_=wq_in[ic]).then_inc(pjw.sem, 16)
                    pjw.inc(16)

            nmm = 0
            for oc in range(8):
                for ns in range(4):
                    pp = nmm % 4

                    @block.tensor
                    def _(tensor):
                        if nmm == 0:
                            tensor.wait_ge(pjw.sem, pjw.v)
                            tensor.wait_ge(S["lnev"].sem, lnT_done)
                        wge(tensor, pjev, nmm - 3)
                        ins = None
                        for ic in range(8):
                            ins = tensor.matmul(pj[:, pp, :],
                                                wsb[:, ic, oc * 128:(oc + 1) * 128],
                                                lnT[:, ic, ns * 512:(ns + 1) * 512],
                                                start=(ic == 0), stop=(ic == 7))
                        ins.then_inc(pjt.sem, 1)
                        pjt.inc()

                    @block.vector
                    def _(vector):
                        vector.wait_ge(pjt.sem, pjt.v)
                        vector.tensor_scalar_mul(out=qfm[:, oc, ns * 512:(ns + 1) * 512],
                                                 in0=pj[:, pp, :],
                                                 scalar1=0.03125).then_inc(pjev.sem, 1)
                        pjev.inc()
                    nmm += 1
                    drain_prep(2)
            q_done = pjt.v

            @block.gpsimd
            def _(g):
                g.wait_ge(pjt.sem, q_done)
                for ic in range(8):
                    g.dma_start(out=wsb[:, ic, :], in_=wk_in[ic]).then_inc(pjw.sem, 16)
                    pjw.inc(16)
            kw_ready = pjw.v

            nst = 0
            for oc in range(8):
                for ns in range(4):
                    pp = nmm % 4
                    sp = nst % 2

                    @block.tensor
                    def _(tensor):
                        if nmm == 32:
                            tensor.wait_ge(pjw.sem, kw_ready)
                        wge(tensor, pjev, nmm - 3)
                        ins = None
                        for ic in range(8):
                            ins = tensor.matmul(pj[:, pp, :],
                                                wsb[:, ic, oc * 128:(oc + 1) * 128],
                                                lnT[:, ic, ns * 512:(ns + 1) * 512],
                                                start=(ic == 0), stop=(ic == 7))
                        ins.then_inc(pjt.sem, 1)
                        pjt.inc()

                    @block.vector
                    def _(vector):
                        vector.wait_ge(pjt.sem, pjt.v)
                        wge(vector, pjst, (nst - 1) * 16)
                        vector.tensor_copy(out=stg[:, sp, :],
                                           in_=pj[:, pp, :]).then_inc(pjev.sem, 1)
                        pjev.inc()

                    @block.sync
                    def _(sync):
                        sync.wait_ge(pjev.sem, pjev.v)
                        sync.dma_start(out=agin_k[oc][:, ns * 512:(ns + 1) * 512],
                                       in_=stg[:, sp, :]).then_inc(pjst.sem, 16)
                        pjst.inc(16)
                    nmm += 1
                    nst += 1
                    drain_prep(1)
            k_done = pjt.v

            @block.gpsimd
            def _(g):
                g.wait_ge(pjt.sem, k_done)
                for ic in range(8):
                    g.dma_start(out=wsb[:, ic, :], in_=wv_in[ic]).then_inc(pjw.sem, 16)
                    pjw.inc(16)
            vw_ready = pjw.v

            for tc in range(NTC):
                for nh in range(2):
                    pp = nmm % 4
                    sp = nst % 2

                    @block.tensor
                    def _(tensor):
                        if nmm == 64:
                            tensor.wait_ge(pjw.sem, vw_ready)
                        wge(tensor, pjev, nmm - 3)
                        ins = None
                        for ic in range(8):
                            ins = tensor.matmul(pj[:, pp, :],
                                                lnT[:, ic, tc * 128:(tc + 1) * 128],
                                                wsb[:, ic, nh * 512:(nh + 1) * 512],
                                                start=(ic == 0), stop=(ic == 7))
                        ins.then_inc(pjt.sem, 1)
                        pjt.inc()

                    @block.vector
                    def _(vector):
                        vector.wait_ge(pjt.sem, pjt.v)
                        wge(vector, pjst, (nst - 1) * 16)
                        vector.tensor_copy(out=stg[:, sp, :],
                                           in_=pj[:, pp, :]).then_inc(pjev.sem, 1)
                        pjev.inc()

                    @block.sync
                    def _(sync):
                        sync.wait_ge(pjev.sem, pjev.v)
                        sync.dma_start(out=agin_v[tc][:, nh * 512:(nh + 1) * 512],
                                       in_=stg[:, sp, :]).then_inc(pjst.sem, 16)
                        pjst.inc(16)
                    nmm += 1
                    nst += 1
                    drain_prep(1)
            agin_done = pjst.v

            @block.gpsimd
            def _(g):
                g.wait_ge(pjst.sem, agin_done)
                g.collective_compute("AllGather", ALU.bypass,
                                     replica_groups=[list(range(NCORES))],
                                     ins=[dram["agin_k_h"].ap().opt()],
                                     outs=[dram["agout_k_h"].ap().opt()]).then_inc(cc.sem, 1)
                cc.inc()
                g.collective_compute("AllGather", ALU.bypass,
                                     replica_groups=[list(range(NCORES))],
                                     ins=[dram["agin_v_h"].ap().opt()],
                                     outs=[dram["agout_v_h"].ap().opt()]).then_inc(cc.sem, 1)
                cc.inc()
            ag_done = cc.v

            drain_prep(1 << 30)

        with ExitStack() as ctx:
            e2 = ctx.enter_context
            osb = e2(nc.sbuf_tensor("osb", [128, 8, D], F32))
            lbuf = e2(nc.sbuf_tensor("lbuf", [128, 8, NSTEP], F32))
            lrec = e2(nc.sbuf_tensor("lrec", [128, 8], F32))
            kt = e2(nc.sbuf_tensor("ktile", [128, 2, 8, 1024], BF16))
            vt = e2(nc.sbuf_tensor("vtile", [128, 2, 8, 1024], BF16))
            pbuf = e2(nc.sbuf_tensor("pbuf", [128, 2, 1024], BF16))
            ptsb = e2(nc.sbuf_tensor("ptsb", [128, 2, 1024], BF16))
            spsum = e2(nc.psum_tensor("spsum", [128, 2, 1024], F32))
            ptp = e2(nc.psum_tensor("ptp", [128, 2, 1024], BF16))
            pvp = e2(nc.psum_tensor("pvp", [128, 1024], F32))

            kldp = [S["kld0"], S["kld1"]]
            vldp = [S["vld0"], S["vld1"]]
            smm, sexp = S["smm"], S["sexp"]
            stmm, sptev, spv = S["stmm"], S["sptev"], S["spv"]
            svadd, vmask, fin = S["svadd"], S["vmask"], S["fin"]

            def emit_transp(c):
                qs, sp = c["qs"], c["sp"]

                @block.tensor
                def _(tensor):
                    tensor.wait_ge(sexp.sem, qs + 1)
                    wge(tensor, sptev, qs - 1)
                    ins = None
                    for kc in range(8):
                        ins = tensor.transpose(ptp[:, sp, kc * 128:(kc + 1) * 128],
                                               pbuf[:, sp, kc * 128:(kc + 1) * 128],
                                               ident[:])
                    ins.then_inc(stmm.sem, 1)
                    stmm.inc()

                @block.vector
                def _(vector):
                    vector.wait_ge(stmm.sem, qs + 1)
                    wge(vector, spv, qs - 1)
                    vector.tensor_copy(out=ptsb[:, sp, :],
                                       in_=ptp[:, sp, :]).then_inc(sptev.sem, 1)
                    sptev.inc()

            def emit_exps(specs):
                @block.scalar
                def _(scalar):
                    for qs, sp, r, j, s_idx, vm in specs:
                        scalar.wait_ge(smm.sem, qs + 1)
                        if vm is not None:
                            scalar.wait_ge(vmask.sem, vm)
                        if s_idx == 1 and j == 0 and r == 0:
                            scalar.wait_ge(fin.sem, 1)
                        bcol = (s_idx * 8 + r) * NSTEP + j
                        scalar.activation(out=pbuf[:, sp, :], in_=spsum[:, sp, :],
                                          func=AF.Exp, bias=btab[:, bcol:bcol + 1],
                                          scale=1.0,
                                          accum_out=lbuf[:, r, j:j + 1]).then_inc(sexp.sem, 1)

            def emit_pv(c):
                qs, sp, ring, r, j = c["qs"], c["sp"], c["ring"], c["r"], c["j"]

                @block.tensor
                def _(tensor):
                    tensor.wait_ge(sptev.sem, qs + 1)
                    wge(tensor, svadd, qs)
                    ins = None
                    for kc in range(8):
                        for h in range(2):
                            ins = tensor.matmul(
                                pvp[:, h * 512:(h + 1) * 512],
                                ptsb[:, sp, kc * 128:(kc + 1) * 128],
                                vt[:, ring, kc, h * 512:(h + 1) * 512],
                                start=(kc == 0), stop=(kc == 7))
                    ins.then_inc(spv.sem, 1)
                    spv.inc()

                @block.vector
                def _(vector):
                    vector.wait_ge(spv.sem, qs + 1)
                    if j == 0:
                        vector.tensor_copy(out=osb[:, r, :], in_=pvp[:, :]).then_inc(svadd.sem, 1)
                    else:
                        vector.tensor_add(out=osb[:, r, :], in0=osb[:, r, :],
                                          in1=pvp[:, :]).then_inc(svadd.sem, 1)
                    svadd.inc()

            for s_idx in range(2):
                pend = []
                for j in range(NSTEP):
                    exp_specs = []
                    g = s_idx * NSTEP + j
                    ring = g % 2

                    @block.sync
                    def _(sync):
                        if g == 0:
                            sync.wait_ge(pjst.sem, agin_done)
                            sync.wait_ge(S["ppd"].sem, 2)
                        if g == 1:
                            sync.wait_ge(cc.sem, ag_done)
                        wge(sync, smm, 8 * (g - 1))
                        if j == 0:
                            for ic in range(8):
                                sync.dma_start(
                                    out=kt[:, ring, ic, :],
                                    in_=agin_k[ic][:, s_idx * 1024:(s_idx + 1) * 1024],
                                ).then_inc(kldp[ring].sem, 16)
                                kldp[ring].inc(16)
                        else:
                            rank, sub = _posblock_src(j - 1)
                            for ic in range(8):
                                sync.dma_start(
                                    out=kt[:, ring, ic, :],
                                    in_=agout_k[8 * rank + ic][:, sub * 1024:(sub + 1) * 1024],
                                ).then_inc(kldp[ring].sem, 16)
                                kldp[ring].inc(16)

                    @block.gpsimd
                    def _(gq):
                        if g == 0:
                            gq.wait_ge(pjst.sem, agin_done)
                            gq.wait_ge(S["ppd"].sem, 2)
                        if g == 1:
                            gq.wait_ge(cc.sem, ag_done)
                        wge(gq, spv, 8 * (g - 1))
                        if j == 0:
                            for kc in range(8):
                                gq.dma_start(out=vt[:, ring, kc, :],
                                             in_=agin_v[s_idx * 8 + kc]).then_inc(vldp[ring].sem, 16)
                                vldp[ring].inc(16)
                        else:
                            rank, sub = _posblock_src(j - 1)
                            for kc in range(8):
                                gq.dma_start(
                                    out=vt[:, ring, kc, :],
                                    in_=agout_v[NTC * rank + sub * 8 + kc]).then_inc(vldp[ring].sem, 16)
                                vldp[ring].inc(16)
                    k_tgt, v_tgt = kldp[ring].v, vldp[ring].v

                    for r in range(8):
                        qs = g * 8 + r
                        sp = qs % 2
                        qcol0 = s_idx * 1024

                        @block.tensor
                        def _(tensor):
                            if r == 0:
                                tensor.wait_ge(kldp[ring].sem, k_tgt)
                                tensor.wait_ge(vldp[ring].sem, v_tgt)
                            wge(tensor, sexp, qs - 1)
                            ins = None
                            for ic in range(8):
                                for h in range(2):
                                    ins = tensor.matmul(
                                        spsum[:, sp, h * 512:(h + 1) * 512],
                                        qfm[:, ic, qcol0 + r * 128:qcol0 + (r + 1) * 128],
                                        kt[:, ring, ic, h * 512:(h + 1) * 512],
                                        start=(ic == 0), stop=(ic == 7))
                            ins.then_inc(smm.sem, 1)
                            smm.inc()

                        if j == 0:
                            @block.vector
                            def _(vector):
                                vector.wait_ge(smm.sem, qs + 1)
                                ins = vector.tensor_add(
                                    out=spsum[:, sp, r * 128:(r + 1) * 128],
                                    in0=spsum[:, sp, r * 128:(r + 1) * 128],
                                    in1=tri[:])
                                if r < 7:
                                    ins = vector.tensor_scalar_add(
                                        out=spsum[:, sp, (r + 1) * 128:1024],
                                        in0=spsum[:, sp, (r + 1) * 128:1024],
                                        scalar1=MASKNEG)
                                ins.then_inc(vmask.sem, 1)
                                vmask.inc()

                        exp_specs.append((qs, sp, r, j, s_idx,
                                          vmask.v if j == 0 else None))
                        sexp.inc()

                        ctxq = dict(qs=qs, sp=sp, ring=ring, r=r, j=j)
                        pend.append(ctxq)
                        i_local = len(pend) - 1
                        if i_local >= 1:
                            emit_transp(pend[i_local - 1])
                        if i_local >= 2:
                            emit_pv(pend[i_local - 2])
                    emit_exps(exp_specs)

                emit_transp(pend[-1])
                emit_pv(pend[-2])
                emit_pv(pend[-1])
                pend.clear()

                @block.vector
                def _(vector):
                    vector.wait_ge(svadd.sem, svadd.v)
                    vector.reduce_sum(out=lrec[:, 0:8].rearrange("p a -> p a ()"),
                                      in_=lbuf[:, :, :], axis=AX.X).then_inc(S["fv"].sem, 1)
                    S["fv"].inc()
                    vector.wait_ge(S["fv"].sem, S["fv"].v)
                    vector.reciprocal(out=lrec[:, 0:8], in_=lrec[:, 0:8]).then_inc(S["fv"].sem, 1)
                    S["fv"].inc()
                    vector.wait_ge(S["fv"].sem, S["fv"].v)
                    ins = None
                    for r in range(8):
                        tc = s_idx * 8 + r
                        ins = vector.scalar_tensor_tensor(
                            out=resid[:, tc, :], in0=osb[:, r, :],
                            scalar=lrec[:, r:r + 1], in1=resid[:, tc, :],
                            op0=ALU.mult, op1=ALU.add)
                    ins.then_inc(fin.sem, 1)
                    fin.inc()


def emit_l0_ffn(nc, block, S, resid, ident, epsc, dram, dbg_h=None, dbg_w=None):
    fl0 = [S["fl00"], S["fl01"]]
    fl1 = [S["fl10"], S["fl11"]]
    fl2 = [S["fl20"], S["fl21"]]
    ft, fev, fv = S["ft"], S["fev"], S["fv"]
    lnv = S["lnv"]
    w0c, w1c, w2c = dram["w0c"], dram["w1c"], dram["w2c"]

    with ExitStack() as ctx:
        e = ctx.enter_context
        lnb = e(nc.sbuf_tensor("lnbB", [128, 2, D], BF16))
        lnS = e(nc.sbuf_tensor("lnSB", [128, 8, 512], BF16))
        h1 = e(nc.sbuf_tensor("h1B", [128, 32, 512], BF16))
        h2 = e(nc.sbuf_tensor("h2B", [128, 32, 512], BF16))
        w0b = e(nc.sbuf_tensor("w0bB", [128, 2, 8, 512], BF16))
        w1b = e(nc.sbuf_tensor("w1bB", [128, 2, 32, 256], BF16))
        w2b = e(nc.sbuf_tensor("w2bB", [128, 2, 1024], BF16))
        scr = _ln_scratch(nc, e, "B")
        ptT = e(nc.psum_tensor("ptTB", [128, 2, 1024], BF16))
        hp = e(nc.psum_tensor("hpB", [128, 2, 512], F32))
        h3p = e(nc.psum_tensor("h3pB", [128, 2, 1024], F32))

        for t in range(4):
            for i in range(4):
                tc = t * 4 + i
                pp = i % 2
                emit_ln(block, S, resid, tc, pp, scr, epsc, lnb[:, pp, :],
                        norm_waits=((S["lnt"].sem, S["lnt"].v - 1),))
                emit_transpose_1024(
                    block, S, lnb[:, pp, :], ptT, pp,
                    lnS[:, :, i * 128:(i + 1) * 128],
                    ident, tensor_waits=((lnv.sem, lnv.v),),
                    vector_waits=((ft.sem, ft.v),))
            lnS_done = S["lnev"].v

            for f1g in range(8):
                gp = f1g % 2

                @block.sync
                def _(g):
                    wge(g, ft, ft.v - 4 if (t > 0 or f1g >= 2) else 0)
                    g.dma_start(out=w0b[:, gp, :, :],
                                in_=w0c[:, :, f1g * 512:(f1g + 1) * 512]
                                .rearrange("a p b -> p a b")).then_inc(fl0[gp].sem, 16)
                    fl0[gp].inc(16)

                for lf in range(4):
                    f1c = f1g * 4 + lf
                    pp = f1c % 2

                    @block.tensor
                    def _(tensor):
                        if lf == 0:
                            tensor.wait_ge(fl0[gp].sem, fl0[gp].v)
                        if f1c == 0:
                            tensor.wait_ge(S["lnev"].sem, lnS_done)
                        wge(tensor, fev, fev.v - 1)
                        ins = None
                        for ic in range(8):
                            ins = tensor.matmul(hp[:, pp, :],
                                                w0b[:, gp, ic, lf * 128:(lf + 1) * 128],
                                                lnS[:, ic, :],
                                                start=(ic == 0), stop=(ic == 7))
                        ins.then_inc(ft.sem, 1)
                        ft.inc()

                    @block.vector
                    def _(vector):
                        vector.wait_ge(ft.sem, ft.v)
                        vector.tensor_relu(out=h1[:, f1c, :],
                                           in_=hp[:, pp, :]).then_inc(fev.sem, 1)
                        fev.inc()

            for f2g in range(16):
                gp = f2g % 2

                @block.sync
                def _(g):
                    wge(g, ft, ft.v - 2 if (t > 0 or f2g >= 2) else 0)
                    g.dma_start(out=w1b[:, gp, :, :],
                                in_=w1c[f2g].rearrange("a p b -> p a b")).then_inc(fl1[gp].sem, 16)
                    fl1[gp].inc(16)

                for lf in range(2):
                    f2c = f2g * 2 + lf
                    pp = f2c % 2

                    @block.tensor
                    def _(tensor):
                        if lf == 0:
                            tensor.wait_ge(fl1[gp].sem, fl1[gp].v)
                        wge(tensor, fev, fev.v - 1)
                        ins = None
                        for f1c in range(32):
                            ins = tensor.matmul(hp[:, pp, :],
                                                w1b[:, gp, f1c, lf * 128:(lf + 1) * 128],
                                                h1[:, f1c, :],
                                                start=(f1c == 0), stop=(f1c == 31))
                        ins.then_inc(ft.sem, 1)
                        ft.inc()

                    @block.vector
                    def _(vector):
                        vector.wait_ge(ft.sem, ft.v)
                        vector.tensor_relu(out=h2[:, f2c, :],
                                           in_=hp[:, pp, :]).then_inc(fev.sem, 1)
                        fev.inc()
            h2_done = fev.v

            if dbg_h is not None and t == 0:
                @block.gpsimd
                def _(g):
                    g.wait_ge(fev.sem, h2_done)
                    g.dma_start(out=dbg_h[0], in_=h1[:, :, :]).then_inc(S["st"].sem, 16)
                    g.dma_start(out=dbg_h[1], in_=h2[:, :, :]).then_inc(S["st"].sem, 16)
                    g.dma_start(out=dbg_w[0], in_=lnS[:, :, :]).then_inc(S["st"].sem, 16)
                    g.dma_start(out=dbg_w[1].rearrange("p a b -> a p b"),
                                in_=dram["w0c"][:, :, 6 * 512:7 * 512]).then_inc(S["st"].sem, 16)
                    S["st"].inc(64)

                @block.tensor
                def _(tensor):
                    tensor.wait_ge(S["st"].sem, S["st"].v)

            for i in range(4):
                tc = t * 4 + i
                hp3 = i % 2
                for f2c in range(32):
                    wp = (i * 32 + f2c) % 2

                    @block.sync
                    def _(g):
                        wge(g, ft, ft.v - 1 if (t > 0 or i > 0 or f2c >= 2) else 0)
                        g.dma_start(out=w2b[:, wp, :], in_=w2c[f2c]).then_inc(fl2[wp].sem, 16)
                        fl2[wp].inc(16)

                    @block.tensor
                    def _(tensor):
                        tensor.wait_ge(fl2[wp].sem, fl2[wp].v)
                        if i == 0 and f2c == 0:
                            tensor.wait_ge(fev.sem, h2_done)
                        if f2c == 0:
                            wge(tensor, fv, fv.v - 1)
                        ins = None
                        for h in range(2):
                            ins = tensor.matmul(h3p[:, hp3, h * 512:(h + 1) * 512],
                                                h2[:, f2c, i * 128:(i + 1) * 128],
                                                w2b[:, wp, h * 512:(h + 1) * 512],
                                                start=(f2c == 0), stop=(f2c == 31))
                        ins.then_inc(ft.sem, 1)
                        ft.inc()

                @block.vector
                def _(vector):
                    vector.wait_ge(ft.sem, ft.v)
                    vector.tensor_add(out=resid[:, tc, :], in0=resid[:, tc, :],
                                      in1=h3p[:, hp3, :]).then_inc(fv.sem, 1)
                    fv.inc()


def emit_l1_attn(nc, block, S, resid, ident, epsc, onebf, wqkvT_in, dram):
    cld, ct, cev, cv, cs = S["cld"], S["ct"], S["cev"], S["cv"], S["cs"]
    ccc, cst = S["ccc"], S["cst"]
    lnv = S["lnv"]
    arin, arout = dram["arin"], dram["arout"]

    with ExitStack() as ctx:
        e = ctx.enter_context
        lnb = e(nc.sbuf_tensor("lnbC", [128, 2, D], BF16))
        qhfm = e(nc.sbuf_tensor("qhfmC", [128, 8, TPC], BF16))
        hs = e(nc.sbuf_tensor("hsC", [128, 2, 16], F32))
        kvsb = e(nc.sbuf_tensor("kvsbC", [128, 1040], F32))
        scr = _ln_scratch(nc, e, "C")

        with ExitStack() as pctx:
            e2 = pctx.enter_context
            lnT = e2(nc.sbuf_tensor("lnTC", [128, 8, TPC], BF16))
            khf = e2(nc.sbuf_tensor("khfC", [128, NTC, D], BF16))
            wqb = e2(nc.sbuf_tensor("wqbC", [128, 8, 1024], BF16))
            pt32 = e2(nc.sbuf_tensor("pt32C", [128, 2, D], BF16))
            sqt = e2(nc.sbuf_tensor("sqtC", [128, D], F32))
            qhat = e2(nc.sbuf_tensor("qhatC", [128, 2, D], BF16))
            vta = e2(nc.sbuf_tensor("vtaC", [128, 2, 1024], BF16))
            ptT = e2(nc.psum_tensor("ptTC", [128, 1, 1024], BF16))
            pjc = e2(nc.psum_tensor("pjcC", [128, 2, 1024], F32))
            kvp = e2(nc.psum_tensor("kvpC", [64, 1024], F32))
            kvq = e2(nc.psum_tensor("kvqC", [64, 16], F32))

            for tc in range(NTC):
                pp = tc % 2
                emit_ln(block, S, resid, tc, pp, scr, epsc, lnb[:, pp, :],
                        norm_waits=((S["lnt"].sem, S["lnt"].v - 1),))
                emit_transpose_1024(
                    block, S, lnb[:, pp, :], ptT, 0,
                    lnT[:, :, tc * 128:(tc + 1) * 128],
                    ident, tensor_waits=((lnv.sem, lnv.v),), ping=False)
            lnT_done = S["lnev"].v

            @block.gpsimd
            def _(g):
                for ic in range(8):
                    g.dma_start(out=wqb[:, ic, :],
                                in_=wqkvT_in[ic][:, 0:1024]).then_inc(cld.sem, 16)
                    cld.inc(16)
            qw_ready = cld.v

            for tc in range(NTC):
                pp = tc % 2

                @block.tensor
                def _(tensor):
                    if tc == 0:
                        tensor.wait_ge(cld.sem, qw_ready)
                        tensor.wait_ge(S["lnev"].sem, lnT_done)
                    wge(tensor, cev, cev.v - 1)
                    ins = None
                    for ic in range(8):
                        for h in range(2):
                            ins = tensor.matmul(pjc[:, pp, h * 512:(h + 1) * 512],
                                                lnT[:, ic, tc * 128:(tc + 1) * 128],
                                                wqb[:, ic, h * 512:(h + 1) * 512],
                                                start=(ic == 0), stop=(ic == 7))
                    ins.then_inc(ct.sem, 1)
                    ct.inc()

                @block.vector
                def _(vector):
                    vector.wait_ge(ct.sem, ct.v)
                    wge(vector, cs, cs.v)
                    vector.tensor_copy(out=pt32[:, pp, :], in_=pjc[:, pp, :]).then_inc(cev.sem, 1)
                    cev.inc()

                _emit_spl2(nc, block, S, pt32, sqt, hs, qhat[:, pp, :], pp, cs, cv, cev)

                emit_transpose_1024(
                    block, S, qhat[:, pp, :], ptT, 0,
                    qhfm[:, :, tc * 128:(tc + 1) * 128],
                    ident, tensor_waits=((cs.sem, cs.v),), ping=False)

            if C_STAGE < 2:
                return
            @block.gpsimd
            def _(g):
                g.wait_ge(ct.sem, ct.v)
                for ic in range(8):
                    g.dma_start(out=wqb[:, ic, :],
                                in_=wqkvT_in[ic][:, 1024:2048]).then_inc(cld.sem, 16)
                    cld.inc(16)
            kw_ready = cld.v

            for tc in range(NTC):
                pp = tc % 2

                @block.tensor
                def _(tensor):
                    if tc == 0:
                        tensor.wait_ge(cld.sem, kw_ready)
                    wge(tensor, cev, cev.v - 1)
                    ins = None
                    for ic in range(8):
                        for h in range(2):
                            ins = tensor.matmul(pjc[:, pp, h * 512:(h + 1) * 512],
                                                lnT[:, ic, tc * 128:(tc + 1) * 128],
                                                wqb[:, ic, h * 512:(h + 1) * 512],
                                                start=(ic == 0), stop=(ic == 7))
                    ins.then_inc(ct.sem, 1)
                    ct.inc()

                @block.vector
                def _(vector):
                    vector.wait_ge(ct.sem, ct.v)
                    wge(vector, cs, cs.v)
                    vector.tensor_copy(out=pt32[:, pp, :], in_=pjc[:, pp, :]).then_inc(cev.sem, 1)
                    cev.inc()

                _emit_spl2(nc, block, S, pt32, sqt, hs, khf[:, tc, :], pp, cs, cv, cev)

            khf_done = cs.v

            if C_STAGE < 3:
                return
            @block.gpsimd
            def _(g):
                g.wait_ge(ct.sem, ct.v)
                for ic in range(8):
                    g.dma_start(out=wqb[:, ic, :],
                                in_=wqkvT_in[ic][:, 2048:3072]).then_inc(cld.sem, 16)
                    cld.inc(16)
            vw_ready = cld.v

            for tc in range(NTC):
                pp = tc % 2

                @block.tensor
                def _(tensor):
                    if tc == 0:
                        tensor.wait_ge(cld.sem, vw_ready)
                        tensor.wait_ge(cs.sem, khf_done)
                    wge(tensor, cev, cev.v - 1)
                    ins = None
                    for ic in range(8):
                        for h in range(2):
                            ins = tensor.matmul(pjc[:, pp, h * 512:(h + 1) * 512],
                                                lnT[:, ic, tc * 128:(tc + 1) * 128],
                                                wqb[:, ic, h * 512:(h + 1) * 512],
                                                start=(ic == 0), stop=(ic == 7))
                    ins.then_inc(ct.sem, 1)
                    ct.inc()

                @block.vector
                def _(vector):
                    vector.wait_ge(ct.sem, ct.v)
                    if tc >= 2:
                        wge(vector, ct, ct.v)
                    vector.tensor_copy(out=vta[:, pp, :],
                                       in_=pjc[:, pp, :]).then_inc(cev.sem, 1)
                    cev.inc()

                @block.tensor
                def _(tensor):
                    tensor.wait_ge(cev.sem, cev.v)
                    ins = None
                    for h in range(16):
                        tensor.matmul(kvp[0:64, h * 64:(h + 1) * 64],
                                      khf[:, tc, h * 64:(h + 1) * 64],
                                      vta[:, pp, h * 64:(h + 1) * 64],
                                      start=(tc == 0 and h % 8 == 0),
                                      stop=(tc == NTC - 1 and h % 8 == 7))
                        ins = tensor.matmul(kvq[0:64, h:h + 1],
                                            khf[:, tc, h * 64:(h + 1) * 64],
                                            onebf[:, 0:1],
                                            start=(tc == 0 and h == 0),
                                            stop=(tc == NTC - 1 and h == 15))
                    ins.then_inc(ct.sem, 1)
                    ct.inc()
            kv_done = ct.v

            @block.vector
            def _(vector):
                vector.wait_ge(ct.sem, kv_done)
                vector.tensor_scalar_mul(out=kvsb[0:64, 0:1024], in0=kvp[0:64, :],
                                         scalar1=0.125)
                vector.tensor_copy(out=kvsb[0:64, 1024:1040],
                                   in_=kvq[0:64, :]).then_inc(cv.sem, 1)
                cv.inc()

            @block.sync
            def _(sync):
                sync.wait_ge(cv.sem, cv.v)
                sync.dma_start(out=arin, in_=kvsb[0:64, :]).then_inc(cst.sem, 16)
                cst.inc(16)

            if C_STAGE < 4:
                return

            @block.gpsimd
            def _(g):
                g.wait_ge(cst.sem, cst.v)
                g.collective_compute("AllReduce", ALU.add,
                                     replica_groups=[list(range(NCORES))],
                                     ins=[dram["arin_h"].ap().opt()],
                                     outs=[dram["arout_h"].ap().opt()]).then_inc(ccc.sem, 1)
                ccc.inc()

            @block.sync
            def _(sync):
                sync.wait_ge(ccc.sem, ccc.v)
                sync.dma_start(out=kvsb[0:64, :], in_=arout).then_inc(cst.sem, 16)
                sync.dma_start(out=kvsb[64:128, :], in_=arout).then_inc(cst.sem, 16)
                cst.inc(32)

            @block.vector
            def _(vector):
                vector.wait_ge(cst.sem, cst.v)
                vector.tensor_scalar_add(out=kvsb[:, 1024:1040],
                                         in0=kvsb[:, 1024:1040],
                                         scalar1=1e-6).then_inc(cv.sem, 1)
                cv.inc()
        if C_STAGE < 5:
            return
        with ExitStack() as pctx:
            e2 = pctx.enter_context
            normc = e2(nc.sbuf_tensor("normcC", [128, 2, 2, 8], F32))
            otmp = e2(nc.sbuf_tensor("otmpC", [128, 2, D], BF16))
            kvsbb = e2(nc.sbuf_tensor("kvsbbC", [128, 1040], BF16))
            op = e2(nc.psum_tensor("opC", [128, 2, 2, 512], F32))
            oq = e2(nc.psum_tensor("oqC", [128, 2, 2, 512], F32))

            @block.vector
            def _(vector):
                vector.wait_ge(cv.sem, cv.v)
                vector.tensor_copy(out=kvsbb[:, :], in_=kvsb[:, :]).then_inc(cv.sem, 1)
                cv.inc()
            kv_ready = cv.v

            for tc in range(NTC):
                pp = tc % 2

                @block.tensor
                def _(tensor):
                    if tc == 0:
                        tensor.wait_ge(cv.sem, kv_ready)
                    wge(tensor, cv, kv_ready + 3 * (tc - 1))
                    ins = None
                    for h in range(16):
                        po = (h % 2) * 64
                        par = h % 2
                        hh = h // 2
                        tensor.matmul(op[:, pp, par, hh * 64:(hh + 1) * 64],
                                      qhfm[po:po + 64, hh, tc * 128:(tc + 1) * 128],
                                      kvsbb[po:po + 64, h * 64:(h + 1) * 64],
                                      start=(hh == 0), stop=(hh == 7))
                        ins = tensor.matmul(oq[:, pp, par, hh:hh + 1],
                                            qhfm[po:po + 64, hh, tc * 128:(tc + 1) * 128],
                                            kvsbb[po:po + 64, 1024 + h:1025 + h],
                                            start=(hh == 0), stop=(hh == 7))
                    ins.then_inc(ct.sem, 1)
                    ct.inc()

                @block.vector
                def _(vector):
                    vector.wait_ge(ct.sem, ct.v)
                    vector.reciprocal(out=normc[:, pp, :, :],
                                      in_=oq[:, pp, :, 0:8]).then_inc(cv.sem, 1)
                    cv.inc()
                    vector.wait_ge(cv.sem, cv.v)
                    vector.tensor_tensor(
                        out=otmp[:, pp, :].rearrange("p (a q b) -> p a q b", a=8, q=2),
                        in0=op[:, pp, :, :].rearrange("p q (a b) -> p a q b", a=8),
                        in1=normc[:, pp, :, :].rearrange("p q a -> p a q ()").broadcast_to([128, 8, 2, 64]),
                        op=ALU.mult).then_inc(cv.sem, 1)
                    cv.inc()
                    vector.wait_ge(cv.sem, cv.v)
                    vector.tensor_add(out=resid[:, tc, :], in0=resid[:, tc, :],
                                      in1=otmp[:, pp, :]).then_inc(cv.sem, 1)
                    cv.inc()


def _emit_spl2(nc, block, S, src, sqt, hs, dst_ap, pp, cs, cv, cev):
    @block.vector
    def _(vector):
        vector.wait_ge(cev.sem, cev.v)
        vector.tensor_mul(out=sqt[:, :], in0=src[:, pp, :],
                          in1=src[:, pp, :]).then_inc(cv.sem, 1)
        cv.inc()
        vector.wait_ge(cv.sem, cv.v)
        vector.reduce_sum(out=hs[:, pp, :].rearrange("p a -> p a ()"),
                          in_=sqt[:, :].rearrange("p (a b) -> p a b", a=16),
                          axis=AX.X).then_inc(cv.sem, 1)
        cv.inc()
        vector.wait_ge(cv.sem, cv.v)
        vector.tensor_scalar_max(out=hs[:, pp, :], in0=hs[:, pp, :],
                                 scalar1=1e-24).then_inc(cv.sem, 1)
        cv.inc()
    hs_ready = cv.v

    @block.scalar
    def _(scalar):
        scalar.wait_ge(cv.sem, hs_ready)
        scalar.activation(out=hs[:, pp, :], in_=hs[:, pp, :], func=AF.Ln)
        scalar.activation(out=hs[:, pp, :], in_=hs[:, pp, :], func=AF.Exp,
                          scale=-0.5).then_inc(cs.sem, 1)
        cs.inc()

    @block.vector
    def _(vector):
        vector.wait_ge(cs.sem, cs.v)
        vector.tensor_tensor(
            out=sqt[:, :].rearrange("p (a b) -> p a b", a=16),
            in0=src[:, pp, :].rearrange("p (a b) -> p a b", a=16),
            in1=hs[:, pp, :].rearrange("p a -> p a ()").broadcast_to([128, 16, 64]),
            op=ALU.mult).then_inc(cv.sem, 1)
        cv.inc()
    mult_ready = cv.v

    @block.scalar
    def _(scalar):
        scalar.wait_ge(cv.sem, mult_ready)
        scalar.activation(out=sqt[:, :], in_=sqt[:, :],
                          func=AF.Exp).then_inc(cs.sem, 1)
        cs.inc()

    @block.vector
    def _(vector):
        vector.wait_ge(cs.sem, cs.v)
        vector.tensor_scalar_add(out=sqt[:, :], in0=sqt[:, :],
                                 scalar1=1.0).then_inc(cv.sem, 1)
        cv.inc()
    add_ready = cv.v

    @block.scalar
    def _(scalar):
        scalar.wait_ge(cv.sem, add_ready)
        scalar.activation(out=dst_ap, in_=sqt[:, :],
                          func=AF.Ln).then_inc(cs.sem, 1)
        cs.inc()


def emit_l1_ffn(nc, block, S, resid, ident, epsc, nmagic, dram, dbg_w=None, dbg_h=None):
    dg = [S["dg0"], S["dg1"]]
    dw = [S["dw0"], S["dw1"]]
    dt, dev, dv, ds = S["dt"], S["dev"], S["dv"], S["ds"]
    lnv, lnt, lnev = S["lnv"], S["lnt"], S["lnev"]
    wgc, wuc, wdc = dram["wgc"], dram["wuc"], dram["wdc"]

    with ExitStack() as ctx:
        e = ctx.enter_context
        xqfm = e(nc.sbuf_tensor("xqfmD", [128, NTC, 8, 128], BF16))
        colqx = e(nc.sbuf_tensor("colqxD", [128, 2, NTC], F32))
        xq_tgt = [None] * NTC

        with ExitStack() as p1:
            e1 = p1.enter_context
            lnf = e1(nc.sbuf_tensor("lnfD", [128, 2, D], F32))
            xqb = e1(nc.sbuf_tensor("xqbD", [128, 2, D], BF16))
            tmpc = e1(nc.sbuf_tensor("tmpcD", [128, 1], F32))
            scr = _ln_scratch(nc, e1, "D")
            ptT = e1(nc.psum_tensor("ptTD1", [128, 2, 1024], BF16))

            for tc in range(NTC):
                pp = tc % 2
                emit_ln(block, S, resid, tc, pp, scr, epsc, lnf[:, pp, :],
                        norm_waits=((S["dv"].sem, S["dv"].v),))

                @block.vector
                def _(vector):
                    vector.wait_ge(lnv.sem, lnv.v)
                    vector.reduce_max(out=tmpc[:], in_=lnf[:, pp, :], axis=AX.X,
                                      apply_absolute_value=True).then_inc(dv.sem, 1)
                    dv.inc()
                    vector.wait_ge(dv.sem, dv.v)
                    vector.tensor_scalar_max(out=tmpc[:], in0=tmpc[:],
                                             scalar1=1e-5).then_inc(dv.sem, 1)
                    dv.inc()
                    vector.wait_ge(dv.sem, dv.v)
                    vector.tensor_scalar_mul(out=colqx[:, 1, tc:tc + 1], in0=tmpc[:],
                                             scalar1=1.0 / 127.0)
                    vector.reciprocal(out=tmpc[:], in_=tmpc[:]).then_inc(dv.sem, 1)
                    dv.inc()
                    vector.wait_ge(dv.sem, dv.v)
                    vector.tensor_scalar_mul(out=colqx[:, 0, tc:tc + 1], in0=tmpc[:],
                                             scalar1=127.0).then_inc(dv.sem, 1)
                    dv.inc()
                    vector.wait_ge(dv.sem, dv.v)
                    vector.tensor_scalar(out=lnf[:, pp, :], in0=lnf[:, pp, :],
                                         scalar1=colqx[:, 0, tc:tc + 1], scalar2=MAGIC,
                                         op0=ALU.mult, op1=ALU.add).then_inc(dv.sem, 1)
                    dv.inc()
                    vector.wait_ge(dv.sem, dv.v)
                    vector.tensor_scalar_add(out=xqb[:, pp, :], in0=lnf[:, pp, :],
                                             scalar1=nmagic[:, 0:1]).then_inc(dv.sem, 1)
                    dv.inc()

                emit_transpose_1024(
                    block, S, xqb[:, pp, :], ptT, pp,
                    xqfm[:, tc, :, :],
                    ident, tensor_waits=((dv.sem, dv.v),))
                xq_tgt[tc] = S["lnev"].v

        pass1_dv = S["dv"].v
        pass1_ev = S["lnev"].v

        with ExitStack() as p2:
            e2 = p2.enter_context
            wgb = e2(nc.sbuf_tensor("wgbD", [128, 2, 8, 512], BF16))
            wub = e2(nc.sbuf_tensor("wubD", [128, 2, 8, 512], BF16))
            wdb = e2(nc.sbuf_tensor("wdbD", [128, 2, 1024], BF16))
            gr = e2(nc.sbuf_tensor("grD", [128, 2, 4096], BF16))
            us = e2(nc.sbuf_tensor("usD", [128, 2, 4096], BF16))
            hbuf = e2(nc.sbuf_tensor("hbufD", [128, 4096], F32))
            hqb = e2(nc.sbuf_tensor("hqbD", [128, 4096], BF16))
            hqfm = e2(nc.sbuf_tensor("hqfmD", [128, 2, 32, 128], BF16))
            colqh = e2(nc.sbuf_tensor("colqhD", [128, 4, 8], F32))
            gup = e2(nc.psum_tensor("gupD", [128, 2, 512], F32))
            dnp = e2(nc.psum_tensor("dnpD", [128, 2, 1024], F32))
            ptT2 = e2(nc.psum_tensor("ptTD2", [128, 1024], BF16))

            hq_tgt = [None] * NTC
            hqfm_tgt = [None] * NTC
            devac_tgt = [None] * NTC
            silu_tgt = [None] * NTC
            hqt_lnt_tgt = [None] * NTC
            gu_done = [None] * (NTC // 2)
            uev_prev = [0, 0]
            wg_carry = [0, 0]
            wd_carry = [0, 0]

            def emit_gu(pr):
                wg_tgt = [0] * 8
                for og in range(8):
                    wp = og % 2

                    @block.sync
                    def _(g, og=og, wp=wp):
                        if pr == 0 and og == 0:
                            g.wait_ge(S["dv"].sem, pass1_dv)
                            g.wait_ge(S["lnev"].sem, pass1_ev)
                        wge(g, dt, wg_tgt[og - 2] if og >= 2 else wg_carry[wp])
                        g.dma_start(out=wgb[:, wp, :, :],
                                    in_=wgc[:, :, og * 512:(og + 1) * 512]
                                    .rearrange("a p b -> p a b")).then_inc(dg[wp].sem, 16)
                        g.dma_start(out=wub[:, wp, :, :],
                                    in_=wuc[:, :, og * 512:(og + 1) * 512]
                                    .rearrange("a p b -> p a b")).then_inc(dg[wp].sem, 16)
                        dg[wp].inc(32)

                    t_g = [0, 0]
                    v_g = [0, 0]
                    t_u = [0, 0]
                    for ti in range(2):
                        tc = 2 * pr + ti

                        @block.tensor
                        def _(tensor, ti=ti, tc=tc, og=og, wp=wp):
                            if ti == 0:
                                tensor.wait_ge(dg[wp].sem, dg[wp].v)
                            if og == 0 and ti == 0:
                                tensor.wait_ge(S["lnev"].sem,
                                               pass1_ev if pr == 0 else
                                               max(xq_tgt[2 * pr], xq_tgt[2 * pr + 1]))
                            wge(tensor, dev, uev_prev[ti])
                            ins = None
                            for ic in range(8):
                                ins = tensor.matmul(gup[:, ti, :],
                                                    xqfm[:, tc, ic, :],
                                                    wgb[:, wp, ic, :],
                                                    start=(ic == 0), stop=(ic == 7))
                            ins.then_inc(dt.sem, 1)
                            dt.inc()
                        t_g[ti] = dt.v

                    for ti in range(2):
                        @block.vector
                        def _(vector, ti=ti, og=og):
                            vector.wait_ge(dt.sem, t_g[ti])
                            if og == 0 and pr > 0:
                                wge(vector, ds, silu_tgt[2 * (pr - 1) + ti])
                            vector.tensor_copy(out=gr[:, ti, og * 512:(og + 1) * 512],
                                               in_=gup[:, ti, :]).then_inc(dev.sem, 1)
                            dev.inc()
                        v_g[ti] = dev.v

                    for ti in range(2):
                        tc = 2 * pr + ti

                        @block.tensor
                        def _(tensor, ti=ti, tc=tc, og=og, wp=wp):
                            wge(tensor, dev, v_g[ti])
                            ins = None
                            for ic in range(8):
                                ins = tensor.matmul(gup[:, ti, :],
                                                    xqfm[:, tc, ic, :],
                                                    wub[:, wp, ic, :],
                                                    start=(ic == 0), stop=(ic == 7))
                            ins.then_inc(dt.sem, 1)
                            dt.inc()
                        t_u[ti] = dt.v

                    for ti in range(2):
                        tc = 2 * pr + ti

                        @block.vector
                        def _(vector, ti=ti, tc=tc, og=og):
                            vector.wait_ge(dt.sem, t_u[ti])
                            vector.tensor_scalar_mul(
                                out=us[:, ti, og * 512:(og + 1) * 512],
                                in0=gup[:, ti, :],
                                scalar1=colqx[:, 1, tc:tc + 1]).then_inc(dev.sem, 1)
                            dev.inc()
                        uev_prev[ti] = dev.v
                    wg_tgt[og] = t_u[1]
                wg_carry[0] = wg_tgt[6]
                wg_carry[1] = wg_tgt[7]
                gu_done[pr] = dev.v

            def emit_hq(tc):
                ti = tc % 2
                cq = tc % 4

                @block.scalar
                def _(scalar):
                    scalar.wait_ge(dev.sem, gu_done[tc // 2])
                    scalar.activation(out=gr[:, ti, :], in_=gr[:, ti, :], func=AF.Silu,
                                      scale=colqx[:, 1, tc:tc + 1]).then_inc(ds.sem, 1)
                    ds.inc()
                silu_tgt[tc] = ds.v

                @block.vector
                def _(vector):
                    vector.wait_ge(ds.sem, silu_tgt[tc])
                    vector.tensor_mul(out=hbuf[:, :], in0=gr[:, ti, :],
                                      in1=us[:, ti, :]).then_inc(dv.sem, 1)
                    dv.inc()
                    vector.wait_ge(dv.sem, dv.v)
                    vector.reduce_max(out=colqh[:, cq, 0:1], in_=hbuf[:, :], axis=AX.X,
                                      apply_absolute_value=True).then_inc(dv.sem, 1)
                    dv.inc()
                    vector.wait_ge(dv.sem, dv.v)
                    vector.tensor_scalar_max(out=colqh[:, cq, 0:1], in0=colqh[:, cq, 0:1],
                                             scalar1=1e-5).then_inc(dv.sem, 1)
                    dv.inc()
                    vector.wait_ge(dv.sem, dv.v)
                    vector.tensor_scalar_mul(out=colqh[:, cq, 2:3], in0=colqh[:, cq, 0:1],
                                             scalar1=1.0 / 127.0)
                    vector.reciprocal(out=colqh[:, cq, 1:2],
                                      in_=colqh[:, cq, 0:1]).then_inc(dv.sem, 1)
                    dv.inc()
                    vector.wait_ge(dv.sem, dv.v)
                    vector.tensor_scalar_mul(out=colqh[:, cq, 3:4], in0=colqh[:, cq, 1:2],
                                             scalar1=127.0).then_inc(dv.sem, 1)
                    dv.inc()
                    vector.wait_ge(dv.sem, dv.v)
                    vector.tensor_scalar(out=hbuf[:, :], in0=hbuf[:, :],
                                         scalar1=colqh[:, cq, 3:4], scalar2=MAGIC,
                                         op0=ALU.mult, op1=ALU.add).then_inc(dv.sem, 1)
                    dv.inc()
                    vector.wait_ge(dv.sem, dv.v)
                    if tc >= 1:
                        wge(vector, lnt, hqt_lnt_tgt[tc - 1])
                    vector.tensor_scalar_add(out=hqb[:, :], in0=hbuf[:, :],
                                             scalar1=nmagic[:, 0:1]).then_inc(dv.sem, 1)
                    dv.inc()
                hq_tgt[tc] = dv.v

            def emit_hqt(tc):
                tp = tc % 2
                for qd in range(4):
                    @block.tensor
                    def _(tensor, qd=qd):
                        if qd == 0:
                            tensor.wait_ge(dv.sem, hq_tgt[tc])
                        wge(tensor, lnev, lnev.v)
                        ins = None
                        for ic in range(8):
                            ins = tensor.transpose(
                                ptT2[:, ic * 128:(ic + 1) * 128],
                                hqb[:, (qd * 8 + ic) * 128:(qd * 8 + ic + 1) * 128],
                                ident[:])
                        ins.then_inc(lnt.sem, 1)
                        lnt.inc()

                    @block.vector
                    def _(vector, qd=qd):
                        vector.wait_ge(lnt.sem, lnt.v)
                        vector.tensor_copy(
                            out=hqfm[:, tp, qd * 8:(qd + 1) * 8, :],
                            in_=ptT2[:, :].rearrange("p (a b) -> p a b", a=8)).then_inc(lnev.sem, 1)
                        lnev.inc()
                hqfm_tgt[tc] = lnev.v
                hqt_lnt_tgt[tc] = lnt.v

            def emit_down(pr):
                dn_tgt = [0] * 32
                for f2c in range(32):
                    wp = f2c % 2

                    @block.gpsimd
                    def _(g, f2c=f2c, wp=wp):
                        wge(g, dt, dn_tgt[f2c - 2] if f2c >= 2 else wd_carry[wp])
                        g.dma_start(out=wdb[:, wp, :], in_=wdc[f2c]).then_inc(dw[wp].sem, 16)
                        dw[wp].inc(16)

                    for ti in range(2):
                        tc = 2 * pr + ti

                        @block.tensor
                        def _(tensor, ti=ti, tc=tc, f2c=f2c, wp=wp):
                            if ti == 0:
                                tensor.wait_ge(dw[wp].sem, dw[wp].v)
                            if f2c == 0:
                                tensor.wait_ge(lnev.sem, hqfm_tgt[tc])
                                if tc >= 2:
                                    tensor.wait_ge(dv.sem, devac_tgt[tc - 2])
                            ins = None
                            for h in range(2):
                                ins = tensor.matmul(dnp[:, ti, h * 512:(h + 1) * 512],
                                                    hqfm[:, tc % 2, f2c, :],
                                                    wdb[:, wp, h * 512:(h + 1) * 512],
                                                    start=(f2c == 0), stop=(f2c == 31))
                            ins.then_inc(dt.sem, 1)
                            dt.inc()
                        if ti == 1:
                            dn_tgt[f2c] = dt.v
                wd_carry[0] = dn_tgt[30]
                wd_carry[1] = dn_tgt[31]

                for ti in range(2):
                    tc = 2 * pr + ti

                    @block.vector
                    def _(vector, ti=ti, tc=tc):
                        vector.wait_ge(dt.sem, dt.v)
                        vector.scalar_tensor_tensor(out=resid[:, tc, :], in0=dnp[:, ti, :],
                                                    scalar=colqh[:, tc % 4, 2:3],
                                                    in1=resid[:, tc, :],
                                                    op0=ALU.mult, op1=ALU.add).then_inc(dv.sem, 1)
                        dv.inc()
                    devac_tgt[tc] = dv.v

            for pr in range(NTC // 2):
                emit_gu(pr)
                emit_hq(2 * pr)
                if pr >= 1:
                    emit_down(pr - 1)
                emit_hqt(2 * pr)
                emit_hq(2 * pr + 1)
                emit_hqt(2 * pr + 1)
            emit_down(NTC // 2 - 1)


_NC_CACHE = {}


DEBUG = False


def _get_nc(n_phases=4):
    key = (n_phases, DEBUG)
    if key not in _NC_CACHE:
        _NC_CACHE[key] = build_nc(n_phases, dbg=DEBUG)
    return _NC_CACHE[key]


def kernel(L0_attn_x, L0_attn_w_q, L0_attn_w_k, L0_attn_w_v,
           L0_ffn_k0, L0_ffn_k1, L0_ffn_k2, L0_ffn_s0, L0_ffn_s1, L0_ffn_s2,
           L1_attn_W_qkv, L1_ffn_w_gate, L1_ffn_w_up, L1_ffn_w_down,
           _phases=4):
    x = np.ascontiguousarray(np.asarray(L0_attn_x, dtype=np.float32).reshape(T, D))
    wq = np.ascontiguousarray(np.asarray(L0_attn_w_q, np.float32).reshape(8, 128, D))
    wk = np.ascontiguousarray(np.asarray(L0_attn_w_k, np.float32).reshape(8, 128, D))
    wv = np.ascontiguousarray(np.asarray(L0_attn_w_v, np.float32).reshape(8, 128, D))
    k0u = np.asarray(L0_ffn_k0).view(np.uint8)
    k0a = np.ascontiguousarray(k0u[:1024].reshape(8, 128, 4096))
    k0b = np.ascontiguousarray(k0u[1024:].reshape(8, 128, 4096))
    k1u = np.ascontiguousarray(np.asarray(L0_ffn_k1).view(np.uint8).reshape(32, 128, 4096))
    k2u = np.ascontiguousarray(np.asarray(L0_ffn_k2).view(np.uint8).reshape(32, 128, 1024))
    s0 = np.ascontiguousarray(np.broadcast_to(
        np.asarray(L0_ffn_s0, np.float32).reshape(1, 1), (128, 1)))
    s1 = np.ascontiguousarray(np.broadcast_to(
        np.asarray(L0_ffn_s1, np.float32).reshape(1, 1), (128, 1)))
    s2 = np.ascontiguousarray(np.broadcast_to(
        np.asarray(L0_ffn_s2, np.float32).reshape(1, 1), (128, 1)))
    wqkvT = np.ascontiguousarray(
        np.asarray(L1_attn_W_qkv, np.float32).T.reshape(8, 128, 3072))
    wgT = np.ascontiguousarray(
        np.asarray(L1_ffn_w_gate, np.float32).T.reshape(8, 128, 4096))
    wuT = np.ascontiguousarray(
        np.asarray(L1_ffn_w_up, np.float32).T.reshape(8, 128, 4096))
    wdT = np.ascontiguousarray(
        np.asarray(L1_ffn_w_down, np.float32).T.reshape(32, 128, 1024))

    in_maps = []
    for c in range(NCORES):
        sA, sB = c, NSTRIPE - 1 - c
        xs = np.ascontiguousarray(
            np.concatenate([x[sA * 1024:(sA + 1) * 1024],
                            x[sB * 1024:(sB + 1) * 1024]], axis=0)).reshape(NTC, 128, D)
        btab_row = np.empty((2, 8, NSTEP), np.float32)
        for si, stripe in enumerate((sA, sB)):
            for j in range(NSTEP):
                btab_row[si, :, j] = SHIFT if (j == 0 or (j - 1 < stripe)) else CLOSED
        btab = np.ascontiguousarray(
            np.broadcast_to(btab_row.reshape(1, -1), (128, 2 * 8 * NSTEP)))
        in_maps.append({
            "x_in": xs, "btab": btab,
            "w_q": wq, "w_k": wk, "w_v": wv,
            "k0a": k0a, "k0b": k0b, "k1": k1u, "k2": k2u,
            "s0": s0, "s1": s1, "s2": s2,
            "wqkvT": wqkvT, "wgT": wgT, "wuT": wuT, "wdT": wdT,
        })

    nc = _get_nc(_phases)
    res = run_bass_kernel_spmd(nc, in_maps, core_ids=list(range(NCORES)))

    y = np.empty((T, D), np.float32)
    for c in range(NCORES):
        out = np.asarray(res.results[c]["y_out"]).reshape(TPC, D)
        sA, sB = c, NSTRIPE - 1 - c
        y[sA * 1024:(sA + 1) * 1024] = out[:1024]
        y[sB * 1024:(sB + 1) * 1024] = out[1024:]
    return y.reshape(1, T, D)


# version 06 (final)
# speedup vs baseline: 1.2991x; 1.1365x over previous
import numpy as np
import ml_dtypes
from contextlib import ExitStack

import concourse.bass as bass
import concourse.mybir as mybir
from concourse.bass_utils import run_bass_kernel_spmd

F32 = mybir.dt.float32
BF16 = mybir.dt.bfloat16
U8 = mybir.dt.uint8
AF = mybir.ActivationFunctionType
ALU = mybir.AluOpType
AX = mybir.AxisListType

NCORES = 8
D = 1024
T = 16384
TPC = 2048
NTC = 16
NSTRIPE = 16
NSTEP = 16
MAGIC = 12582912.0
SHIFT = -12.0
CLOSED = -30012.0
MASKNEG = -30000.0


class Ctr:
    def __init__(self, sem):
        self.sem = sem
        self.v = 0

    def inc(self, n=1):
        self.v += n
        return self.v


def wge(eng, ctr, v):
    if v > 0:
        eng.wait_ge(ctr.sem, v)


def emit_barrier(nc, block, S):
    bar = S["bar"]
    tgt = bar.v + 5
    for engname in ("tensor", "vector", "scalar", "gpsimd", "sync"):
        def _sec(eng, tgt=tgt):
            eng.drain().then_inc(bar.sem, 1)
            eng.wait_ge(bar.sem, tgt)
        getattr(block, engname)(_sec)
    bar.v = tgt


def _posblock_src(b):
    rank = b if b <= 7 else 15 - b
    sub = 0 if b <= 7 else 1
    return rank, sub


C_STAGE = 9
RACECHECK = False


def build_nc(n_phases=4, dbg=False):
    nc = bass.Bass("TRN2", target_bir_lowering=False,
                   detect_race_conditions=RACECHECK)

    x_in = nc.dram_tensor("x_in", [NTC, 128, D], F32, kind="ExternalInput").ap()
    btab_in = nc.dram_tensor("btab", [128, 2 * 8 * NSTEP], F32, kind="ExternalInput").ap()
    wq_in = nc.dram_tensor("w_q", [8, 128, D], F32, kind="ExternalInput").ap()
    wk_in = nc.dram_tensor("w_k", [8, 128, D], F32, kind="ExternalInput").ap()
    wv_in = nc.dram_tensor("w_v", [8, 128, D], F32, kind="ExternalInput").ap()
    k0a_in = nc.dram_tensor("k0a", [1, 128, 4096], U8, kind="ExternalInput").ap()
    k0b_in = nc.dram_tensor("k0b", [1, 128, 4096], U8, kind="ExternalInput").ap()
    k1_in = nc.dram_tensor("k1", [32, 128, 512], U8, kind="ExternalInput").ap()
    k2_in = nc.dram_tensor("k2", [4, 128, 1024], U8, kind="ExternalInput").ap()
    s0_in = nc.dram_tensor("s0", [128, 1], F32, kind="ExternalInput").ap()
    s1_in = nc.dram_tensor("s1", [128, 1], F32, kind="ExternalInput").ap()
    s2_in = nc.dram_tensor("s2", [128, 1], F32, kind="ExternalInput").ap()
    wqkvT_in = nc.dram_tensor("wqkvT", [8, 128, 3072], F32, kind="ExternalInput").ap()
    wgT_in = nc.dram_tensor("wgT", [1, 128, 4096], F32, kind="ExternalInput").ap()
    wuT_in = nc.dram_tensor("wuT", [1, 128, 4096], F32, kind="ExternalInput").ap()
    wdT_in = nc.dram_tensor("wdT", [4, 128, 1024], F32, kind="ExternalInput").ap()
    y_out = nc.dram_tensor("y_out", [NTC, 128, D], F32, kind="ExternalOutput").ap()
    dbg_h = (nc.dram_tensor("dbg_h", [2, 128, 32, 512], F32, kind="ExternalOutput").ap()
             if dbg else None)
    dbg_w = (nc.dram_tensor("dbg_w", [2, 128, 8, 512], F32, kind="ExternalOutput").ap()
             if dbg else None)

    agin_k_h = nc.dram_tensor("agin_k", [8, 128, TPC], BF16)
    agin_v_h = nc.dram_tensor("agin_v", [NTC, 128, D], BF16)
    agout_k_h = nc.dram_tensor("agout_k", [NCORES * 8, 128, TPC], BF16)
    agout_v_h = nc.dram_tensor("agout_v", [NCORES * NTC, 128, D], BF16)
    arin_h = nc.dram_tensor("arin_kv", [64, 1040], F32)
    arout_h = nc.dram_tensor("arout_kv", [64, 1040], F32)
    w0c_h = nc.dram_tensor("w0c", [8, 128, 4096], BF16)
    w1c_h = nc.dram_tensor("w1c", [16, 32, 128, 256], BF16)
    w2c_h = nc.dram_tensor("w2c", [32, 128, 1024], BF16)
    wgc_h = nc.dram_tensor("wgc", [8, 128, 4096], BF16)
    wuc_h = nc.dram_tensor("wuc", [8, 128, 4096], BF16)
    wdc_h = nc.dram_tensor("wdc", [32, 128, 1024], BF16)
    w0s_h = nc.dram_tensor("w0s", [1, 128, 4096], BF16)
    w1s_h = nc.dram_tensor("w1s", [2, 32, 128, 256], BF16)
    w2s_h = nc.dram_tensor("w2s", [4, 128, 1024], BF16)
    wgs_h = nc.dram_tensor("wgs", [1, 128, 4096], BF16)
    wus_h = nc.dram_tensor("wus", [1, 128, 4096], BF16)
    wds_h = nc.dram_tensor("wds", [4, 128, 1024], BF16)
    arwin_h = nc.dram_tensor("arwin", [1, 128], F32)
    arwout_h = nc.dram_tensor("arwout", [1, 128], F32)
    dram = dict(
        agin_k=agin_k_h.ap(), agin_v=agin_v_h.ap(),
        agout_k=agout_k_h.ap(), agout_v=agout_v_h.ap(),
        arin=arin_h.ap(), arout=arout_h.ap(),
        w0c=w0c_h.ap(), w1c=w1c_h.ap(), w2c=w2c_h.ap(),
        wgc=wgc_h.ap(), wuc=wuc_h.ap(), wdc=wdc_h.ap(),
        w0s=w0s_h.ap(), w1s=w1s_h.ap(), w2s=w2s_h.ap(),
        wgs=wgs_h.ap(), wus=wus_h.ap(), wds=wds_h.ap(),
        arwin=arwin_h.ap(), arwout=arwout_h.ap(),
        agin_k_h=agin_k_h, agin_v_h=agin_v_h, agout_k_h=agout_k_h,
        agout_v_h=agout_v_h, arin_h=arin_h, arout_h=arout_h,
        w0s_h=w0s_h, w1s_h=w1s_h, w2s_h=w2s_h,
        wgs_h=wgs_h, wus_h=wus_h, wds_h=wds_h,
        arwin_h=arwin_h, arwout_h=arwout_h,
        w0c_h=w0c_h, w1c_h=w1c_h, w2c_h=w2c_h,
        wgc_h=wgc_h, wuc_h=wuc_h, wdc_h=wdc_h,
    )

    ident_d = nc.inline_tensor(np.eye(128, dtype=ml_dtypes.bfloat16), "identc")
    tri_np = np.where(np.arange(128)[None, :] <= np.arange(128)[:, None],
                      np.float32(0.0), np.float32(MASKNEG)).astype(np.float32)
    tri_d = nc.inline_tensor(tri_np, "tric")
    onebf_d = nc.inline_tensor(np.ones((128, 1), dtype=ml_dtypes.bfloat16), "onebfc")
    onesf_d = nc.inline_tensor(np.ones((128, 1), dtype=np.float32), "onesfc")
    onesrow_d = nc.inline_tensor(np.ones((1, 128), dtype=np.float32), "onesrowc")

    with ExitStack() as top:
        e = top.enter_context
        resid = e(nc.sbuf_tensor("resid", [128, NTC, D], F32))
        btab = e(nc.sbuf_tensor("btabs", [128, 2 * 8 * NSTEP], F32))
        ident = e(nc.sbuf_tensor("ident_s", [128, 128], BF16))
        tri = e(nc.sbuf_tensor("tri_s", [128, 128], F32))
        onebf = e(nc.sbuf_tensor("onebf_s", [128, 1], BF16))
        onesf = e(nc.sbuf_tensor("onesf_s", [128, 1], F32))
        onesrow = e(nc.sbuf_tensor("onesrow_s", [1, 128], F32))
        epsc = e(nc.sbuf_tensor("epsc", [128, 1], F32))
        nmagic = e(nc.sbuf_tensor("nmagic", [128, 1], F32))
        s0c = e(nc.sbuf_tensor("s0c", [128, 1], F32))
        s1c = e(nc.sbuf_tensor("s1c", [128, 1], F32))
        s2c = e(nc.sbuf_tensor("s2c", [128, 1], F32))

        S = {}
        for name in ("ld", "pv", "psc", "pd", "pd1", "pd2", "pd3", "pd4", "pst", "ptn",
                     "lnv", "lns", "lnt", "lnev",
                     "pjw", "pjt", "pjev", "pjst", "cc",
                     "kld0", "kld1", "vld0", "vld1",
                     "smm", "sexp", "stmm", "sptev", "spv", "svadd",
                     "vmask", "fin", "fv",
                     "fl00", "fl01", "fl10", "fl11", "fl20", "fl21", "ft", "fev", "fs",
                     "cld", "ct", "cev", "cv", "cs", "ccc", "cst",
                     "dg0", "dg1", "dw0", "dw1", "dt", "dev", "dv", "ds", "ppd",
                     "agk", "st", "bar"):
            S[name] = Ctr(e(nc.semaphore("sem_" + name)))

        block = e(nc.Block())
        ld = S["ld"]

        @block.sync
        def _(sync):
            for tc in range(NTC):
                sync.dma_start(out=resid[:, tc, :], in_=x_in[tc]).then_inc(ld.sem, 16)
                ld.inc(16)
            for dst, src in ((btab, btab_in), (ident, ident_d.ap()),
                             (tri, tri_d.ap()), (onebf, onebf_d.ap()),
                             (onesf, onesf_d.ap()), (onesrow, onesrow_d.ap()),
                             (s0c, s0_in), (s1c, s1_in), (s2c, s2_in)):
                sync.dma_start(out=dst[:], in_=src).then_inc(ld.sem, 16)
                ld.inc(16)
        ld_all = ld.v

        @block.vector
        def _(vector):
            vector.memset(epsc[:], 1e-5)
            vector.memset(nmagic[:], -MAGIC)

        prep_alloc, prep_steps = emit_weight_prep(nc, block, S, ld_all,
                                                  k0a_in, k0b_in, k1_in, k2_in,
                                                  s0c, s1c, s2c,
                                                  wgT_in, wuT_in, wdT_in, dram,
                                                  onesf, onesrow)

        def drain_prep(k):
            while prep_steps and k > 0:
                prep_steps.pop(0)()
                k -= 1
        drain_prep.alloc = prep_alloc

        emit_l0_attn(nc, block, S, ld_all, resid, btab, ident, tri, epsc,
                     wq_in, wk_in, wv_in, dram, drain_prep)

        if n_phases >= 2:
            emit_barrier(nc, block, S)
            emit_l0_ffn(nc, block, S, resid, ident, epsc, dram, dbg_h, dbg_w)
        if n_phases >= 3:
            emit_barrier(nc, block, S)
            emit_l1_attn(nc, block, S, resid, ident, epsc, onebf,
                         wqkvT_in, dram)
        if n_phases >= 4:
            emit_barrier(nc, block, S)
            emit_l1_ffn(nc, block, S, resid, ident, epsc, nmagic, dram, dbg_w, dbg_h)

        emit_barrier(nc, block, S)
        st = S["st"]

        @block.sync
        def _(sync):
            for tc in range(NTC):
                sync.dma_start(out=y_out[tc], in_=resid[:, tc, :]).then_inc(st.sem, 16)
                st.inc(16)
            sync.wait_ge(st.sem, st.v)

    return nc


def emit_ln(block, S, resid, tc, pp, scr, epsc, out_ap, first_wait=None,
            norm_waits=()):
    lnv, lns = S["lnv"], S["lns"]
    bnstat, mv, lnvv, rstd = scr

    @block.vector
    def _(vector):
        if first_wait is not None:
            sem, val = first_wait
            if val > 0:
                vector.wait_ge(sem, val)
        wge(vector, lns, lns.v)
        vector.bn_stats(out=bnstat[:, pp, 0:6], in_=resid[:, tc, 0:512])
        vector.bn_stats(out=bnstat[:, pp, 6:12], in_=resid[:, tc, 512:1024]).then_inc(lnv.sem, 1)
        lnv.inc()
        vector.wait_ge(lnv.sem, lnv.v)
        vector.bn_aggr(out=mv[:, pp, :], in_=bnstat[:, pp, :]).then_inc(lnv.sem, 1)
        lnv.inc()

    @block.scalar
    def _(scalar):
        scalar.wait_ge(lnv.sem, lnv.v)
        scalar.activation(out=lnvv[:, pp, :], in_=mv[:, pp, 1:2], func=AF.Ln,
                          bias=epsc[:, 0:1], scale=1.0)
        scalar.activation(out=rstd[:, pp, :], in_=lnvv[:, pp, :], func=AF.Exp,
                          scale=-0.5).then_inc(lns.sem, 1)
        lns.inc()

    @block.vector
    def _(vector):
        vector.wait_ge(lns.sem, lns.v)
        for sem, val in norm_waits:
            if val > 0:
                vector.wait_ge(sem, val)
        vector.tensor_scalar(out=out_ap, in0=resid[:, tc, :],
                             scalar1=mv[:, pp, 0:1], scalar2=rstd[:, pp, 0:1],
                             op0=ALU.subtract, op1=ALU.mult).then_inc(lnv.sem, 1)
        lnv.inc()


def _ln_scratch(nc, e, tag):
    return (e(nc.sbuf_tensor("bnstat" + tag, [128, 2, 12], F32)),
            e(nc.sbuf_tensor("mv" + tag, [128, 2, 2], F32)),
            e(nc.sbuf_tensor("lnvv" + tag, [128, 2, 1], F32)),
            e(nc.sbuf_tensor("rstd" + tag, [128, 2, 1], F32)))


def emit_transpose_1024(block, S, src_ap, ptT, pp, dst_ap, ident,
                        tensor_waits=(), vector_waits=(), ping=True):
    lnt, lnev = S["lnt"], S["lnev"]

    @block.tensor
    def _(tensor):
        for sem, val in tensor_waits:
            if val > 0:
                tensor.wait_ge(sem, val)
        wge(tensor, lnev, lnev.v - (1 if ping else 0))
        ins = None
        for ic in range(8):
            ins = tensor.transpose(ptT[:, pp, ic * 128:(ic + 1) * 128],
                                   src_ap[:, ic * 128:(ic + 1) * 128],
                                   ident[:])
        ins.then_inc(lnt.sem, 1)
        lnt.inc()

    @block.vector
    def _(vector):
        vector.wait_ge(lnt.sem, lnt.v)
        for sem, val in vector_waits:
            if val > 0:
                vector.wait_ge(sem, val)
        src2 = ptT[:, pp, :]
        if len(dst_ap.shape) == 3:
            src2 = src2.rearrange("p (a b) -> p a b", a=dst_ap.shape[1])
        vector.tensor_copy(out=dst_ap, in_=src2).then_inc(lnev.sem, 1)
        lnev.inc()


def emit_weight_prep(nc, block, S, ld_all,
                     k0a_in, k0b_in, k1_in, k2_in, s0c, s1c, s2c,
                     wgT_in, wuT_in, wdT_in, dram, onesf, onesrow):
    pv, psc, pst, ptn = S["pv"], S["psc"], S["pst"], S["ptn"]
    pd, pd1, pd2, pd3, pd4 = S["pd"], S["pd1"], S["pd2"], S["pd3"], S["pd4"]
    w0c, w1c, w2c = dram["w0c"], dram["w1c"], dram["w2c"]
    wgc, wuc, wdc = dram["wgc"], dram["wuc"], dram["wdc"]

    B = {}

    def alloc():
        ctx = ExitStack()
        S["prep_ctx"] = ctx
        e = ctx.enter_context
        B["au8"] = e(nc.sbuf_tensor("au8", [128, 4096], U8))
        B["bu8"] = e(nc.sbuf_tensor("bu8", [128, 4096], U8))
        B["wtmp"] = e(nc.sbuf_tensor("wtmp", [128, 4096], F32))
        B["wbf"] = e(nc.sbuf_tensor("wbf", [128, 4096], BF16))
        B["cols"] = e(nc.sbuf_tensor("pcols", [128, 8], F32))
        B["absacc"] = e(nc.sbuf_tensor("absacc", [128, 3], F32))
        B["redc"] = e(nc.sbuf_tensor("redc", [128, 1], F32))
        B["scl"] = e(nc.sbuf_tensor("scl", [128, 2, 3], F32))
        B["tot"] = e(nc.sbuf_tensor("tot", [1, 128], F32))
        B["tot2"] = e(nc.sbuf_tensor("tot2", [1, 128], F32))
        B["pbc"] = e(nc.psum_tensor("pbc", [128, 8], F32))

    class _Lazy:
        def __init__(self, key):
            self._key = key

        def __getitem__(self, item):
            return B[self._key][item]

    au8 = _Lazy("au8"); bu8 = _Lazy("bu8"); wtmp = _Lazy("wtmp")
    wbf = _Lazy("wbf"); cols = _Lazy("cols"); absacc = _Lazy("absacc")
    redc = _Lazy("redc"); scl = _Lazy("scl"); tot = _Lazy("tot")
    tot2 = _Lazy("tot2"); pbc = _Lazy("pbc")

    steps = []

    def s_cols():
        @block.vector
        def _(vector):
            vector.wait_ge(S["ld"].sem, ld_all)
            vector.tensor_scalar_mul(out=cols[:, 0:1], in0=s0c[:], scalar1=2.0)
            vector.tensor_scalar_mul(out=cols[:, 1:2], in0=s1c[:], scalar1=4.0)
            vector.tensor_scalar_mul(out=cols[:, 2:3], in0=s1c[:], scalar1=-2.0)
            vector.tensor_scalar_mul(out=cols[:, 3:4], in0=s2c[:], scalar1=4.0)
            vector.tensor_scalar_mul(out=cols[:, 4:5], in0=s2c[:], scalar1=-2.0).then_inc(pv.sem, 1)
            pv.inc()
    steps.append(s_cols)
    scales_tgt = [None]

    w0s, w1s, w2s = dram["w0s"], dram["w1s"], dram["w2s"]
    wgs, wus, wds = dram["wgs"], dram["wus"], dram["wds"]

    def s_w0():
        @block.gpsimd
        def _(g):
            wge(g, pv, pv.v)
            g.dma_start(out=au8[:], in_=k0a_in[0]).then_inc(pd.sem, 16)
            g.dma_start(out=bu8[:], in_=k0b_in[0]).then_inc(pd.sem, 16)
            pd.inc(32)

        @block.vector
        def _(vector):
            vector.wait_ge(pd.sem, pd.v)
            vector.wait_ge(pv.sem, 1)
            wge(vector, pst, pst.v)
            vector.tensor_sub(out=wtmp[:], in0=au8[:], in1=bu8[:]).then_inc(pv.sem, 1)
            pv.inc()
            vector.wait_ge(pv.sem, pv.v)
            vector.tensor_scalar_mul(out=wbf[:], in0=wtmp[:],
                                     scalar1=cols[:, 0:1]).then_inc(pv.sem, 1)
            pv.inc()

        @block.sync
        def _(sync):
            sync.wait_ge(pv.sem, pv.v)
            sync.dma_start(out=w0s[0], in_=wbf[:]).then_inc(pst.sem, 16)
            pst.inc(16)
    steps.append(s_w0)

    for ic in range(32):
        def s_w1(ic=ic):
            @block.gpsimd
            def _(g):
                wge(g, pv, pv.v)
                wge(g, psc, psc.v)
                g.dma_start(out=au8[:, 0:512], in_=k1_in[ic]).then_inc(pd1.sem, 16)
                pd1.inc(16)

            @block.scalar
            def _(scalar):
                scalar.wait_ge(pd1.sem, pd1.v)
                scalar.wait_ge(pv.sem, 1)
                wge(scalar, pst, pst.v)
                scalar.activation(out=wbf[:, 0:512], in_=au8[:, 0:512], func=AF.Identity,
                                  scale=cols[:, 1:2], bias=cols[:, 2:3]).then_inc(psc.sem, 1)
                psc.inc()

            @block.sync
            def _(sync):
                sync.wait_ge(psc.sem, psc.v)
                for g2 in range(2):
                    sync.dma_start(out=w1s[g2, ic],
                                   in_=wbf[:, g2 * 256:(g2 + 1) * 256]).then_inc(pst.sem, 16)
                    pst.inc(16)
        steps.append(s_w1)

    def s_w2():
        @block.gpsimd
        def _(g):
            wge(g, psc, psc.v)
            for j in range(4):
                g.dma_start(out=au8[:, j * 1024:(j + 1) * 1024],
                            in_=k2_in[j]).then_inc(pd2.sem, 16)
                pd2.inc(16)

        @block.scalar
        def _(scalar):
            scalar.wait_ge(pd2.sem, pd2.v)
            wge(scalar, pst, pst.v)
            scalar.activation(out=wbf[:], in_=au8[:], func=AF.Identity,
                              scale=cols[:, 3:4], bias=cols[:, 4:5]).then_inc(psc.sem, 1)
            psc.inc()

        @block.sync
        def _(sync):
            sync.wait_ge(psc.sem, psc.v)
            for j in range(4):
                sync.dma_start(out=w2s[j],
                               in_=wbf[:, j * 1024:(j + 1) * 1024]).then_inc(pst.sem, 16)
                pst.inc(16)
    steps.append(s_w2)

    specs = [(wgT_in, wgs, 1, 4096, 0), (wuT_in, wus, 1, 4096, 1),
             (wdT_in, wds, 4, 1024, 2)]

    for src_, dst, nic, F, mi in specs:
        for ic in range(nic):
            def s_p1(src_=src_, ic=ic, F=F, mi=mi):
                @block.gpsimd
                def _(g):
                    wge(g, pv, pv.v)
                    g.dma_start(out=wtmp[:, 0:F], in_=src_[ic]).then_inc(pd3.sem, 16)
                    pd3.inc(16)

                @block.vector
                def _(vector):
                    vector.wait_ge(pd3.sem, pd3.v)
                    vector.reduce_sum(out=redc[:], in_=wtmp[:, 0:F], axis=AX.X,
                                      apply_absolute_value=True).then_inc(pv.sem, 1)
                    pv.inc()
                    vector.wait_ge(pv.sem, pv.v)
                    if ic == 0:
                        vector.tensor_copy(out=absacc[:, mi:mi + 1],
                                           in_=redc[:]).then_inc(pv.sem, 1)
                    else:
                        vector.tensor_add(out=absacc[:, mi:mi + 1],
                                          in0=absacc[:, mi:mi + 1],
                                          in1=redc[:]).then_inc(pv.sem, 1)
                    pv.inc()
            steps.append(s_p1)

    def s_scale():
        cc = S["cc"]

        @block.tensor
        def _(tensor):
            tensor.wait_ge(pv.sem, pv.v)
            tensor.matmul(pbc[0:1, 4:7], onesf[:, 0:1], absacc[:, 0:3],
                          start=True, stop=True).then_inc(ptn.sem, 1)
            ptn.inc()

        @block.vector
        def _(vector):
            vector.wait_ge(ptn.sem, ptn.v)
            vector.memset(tot[0:1, :], 0.0)
            vector.tensor_copy(out=tot[0:1, 0:3], in_=pbc[0:1, 4:7]).then_inc(pv.sem, 1)
            pv.inc()

        @block.sync
        def _(sync):
            sync.wait_ge(pv.sem, pv.v)
            sync.dma_start(out=dram["arwin"], in_=tot[0:1, :]).then_inc(pst.sem, 16)
            pst.inc(16)
        ar_store = pst.v

        @block.gpsimd
        def _(g):
            g.wait_ge(pst.sem, ar_store)
            g.collective_compute("AllReduce", ALU.add,
                                 replica_groups=[list(range(NCORES))],
                                 ins=[dram["arwin_h"].ap().opt()],
                                 outs=[dram["arwout_h"].ap().opt()]).then_inc(cc.sem, 1)
            cc.inc()
            g.wait_ge(cc.sem, cc.v)
            g.dma_start(out=tot2[0:1, :], in_=dram["arwout"]).then_inc(pd3.sem, 16)
            pd3.inc(16)

        @block.tensor
        def _(tensor):
            tensor.wait_ge(pd3.sem, pd3.v)
            tensor.matmul(pbc[:, 0:3], onesrow[0:1, :],
                          tot2[0:1, 0:3], start=True, stop=True).then_inc(ptn.sem, 1)
            ptn.inc()

        @block.vector
        def _(vector):
            vector.wait_ge(ptn.sem, ptn.v)
            for mi, numel in ((0, 8 * 128 * 4096), (1, 8 * 128 * 4096),
                              (2, 32 * 128 * 1024)):
                vector.tensor_scalar(out=scl[:, 0, mi:mi + 1], in0=pbc[:, mi:mi + 1],
                                     scalar1=1.0 / numel, scalar2=1e-5,
                                     op0=ALU.mult, op1=ALU.max).then_inc(pv.sem, 1)
                pv.inc()
            vector.wait_ge(pv.sem, pv.v)
            vector.reciprocal(out=scl[:, 1, 0:1], in_=scl[:, 0, 0:1])
            vector.reciprocal(out=scl[:, 1, 1:2], in_=scl[:, 0, 1:2])
            vector.reciprocal(out=scl[:, 1, 2:3], in_=scl[:, 0, 2:3]).then_inc(pv.sem, 1)
            pv.inc()
    steps.append(s_scale)

    for src_, dst, nic, F, mi in specs:
        for ic in range(nic):
            def s_p2(src_=src_, dst=dst, ic=ic, F=F, mi=mi):
                @block.gpsimd
                def _(g):
                    wge(g, pv, pv.v)
                    g.dma_start(out=wtmp[:, 0:F], in_=src_[ic]).then_inc(pd4.sem, 16)
                    pd4.inc(16)

                @block.vector
                def _(vector):
                    vector.wait_ge(pd4.sem, pd4.v)
                    wge(vector, pst, pst.v)
                    H = F // 2
                    for hh_ in range(2):
                        sl = slice(hh_ * H, (hh_ + 1) * H)
                        vector.tensor_scalar(out=wtmp[:, sl], in0=wtmp[:, sl],
                                             scalar1=scl[:, 1, mi:mi + 1], scalar2=1.4999999,
                                             op0=ALU.mult, op1=ALU.min).then_inc(pv.sem, 1)
                        pv.inc()
                        vector.wait_ge(pv.sem, pv.v)
                        vector.tensor_scalar(out=wtmp[:, sl], in0=wtmp[:, sl],
                                             scalar1=-1.4999999, scalar2=MAGIC,
                                             op0=ALU.max, op1=ALU.add).then_inc(pv.sem, 1)
                        pv.inc()
                        vector.wait_ge(pv.sem, pv.v)
                        vector.tensor_scalar(out=wbf[:, sl], in0=wtmp[:, sl],
                                             scalar1=-MAGIC, scalar2=scl[:, 0, mi:mi + 1],
                                             op0=ALU.add, op1=ALU.mult).then_inc(pv.sem, 1)
                        pv.inc()

                @block.sync
                def _(sync):
                    sync.wait_ge(pv.sem, pv.v)
                    sync.dma_start(out=dst[ic], in_=wbf[:, 0:F]).then_inc(pst.sem, 16)
                    pst.inc(16)
            steps.append(s_p2)

    def s_fin():
        @block.sync
        def _(sync):
            sync.wait_ge(pst.sem, pst.v)
            sync.nop().then_inc(S["ppd"].sem, 1)
            S["ppd"].inc()

        @block.gpsimd
        def _(g):
            g.wait_ge(pd.sem, pd.v)
            g.wait_ge(pd1.sem, pd1.v)
            g.wait_ge(pd2.sem, pd2.v)
            g.wait_ge(pd3.sem, pd3.v)
            g.wait_ge(pd4.sem, pd4.v)
            g.nop().then_inc(S["ppd"].sem, 1)
            S["ppd"].inc()
        S["prep_ctx"].close()
    steps.append(s_fin)
    return alloc, steps


def emit_l0_attn(nc, block, S, ld_all, resid, btab, ident, tri, epsc,
                 wq_in, wk_in, wv_in, dram, drain_prep=lambda k: None):
    lnv = S["lnv"]
    pjw, pjt, pjev, pjst, cc = S["pjw"], S["pjt"], S["pjev"], S["pjst"], S["cc"]
    agin_k, agin_v = dram["agin_k"], dram["agin_v"]
    agout_k, agout_v = dram["agout_k"], dram["agout_v"]

    with ExitStack() as actx:
        e = actx.enter_context
        qfm = e(nc.sbuf_tensor("qfm", [128, 8, TPC], BF16))

        with ExitStack() as ctx:
            e2 = ctx.enter_context
            lnb = e2(nc.sbuf_tensor("lnbA", [128, 2, D], BF16))
            lnT = e2(nc.sbuf_tensor("lnTA", [128, 8, TPC], BF16))
            wsb = e2(nc.sbuf_tensor("wsbA", [128, 8, D], BF16))
            scr = _ln_scratch(nc, e2, "A")
            stg = e2(nc.sbuf_tensor("stgA", [128, 2, 512], BF16))
            ptT = e2(nc.psum_tensor("ptTA", [128, 2, 1024], BF16))
            pj = e2(nc.psum_tensor("pjA", [128, 4, 512], F32))
            if hasattr(drain_prep, "alloc"):
                drain_prep.alloc()

            for tc in range(NTC):
                pp = tc % 2
                emit_ln(block, S, resid, tc, pp, scr, epsc, lnb[:, pp, :],
                        first_wait=(S["ld"].sem, ld_all if tc == 0 else 0),
                        norm_waits=((S["lnt"].sem, S["lnt"].v - 1),))
                emit_transpose_1024(
                    block, S, lnb[:, pp, :], ptT, pp,
                    lnT[:, :, tc * 128:(tc + 1) * 128],
                    ident, tensor_waits=((lnv.sem, lnv.v),))
                drain_prep(2)
            lnT_done = S["lnev"].v

            @block.gpsimd
            def _(g):
                for ic in range(8):
                    g.dma_start(out=wsb[:, ic, :], in_=wq_in[ic]).then_inc(pjw.sem, 16)
                    pjw.inc(16)

            nmm = 0
            for oc in range(8):
                for ns in range(4):
                    pp = nmm % 4

                    @block.tensor
                    def _(tensor):
                        if nmm == 0:
                            tensor.wait_ge(pjw.sem, pjw.v)
                            tensor.wait_ge(S["lnev"].sem, lnT_done)
                        wge(tensor, pjev, nmm - 3)
                        ins = None
                        for ic in range(8):
                            ins = tensor.matmul(pj[:, pp, :],
                                                wsb[:, ic, oc * 128:(oc + 1) * 128],
                                                lnT[:, ic, ns * 512:(ns + 1) * 512],
                                                start=(ic == 0), stop=(ic == 7))
                        ins.then_inc(pjt.sem, 1)
                        pjt.inc()

                    @block.vector
                    def _(vector):
                        vector.wait_ge(pjt.sem, pjt.v)
                        vector.tensor_scalar_mul(out=qfm[:, oc, ns * 512:(ns + 1) * 512],
                                                 in0=pj[:, pp, :],
                                                 scalar1=0.03125).then_inc(pjev.sem, 1)
                        pjev.inc()
                    nmm += 1
                    drain_prep(2)
            q_done = pjt.v

            @block.gpsimd
            def _(g):
                g.wait_ge(pjt.sem, q_done)
                for ic in range(8):
                    g.dma_start(out=wsb[:, ic, :], in_=wk_in[ic]).then_inc(pjw.sem, 16)
                    pjw.inc(16)
            kw_ready = pjw.v

            nst = 0
            for oc in range(8):
                for ns in range(4):
                    pp = nmm % 4
                    sp = nst % 2

                    @block.tensor
                    def _(tensor):
                        if nmm == 32:
                            tensor.wait_ge(pjw.sem, kw_ready)
                        wge(tensor, pjev, nmm - 3)
                        ins = None
                        for ic in range(8):
                            ins = tensor.matmul(pj[:, pp, :],
                                                wsb[:, ic, oc * 128:(oc + 1) * 128],
                                                lnT[:, ic, ns * 512:(ns + 1) * 512],
                                                start=(ic == 0), stop=(ic == 7))
                        ins.then_inc(pjt.sem, 1)
                        pjt.inc()

                    @block.vector
                    def _(vector):
                        vector.wait_ge(pjt.sem, pjt.v)
                        wge(vector, pjst, (nst - 1) * 16)
                        vector.tensor_copy(out=stg[:, sp, :],
                                           in_=pj[:, pp, :]).then_inc(pjev.sem, 1)
                        pjev.inc()

                    @block.sync
                    def _(sync):
                        sync.wait_ge(pjev.sem, pjev.v)
                        sync.dma_start(out=agin_k[oc][:, ns * 512:(ns + 1) * 512],
                                       in_=stg[:, sp, :]).then_inc(pjst.sem, 16)
                        pjst.inc(16)
                    nmm += 1
                    nst += 1
                    drain_prep(1)
            k_done = pjt.v
            k_st = pjst.v

            @block.sync
            def _(sync):
                sync.wait_ge(pjst.sem, k_st)
                sync.nop().then_inc(S["agk"].sem, 1)
                S["agk"].inc()
            agk_done = S["agk"].v

            @block.gpsimd
            def _(g):
                g.wait_ge(pjt.sem, k_done)
                for ic in range(8):
                    g.dma_start(out=wsb[:, ic, :], in_=wv_in[ic]).then_inc(pjw.sem, 16)
                    pjw.inc(16)
            vw_ready = pjw.v

            @block.gpsimd
            def _(g):
                g.wait_ge(S["agk"].sem, agk_done)
                g.wait_ge(pjst.sem, (NTC * 2) * 16 + k_st)
                g.collective_compute("AllGather", ALU.bypass,
                                     replica_groups=[list(range(NCORES))],
                                     ins=[dram["agin_k_h"].ap().opt()],
                                     outs=[dram["agout_k_h"].ap().opt()]).then_inc(cc.sem, 1)
                cc.inc()

            for tc in range(NTC):
                for nh in range(2):
                    pp = nmm % 4
                    sp = nst % 2

                    @block.tensor
                    def _(tensor):
                        if nmm == 64:
                            tensor.wait_ge(pjw.sem, vw_ready)
                        wge(tensor, pjev, nmm - 3)
                        ins = None
                        for ic in range(8):
                            ins = tensor.matmul(pj[:, pp, :],
                                                lnT[:, ic, tc * 128:(tc + 1) * 128],
                                                wsb[:, ic, nh * 512:(nh + 1) * 512],
                                                start=(ic == 0), stop=(ic == 7))
                        ins.then_inc(pjt.sem, 1)
                        pjt.inc()

                    @block.vector
                    def _(vector):
                        vector.wait_ge(pjt.sem, pjt.v)
                        wge(vector, pjst, (nst - 1) * 16)
                        vector.tensor_copy(out=stg[:, sp, :],
                                           in_=pj[:, pp, :]).then_inc(pjev.sem, 1)
                        pjev.inc()

                    @block.sync
                    def _(sync):
                        sync.wait_ge(pjev.sem, pjev.v)
                        sync.dma_start(out=agin_v[tc][:, nh * 512:(nh + 1) * 512],
                                       in_=stg[:, sp, :]).then_inc(pjst.sem, 16)
                        pjst.inc(16)
                    nmm += 1
                    nst += 1
                    drain_prep(1)
            agin_done = pjst.v

            drain_prep(1 << 30)
            prep_stores = S["pst"].v

            @block.gpsimd
            def _(g):
                g.wait_ge(pjst.sem, agin_done)
                g.collective_compute("AllGather", ALU.bypass,
                                     replica_groups=[list(range(NCORES))],
                                     ins=[dram["agin_v_h"].ap().opt()],
                                     outs=[dram["agout_v_h"].ap().opt()]).then_inc(cc.sem, 1)
                cc.inc()
            ag_done = cc.v

            agw = {}

            @block.gpsimd
            def _(g):
                g.wait_ge(S["pst"].sem, prep_stores)
                for key, src, dst in (("w0", "w0s_h", "w0c_h"), ("w1", "w1s_h", "w1c_h"),
                                      ("w2", "w2s_h", "w2c_h"), ("wg", "wgs_h", "wgc_h"),
                                      ("wu", "wus_h", "wuc_h"), ("wd", "wds_h", "wdc_h")):
                    g.collective_compute("AllGather", ALU.bypass,
                                         replica_groups=[list(range(NCORES))],
                                         ins=[dram[src].ap().opt()],
                                         outs=[dram[dst].ap().opt()]).then_inc(cc.sem, 1)
                    cc.inc()
                    agw[key] = cc.v
            S["agw"] = agw

        with ExitStack() as ctx:
            e2 = ctx.enter_context
            osb = e2(nc.sbuf_tensor("osb", [128, 8, D], F32))
            lbuf = e2(nc.sbuf_tensor("lbuf", [128, 8, NSTEP], F32))
            lrec = e2(nc.sbuf_tensor("lrec", [128, 8], F32))
            kt = e2(nc.sbuf_tensor("ktile", [128, 2, 8, 1024], BF16))
            vt = e2(nc.sbuf_tensor("vtile", [128, 2, 8, 1024], BF16))
            pbuf = e2(nc.sbuf_tensor("pbuf", [128, 2, 1024], BF16))
            ptsb = e2(nc.sbuf_tensor("ptsb", [128, 2, 1024], BF16))
            spsum = e2(nc.psum_tensor("spsum", [128, 2, 1024], F32))
            ptp = e2(nc.psum_tensor("ptp", [128, 2, 1024], BF16))
            pvp = e2(nc.psum_tensor("pvp", [128, 1024], F32))

            kldp = [S["kld0"], S["kld1"]]
            vldp = [S["vld0"], S["vld1"]]
            smm, sexp = S["smm"], S["sexp"]
            stmm, sptev, spv = S["stmm"], S["sptev"], S["spv"]
            svadd, vmask, fin = S["svadd"], S["vmask"], S["fin"]

            def emit_transp(c):
                qs, sp = c["qs"], c["sp"]

                @block.tensor
                def _(tensor):
                    tensor.wait_ge(sexp.sem, qs + 1)
                    wge(tensor, sptev, qs - 1)
                    ins = None
                    for kc in range(8):
                        ins = tensor.transpose(ptp[:, sp, kc * 128:(kc + 1) * 128],
                                               pbuf[:, sp, kc * 128:(kc + 1) * 128],
                                               ident[:])
                    ins.then_inc(stmm.sem, 1)
                    stmm.inc()

                @block.vector
                def _(vector):
                    vector.wait_ge(stmm.sem, qs + 1)
                    wge(vector, spv, qs - 1)
                    vector.tensor_copy(out=ptsb[:, sp, :],
                                       in_=ptp[:, sp, :]).then_inc(sptev.sem, 1)
                    sptev.inc()

            def emit_exps(specs):
                @block.scalar
                def _(scalar):
                    for qs, sp, r, j, s_idx, vm in specs:
                        scalar.wait_ge(smm.sem, qs + 1)
                        if vm is not None:
                            scalar.wait_ge(vmask.sem, vm)
                        if s_idx == 1 and j == 0 and r == 0:
                            scalar.wait_ge(fin.sem, 1)
                        bcol = (s_idx * 8 + r) * NSTEP + j
                        scalar.activation(out=pbuf[:, sp, :], in_=spsum[:, sp, :],
                                          func=AF.Exp, bias=btab[:, bcol:bcol + 1],
                                          scale=1.0,
                                          accum_out=lbuf[:, r, j:j + 1]).then_inc(sexp.sem, 1)

            def emit_pv(c):
                qs, sp, ring, r, j = c["qs"], c["sp"], c["ring"], c["r"], c["j"]

                @block.tensor
                def _(tensor):
                    tensor.wait_ge(sptev.sem, qs + 1)
                    wge(tensor, svadd, qs)
                    ins = None
                    for kc in range(8):
                        for h in range(2):
                            ins = tensor.matmul(
                                pvp[:, h * 512:(h + 1) * 512],
                                ptsb[:, sp, kc * 128:(kc + 1) * 128],
                                vt[:, ring, kc, h * 512:(h + 1) * 512],
                                start=(kc == 0), stop=(kc == 7))
                    ins.then_inc(spv.sem, 1)
                    spv.inc()

                @block.vector
                def _(vector):
                    vector.wait_ge(spv.sem, qs + 1)
                    if j == 0:
                        vector.tensor_copy(out=osb[:, r, :], in_=pvp[:, :]).then_inc(svadd.sem, 1)
                    else:
                        vector.tensor_add(out=osb[:, r, :], in0=osb[:, r, :],
                                          in1=pvp[:, :]).then_inc(svadd.sem, 1)
                    svadd.inc()

            for s_idx in range(2):
                pend = []
                for j in range(NSTEP):
                    exp_specs = []
                    g = s_idx * NSTEP + j
                    ring = g % 2

                    @block.sync
                    def _(sync):
                        if g == 0:
                            sync.wait_ge(pjst.sem, agin_done)
                            sync.wait_ge(S["ppd"].sem, 2)
                        if g == 1:
                            sync.wait_ge(cc.sem, ag_done)
                        wge(sync, smm, 8 * (g - 1))
                        if j == 0:
                            for ic in range(8):
                                sync.dma_start(
                                    out=kt[:, ring, ic, :],
                                    in_=agin_k[ic][:, s_idx * 1024:(s_idx + 1) * 1024],
                                ).then_inc(kldp[ring].sem, 16)
                                kldp[ring].inc(16)
                        else:
                            rank, sub = _posblock_src(j - 1)
                            for ic in range(8):
                                sync.dma_start(
                                    out=kt[:, ring, ic, :],
                                    in_=agout_k[8 * rank + ic][:, sub * 1024:(sub + 1) * 1024],
                                ).then_inc(kldp[ring].sem, 16)
                                kldp[ring].inc(16)

                    @block.gpsimd
                    def _(gq):
                        if g == 0:
                            gq.wait_ge(pjst.sem, agin_done)
                            gq.wait_ge(S["ppd"].sem, 2)
                        if g == 1:
                            gq.wait_ge(cc.sem, ag_done)
                        wge(gq, spv, 8 * (g - 1))
                        if j == 0:
                            for kc in range(8):
                                gq.dma_start(out=vt[:, ring, kc, :],
                                             in_=agin_v[s_idx * 8 + kc]).then_inc(vldp[ring].sem, 16)
                                vldp[ring].inc(16)
                        else:
                            rank, sub = _posblock_src(j - 1)
                            for kc in range(8):
                                gq.dma_start(
                                    out=vt[:, ring, kc, :],
                                    in_=agout_v[NTC * rank + sub * 8 + kc]).then_inc(vldp[ring].sem, 16)
                                vldp[ring].inc(16)
                    k_tgt, v_tgt = kldp[ring].v, vldp[ring].v

                    for r in range(8):
                        qs = g * 8 + r
                        sp = qs % 2
                        qcol0 = s_idx * 1024

                        @block.tensor
                        def _(tensor):
                            if r == 0:
                                tensor.wait_ge(kldp[ring].sem, k_tgt)
                                tensor.wait_ge(vldp[ring].sem, v_tgt)
                            wge(tensor, sexp, qs - 1)
                            ins = None
                            for ic in range(8):
                                for h in range(2):
                                    ins = tensor.matmul(
                                        spsum[:, sp, h * 512:(h + 1) * 512],
                                        qfm[:, ic, qcol0 + r * 128:qcol0 + (r + 1) * 128],
                                        kt[:, ring, ic, h * 512:(h + 1) * 512],
                                        start=(ic == 0), stop=(ic == 7))
                            ins.then_inc(smm.sem, 1)
                            smm.inc()

                        if j == 0:
                            @block.vector
                            def _(vector):
                                vector.wait_ge(smm.sem, qs + 1)
                                ins = vector.tensor_add(
                                    out=spsum[:, sp, r * 128:(r + 1) * 128],
                                    in0=spsum[:, sp, r * 128:(r + 1) * 128],
                                    in1=tri[:])
                                if r < 7:
                                    ins = vector.tensor_scalar_add(
                                        out=spsum[:, sp, (r + 1) * 128:1024],
                                        in0=spsum[:, sp, (r + 1) * 128:1024],
                                        scalar1=MASKNEG)
                                ins.then_inc(vmask.sem, 1)
                                vmask.inc()

                        exp_specs.append((qs, sp, r, j, s_idx,
                                          vmask.v if j == 0 else None))
                        sexp.inc()

                        ctxq = dict(qs=qs, sp=sp, ring=ring, r=r, j=j)
                        pend.append(ctxq)
                        i_local = len(pend) - 1
                        if i_local >= 1:
                            emit_transp(pend[i_local - 1])
                        if i_local >= 2:
                            emit_pv(pend[i_local - 2])
                    emit_exps(exp_specs)

                emit_transp(pend[-1])
                emit_pv(pend[-2])
                emit_pv(pend[-1])
                pend.clear()

                @block.vector
                def _(vector):
                    vector.wait_ge(svadd.sem, svadd.v)
                    vector.reduce_sum(out=lrec[:, 0:8].rearrange("p a -> p a ()"),
                                      in_=lbuf[:, :, :], axis=AX.X).then_inc(S["fv"].sem, 1)
                    S["fv"].inc()
                    vector.wait_ge(S["fv"].sem, S["fv"].v)
                    vector.reciprocal(out=lrec[:, 0:8], in_=lrec[:, 0:8]).then_inc(S["fv"].sem, 1)
                    S["fv"].inc()
                    vector.wait_ge(S["fv"].sem, S["fv"].v)
                    ins = None
                    for r in range(8):
                        tc = s_idx * 8 + r
                        ins = vector.scalar_tensor_tensor(
                            out=resid[:, tc, :], in0=osb[:, r, :],
                            scalar=lrec[:, r:r + 1], in1=resid[:, tc, :],
                            op0=ALU.mult, op1=ALU.add)
                    ins.then_inc(fin.sem, 1)
                    fin.inc()


def emit_l0_ffn(nc, block, S, resid, ident, epsc, dram, dbg_h=None, dbg_w=None):
    fl0 = [S["fl00"], S["fl01"]]
    fl1 = [S["fl10"], S["fl11"]]
    fl2 = [S["fl20"], S["fl21"]]
    ft, fev, fv = S["ft"], S["fev"], S["fv"]
    lnv = S["lnv"]
    w0c, w1c, w2c = dram["w0c"], dram["w1c"], dram["w2c"]

    with ExitStack() as ctx:
        e = ctx.enter_context
        lnb = e(nc.sbuf_tensor("lnbB", [128, 2, D], BF16))
        lnS = e(nc.sbuf_tensor("lnSB", [128, 8, 512], BF16))
        h1 = e(nc.sbuf_tensor("h1B", [128, 32, 512], BF16))
        h2 = e(nc.sbuf_tensor("h2B", [128, 32, 512], BF16))
        w0b = e(nc.sbuf_tensor("w0bB", [128, 2, 8, 512], BF16))
        w1b = e(nc.sbuf_tensor("w1bB", [128, 2, 32, 256], BF16))
        w2b = e(nc.sbuf_tensor("w2bB", [128, 2, 1024], BF16))
        scr = _ln_scratch(nc, e, "B")
        ptT = e(nc.psum_tensor("ptTB", [128, 2, 1024], BF16))
        hp = e(nc.psum_tensor("hpB", [128, 2, 512], F32))
        h3p = e(nc.psum_tensor("h3pB", [128, 2, 1024], F32))

        for t in range(4):
            for i in range(4):
                tc = t * 4 + i
                pp = i % 2
                emit_ln(block, S, resid, tc, pp, scr, epsc, lnb[:, pp, :],
                        norm_waits=((S["lnt"].sem, S["lnt"].v - 1),))
                emit_transpose_1024(
                    block, S, lnb[:, pp, :], ptT, pp,
                    lnS[:, :, i * 128:(i + 1) * 128],
                    ident, tensor_waits=((lnv.sem, lnv.v),),
                    vector_waits=((ft.sem, ft.v),))
            lnS_done = S["lnev"].v

            for f1g in range(8):
                gp = f1g % 2

                @block.sync
                def _(g):
                    if t == 0 and f1g == 0:
                        g.wait_ge(S["cc"].sem, S["agw"]["w0"])
                    wge(g, ft, ft.v - 4 if (t > 0 or f1g >= 2) else 0)
                    g.dma_start(out=w0b[:, gp, :, :],
                                in_=w0c[:, :, f1g * 512:(f1g + 1) * 512]
                                .rearrange("a p b -> p a b")).then_inc(fl0[gp].sem, 16)
                    fl0[gp].inc(16)

                for lf in range(4):
                    f1c = f1g * 4 + lf
                    pp = f1c % 2

                    @block.tensor
                    def _(tensor):
                        if lf == 0:
                            tensor.wait_ge(fl0[gp].sem, fl0[gp].v)
                        if f1c == 0:
                            tensor.wait_ge(S["lnev"].sem, lnS_done)
                        wge(tensor, fev, fev.v - 1)
                        ins = None
                        for ic in range(8):
                            ins = tensor.matmul(hp[:, pp, :],
                                                w0b[:, gp, ic, lf * 128:(lf + 1) * 128],
                                                lnS[:, ic, :],
                                                start=(ic == 0), stop=(ic == 7))
                        ins.then_inc(ft.sem, 1)
                        ft.inc()

                    @block.vector
                    def _(vector):
                        vector.wait_ge(ft.sem, ft.v)
                        vector.tensor_relu(out=h1[:, f1c, :],
                                           in_=hp[:, pp, :]).then_inc(fev.sem, 1)
                        fev.inc()

            for f2g in range(16):
                gp = f2g % 2

                @block.sync
                def _(g):
                    if t == 0 and f2g == 0:
                        g.wait_ge(S["cc"].sem, S["agw"]["w1"])
                    wge(g, ft, ft.v - 2 if (t > 0 or f2g >= 2) else 0)
                    g.dma_start(out=w1b[:, gp, :, :],
                                in_=w1c[f2g].rearrange("a p b -> p a b")).then_inc(fl1[gp].sem, 16)
                    fl1[gp].inc(16)

                for lf in range(2):
                    f2c = f2g * 2 + lf
                    pp = f2c % 2

                    @block.tensor
                    def _(tensor):
                        if lf == 0:
                            tensor.wait_ge(fl1[gp].sem, fl1[gp].v)
                        wge(tensor, fev, fev.v - 1)
                        ins = None
                        for f1c in range(32):
                            ins = tensor.matmul(hp[:, pp, :],
                                                w1b[:, gp, f1c, lf * 128:(lf + 1) * 128],
                                                h1[:, f1c, :],
                                                start=(f1c == 0), stop=(f1c == 31))
                        ins.then_inc(ft.sem, 1)
                        ft.inc()

                    @block.vector
                    def _(vector):
                        vector.wait_ge(ft.sem, ft.v)
                        vector.tensor_relu(out=h2[:, f2c, :],
                                           in_=hp[:, pp, :]).then_inc(fev.sem, 1)
                        fev.inc()
            h2_done = fev.v

            if dbg_h is not None and t == 0:
                @block.gpsimd
                def _(g):
                    g.wait_ge(fev.sem, h2_done)
                    g.dma_start(out=dbg_h[0], in_=h1[:, :, :]).then_inc(S["st"].sem, 16)
                    g.dma_start(out=dbg_h[1], in_=h2[:, :, :]).then_inc(S["st"].sem, 16)
                    g.dma_start(out=dbg_w[0], in_=lnS[:, :, :]).then_inc(S["st"].sem, 16)
                    g.dma_start(out=dbg_w[1].rearrange("p a b -> a p b"),
                                in_=dram["w0c"][:, :, 6 * 512:7 * 512]).then_inc(S["st"].sem, 16)
                    S["st"].inc(64)

                @block.tensor
                def _(tensor):
                    tensor.wait_ge(S["st"].sem, S["st"].v)

            for i in range(4):
                tc = t * 4 + i
                hp3 = i % 2
                for f2c in range(32):
                    wp = (i * 32 + f2c) % 2

                    @block.sync
                    def _(g):
                        if t == 0 and i == 0 and f2c == 0:
                            g.wait_ge(S["cc"].sem, S["agw"]["w2"])
                        wge(g, ft, ft.v - 1 if (t > 0 or i > 0 or f2c >= 2) else 0)
                        g.dma_start(out=w2b[:, wp, :], in_=w2c[f2c]).then_inc(fl2[wp].sem, 16)
                        fl2[wp].inc(16)

                    @block.tensor
                    def _(tensor):
                        tensor.wait_ge(fl2[wp].sem, fl2[wp].v)
                        if i == 0 and f2c == 0:
                            tensor.wait_ge(fev.sem, h2_done)
                        if f2c == 0:
                            wge(tensor, fv, fv.v - 1)
                        ins = None
                        for h in range(2):
                            ins = tensor.matmul(h3p[:, hp3, h * 512:(h + 1) * 512],
                                                h2[:, f2c, i * 128:(i + 1) * 128],
                                                w2b[:, wp, h * 512:(h + 1) * 512],
                                                start=(f2c == 0), stop=(f2c == 31))
                        ins.then_inc(ft.sem, 1)
                        ft.inc()

                @block.vector
                def _(vector):
                    vector.wait_ge(ft.sem, ft.v)
                    vector.tensor_add(out=resid[:, tc, :], in0=resid[:, tc, :],
                                      in1=h3p[:, hp3, :]).then_inc(fv.sem, 1)
                    fv.inc()


def emit_l1_attn(nc, block, S, resid, ident, epsc, onebf, wqkvT_in, dram):
    cld, ct, cev, cv, cs = S["cld"], S["ct"], S["cev"], S["cv"], S["cs"]
    ccc, cst = S["ccc"], S["cst"]
    lnv = S["lnv"]
    arin, arout = dram["arin"], dram["arout"]

    with ExitStack() as ctx:
        e = ctx.enter_context
        lnb = e(nc.sbuf_tensor("lnbC", [128, 2, D], BF16))
        qhfm = e(nc.sbuf_tensor("qhfmC", [128, 8, TPC], BF16))
        hs = e(nc.sbuf_tensor("hsC", [128, 2, 16], F32))
        kvsb = e(nc.sbuf_tensor("kvsbC", [128, 1040], F32))
        scr = _ln_scratch(nc, e, "C")

        with ExitStack() as pctx:
            e2 = pctx.enter_context
            lnT = e2(nc.sbuf_tensor("lnTC", [128, 8, TPC], BF16))
            khf = e2(nc.sbuf_tensor("khfC", [128, NTC, D], BF16))
            wqb = e2(nc.sbuf_tensor("wqbC", [128, 8, 1024], BF16))
            pt32 = e2(nc.sbuf_tensor("pt32C", [128, 2, D], BF16))
            sqt = e2(nc.sbuf_tensor("sqtC", [128, D], F32))
            qhat = e2(nc.sbuf_tensor("qhatC", [128, 2, D], BF16))
            vta = e2(nc.sbuf_tensor("vtaC", [128, 2, 1024], BF16))
            ptT = e2(nc.psum_tensor("ptTC", [128, 1, 1024], BF16))
            pjc = e2(nc.psum_tensor("pjcC", [128, 2, 1024], F32))
            kvp = e2(nc.psum_tensor("kvpC", [64, 1024], F32))
            kvq = e2(nc.psum_tensor("kvqC", [64, 16], F32))

            for tc in range(NTC):
                pp = tc % 2
                emit_ln(block, S, resid, tc, pp, scr, epsc, lnb[:, pp, :],
                        norm_waits=((S["lnt"].sem, S["lnt"].v - 1),))
                emit_transpose_1024(
                    block, S, lnb[:, pp, :], ptT, 0,
                    lnT[:, :, tc * 128:(tc + 1) * 128],
                    ident, tensor_waits=((lnv.sem, lnv.v),), ping=False)
            lnT_done = S["lnev"].v

            @block.gpsimd
            def _(g):
                for ic in range(8):
                    g.dma_start(out=wqb[:, ic, :],
                                in_=wqkvT_in[ic][:, 0:1024]).then_inc(cld.sem, 16)
                    cld.inc(16)
            qw_ready = cld.v

            for tc in range(NTC):
                pp = tc % 2

                @block.tensor
                def _(tensor):
                    if tc == 0:
                        tensor.wait_ge(cld.sem, qw_ready)
                        tensor.wait_ge(S["lnev"].sem, lnT_done)
                    wge(tensor, cev, cev.v - 1)
                    ins = None
                    for ic in range(8):
                        for h in range(2):
                            ins = tensor.matmul(pjc[:, pp, h * 512:(h + 1) * 512],
                                                lnT[:, ic, tc * 128:(tc + 1) * 128],
                                                wqb[:, ic, h * 512:(h + 1) * 512],
                                                start=(ic == 0), stop=(ic == 7))
                    ins.then_inc(ct.sem, 1)
                    ct.inc()

                @block.vector
                def _(vector):
                    vector.wait_ge(ct.sem, ct.v)
                    wge(vector, cs, cs.v)
                    vector.tensor_copy(out=pt32[:, pp, :], in_=pjc[:, pp, :]).then_inc(cev.sem, 1)
                    cev.inc()

                _emit_spl2(nc, block, S, pt32, sqt, hs, qhat[:, pp, :], pp, cs, cv, cev)

                emit_transpose_1024(
                    block, S, qhat[:, pp, :], ptT, 0,
                    qhfm[:, :, tc * 128:(tc + 1) * 128],
                    ident, tensor_waits=((cs.sem, cs.v),), ping=False)

            if C_STAGE < 2:
                return
            @block.gpsimd
            def _(g):
                g.wait_ge(ct.sem, ct.v)
                for ic in range(8):
                    g.dma_start(out=wqb[:, ic, :],
                                in_=wqkvT_in[ic][:, 1024:2048]).then_inc(cld.sem, 16)
                    cld.inc(16)
            kw_ready = cld.v

            for tc in range(NTC):
                pp = tc % 2

                @block.tensor
                def _(tensor):
                    if tc == 0:
                        tensor.wait_ge(cld.sem, kw_ready)
                    wge(tensor, cev, cev.v - 1)
                    ins = None
                    for ic in range(8):
                        for h in range(2):
                            ins = tensor.matmul(pjc[:, pp, h * 512:(h + 1) * 512],
                                                lnT[:, ic, tc * 128:(tc + 1) * 128],
                                                wqb[:, ic, h * 512:(h + 1) * 512],
                                                start=(ic == 0), stop=(ic == 7))
                    ins.then_inc(ct.sem, 1)
                    ct.inc()

                @block.vector
                def _(vector):
                    vector.wait_ge(ct.sem, ct.v)
                    wge(vector, cs, cs.v)
                    vector.tensor_copy(out=pt32[:, pp, :], in_=pjc[:, pp, :]).then_inc(cev.sem, 1)
                    cev.inc()

                _emit_spl2(nc, block, S, pt32, sqt, hs, khf[:, tc, :], pp, cs, cv, cev)

            khf_done = cs.v

            if C_STAGE < 3:
                return
            @block.gpsimd
            def _(g):
                g.wait_ge(ct.sem, ct.v)
                for ic in range(8):
                    g.dma_start(out=wqb[:, ic, :],
                                in_=wqkvT_in[ic][:, 2048:3072]).then_inc(cld.sem, 16)
                    cld.inc(16)
            vw_ready = cld.v

            for tc in range(NTC):
                pp = tc % 2

                @block.tensor
                def _(tensor):
                    if tc == 0:
                        tensor.wait_ge(cld.sem, vw_ready)
                        tensor.wait_ge(cs.sem, khf_done)
                    wge(tensor, cev, cev.v - 1)
                    ins = None
                    for ic in range(8):
                        for h in range(2):
                            ins = tensor.matmul(pjc[:, pp, h * 512:(h + 1) * 512],
                                                lnT[:, ic, tc * 128:(tc + 1) * 128],
                                                wqb[:, ic, h * 512:(h + 1) * 512],
                                                start=(ic == 0), stop=(ic == 7))
                    ins.then_inc(ct.sem, 1)
                    ct.inc()

                @block.vector
                def _(vector):
                    vector.wait_ge(ct.sem, ct.v)
                    if tc >= 2:
                        wge(vector, ct, ct.v)
                    vector.tensor_copy(out=vta[:, pp, :],
                                       in_=pjc[:, pp, :]).then_inc(cev.sem, 1)
                    cev.inc()

                @block.tensor
                def _(tensor):
                    tensor.wait_ge(cev.sem, cev.v)
                    ins = None
                    for h in range(16):
                        tensor.matmul(kvp[0:64, h * 64:(h + 1) * 64],
                                      khf[:, tc, h * 64:(h + 1) * 64],
                                      vta[:, pp, h * 64:(h + 1) * 64],
                                      start=(tc == 0 and h % 8 == 0),
                                      stop=(tc == NTC - 1 and h % 8 == 7))
                        ins = tensor.matmul(kvq[0:64, h:h + 1],
                                            khf[:, tc, h * 64:(h + 1) * 64],
                                            onebf[:, 0:1],
                                            start=(tc == 0 and h == 0),
                                            stop=(tc == NTC - 1 and h == 15))
                    ins.then_inc(ct.sem, 1)
                    ct.inc()
            kv_done = ct.v

            @block.vector
            def _(vector):
                vector.wait_ge(ct.sem, kv_done)
                vector.tensor_scalar_mul(out=kvsb[0:64, 0:1024], in0=kvp[0:64, :],
                                         scalar1=0.125)
                vector.tensor_copy(out=kvsb[0:64, 1024:1040],
                                   in_=kvq[0:64, :]).then_inc(cv.sem, 1)
                cv.inc()

            @block.sync
            def _(sync):
                sync.wait_ge(cv.sem, cv.v)
                sync.dma_start(out=arin, in_=kvsb[0:64, :]).then_inc(cst.sem, 16)
                cst.inc(16)

            if C_STAGE < 4:
                return

            @block.gpsimd
            def _(g):
                g.wait_ge(cst.sem, cst.v)
                g.collective_compute("AllReduce", ALU.add,
                                     replica_groups=[list(range(NCORES))],
                                     ins=[dram["arin_h"].ap().opt()],
                                     outs=[dram["arout_h"].ap().opt()]).then_inc(ccc.sem, 1)
                ccc.inc()

            @block.sync
            def _(sync):
                sync.wait_ge(ccc.sem, ccc.v)
                sync.dma_start(out=kvsb[0:64, :], in_=arout).then_inc(cst.sem, 16)
                sync.dma_start(out=kvsb[64:128, :], in_=arout).then_inc(cst.sem, 16)
                cst.inc(32)

            @block.vector
            def _(vector):
                vector.wait_ge(cst.sem, cst.v)
                vector.tensor_scalar_add(out=kvsb[:, 1024:1040],
                                         in0=kvsb[:, 1024:1040],
                                         scalar1=1e-6).then_inc(cv.sem, 1)
                cv.inc()
        if C_STAGE < 5:
            return
        with ExitStack() as pctx:
            e2 = pctx.enter_context
            normc = e2(nc.sbuf_tensor("normcC", [128, 2, 2, 8], F32))
            otmp = e2(nc.sbuf_tensor("otmpC", [128, 2, D], BF16))
            kvsbb = e2(nc.sbuf_tensor("kvsbbC", [128, 1040], BF16))
            op = e2(nc.psum_tensor("opC", [128, 2, 2, 512], F32))
            oq = e2(nc.psum_tensor("oqC", [128, 2, 2, 512], F32))

            @block.vector
            def _(vector):
                vector.wait_ge(cv.sem, cv.v)
                vector.tensor_copy(out=kvsbb[:, :], in_=kvsb[:, :]).then_inc(cv.sem, 1)
                cv.inc()
            kv_ready = cv.v

            for tc in range(NTC):
                pp = tc % 2

                @block.tensor
                def _(tensor):
                    if tc == 0:
                        tensor.wait_ge(cv.sem, kv_ready)
                    wge(tensor, cv, kv_ready + 3 * (tc - 1))
                    ins = None
                    for h in range(16):
                        po = (h % 2) * 64
                        par = h % 2
                        hh = h // 2
                        tensor.matmul(op[:, pp, par, hh * 64:(hh + 1) * 64],
                                      qhfm[po:po + 64, hh, tc * 128:(tc + 1) * 128],
                                      kvsbb[po:po + 64, h * 64:(h + 1) * 64],
                                      start=(hh == 0), stop=(hh == 7))
                        ins = tensor.matmul(oq[:, pp, par, hh:hh + 1],
                                            qhfm[po:po + 64, hh, tc * 128:(tc + 1) * 128],
                                            kvsbb[po:po + 64, 1024 + h:1025 + h],
                                            start=(hh == 0), stop=(hh == 7))
                    ins.then_inc(ct.sem, 1)
                    ct.inc()

                @block.vector
                def _(vector):
                    vector.wait_ge(ct.sem, ct.v)
                    vector.reciprocal(out=normc[:, pp, :, :],
                                      in_=oq[:, pp, :, 0:8]).then_inc(cv.sem, 1)
                    cv.inc()
                    vector.wait_ge(cv.sem, cv.v)
                    vector.tensor_tensor(
                        out=otmp[:, pp, :].rearrange("p (a q b) -> p a q b", a=8, q=2),
                        in0=op[:, pp, :, :].rearrange("p q (a b) -> p a q b", a=8),
                        in1=normc[:, pp, :, :].rearrange("p q a -> p a q ()").broadcast_to([128, 8, 2, 64]),
                        op=ALU.mult).then_inc(cv.sem, 1)
                    cv.inc()
                    vector.wait_ge(cv.sem, cv.v)
                    vector.tensor_add(out=resid[:, tc, :], in0=resid[:, tc, :],
                                      in1=otmp[:, pp, :]).then_inc(cv.sem, 1)
                    cv.inc()


def _emit_spl2(nc, block, S, src, sqt, hs, dst_ap, pp, cs, cv, cev):
    @block.vector
    def _(vector):
        vector.wait_ge(cev.sem, cev.v)
        vector.tensor_mul(out=sqt[:, :], in0=src[:, pp, :],
                          in1=src[:, pp, :]).then_inc(cv.sem, 1)
        cv.inc()
        vector.wait_ge(cv.sem, cv.v)
        vector.reduce_sum(out=hs[:, pp, :].rearrange("p a -> p a ()"),
                          in_=sqt[:, :].rearrange("p (a b) -> p a b", a=16),
                          axis=AX.X).then_inc(cv.sem, 1)
        cv.inc()
        vector.wait_ge(cv.sem, cv.v)
        vector.tensor_scalar_max(out=hs[:, pp, :], in0=hs[:, pp, :],
                                 scalar1=1e-24).then_inc(cv.sem, 1)
        cv.inc()
    hs_ready = cv.v

    @block.scalar
    def _(scalar):
        scalar.wait_ge(cv.sem, hs_ready)
        scalar.activation(out=hs[:, pp, :], in_=hs[:, pp, :], func=AF.Ln)
        scalar.activation(out=hs[:, pp, :], in_=hs[:, pp, :], func=AF.Exp,
                          scale=-0.5).then_inc(cs.sem, 1)
        cs.inc()

    @block.vector
    def _(vector):
        vector.wait_ge(cs.sem, cs.v)
        vector.tensor_tensor(
            out=sqt[:, :].rearrange("p (a b) -> p a b", a=16),
            in0=src[:, pp, :].rearrange("p (a b) -> p a b", a=16),
            in1=hs[:, pp, :].rearrange("p a -> p a ()").broadcast_to([128, 16, 64]),
            op=ALU.mult).then_inc(cv.sem, 1)
        cv.inc()
    mult_ready = cv.v

    @block.scalar
    def _(scalar):
        scalar.wait_ge(cv.sem, mult_ready)
        scalar.activation(out=sqt[:, :], in_=sqt[:, :],
                          func=AF.Exp).then_inc(cs.sem, 1)
        cs.inc()

    @block.vector
    def _(vector):
        vector.wait_ge(cs.sem, cs.v)
        vector.tensor_scalar_add(out=sqt[:, :], in0=sqt[:, :],
                                 scalar1=1.0).then_inc(cv.sem, 1)
        cv.inc()
    add_ready = cv.v

    @block.scalar
    def _(scalar):
        scalar.wait_ge(cv.sem, add_ready)
        scalar.activation(out=dst_ap, in_=sqt[:, :],
                          func=AF.Ln).then_inc(cs.sem, 1)
        cs.inc()


def emit_l1_ffn(nc, block, S, resid, ident, epsc, nmagic, dram, dbg_w=None, dbg_h=None):
    dg = [S["dg0"], S["dg1"]]
    dw = [S["dw0"], S["dw1"]]
    dt, dev, dv, ds = S["dt"], S["dev"], S["dv"], S["ds"]
    lnv, lnt, lnev = S["lnv"], S["lnt"], S["lnev"]
    wgc, wuc, wdc = dram["wgc"], dram["wuc"], dram["wdc"]

    with ExitStack() as ctx:
        e = ctx.enter_context
        xqfm = e(nc.sbuf_tensor("xqfmD", [128, NTC, 8, 128], BF16))
        colqx = e(nc.sbuf_tensor("colqxD", [128, 2, NTC], F32))
        xq_tgt = [None] * NTC

        with ExitStack() as p1:
            e1 = p1.enter_context
            lnf = e1(nc.sbuf_tensor("lnfD", [128, 2, D], F32))
            xqb = e1(nc.sbuf_tensor("xqbD", [128, 2, D], BF16))
            tmpc = e1(nc.sbuf_tensor("tmpcD", [128, 1], F32))
            scr = _ln_scratch(nc, e1, "D")
            ptT = e1(nc.psum_tensor("ptTD1", [128, 2, 1024], BF16))

            for tc in range(NTC):
                pp = tc % 2
                emit_ln(block, S, resid, tc, pp, scr, epsc, lnf[:, pp, :],
                        norm_waits=((S["dv"].sem, S["dv"].v),))

                @block.vector
                def _(vector):
                    vector.wait_ge(lnv.sem, lnv.v)
                    vector.reduce_max(out=tmpc[:], in_=lnf[:, pp, :], axis=AX.X,
                                      apply_absolute_value=True).then_inc(dv.sem, 1)
                    dv.inc()
                    vector.wait_ge(dv.sem, dv.v)
                    vector.tensor_scalar_max(out=tmpc[:], in0=tmpc[:],
                                             scalar1=1e-5).then_inc(dv.sem, 1)
                    dv.inc()
                    vector.wait_ge(dv.sem, dv.v)
                    vector.tensor_scalar_mul(out=colqx[:, 1, tc:tc + 1], in0=tmpc[:],
                                             scalar1=1.0 / 127.0)
                    vector.reciprocal(out=tmpc[:], in_=tmpc[:]).then_inc(dv.sem, 1)
                    dv.inc()
                    vector.wait_ge(dv.sem, dv.v)
                    vector.tensor_scalar_mul(out=colqx[:, 0, tc:tc + 1], in0=tmpc[:],
                                             scalar1=127.0).then_inc(dv.sem, 1)
                    dv.inc()
                    vector.wait_ge(dv.sem, dv.v)
                    vector.tensor_scalar(out=lnf[:, pp, :], in0=lnf[:, pp, :],
                                         scalar1=colqx[:, 0, tc:tc + 1], scalar2=MAGIC,
                                         op0=ALU.mult, op1=ALU.add).then_inc(dv.sem, 1)
                    dv.inc()
                    vector.wait_ge(dv.sem, dv.v)
                    vector.tensor_scalar_add(out=xqb[:, pp, :], in0=lnf[:, pp, :],
                                             scalar1=nmagic[:, 0:1]).then_inc(dv.sem, 1)
                    dv.inc()

                emit_transpose_1024(
                    block, S, xqb[:, pp, :], ptT, pp,
                    xqfm[:, tc, :, :],
                    ident, tensor_waits=((dv.sem, dv.v),))
                xq_tgt[tc] = S["lnev"].v

        pass1_dv = S["dv"].v
        pass1_ev = S["lnev"].v

        with ExitStack() as p2:
            e2 = p2.enter_context
            wgb = e2(nc.sbuf_tensor("wgbD", [128, 2, 8, 512], BF16))
            wub = e2(nc.sbuf_tensor("wubD", [128, 2, 8, 512], BF16))
            wdb = e2(nc.sbuf_tensor("wdbD", [128, 2, 1024], BF16))
            gr = e2(nc.sbuf_tensor("grD", [128, 2, 4096], BF16))
            us = e2(nc.sbuf_tensor("usD", [128, 2, 4096], BF16))
            hbuf = e2(nc.sbuf_tensor("hbufD", [128, 4096], F32))
            hqb = e2(nc.sbuf_tensor("hqbD", [128, 4096], BF16))
            hqfm = e2(nc.sbuf_tensor("hqfmD", [128, 2, 32, 128], BF16))
            colqh = e2(nc.sbuf_tensor("colqhD", [128, 4, 8], F32))
            gup = e2(nc.psum_tensor("gupD", [128, 2, 512], F32))
            dnp = e2(nc.psum_tensor("dnpD", [128, 2, 1024], F32))
            ptT2 = e2(nc.psum_tensor("ptTD2", [128, 1024], BF16))

            hq_tgt = [None] * NTC
            hqfm_tgt = [None] * NTC
            devac_tgt = [None] * NTC
            silu_tgt = [None] * NTC
            hqt_lnt_tgt = [None] * NTC
            gu_done = [None] * (NTC // 2)
            uev_prev = [0, 0]
            wg_carry = [0, 0]
            wd_carry = [0, 0]

            def emit_gu(pr):
                wg_tgt = [0] * 8
                for og in range(8):
                    wp = og % 2

                    @block.sync
                    def _(g, og=og, wp=wp):
                        if pr == 0 and og == 0:
                            g.wait_ge(S["cc"].sem, S["agw"]["wu"])
                            g.wait_ge(S["dv"].sem, pass1_dv)
                            g.wait_ge(S["lnev"].sem, pass1_ev)
                        wge(g, dt, wg_tgt[og - 2] if og >= 2 else wg_carry[wp])
                        g.dma_start(out=wgb[:, wp, :, :],
                                    in_=wgc[:, :, og * 512:(og + 1) * 512]
                                    .rearrange("a p b -> p a b")).then_inc(dg[wp].sem, 16)
                        g.dma_start(out=wub[:, wp, :, :],
                                    in_=wuc[:, :, og * 512:(og + 1) * 512]
                                    .rearrange("a p b -> p a b")).then_inc(dg[wp].sem, 16)
                        dg[wp].inc(32)

                    t_g = [0, 0]
                    v_g = [0, 0]
                    t_u = [0, 0]
                    for ti in range(2):
                        tc = 2 * pr + ti

                        @block.tensor
                        def _(tensor, ti=ti, tc=tc, og=og, wp=wp):
                            if ti == 0:
                                tensor.wait_ge(dg[wp].sem, dg[wp].v)
                            if og == 0 and ti == 0:
                                tensor.wait_ge(S["lnev"].sem,
                                               pass1_ev if pr == 0 else
                                               max(xq_tgt[2 * pr], xq_tgt[2 * pr + 1]))
                            wge(tensor, dev, uev_prev[ti])
                            ins = None
                            for ic in range(8):
                                ins = tensor.matmul(gup[:, ti, :],
                                                    xqfm[:, tc, ic, :],
                                                    wgb[:, wp, ic, :],
                                                    start=(ic == 0), stop=(ic == 7))
                            ins.then_inc(dt.sem, 1)
                            dt.inc()
                        t_g[ti] = dt.v

                    for ti in range(2):
                        @block.vector
                        def _(vector, ti=ti, og=og):
                            vector.wait_ge(dt.sem, t_g[ti])
                            if og == 0 and pr > 0:
                                wge(vector, ds, silu_tgt[2 * (pr - 1) + ti])
                            vector.tensor_copy(out=gr[:, ti, og * 512:(og + 1) * 512],
                                               in_=gup[:, ti, :]).then_inc(dev.sem, 1)
                            dev.inc()
                        v_g[ti] = dev.v

                    for ti in range(2):
                        tc = 2 * pr + ti

                        @block.tensor
                        def _(tensor, ti=ti, tc=tc, og=og, wp=wp):
                            wge(tensor, dev, v_g[ti])
                            ins = None
                            for ic in range(8):
                                ins = tensor.matmul(gup[:, ti, :],
                                                    xqfm[:, tc, ic, :],
                                                    wub[:, wp, ic, :],
                                                    start=(ic == 0), stop=(ic == 7))
                            ins.then_inc(dt.sem, 1)
                            dt.inc()
                        t_u[ti] = dt.v

                    for ti in range(2):
                        tc = 2 * pr + ti

                        @block.vector
                        def _(vector, ti=ti, tc=tc, og=og):
                            vector.wait_ge(dt.sem, t_u[ti])
                            vector.tensor_scalar_mul(
                                out=us[:, ti, og * 512:(og + 1) * 512],
                                in0=gup[:, ti, :],
                                scalar1=colqx[:, 1, tc:tc + 1]).then_inc(dev.sem, 1)
                            dev.inc()
                        uev_prev[ti] = dev.v
                    wg_tgt[og] = t_u[1]
                wg_carry[0] = wg_tgt[6]
                wg_carry[1] = wg_tgt[7]
                gu_done[pr] = dev.v

            def emit_hq(tc):
                ti = tc % 2
                cq = tc % 4

                @block.scalar
                def _(scalar):
                    scalar.wait_ge(dev.sem, gu_done[tc // 2])
                    scalar.activation(out=gr[:, ti, :], in_=gr[:, ti, :], func=AF.Silu,
                                      scale=colqx[:, 1, tc:tc + 1]).then_inc(ds.sem, 1)
                    ds.inc()
                silu_tgt[tc] = ds.v

                @block.vector
                def _(vector):
                    vector.wait_ge(ds.sem, silu_tgt[tc])
                    vector.tensor_mul(out=hbuf[:, :], in0=gr[:, ti, :],
                                      in1=us[:, ti, :]).then_inc(dv.sem, 1)
                    dv.inc()
                    vector.wait_ge(dv.sem, dv.v)
                    vector.reduce_max(out=colqh[:, cq, 0:1], in_=hbuf[:, :], axis=AX.X,
                                      apply_absolute_value=True).then_inc(dv.sem, 1)
                    dv.inc()
                    vector.wait_ge(dv.sem, dv.v)
                    vector.tensor_scalar_max(out=colqh[:, cq, 0:1], in0=colqh[:, cq, 0:1],
                                             scalar1=1e-5).then_inc(dv.sem, 1)
                    dv.inc()
                    vector.wait_ge(dv.sem, dv.v)
                    vector.tensor_scalar_mul(out=colqh[:, cq, 2:3], in0=colqh[:, cq, 0:1],
                                             scalar1=1.0 / 127.0)
                    vector.reciprocal(out=colqh[:, cq, 1:2],
                                      in_=colqh[:, cq, 0:1]).then_inc(dv.sem, 1)
                    dv.inc()
                    vector.wait_ge(dv.sem, dv.v)
                    vector.tensor_scalar_mul(out=colqh[:, cq, 3:4], in0=colqh[:, cq, 1:2],
                                             scalar1=127.0).then_inc(dv.sem, 1)
                    dv.inc()
                    vector.wait_ge(dv.sem, dv.v)
                    vector.tensor_scalar(out=hbuf[:, :], in0=hbuf[:, :],
                                         scalar1=colqh[:, cq, 3:4], scalar2=MAGIC,
                                         op0=ALU.mult, op1=ALU.add).then_inc(dv.sem, 1)
                    dv.inc()
                    vector.wait_ge(dv.sem, dv.v)
                    if tc >= 1:
                        wge(vector, lnt, hqt_lnt_tgt[tc - 1])
                    vector.tensor_scalar_add(out=hqb[:, :], in0=hbuf[:, :],
                                             scalar1=nmagic[:, 0:1]).then_inc(dv.sem, 1)
                    dv.inc()
                hq_tgt[tc] = dv.v

            def emit_hqt(tc):
                tp = tc % 2
                for qd in range(4):
                    @block.tensor
                    def _(tensor, qd=qd):
                        if qd == 0:
                            tensor.wait_ge(dv.sem, hq_tgt[tc])
                        wge(tensor, lnev, lnev.v)
                        ins = None
                        for ic in range(8):
                            ins = tensor.transpose(
                                ptT2[:, ic * 128:(ic + 1) * 128],
                                hqb[:, (qd * 8 + ic) * 128:(qd * 8 + ic + 1) * 128],
                                ident[:])
                        ins.then_inc(lnt.sem, 1)
                        lnt.inc()

                    @block.vector
                    def _(vector, qd=qd):
                        vector.wait_ge(lnt.sem, lnt.v)
                        vector.tensor_copy(
                            out=hqfm[:, tp, qd * 8:(qd + 1) * 8, :],
                            in_=ptT2[:, :].rearrange("p (a b) -> p a b", a=8)).then_inc(lnev.sem, 1)
                        lnev.inc()
                hqfm_tgt[tc] = lnev.v
                hqt_lnt_tgt[tc] = lnt.v

            def emit_down(pr):
                dn_tgt = [0] * 32
                for f2c in range(32):
                    wp = f2c % 2

                    @block.gpsimd
                    def _(g, f2c=f2c, wp=wp):
                        if pr == 0 and f2c == 0:
                            g.wait_ge(S["cc"].sem, S["agw"]["wd"])
                        wge(g, dt, dn_tgt[f2c - 2] if f2c >= 2 else wd_carry[wp])
                        g.dma_start(out=wdb[:, wp, :], in_=wdc[f2c]).then_inc(dw[wp].sem, 16)
                        dw[wp].inc(16)

                    for ti in range(2):
                        tc = 2 * pr + ti

                        @block.tensor
                        def _(tensor, ti=ti, tc=tc, f2c=f2c, wp=wp):
                            if ti == 0:
                                tensor.wait_ge(dw[wp].sem, dw[wp].v)
                            if f2c == 0:
                                tensor.wait_ge(lnev.sem, hqfm_tgt[tc])
                                if tc >= 2:
                                    tensor.wait_ge(dv.sem, devac_tgt[tc - 2])
                            ins = None
                            for h in range(2):
                                ins = tensor.matmul(dnp[:, ti, h * 512:(h + 1) * 512],
                                                    hqfm[:, tc % 2, f2c, :],
                                                    wdb[:, wp, h * 512:(h + 1) * 512],
                                                    start=(f2c == 0), stop=(f2c == 31))
                            ins.then_inc(dt.sem, 1)
                            dt.inc()
                        if ti == 1:
                            dn_tgt[f2c] = dt.v
                wd_carry[0] = dn_tgt[30]
                wd_carry[1] = dn_tgt[31]

                for ti in range(2):
                    tc = 2 * pr + ti

                    @block.vector
                    def _(vector, ti=ti, tc=tc):
                        vector.wait_ge(dt.sem, dt.v)
                        vector.scalar_tensor_tensor(out=resid[:, tc, :], in0=dnp[:, ti, :],
                                                    scalar=colqh[:, tc % 4, 2:3],
                                                    in1=resid[:, tc, :],
                                                    op0=ALU.mult, op1=ALU.add).then_inc(dv.sem, 1)
                        dv.inc()
                    devac_tgt[tc] = dv.v

            for pr in range(NTC // 2):
                emit_gu(pr)
                emit_hq(2 * pr)
                if pr >= 1:
                    emit_down(pr - 1)
                emit_hqt(2 * pr)
                emit_hq(2 * pr + 1)
                emit_hqt(2 * pr + 1)
            emit_down(NTC // 2 - 1)


_NC_CACHE = {}


DEBUG = False


def _get_nc(n_phases=4):
    key = (n_phases, DEBUG)
    if key not in _NC_CACHE:
        _NC_CACHE[key] = build_nc(n_phases, dbg=DEBUG)
    return _NC_CACHE[key]


def kernel(L0_attn_x, L0_attn_w_q, L0_attn_w_k, L0_attn_w_v,
           L0_ffn_k0, L0_ffn_k1, L0_ffn_k2, L0_ffn_s0, L0_ffn_s1, L0_ffn_s2,
           L1_attn_W_qkv, L1_ffn_w_gate, L1_ffn_w_up, L1_ffn_w_down,
           _phases=4):
    x = np.ascontiguousarray(np.asarray(L0_attn_x, dtype=np.float32).reshape(T, D))
    wq = np.ascontiguousarray(np.asarray(L0_attn_w_q, np.float32).reshape(8, 128, D))
    wk = np.ascontiguousarray(np.asarray(L0_attn_w_k, np.float32).reshape(8, 128, D))
    wv = np.ascontiguousarray(np.asarray(L0_attn_w_v, np.float32).reshape(8, 128, D))
    k0u = np.asarray(L0_ffn_k0).view(np.uint8)
    k0a = np.ascontiguousarray(k0u[:1024].reshape(8, 128, 4096))
    k0b = np.ascontiguousarray(k0u[1024:].reshape(8, 128, 4096))
    k1u = np.ascontiguousarray(np.asarray(L0_ffn_k1).view(np.uint8).reshape(32, 128, 4096))
    k2u = np.ascontiguousarray(np.asarray(L0_ffn_k2).view(np.uint8).reshape(32, 128, 1024))
    s0 = np.ascontiguousarray(np.broadcast_to(
        np.asarray(L0_ffn_s0, np.float32).reshape(1, 1), (128, 1)))
    s1 = np.ascontiguousarray(np.broadcast_to(
        np.asarray(L0_ffn_s1, np.float32).reshape(1, 1), (128, 1)))
    s2 = np.ascontiguousarray(np.broadcast_to(
        np.asarray(L0_ffn_s2, np.float32).reshape(1, 1), (128, 1)))
    wqkvT = np.ascontiguousarray(
        np.asarray(L1_attn_W_qkv, np.float32).T.reshape(8, 128, 3072))
    wgT = np.ascontiguousarray(
        np.asarray(L1_ffn_w_gate, np.float32).T.reshape(8, 128, 4096))
    wuT = np.ascontiguousarray(
        np.asarray(L1_ffn_w_up, np.float32).T.reshape(8, 128, 4096))
    wdT = np.ascontiguousarray(
        np.asarray(L1_ffn_w_down, np.float32).T.reshape(32, 128, 1024))

    in_maps = []
    for c in range(NCORES):
        sA, sB = c, NSTRIPE - 1 - c
        xs = np.ascontiguousarray(
            np.concatenate([x[sA * 1024:(sA + 1) * 1024],
                            x[sB * 1024:(sB + 1) * 1024]], axis=0)).reshape(NTC, 128, D)
        btab_row = np.empty((2, 8, NSTEP), np.float32)
        for si, stripe in enumerate((sA, sB)):
            for j in range(NSTEP):
                btab_row[si, :, j] = SHIFT if (j == 0 or (j - 1 < stripe)) else CLOSED
        btab = np.ascontiguousarray(
            np.broadcast_to(btab_row.reshape(1, -1), (128, 2 * 8 * NSTEP)))
        in_maps.append({
            "x_in": xs, "btab": btab,
            "w_q": wq, "w_k": wk, "w_v": wv,
            "k0a": np.ascontiguousarray(k0a[c:c + 1]),
            "k0b": np.ascontiguousarray(k0b[c:c + 1]),
            "k1": np.ascontiguousarray(k1u[:, :, c * 512:(c + 1) * 512]),
            "k2": np.ascontiguousarray(k2u[4 * c:4 * c + 4]),
            "s0": s0, "s1": s1, "s2": s2,
            "wqkvT": wqkvT,
            "wgT": np.ascontiguousarray(wgT[c:c + 1]),
            "wuT": np.ascontiguousarray(wuT[c:c + 1]),
            "wdT": np.ascontiguousarray(wdT[4 * c:4 * c + 4]),
        })

    nc = _get_nc(_phases)
    res = run_bass_kernel_spmd(nc, in_maps, core_ids=list(range(NCORES)))

    y = np.empty((T, D), np.float32)
    for c in range(NCORES):
        out = np.asarray(res.results[c]["y_out"]).reshape(TPC, D)
        sA, sB = c, NSTRIPE - 1 - c
        y[sA * 1024:(sA + 1) * 1024] = out[:1024]
        y[sB * 1024:(sB + 1) * 1024] = out[1024:]
    return y.reshape(1, T, D)
